# version 08
# baseline (speedup 1.0000x reference)
import numpy as np
import ml_dtypes
from collections import defaultdict
from contextlib import ExitStack

import concourse.bass as bass
import concourse.mybir as mybir

F32 = mybir.dt.float32
BF16 = mybir.dt.bfloat16
AF = mybir.ActivationFunctionType
ALU = mybir.AluOpType
AX = mybir.AxisListType

D = 1024
NFC = 8
HID = 4096
NHC = 32
NE = 8
LN_EPS = 1e-5

bfc = lambda a: np.ascontiguousarray(np.asarray(a, dtype=np.float32)).astype(ml_dtypes.bfloat16)


ENG_ATTR = {"pe": "tensor", "act": "scalar", "dve": "vector", "pool": "gpsimd", "sp": "sync"}


class Prog:
    def __init__(self, nc):
        self.nc = nc
        self.queues = {e: [] for e in ENG_ATTR}
        self.sem = {}
        for e in ENG_ATTR:
            self.sem[e] = nc.alloc_semaphore(name=f"sem_{e}")
        self.count = defaultdict(int)
        self.last_write = {}
        self.readers = defaultdict(list)
        self.floors = {e: defaultdict(int) for e in ENG_ATTR}
        self.alias = lambda s: s

    def dma_sem(self, stream):
        stream = self.alias(stream)
        name = f"dma_{stream}"
        if name not in self.sem:
            self.sem[name] = self.nc.alloc_semaphore(name=f"sem_{name}")
        return name

    @staticmethod
    def key(r):
        if isinstance(r, tuple):
            return (id(r[0]), r[1])
        return id(r)

    def _wait(self, eng, sem_name, val):
        if val <= 0:
            return
        if eng == 'pe' and sem_name == 'pe':
            return
        if self.floors[eng][sem_name] >= val:
            return
        self.floors[eng][sem_name] = val
        self.queues[eng].append(("wait", self.sem[sem_name], val))

    def op(self, eng, fn, reads=(), writes=(), dma=False, stream=None):
        for r in reads:
            k = self.key(r)
            if k in self.last_write:
                self._wait(eng, *self.last_write[k])
        for w in writes:
            k = self.key(w)
            if k in self.last_write:
                self._wait(eng, *self.last_write[k])
            for rd in self.readers[k]:
                self._wait(eng, *rd)
        if dma:
            if stream is None:
                stream = self.key(writes[0]) if writes else "out"
            sem_name = self.dma_sem(stream)
            amt = 16
        else:
            sem_name = eng
            amt = 1
        self.count[sem_name] += amt
        cnt = self.count[sem_name]
        self.queues[eng].append(("op", fn, self.sem[sem_name], amt))
        done = (sem_name, cnt)
        for w in writes:
            k = self.key(w)
            self.last_write[k] = done
            self.readers[k] = []
        for r in reads:
            self.readers[self.key(r)].append(done)
        return done

    def collective(self, fn, reads=(), writes=(), tag="cc"):
        for r in reads:
            k = self.key(r)
            if k in self.last_write:
                self._wait("pool", *self.last_write[k])
        for w in writes:
            k = self.key(w)
            if k in self.last_write:
                self._wait("pool", *self.last_write[k])
            for rd in self.readers[k]:
                self._wait("pool", *rd)
        name = f"cc_{tag}"
        if name not in self.sem:
            self.sem[name] = self.nc.alloc_semaphore(name=f"sem_{name}")
        self.count[name] += 1
        cnt = self.count[name]
        self.queues["pool"].append(("op", fn, self.sem[name], None))
        done = (name, cnt)
        for w in writes:
            k = self.key(w)
            self.last_write[k] = done
            self.readers[k] = []
        for r in reads:
            self.readers[self.key(r)].append(done)
        return done

    def fence(self, engs=("pe", "act", "dve", "pool", "sp")):
        snap = list(self.count.items())
        for e in engs:
            for name, cnt in snap:
                self._wait(e, name, cnt)

    def finish(self, eng="sp"):
        for name, cnt in sorted(self.count.items(), key=lambda kv: str(kv[0])):
            self._wait(eng, name, cnt)

    def build(self, block):
        for e, attr in ENG_ATTR.items():
            q = self.queues[e]
            if not q:
                continue
            def make(q=q):
                def emit(engine):
                    for item in q:
                        if item[0] == "wait":
                            engine.wait_ge(item[1], item[2])
                        else:
                            inst = item[1](engine)
                            if inst is not None:
                                if item[3] is None:
                                    inst.then_inc(item[2])
                                else:
                                    inst.then_inc(item[2], item[3])
                return emit
            getattr(block, attr)(make())


def mm_group(P, ps, tag, emits):
    n = len(emits)
    for i, (fn, rds) in enumerate(emits):
        start = i == 0
        stop = i == n - 1
        w = []
        if start:
            w.append(ps)
        if stop:
            w = [ps]
        else:
            w.append((ps, tag))
        r = list(rds) + ([(ps, tag)] if not start else [])
        P('pe', (lambda e, fn=fn, start=start, stop=stop: fn(e, start, stop)), reads=r, writes=w)



def make_consts(nc):
    c = {}
    c["id_f32"] = nc.inline_tensor(np.eye(128, dtype=np.float32), "c_id_f32")
    c["id_bf"] = nc.inline_tensor(bfc(np.eye(128)), "c_id_bf")
    c["ones_col_f32"] = nc.inline_tensor(np.ones((128, 1), np.float32), "c_onescol32")
    c["ones_row_f32"] = nc.inline_tensor(np.ones((1, 128), np.float32), "c_onesrow32")
    c["ones_row_bf"] = nc.inline_tensor(bfc(np.ones((1, 128))), "c_onesrow")
    ohr = np.zeros((8, 8 * 128), np.float32)
    for e in range(8):
        ohr[e, e * 128:(e + 1) * 128] = 1.0
    c["ohr_bf"] = nc.inline_tensor(bfc(ohr), "c_ohr")
    c["triu_bf"] = nc.inline_tensor(bfc(np.triu(np.ones((128, 128)))), "c_triu")
    return c


def fm(dram, sl):
    return dram[:, sl].rearrange("(fc p) t -> p fc t", p=128)


def v3(sbuf_ap, nfc=8):
    return sbuf_ap.rearrange("p (fc t) -> p fc t", fc=nfc)



def emit_gating(pr, env, cfg):
    T = cfg["T"]
    P = pr.op
    SCR = env["SCR"]
    WGT = env["WGT"]
    GS = env["GS"]
    CM = env["CM"]
    CMFB = env["CMFB"]
    ID32 = env["ID32"]
    PS4, PS5 = env["PS"][4], env["PS"][3]

    def ldwg(e):
        src = env["wgateT"][:, :].rearrange("(fc p) x -> p fc x", p=128)
        return e.dma_start(out=v3(WGT[:, :]), in_=src)
    P('sp', ldwg, writes=[WGT], dma=True, stream='wgt')

    def _gctx(tg):
        s = tg % 2
        return dict(
            s=s, XG=SCR[:, s * 1024:(s + 1) * 1024], kG=(SCR, ('G', s)),
            LPS=PS4[:, s * 8:(s + 1) * 8], CPS=PS5[0:8, s * 128:(s + 1) * 128],
            g=(lambda a, b, gb=s * 64: GS[:, gb + a: gb + b]),
            kg=(lambda t, s=s: (GS, (s, t))),
            cm=CM[:, s * 8:(s + 1) * 8], kcm=(CM, s),
            qa='sp')

    def _gfront(tg):
        c = _gctx(tg)
        s = c['s']; XG = c['XG']; kG = c['kG']; LPS = c['LPS']
        P(c['qa'], lambda e, tg=tg, XG=XG: e.dma_start(
            out=v3(XG[:, :]), in_=fm(env["xin"], slice(tg * 128, (tg + 1) * 128))),
          writes=[kG], dma=True, stream=f"xg{s}")
        mm_group(P, PS4, f'lg{s}', [
            ((lambda e, start, stop, fc=fc, XG=XG, LPS=LPS: e.matmul(
                LPS, XG[:, fc * 128:(fc + 1) * 128],
                WGT[:, fc * 8:(fc + 1) * 8], start=start, stop=stop)),
             [kG, WGT])
            for fc in range(NFC)])

    def _gback(tg):
        c = _gctx(tg)
        s = c['s']; g = c['g']; kg = c['kg']; cm = c['cm']; kcm = c['kcm']
        LPS = c['LPS']; CPS = c['CPS']
        P('dve', lambda e, g=g, LPS=LPS: e.tensor_copy(g(0, 8), LPS), reads=[PS4], writes=[kg('LG')])
        P('dve', lambda e, g=g: e.max(out=g(8, 16), in_=g(0, 8)), reads=[kg('LG')], writes=[kg('MX')])
        P('dve', lambda e, g=g: e.tensor_single_scalar(g(16, 24), g(0, 8), g(8, 9), ALU.is_ge),
          reads=[kg('LG'), kg('MX')], writes=[kg('E1')])
        P('dve', lambda e, g=g: e.tensor_single_scalar(g(24, 32), g(0, 8), g(9, 10), ALU.is_ge),
          reads=[kg('LG'), kg('MX')], writes=[kg('E2')])
        if cfg["gate_mode"] == "l0":
            P('dve', lambda e, g=g: e.tensor_tensor(g(32, 33), g(8, 9), g(9, 10), ALU.subtract),
              reads=[kg('MX')], writes=[kg('DD')])
            P('act', lambda e, g=g: e.activation(g(33, 34), g(32, 33), AF.Sigmoid),
              reads=[kg('DD')], writes=[kg('G1')])
            P('dve', lambda e, g=g: e.tensor_scalar(g(34, 35), g(33, 34), -1.0, 1.0, ALU.mult, ALU.add),
              reads=[kg('G1')], writes=[kg('G2')])
            P('dve', lambda e, g=g: e.tensor_tensor(g(24, 32), g(24, 32), g(16, 24), ALU.subtract),
              reads=[kg('E1'), kg('E2')], writes=[kg('E2')])
            P('dve', lambda e, g=g: e.tensor_scalar_mul(g(40, 48), g(16, 24), g(33, 34)),
              reads=[kg('E1'), kg('G1')], writes=[kg('T1')])
            P('dve', lambda e, g=g: e.tensor_scalar_mul(g(48, 56), g(24, 32), g(34, 35)),
              reads=[kg('E2'), kg('G2')], writes=[kg('T2')])
            P('dve', lambda e, g=g, cm=cm: e.tensor_tensor(cm, g(40, 48), g(48, 56), ALU.add),
              reads=[kg('T1'), kg('T2')], writes=[kcm])
        else:
            P('dve', lambda e, g=g: e.tensor_scalar_mul(g(32, 33), g(8, 9), -1.0),
              reads=[kg('MX')], writes=[kg('NM')])
            P('act', lambda e, g=g: e.activation(g(40, 48), g(0, 8), AF.Exp, bias=g(32, 33)),
              reads=[kg('LG'), kg('NM')], writes=[kg('EXP')])
            P('dve', lambda e, g=g: e.reduce_sum(g(33, 34), g(40, 48), AX.X),
              reads=[kg('EXP')], writes=[kg('SM')])
            P('dve', lambda e, g=g: e.reciprocal(g(34, 35), g(33, 34)),
              reads=[kg('SM')], writes=[kg('RS')])
            P('dve', lambda e, g=g: e.tensor_scalar_mul(g(40, 48), g(40, 48), g(34, 35)),
              reads=[kg('EXP'), kg('RS')], writes=[kg('EXP')])
            P('dve', lambda e, g=g: e.tensor_tensor(g(48, 56), g(16, 24), g(24, 32), ALU.max),
              reads=[kg('E1'), kg('E2')], writes=[kg('MK')])
            P('dve', lambda e, g=g, cm=cm: e.tensor_tensor(cm, g(40, 48), g(48, 56), ALU.mult),
              reads=[kg('EXP'), kg('MK')], writes=[kcm])
        P('pe', lambda e, cm=cm, CPS=CPS: e.transpose(CPS, cm, ID32[:, :]),
          reads=[kcm, env["ID32"]], writes=[(PS5, s)])
        P('dve', lambda e, tg=tg, CPS=CPS: e.tensor_copy(CMFB[0:8, tg * 128:(tg + 1) * 128], CPS),
          reads=[(PS5, s)], writes=[(CMFB, tg)])

    ng = T // 128
    for tg in range(ng):
        _gfront(tg)
        if tg >= 1:
            _gback(tg - 1)
    _gback(ng - 1)



def emit_moe(pr, env, cfg):
    T = cfg["T"]
    TS = cfg.get("TS", 256)
    nts = T // TS
    ntc = TS // 128
    P = pr.op
    XBF = env["XBF"]; W1SB = env["W1SB"]; W2SB = env["W2SB"]; HSB = env["HSB"]
    YEV = env["YEV"]; CB = env["CB"]; CMFB = env["CMFB"]
    PS = env["PS"]
    HPS = [PS[0][:, 0:TS], PS[1][:, 0:TS]]
    YPS = [PS[2][:, 0:TS], PS[3][:, 0:TS]]
    CBPS = PS[4][:, 0:TS]
    B1SB = env.get("B1SB"); B2SB = env.get("B2SB")
    YACC = env["YACC"]

    TH = T // 2
    nth = nts // 2

    def load_xbf(th):
        for fc in range(NFC):
            P('pool', lambda e, fc=fc, th=th: e.dma_start(
                out=XBF[:, fc * TH:(fc + 1) * TH],
                in_=env["xin"][fc * 128:(fc + 1) * 128, th * TH:(th + 1) * TH]),
              writes=[(XBF, fc)], dma=True, stream=f"xbf{fc}")
    if cfg["has_bias"]:
        P('pool', lambda e: e.dma_start(out=B2SB[:, :], in_=env["b2"][:, :]),
          writes=[B2SB], dma=True, stream="b2")

    def load_w1(ei):
        for fc in range(NFC):
            P('pool', lambda e, ei=ei, fc=fc: e.dma_start(
                out=W1SB[:, fc * HID:(fc + 1) * HID],
                in_=env["W1"][ei, fc * 128:(fc + 1) * 128, :]),
              writes=[(W1SB, fc)], dma=True, stream=f"w1_{fc}")

    def load_w2(ei, deps=()):
        for hg in range(8):
            def ld(e, ei=ei, hg=hg):
                src = env["W2"][ei, hg * 512:(hg + 1) * 512, :].rearrange(
                    "(h p) d -> p h d", p=128)
                dst = W2SB[:, hg * 4096:(hg + 1) * 4096].rearrange("p (h d) -> p h d", h=4)
                return e.dma_start(out=dst, in_=src)
            P('pool', ld, reads=list(deps), writes=[(W2SB, hg)], dma=True,
              stream=f"w2_{hg}")

    def load_b1(ei):
        P('pool', lambda e, ei=ei: e.dma_start(out=B1SB[:, 0:NHC], in_=env["b1r"][ei, :, :]),
          writes=[B1SB], dma=True, stream="b1")

    load_w1(0)
    load_w2(0)
    if cfg["has_bias"]:
        load_b1(0)
    load_xbf(0)
    emit_gating(pr, env, cfg)
    pr.fence()

    prev_th = 0
    for ei in range(NE):
      ths = (0, 1) if ei % 2 == 0 else (1, 0)
      for ti, th in enumerate(ths):
        if th != prev_th:
            load_xbf(th)
            prev_th = th
        for tsi, ts in enumerate(range(th * nth, (th + 1) * nth)):
            t0 = ts * TS
            cmfb_keys = [(CMFB, (t0 // 128) + i) for i in range(ntc)]
            P('pe', lambda e, ei=ei, t0=t0: e.matmul(
                CBPS, env["OHR"][:, ei * 128:(ei + 1) * 128],
                CMFB[0:8, t0:t0 + TS], start=True, stop=True),
              reads=[env["OHR"]] + cmfb_keys, writes=[PS[4]])
            P('dve', lambda e: e.tensor_copy(CB[:, 0:TS], CBPS),
              reads=[PS[4]], writes=[CB])
            for hc in range(NHC):
                hp = HPS[hc % 2]
                hps_full = PS[hc % 2]
                tl = t0 - th * TH
                mm_group(P, hps_full, 'h', [
                    ((lambda e, start, stop, hc=hc, fc=fc, hp=hp, tl=tl: e.matmul(
                        hp, W1SB[:, fc * HID + hc * 128: fc * HID + (hc + 1) * 128],
                        XBF[:, fc * TH + tl: fc * TH + tl + TS], start=start, stop=stop)),
                     [(W1SB, fc), (XBF, fc)])
                    for fc in range(NFC)])
                if cfg["has_act"]:
                    P('act', lambda e, hc=hc, hp=hp: e.activation(
                        HSB[:, hc * TS:(hc + 1) * TS], hp, AF.Gelu, bias=B1SB[:, hc:hc + 1]),
                      reads=[PS[hc % 2], B1SB], writes=[(HSB, hc)])
                else:
                    P('act', lambda e, hc=hc, hp=hp: e.activation(
                        HSB[:, hc * TS:(hc + 1) * TS], hp, AF.Copy),
                      reads=[PS[hc % 2]], writes=[(HSB, hc)])
            if ei > 0 and ti == 0 and tsi == 0:
                load_w2(ei, deps=[(HSB, NHC - 1)])
            for dc in range(NFC):
                yp = YPS[dc % 2]
                yps_full = PS[2 + dc % 2]
                emits = []
                if cfg["has_bias"]:
                    emits.append((
                        (lambda e, start, stop, dc=dc, yp=yp, t0=t0: e.matmul(
                            yp, B2SB[0:8, dc * 128:(dc + 1) * 128],
                            CMFB[0:8, t0:t0 + TS], start=start, stop=stop)),
                        [B2SB] + cmfb_keys))
                for hc in range(NHC):
                    emits.append((
                        (lambda e, start, stop, dc=dc, hc=hc, yp=yp: e.matmul(
                            yp, W2SB[:, hc * D + dc * 128: hc * D + (dc + 1) * 128],
                            HSB[:, hc * TS:(hc + 1) * TS], start=start, stop=stop)),
                        [(W2SB, hc // 4), (HSB, hc)]))
                mm_group(P, yps_full, 'y', emits)
                P('dve', lambda e, dc=dc, yp=yp: e.tensor_tensor(
                    YEV[:, (dc % 4) * TS:(dc % 4 + 1) * TS], yp, CB[:, 0:TS], ALU.mult),
                  reads=[PS[2 + dc % 2], CB], writes=[(YEV, dc % 4)])
                if dc % 4 == 3:
                    half = dc // 4
                    def acc(e, ei=ei, ts=ts, half=half):
                        dst = YACC[half * 512:(half + 1) * 512, ts * TS:(ts + 1) * TS] \
                            .rearrange("(dc p) t -> p dc t", p=128)
                        if ei == 0:
                            return e.dma_start(out=dst, in_=v3(YEV[:, :], nfc=4))
                        return e.dma_start(out=dst, in_=v3(YEV[:, :], nfc=4), accum_op=ALU.add)
                    P('pool', acc,
                      reads=[(YEV, d) for d in range(4)] +
                            ([(YACC, ts)] if (ei or half) else []),
                      writes=[(YACC, ts)], dma=True, stream=f"yacc{ts}")
      if ei + 1 < NE:
        load_w1(ei + 1)
        if cfg["has_bias"]:
            load_b1(ei + 1)

    pr.fence()
    emit_ln_sum(pr, env, cfg, env["xin"], YACC, env["xout"],
                b_deps=[(YACC, ts) for ts in range(nts)])



def emit_ln_sum(pr, env, cfg, a_dram, b_dram, out_dram, b_deps=(), a_deps=(),
                affine=None, out_sbuf=None, sum_dram=None, out_deps_tag=None):
    T = cfg["T"]
    P = pr.op
    SCR = env["SCR"]
    SQ = SCR[:, 4096:5120]
    RW = env["RW"]
    OC32 = env["OC32"]; OR32 = env["OR32"]
    PS = env["PS"]
    W = 128
    assert not (out_dram is not None and out_sbuf is not None)

    def _ctx(tg):
        s = tg % 2
        return dict(
            s=s, sl=slice(tg * W, (tg + 1) * W),
            X=SCR[:, s * 1024:(s + 1) * 1024],
            Bv=SCR[:, 2048 + s * 1024: 3072 + s * 1024],
            kS=(SCR, ('S', s)), kO=(SCR, ('O', s)), kB=(SCR, ('B', s)), kQ=(SCR, 'Q'),
            RPS=PS[6][0:1, s * 128:(s + 1) * 128],
            R2PS=PS[7][0:1, s * 128:(s + 1) * 128],
            BPSF=PS[s], B2PSF=PS[2 + s],
            qa='sp',
            rw=(lambda i, ro=s * 512: RW[0:1, ro + i * 128: ro + i * 128 + W]),
            kr=(lambda i, s=s: (RW, (s, i))))

    def _front(tg):
        c = _ctx(tg)
        s = c['s']; sl = c['sl']; X = c['X']; Bv = c['Bv']
        kS = c['kS']; kO = c['kO']; kB = c['kB']; kQ = c['kQ']
        P(c['qa'], lambda e, sl=sl, X=X: e.dma_start(out=v3(X[:, :]), in_=fm(a_dram, sl)),
          reads=list(a_deps), writes=[kS, kO], dma=True, stream=f"lnx{s}")
        if b_dram is not None:
            P('sp', lambda e, sl=sl, Bv=Bv: e.dma_start(out=v3(Bv[:, :]), in_=fm(b_dram, sl)),
              reads=list(b_deps), writes=[kB], dma=True, stream=f"lny{s}")
            P('dve', lambda e, X=X, Bv=Bv: e.tensor_tensor(X[:, :], X[:, :], Bv[:, :], ALU.add),
              reads=[kS, kB], writes=[kS])
        P('act', lambda e, X=X: e.activation(SQ[:, :], X[:, :], AF.Square),
          reads=[kS], writes=[kQ])
        if sum_dram is not None:
            P('sp', lambda e, sl=sl, X=X: e.dma_start(out=fm(sum_dram, sl), in_=v3(X[:, :])),
              reads=[kS], writes=[(sum_dram, ('sum', tg))], dma=True, stream="lnsum")
        mm_group(P, PS[6], f'r{s}', [
            ((lambda e, start, stop, fc=fc, X=X, RPS=c['RPS']: e.matmul(
                RPS, OC32[:, :], X[:, fc * W:(fc + 1) * W], start=start, stop=stop)),
             [kS, env["OC32"]]) for fc in range(NFC)])
        mm_group(P, PS[7], f'r2{s}', [
            ((lambda e, start, stop, fc=fc, R2PS=c['R2PS']: e.matmul(
                R2PS, OC32[:, :], SQ[:, fc * W:(fc + 1) * W], start=start, stop=stop)),
             [kQ, env["OC32"]]) for fc in range(NFC)])
        rw = c['rw']; kr = c['kr']
        P('dve', lambda e, rw=rw, RPS=c['RPS']: e.tensor_scalar_mul(rw(0), RPS, 1.0 / D),
          reads=[PS[6]], writes=[kr(0)])
        P('dve', lambda e, rw=rw, R2PS=c['R2PS']: e.tensor_scalar_mul(rw(1), R2PS, 1.0 / D),
          reads=[PS[7]], writes=[kr(1)])
        P('dve', lambda e, rw=rw: e.tensor_tensor(rw(2), rw(0), rw(0), ALU.mult),
          reads=[kr(0)], writes=[kr(2)])
        P('dve', lambda e, rw=rw: e.tensor_tensor(rw(1), rw(1), rw(2), ALU.subtract),
          reads=[kr(1), kr(2)], writes=[kr(1)])
        P('dve', lambda e, rw=rw: e.tensor_scalar_add(rw(1), rw(1), LN_EPS),
          reads=[kr(1)], writes=[kr(1)])
        P('act', lambda e, rw=rw: e.activation(rw(2), rw(1), AF.Sqrt),
          reads=[kr(1)], writes=[kr(2)])
        P('dve', lambda e, rw=rw: e.reciprocal(rw(3), rw(2)), reads=[kr(2)], writes=[kr(3)])

    def _back(tg):
        c = _ctx(tg)
        s = c['s']; sl = c['sl']; X = c['X']
        kS = c['kS']; kO = c['kO']
        rw = c['rw']; kr = c['kr']
        BPSF = c['BPSF']; B2PSF = c['B2PSF']
        for q in range(4):
            P('pe', lambda e, q=q, rw=rw, BPSF=BPSF: e.matmul(
                BPSF[:, q * 128:(q + 1) * 128], OR32[0:1, :], rw(0), start=True, stop=True),
              reads=[kr(0), env["OR32"]], writes=[(BPSF, 'b')])
            P('pe', lambda e, q=q, rw=rw, B2PSF=B2PSF: e.matmul(
                B2PSF[:, q * 128:(q + 1) * 128], OR32[0:1, :], rw(3), start=True, stop=True),
              reads=[kr(3), env["OR32"]], writes=[(B2PSF, 'b2')])
        for h in range(2):
            P('dve', lambda e, h=h, X=X, BPSF=BPSF: e.tensor_tensor(
                X[:, h * 512:(h + 1) * 512], X[:, h * 512:(h + 1) * 512],
                BPSF[:, :], ALU.subtract),
              reads=[kS, (BPSF, 'b')], writes=[kS, kO])
        for h in range(2):
            P('dve', lambda e, h=h, X=X, B2PSF=B2PSF: e.tensor_tensor(
                X[:, h * 512:(h + 1) * 512], X[:, h * 512:(h + 1) * 512],
                B2PSF[:, :], ALU.mult),
              reads=[kO, (B2PSF, 'b2')], writes=[kO])
        if out_sbuf is not None:
            XS, XT = out_sbuf
            WT, BT = affine if affine is not None else (None, None)
            for fc in range(NFC):
                if affine is not None:
                    P('dve', lambda e, fc=fc, tg=tg, X=X: e.tensor_scalar(
                        XS[:, fc * XT + tg * W: fc * XT + tg * W + W],
                        X[:, fc * W:(fc + 1) * W],
                        WT[:, fc:fc + 1], BT[:, fc:fc + 1], ALU.mult, ALU.add),
                      reads=[kO, WT, BT], writes=[(XS, (fc, tg * W // 128))])
                else:
                    P('dve', lambda e, fc=fc, tg=tg, X=X: e.tensor_copy(
                        XS[:, fc * XT + tg * W: fc * XT + tg * W + W],
                        X[:, fc * W:(fc + 1) * W]),
                      reads=[kO], writes=[(XS, (fc, tg * W // 128))])
        if out_dram is not None:
            P(c['qa'], lambda e, sl=sl, X=X: e.dma_start(out=fm(out_dram, sl), in_=v3(X[:, :])),
              reads=[kO], writes=[(out_dram, ('ln', tg))], dma=True, stream=f"lnout{s}")

    n = T // W
    for tg in range(n):
        _front(tg)
        if tg >= 1:
            _back(tg - 1)
    _back(n - 1)



def build_moe_launch(T, layer):
    nc = bass.Bass("TRN2", target_bir_lowering=False)
    cfg = dict(T=T, TS=256, gate_mode=("l0" if layer == 0 else "l1"),
               has_act=(layer == 1), has_bias=(layer == 1))
    env = {}
    env["xin"] = nc.dram_tensor("xin", [D, T], F32, kind="ExternalInput").ap()
    env["xout"] = nc.dram_tensor("xout", [D, T], F32, kind="ExternalOutput").ap()
    env["W1"] = nc.dram_tensor("W1", [NE, D, HID], F32, kind="ExternalInput").ap()
    env["W2"] = nc.dram_tensor("W2", [NE, HID, D], F32, kind="ExternalInput").ap()
    env["wgateT"] = nc.dram_tensor("wgateT", [D, NE], F32, kind="ExternalInput").ap()
    if layer == 1:
        env["b1r"] = nc.dram_tensor("b1r", [NE, 128, NHC], F32, kind="ExternalInput").ap()
        env["b2"] = nc.dram_tensor("b2", [NE, D], F32, kind="ExternalInput").ap()
    env["YACC"] = nc.dram_tensor("yacc", [D, T], F32, kind="Internal").ap()
    consts = make_consts(nc)
    TS = cfg["TS"]
    with ExitStack() as stack:
        sb = lambda name, shape, dt=F32: stack.enter_context(nc.sbuf_tensor(name, shape, dt))
        psb = lambda name, shape, dt=F32: stack.enter_context(nc.psum_tensor(name, shape, dt))
        env["XBF"] = sb("XBF", [128, NFC * (T // 2)], BF16)
        env["W1SB"] = sb("W1SB", [128, 32768], BF16)
        env["W2SB"] = sb("W2SB", [128, 32768], BF16)
        env["HSB"] = sb("HSB", [128, NHC * TS], BF16)
        env["YEV"] = sb("YEV", [128, 4 * TS], F32)
        env["CB"] = sb("CBt", [128, TS], BF16)
        env["SCR"] = sb("SCR", [128, 3072], F32)
        env["WGT"] = sb("WGT", [128, NFC * 8], F32)
        env["GS"] = sb("GS", [128, 64], F32)
        env["CM"] = sb("CMtm", [128, 8], F32)
        env["CMFB"] = sb("CMFB", [8, T], BF16)
        env["ID32"] = sb("ID32", [128, 128], F32)
        env["OHR"] = sb("OHR", [8, 8 * 128], BF16)
        env["RW"] = sb("RW", [1, 512], F32)
        env["OC32"] = sb("OC32", [128, 1], F32)
        env["OR32"] = sb("OR32", [1, 128], F32)
        if layer == 1:
            env["B1SB"] = sb("B1SB", [128, NHC], F32)
            env["B2SB"] = sb("B2SB", [8, D], BF16)
        env["PS"] = [psb(f"PS{i}", [128, 512]) for i in range(6)] + \
                    [psb("PS6", [1, 512]), psb("PS7", [1, 512])]
        print("SBUF bytes remaining:", nc.sbuf_bytes_remaining,
              "PSUM banks remaining:", nc.psum_banks_remaining)
        pr = Prog(nc)
        for nm, cn in [("ID32", "id_f32"), ("OHR", "ohr_bf"), ("OC32", "ones_col_f32"),
                       ("OR32", "ones_row_f32")]:
            pr.op('sp', lambda e, nm=nm, cn=cn: e.dma_start(out=env[nm][:, :], in_=consts[cn].ap()),
                  writes=[env[nm]], dma=True)
        emit_moe(pr, env, cfg)
        pr.finish()
        with nc.Block() as block:
            pr.build(block)
    return nc



NH = 16
DH = 64
CC = 128
HIDS = 2730
GAMMAS = [1.0 - 2.0 ** (-5.0 - h) for h in range(NH)]


def make_ret_consts(nc):
    c = {}
    tri = np.triu(np.ones((128, 128), np.float32))
    c["triu4"] = nc.inline_tensor(bfc(np.tile(tri, (1, 4))), "c_triu4")
    decq = np.zeros((128, 8 * 512), np.float32)
    deck = np.zeros((128, 8 * 512), np.float32)
    tr = np.arange(512) % 128
    for dc in range(8):
        for j in range(128):
            h = 2 * dc + j // 64
            decq[j, dc * 512:(dc + 1) * 512] = GAMMAS[h] ** tr * (DH ** -0.5)
            deck[j, dc * 512:(dc + 1) * 512] = GAMMAS[h] ** (-tr)
    c["decq"] = nc.inline_tensor(bfc(decq), "c_decq")
    c["deck"] = nc.inline_tensor(bfc(deck), "c_deck")
    decktm = np.zeros((128, 1024), np.float32)
    for h in range(NH):
        decktm[:, h * 64:(h + 1) * 64] = (np.asarray(GAMMAS[h]) ** (-np.arange(128.0)))[:, None]
    c["decktm"] = nc.inline_tensor(bfc(decktm), "c_decktm")
    gcc = np.zeros((64, 1024), np.float32)
    for h in range(NH):
        gcc[:, h * 64:(h + 1) * 64] = GAMMAS[h] ** CC
    c["gcc"] = nc.inline_tensor(gcc, "c_gcc")
    return c


def emit_proj_fm(pr, env, cfg, wdram, XS, OUT, t0, tw, dec_const=None, qdz=None):
    T = cfg["T"]
    P = pr.op
    WSB = env["PWSB"]
    for fc in range(NFC):
        P('pool', lambda e, fc=fc: e.dma_start(
            out=WSB[:, fc * D:(fc + 1) * D], in_=wdram[fc * 128:(fc + 1) * 128, :]),
          writes=[(WSB, fc)], dma=True, stream=f"pw{fc}")
    for dc in range(NFC):
        for t5 in range(tw // 512):
            psf = env["PS"][t5 % 2]
            ps = psf[:, 0:512]
            xk = [(XS, (fc, (t0 + t5 * 512) // 128 + i)) for fc in range(NFC) for i in range(4)]
            mm_group(P, psf, 'p', [
                ((lambda e, start, stop, fc=fc, dc=dc, t5=t5, ps=ps: e.matmul(
                    ps, WSB[:, fc * D + dc * 128: fc * D + (dc + 1) * 128],
                    XS[:, fc * T + t0 + t5 * 512: fc * T + t0 + (t5 + 1) * 512],
                    start=start, stop=stop)),
                 [(WSB, fc)] + [(XS, (fc, (t0 + t5 * 512) // 128 + i)) for i in range(4)])
                for fc in range(NFC)])
            if qdz is not None:
                QDZE, QDZO = qdz
                P('dve', lambda e, dc=dc, t5=t5: e.tensor_tensor(
                    QDZE[0:64, dc * tw + t5 * 512: dc * tw + (t5 + 1) * 512],
                    psf[0:64, 0:512], dec_const[0:64, dc * 512:(dc + 1) * 512], ALU.mult),
                  reads=[psf, dec_const], writes=[(QDZE, (dc, t5))])
                P('dve', lambda e, dc=dc, t5=t5: e.tensor_tensor(
                    QDZO[64:128, dc * tw + t5 * 512: dc * tw + (t5 + 1) * 512],
                    psf[64:128, 0:512], dec_const[64:128, dc * 512:(dc + 1) * 512], ALU.mult),
                  reads=[psf, dec_const], writes=[(QDZO, (dc, t5))])
            elif dec_const is not None:
                P('dve', lambda e, dc=dc, t5=t5, ps=ps: e.tensor_tensor(
                    OUT[:, dc * tw + t5 * 512: dc * tw + (t5 + 1) * 512], ps,
                    dec_const[:, dc * 512:(dc + 1) * 512], ALU.mult),
                  reads=[psf, dec_const], writes=[(OUT, (dc, t5))])
            else:
                P('dve', lambda e, dc=dc, t5=t5, ps=ps: e.tensor_copy(
                    OUT[:, dc * tw + t5 * 512: dc * tw + (t5 + 1) * 512], ps),
                  reads=[psf], writes=[(OUT, (dc, t5))])


def emit_proj_tm(pr, env, cfg, wdram, XS, OUT, t0, tw, dec_const=None, act=None):
    T = cfg["T"]
    P = pr.op
    WSB = env["PWSB"]
    for fc in range(NFC):
        P('pool', lambda e, fc=fc: e.dma_start(
            out=WSB[:, fc * D:(fc + 1) * D], in_=wdram[fc * 128:(fc + 1) * 128, :]),
          writes=[(WSB, fc)], dma=True, stream=f"pw{fc}")
    for tc in range(tw // 128):
        gtc = (t0 // 128) + tc
        for half in range(2):
            psf = env["PS"][half % 2]
            ps = psf[:, 0:512]
            mm_group(P, psf, 'p', [
                ((lambda e, start, stop, fc=fc, tc=tc, half=half, ps=ps, gtc=gtc: e.matmul(
                    ps, XS[:, fc * T + gtc * 128: fc * T + (gtc + 1) * 128],
                    WSB[:, fc * D + half * 512: fc * D + (half + 1) * 512],
                    start=start, stop=stop)),
                 [(WSB, fc), (XS, (fc, gtc))])
                for fc in range(NFC)])
            if act is not None:
                P('act', lambda e, tc=tc, half=half, ps=ps: e.activation(
                    OUT[:, tc * D + half * 512: tc * D + (half + 1) * 512], ps, act),
                  reads=[psf], writes=[(OUT, (tc, half))])
            elif dec_const is not None:
                P('dve', lambda e, tc=tc, half=half, ps=ps: e.tensor_tensor(
                    OUT[:, tc * D + half * 512: tc * D + (half + 1) * 512], ps,
                    dec_const[:, half * 512:(half + 1) * 512], ALU.mult),
                  reads=[psf, dec_const], writes=[(OUT, (tc, half))])
            else:
                P('dve', lambda e, tc=tc, half=half, ps=ps: e.tensor_copy(
                    OUT[:, tc * D + half * 512: tc * D + (half + 1) * 512], ps),
                  reads=[psf], writes=[(OUT, (tc, half))])


def emit_ret_state_chunks(pr, env, cfg, KT, VT, c0, nch, save):
    P = pr.op
    SAL = env["SAL"]
    GCC = env["GCC"]
    for li in range(nch):
        ci = c0 + li
        if save:
            P('dve', lambda e, li=li: e.tensor_copy(
                env["SCHKB"][0:64, li * 1024:(li + 1) * 1024], SAL[:, :]),
              reads=[SAL], writes=[(env["SCHKB"], li)])
            P('dve', lambda e, li=li: e.tensor_copy(
                env["SCHKB"][64:128, li * 1024:(li + 1) * 1024], SAL[:, :]),
              reads=[SAL], writes=[(env["SCHKB"], (li, 'hi'))])
        for g in range(2):
            psf = env["PS"][2 + g]
            for hh in range(8):
                h = g * 8 + hh
                P('pe', lambda e, li=li, h=h, hh=hh, psf=psf: e.matmul(
                    psf[0:64, hh * 64:(hh + 1) * 64],
                    KT[:, li * D + h * 64: li * D + (h + 1) * 64],
                    VT[:, li * D + h * 64: li * D + (h + 1) * 64],
                    start=True, stop=True),
                  reads=[(KT, (li, h // 8)), (VT, (li, h // 8))], writes=[(psf, hh)])
            P('dve', lambda e, g=g, psf=psf: e.tensor_tensor(
                SAL[:, g * 512:(g + 1) * 512], SAL[:, g * 512:(g + 1) * 512],
                psf[0:64, 0:512], ALU.add),
              reads=[SAL] + [(psf, i) for i in range(8)], writes=[SAL])
        P('dve', lambda e: e.tensor_tensor(SAL[:, :], SAL[:, :], GCC[:, :], ALU.mult),
          reads=[SAL, env["GCC"]], writes=[SAL])


def emit_ret_main_half(pr, env, cfg, QD, KD, KT, VT, GT, t0, tw):
    T = cfg["T"]
    P = pr.op
    TRIU4 = env["TRIU4"]
    PMS = env["PMS"]
    OEV = env["OEV"]
    STT = env["STT"]
    GTD = env["GTD"]
    GFM = env["GFM"]
    HEV = env["HEV"]
    WOSB = env["WOSB"]
    SCHKB = env["SCHKB"]
    TPSB = env["TPSB"]
    PS = env["PS"]
    nch = tw // CC
    for lci in range(nch):
        ci = (t0 // CC) + lci
        if cfg.get("mtrunc", 99) < 1:
            continue
        for hg in range(cfg.get("mhg", 4)):
            psf = PS[hg % 2]
            for hh in range(4):
                h = hg * 4 + hh
                dc = h // 2
                QZ = env["QDZE"] if h % 2 == 0 else env["QDZO"]
                P('pe', lambda e, h=h, hh=hh, dc=dc, lci=lci, psf=psf, QZ=QZ: e.matmul(
                    psf[:, hh * 128:(hh + 1) * 128],
                    KD[:, dc * tw + lci * 128: dc * tw + (lci + 1) * 128],
                    QZ[:, dc * tw + lci * 128: dc * tw + (lci + 1) * 128],
                    start=True, stop=True),
                  reads=[(KD, (dc, (lci * 128) // 512)), (QZ, (dc, (lci * 128) // 512)),
                         (QZ, 'z')],
                  writes=[(psf, hh)])
            if cfg.get("mmask", True):
                P('dve', lambda e, hg=hg, psf=psf: e.tensor_tensor(
                    PMS[:, hg * 512:(hg + 1) * 512], psf[:, 0:512], TRIU4[:, :], ALU.mult),
                  reads=[(psf, 0), (psf, 1), (psf, 2), (psf, 3), env["TRIU4"]],
                  writes=[(PMS, hg)])
            else:
                P('dve', lambda e, hg=hg, psf=psf: e.tensor_copy(
                    PMS[:, hg * 512:(hg + 1) * 512], psf[:, 0:512]),
                  reads=[(psf, 0), (psf, 1), (psf, 2), (psf, 3)],
                  writes=[(PMS, hg)])
        if cfg.get("mtrunc", 99) < 2:
            continue
        for g in range(2):
            psf = PS[2 + g]
            for hh in range(8):
                h = g * 8 + hh
                P('pe', lambda e, h=h, hh=hh, lci=lci, psf=psf: e.matmul(
                    psf[:, hh * 64:(hh + 1) * 64],
                    PMS[:, h * 128:(h + 1) * 128],
                    VT[:, lci * D + h * 64: lci * D + (h + 1) * 64],
                    start=True, stop=False),
                  reads=[(PMS, h // 4), (VT, (lci, h // 8))], writes=[(psf, ('o', hh))])
                dc = h // 2
                QZ = env["QDZE"] if h % 2 == 0 else env["QDZO"]
                P('pe', lambda e, h=h, hh=hh, dc=dc, lci=lci, psf=psf, QZ=QZ: e.matmul(
                    psf[:, hh * 64:(hh + 1) * 64],
                    QZ[:, dc * tw + lci * 128: dc * tw + (lci + 1) * 128],
                    SCHKB[:, lci * 1024 + h * 64: lci * 1024 + (h + 1) * 64],
                    start=False, stop=True),
                  reads=[(QZ, (dc, (lci * 128) // 512)), (QZ, 'z'), (SCHKB, lci),
                         (SCHKB, (lci, 'hi')), (psf, ('o', hh))],
                  writes=[(psf, ('of', hh))])
        for g in range(2):
            psf = PS[2 + g]
            P('dve', lambda e, g=g, psf=psf: e.tensor_copy(
                OEV[:, g * 512:(g + 1) * 512], psf[:, 0:512]),
              reads=[(psf, ('of', hh)) for hh in range(8)], writes=[(OEV, g)])
        if cfg.get("mtrunc", 99) < 3:
            continue
        P('dve', lambda e: e.tensor_reduce(
            STT[:, 0:16], OEV[:, :].rearrange("p (h x) -> p h x", h=16), AX.X, ALU.add),
          reads=[(OEV, 0), (OEV, 1)], writes=[(STT, 'sum')])
        P('act', lambda e: e.activation(env["SQT"][:, :], OEV[:, :], AF.Square),
          reads=[(OEV, 0), (OEV, 1)], writes=[env["SQT"]])
        P('dve', lambda e: e.tensor_reduce(
            STT[:, 16:32], env["SQT"][:, :].rearrange("p (h x) -> p h x", h=16), AX.X, ALU.add),
          reads=[env["SQT"]], writes=[(STT, 'sq'), (STT, 'ex2'), (STT, 'var')])
        P('dve', lambda e: e.tensor_scalar_mul(STT[:, 0:16], STT[:, 0:16], 1.0 / DH),
          reads=[(STT, 'sum')], writes=[(STT, 'mean')])
        P('dve', lambda e: e.tensor_scalar_mul(STT[:, 16:32], STT[:, 16:32], 1.0 / DH),
          reads=[(STT, 'sq')], writes=[(STT, 'ex2')])
        P('dve', lambda e: e.tensor_tensor(STT[:, 32:48], STT[:, 0:16], STT[:, 0:16], ALU.mult),
          reads=[(STT, 'mean')], writes=[(STT, 'm2'), (STT, 'sd')])
        P('dve', lambda e: e.tensor_tensor(STT[:, 16:32], STT[:, 16:32], STT[:, 32:48], ALU.subtract),
          reads=[(STT, 'ex2'), (STT, 'm2')], writes=[(STT, 'var')])
        P('dve', lambda e: e.tensor_scalar_add(STT[:, 16:32], STT[:, 16:32], 1e-6),
          reads=[(STT, 'var')], writes=[(STT, 'var')])
        P('act', lambda e: e.activation(STT[:, 32:48], STT[:, 16:32], AF.Sqrt),
          reads=[(STT, 'var')], writes=[(STT, 'sd')])
        P('dve', lambda e: e.reciprocal(STT[:, 48:64], STT[:, 32:48]),
          reads=[(STT, 'sd')], writes=[(STT, 'rstd')])
        for h in range(NH):
            P('dve', lambda e, h=h: e.tensor_scalar(
                OEV[:, h * 64:(h + 1) * 64], OEV[:, h * 64:(h + 1) * 64],
                STT[:, h:h + 1], STT[:, 48 + h:49 + h], ALU.subtract, ALU.mult),
              reads=[(OEV, h // 8), (STT, 'mean'), (STT, 'rstd')], writes=[(OEV, h // 8)])
        if cfg.get("mtrunc", 99) < 4:
            continue
        P('dve', lambda e, lci=lci: e.tensor_tensor(
            GTD[:, :], OEV[:, :], GT[:, lci * D:(lci + 1) * D], ALU.mult),
          reads=[(OEV, 0), (OEV, 1), (GT, (lci, 0)), (GT, (lci, 1))], writes=[GTD])
        if cfg.get("mtrunc", 99) < 5:
            continue
        for fc in range(NFC):
            P('pe', lambda e, fc=fc: e.transpose(
                TPSB[:, fc * 128:(fc + 1) * 128], GTD[:, fc * 128:(fc + 1) * 128],
                env["IDB"][:, :]),
              reads=[GTD, env["IDB"]], writes=[(TPSB, fc)])
        P('dve', lambda e, lci=lci: e.tensor_copy(
            GFM[:, :].rearrange("p (fc t) -> p fc t", fc=8)[:, :, (lci % 4) * 128:(lci % 4 + 1) * 128],
            TPSB[:, :].rearrange("p (fc t) -> p fc t", fc=8)),
          reads=[(TPSB, fc) for fc in range(NFC)], writes=[(GFM, lci % 4)])
        if lci % 4 == 3:
            g5 = lci // 4
            psf = PS[4]
            for dc in range(NFC):
                mm_group(P, psf, 'wo', [
                    ((lambda e, start, stop, fc=fc, dc=dc: e.matmul(
                        psf[:, 0:512], WOSB[:, fc * D + dc * 128: fc * D + (dc + 1) * 128],
                        GFM[:, fc * 512:(fc + 1) * 512], start=start, stop=stop)),
                     [(WOSB, fc)] + [(GFM, i) for i in range(4)])
                    for fc in range(NFC)])
                P('dve', lambda e, dc=dc, psf=psf: e.tensor_copy(
                    HEV[:, (dc % 2) * 512:(dc % 2) * 512 + 512], psf[:, 0:512]),
                  reads=[psf], writes=[(HEV, dc % 2)])
                def sto(e, t0=t0, g5=g5, dc=dc):
                    sl = slice(t0 + g5 * 512, t0 + (g5 + 1) * 512)
                    return e.dma_start(
                        out=env["HATT"][dc * 128:(dc + 1) * 128, sl],
                        in_=HEV[:, (dc % 2) * 512:(dc % 2) * 512 + 512])
                P('sp', sto, reads=[(HEV, dc % 2)],
                  writes=[(env["HATT"], (t0, g5, dc))], dma=True, stream=f"hatt{dc}")


def emit_soff_init(pr, env, nprev=3):
    P = pr.op
    SAL = env["SAL"]
    SPV = env["SPV"]
    WOF = env["WOF"]
    P('dve', lambda e: e.memset(SAL[:, :], 0.0), writes=[env["SAL"]])
    for j in range(nprev):
        P('sp', lambda e, j=j: e.dma_start(out=SPV[:, :], in_=env["sprev"][j, :, :]),
          writes=[SPV], dma=True, stream="spv")
        P('sp', lambda e, j=j: e.dma_start(out=WOF[:, :], in_=env["woff"][j, :, :]),
          writes=[WOF], dma=True, stream="wof")
        P('dve', lambda e: e.tensor_tensor(SPV[:, :], SPV[:, :], WOF[:, :], ALU.mult),
          reads=[SPV, WOF], writes=[SPV])
        P('dve', lambda e: e.tensor_tensor(SAL[:, :], SAL[:, :], SPV[:, :], ALU.add),
          reads=[SAL, SPV], writes=[SAL])


def alloc_ret_tiles(nc, stack, env, T, tw, with_main):
    sb = lambda name, shape, dt=F32: stack.enter_context(nc.sbuf_tensor(name, shape, dt))
    psb = lambda name, shape, dt=F32: stack.enter_context(nc.psum_tensor(name, shape, dt))
    env["XN"] = sb("XN", [128, NFC * T], BF16)
    env["KT"] = sb("KT", [128, (tw // 128) * D], BF16)
    env["VT"] = sb("VT", [128, (tw // 128) * D], BF16)
    env["PWSB"] = sb("PWSB", [128, NFC * D], BF16)
    env["SAL"] = sb("SAL", [64, 1024], F32)
    env["SPV"] = sb("SPV", [64, 1024], F32)
    env["WOF"] = sb("WOF", [64, 1024], F32)
    env["SCR"] = sb("SCR", [128, 3072], F32)
    env["RW"] = sb("RW", [1, 512], F32)
    env["OC32"] = sb("OC32", [128, 1], F32)
    env["OR32"] = sb("OR32", [1, 128], F32)
    env["LN1W"] = sb("LN1W", [128, 8], F32)
    env["LN1B"] = sb("LN1B", [128, 8], F32)
    env["GCC"] = sb("GCC", [64, 1024], F32)
    if with_main:
        env["QDZE"] = sb("QDZE", [128, NFC * tw], BF16)
        env["QDZO"] = sb("QDZO", [128, NFC * tw], BF16)
        env["KD"] = sb("KD", [128, NFC * tw], BF16)
        env["GT"] = sb("GT", [128, (tw // 128) * D], BF16)
        env["WOSB"] = sb("WOSB", [128, NFC * D], BF16)
        env["SCHKB"] = sb("SCHKB", [128, (tw // CC) * 1024], BF16)
        env["PMS"] = sb("PMS", [128, NH * 128], BF16)
        env["OEV"] = sb("OEV", [128, 1024], F32)
        env["SQT"] = sb("SQT", [128, 1024], F32)
        env["STT"] = sb("STT", [128, 64], F32)
        env["GTD"] = sb("GTD", [128, 1024], BF16)
        env["GFM"] = sb("GFM", [128, NFC * 512], BF16)
        env["HEV"] = sb("HEV", [128, 1024], F32)
        env["TRIU4"] = sb("TRIU4", [128, 512], BF16)
        env["DECQ"] = sb("DECQ", [128, 8 * 512], BF16)
        env["DECK"] = sb("DECK", [128, 8 * 512], BF16)
        env["IDB"] = sb("IDB", [128, 128], BF16)
    env["DECKTM"] = sb("DECKTM", [128, 1024], BF16)
    env["PS"] = [psb(f"PS{i}", [128, 512]) for i in range(5)] + [None] + \
                [psb("PS6", [1, 512]), psb("PS7", [1, 512])]
    if with_main:
        env["TPSB"] = psb("TPSB", [128, 1024], BF16)


def load_ret_consts(pr, env, consts, rconsts, with_main):
    P = pr.op
    loads = [("OC32", consts["ones_col_f32"]), ("OR32", consts["ones_row_f32"]),
             ("GCC", rconsts["gcc"]), ("DECKTM", rconsts["decktm"])]
    if with_main:
        loads += [("TRIU4", rconsts["triu4"]), ("DECQ", rconsts["decq"]),
                  ("DECK", rconsts["deck"]), ("IDB", consts["id_bf"])]
    for nm, h in loads:
        P('sp', lambda e, nm=nm, h=h: e.dma_start(out=env[nm][:, :], in_=h.ap()),
          writes=[env[nm]], dma=True, stream=f"c_{nm}")
    for nm, src in [("LN1W", "ln1w"), ("LN1B", "ln1b")]:
        P('sp', lambda e, nm=nm, src=src: e.dma_start(out=env[nm][:, :], in_=env[src][:, :]),
          writes=[env[nm]], dma=True, stream=f"c_{nm}")


def build_retstate_launch(T):
    nc = bass.Bass("TRN2", target_bir_lowering=False)
    cfg = dict(T=T)
    env = {}
    env["xin"] = nc.dram_tensor("xin", [D, T], F32, kind="ExternalInput").ap()
    env["wk"] = nc.dram_tensor("wk", [D, D], F32, kind="ExternalInput").ap()
    env["wv"] = nc.dram_tensor("wv", [D, D], F32, kind="ExternalInput").ap()
    env["ln1w"] = nc.dram_tensor("ln1w", [128, 8], F32, kind="ExternalInput").ap()
    env["ln1b"] = nc.dram_tensor("ln1b", [128, 8], F32, kind="ExternalInput").ap()
    env["segstate"] = nc.dram_tensor("segstate", [64, 1024], F32, kind="ExternalOutput").ap()
    consts = make_consts(nc)
    rconsts = make_ret_consts(nc)
    tw = 512
    with ExitStack() as stack:
        alloc_ret_tiles(nc, stack, env, T, tw, with_main=False)
        print("C1 SBUF remaining:", nc.sbuf_bytes_remaining)
        pr = Prog(nc)
        load_ret_consts(pr, env, consts, rconsts, with_main=False)
        emit_xn1(pr, env, cfg, out_sbuf=(env["XN"], T))
        pr.fence()
        P = pr.op
        P('dve', lambda e: e.memset(env["SAL"][:, :], 0.0), writes=[env["SAL"]])
        for q in range(T // tw):
            emit_proj_tm(pr, env, cfg, env["wk"], env["XN"], env["KT"], q * tw, tw,
                         dec_const=env["DECKTM"])
            emit_proj_tm(pr, env, cfg, env["wv"], env["XN"], env["VT"], q * tw, tw)
            pr.fence()
            emit_ret_state_chunks(pr, env, cfg, env["KT"], env["VT"], q * (tw // CC),
                                  tw // CC, save=False)
            pr.fence()
        P('sp', lambda e: e.dma_start(out=env["segstate"][:, :], in_=env["SAL"][:, :]),
          reads=[env["SAL"]], dma=True, stream="out")
        pr.finish()
        with nc.Block() as block:
            pr.build(block)
    return nc


def build_l1attn_launch(T, trunc=99, mtrunc=99, mhg=4, mmask=True):
    nc = bass.Bass("TRN2", target_bir_lowering=False)
    cfg = dict(T=T)
    env = {}
    env["xin"] = nc.dram_tensor("xin", [D, T], F32, kind="ExternalInput").ap()
    for w in ["wq", "wk", "wv", "wg", "wo"]:
        env[w] = nc.dram_tensor(w, [D, D], F32, kind="ExternalInput").ap()
    env["ln1w"] = nc.dram_tensor("ln1w", [128, 8], F32, kind="ExternalInput").ap()
    env["ln1b"] = nc.dram_tensor("ln1b", [128, 8], F32, kind="ExternalInput").ap()
    env["sprev"] = nc.dram_tensor("sprev", [3, 64, 1024], F32, kind="ExternalInput").ap()
    env["woff"] = nc.dram_tensor("woff", [3, 64, 1024], F32, kind="ExternalInput").ap()
    env["HATT"] = nc.dram_tensor("hatt", [D, T], F32, kind="ExternalOutput").ap()
    consts = make_consts(nc)
    rconsts = make_ret_consts(nc)
    tw = 512
    with ExitStack() as stack:
        alloc_ret_tiles(nc, stack, env, T, tw, with_main=True)
        print("C2a SBUF remaining:", nc.sbuf_bytes_remaining)
        pr = Prog(nc)
        P = pr.op
        load_ret_consts(pr, env, consts, rconsts, with_main=True)
        for fc in range(NFC):
            P('pool', lambda e, fc=fc: e.dma_start(
                out=env["WOSB"][:, fc * D:(fc + 1) * D], in_=env["wo"][fc * 128:(fc + 1) * 128, :]),
              writes=[(env["WOSB"], fc)], dma=True, stream=f"wo{fc}")
        emit_xn1(pr, env, cfg, out_sbuf=(env["XN"], T))
        emit_soff_init(pr, env)
        pr.fence()
        cfg["mtrunc"] = mtrunc; cfg["mhg"] = mhg; cfg["mmask"] = mmask
        P('dve', lambda e: e.memset(env["QDZE"][64:128, :], 0.0),
          writes=[(env["QDZE"], 'z')])
        P('dve', lambda e: e.memset(env["QDZO"][0:64, :], 0.0),
          writes=[(env["QDZO"], 'z')])
        for q in range(T // tw) if trunc >= 4 else range(1 if trunc >= 2 else 0):
            emit_proj_fm(pr, env, cfg, env["wq"], env["XN"], None, q * tw, tw,
                         dec_const=env["DECQ"], qdz=(env["QDZE"], env["QDZO"]))
            emit_proj_fm(pr, env, cfg, env["wk"], env["XN"], env["KD"], q * tw, tw,
                         dec_const=env["DECK"])
            emit_proj_tm(pr, env, cfg, env["wk"], env["XN"], env["KT"], q * tw, tw,
                         dec_const=env["DECKTM"])
            emit_proj_tm(pr, env, cfg, env["wv"], env["XN"], env["VT"], q * tw, tw)
            emit_proj_tm(pr, env, cfg, env["wg"], env["XN"], env["GT"], q * tw, tw,
                         act=AF.Silu)
            pr.fence()
            if trunc >= 3:
                emit_ret_state_chunks(pr, env, cfg, env["KT"], env["VT"], q * (tw // CC),
                                      tw // CC, save=True)
                pr.fence()
            if trunc >= 4:
                emit_ret_main_half(pr, env, cfg, env["QDZE"], env["KD"], env["KT"], env["VT"],
                                   env["GT"], q * tw, tw)
                pr.fence()
        pr.finish()
        with nc.Block() as block:
            pr.build(block)
    return nc


def emit_xn1(pr, env, cfg, out_sbuf):
    emit_ln_sum(pr, env, cfg, env["xin"], None, None,
                affine=(env["LN1W"], env["LN1B"]), out_sbuf=out_sbuf)



NCC = 22
LASTW = 2730 - 21 * 128


def emit_swiglu(pr, env, cfg):
    T = cfg["T"]
    P = pr.op
    XN2 = env["XN2"]
    WIN = env["WIN"]
    ASB = env["ASB"]; BSB = env["BSB"]
    WO2 = env["WO2"]
    SWEV = env["SWEV"]
    BIN = env["BIN"]
    BOUT = env["BOUT"]
    PS = env["PS"]
    def ldwo2a(e):
        srcw = env["w_out_sw"][0:21 * 128, :].rearrange("(cc p) d -> p cc d", p=128)
        dstw = WO2[:, 0:21 * D].rearrange("p (cc d) -> p cc d", cc=21)
        return e.dma_start(out=dstw, in_=srcw)
    P('pool', ldwo2a, writes=[(WO2, cc) for cc in range(21)], dma=True, stream="wo2a")
    P('pool', lambda e: e.dma_start(
        out=WO2[0:LASTW, 21 * D:22 * D], in_=env["w_out_sw"][21 * 128:2730, :]),
      writes=[(WO2, 21)], dma=True, stream="wo2b")
    for t5 in range(T // 512):
        t0 = t5 * 512
        for part in range(2):
            TGT = ASB if part == 0 else BSB
            for cc in range(NCC):
                cw = LASTW if cc == NCC - 1 else 128
                c0 = part * 2730 + cc * 128
                def ldw(e, c0=c0, cw=cw):
                    src = env["w_in_sw"][:, c0:c0 + cw].rearrange("(fc p) c -> p fc c", p=128)
                    dst = WIN[:, 0:NFC * cw].rearrange("p (fc c) -> p fc c", fc=NFC)
                    return e.dma_start(out=dst, in_=src)
                P('pool', ldw, writes=[WIN], dma=True, stream="win")
                psf = PS[cc % 2]
                ps = psf[0:cw, 0:512]
                mm_group(P, psf, 'sg', [
                    ((lambda e, start, stop, fc=fc, cw=cw, t0=t0, ps=ps: e.matmul(
                        ps, WIN[:, fc * cw:(fc + 1) * cw],
                        XN2[:, fc * T + t0: fc * T + t0 + 512], start=start, stop=stop)),
                     [WIN] + [(XN2, (fc, t0 // 128 + i)) for i in range(4)])
                    for fc in range(NFC)])
                af = AF.Identity if part == 0 else AF.Silu
                P('act', lambda e, cc=cc, cw=cw, part=part, af=af, ps=ps, TGT=TGT: e.activation(
                    TGT[0:cw, cc * 512:(cc + 1) * 512], ps, af,
                    bias=BIN[0:cw, part * NCC + cc: part * NCC + cc + 1]),
                  reads=[psf, BIN], writes=[(TGT, cc)])
        for cc in range(NCC):
            cw = LASTW if cc == NCC - 1 else 128
            P('dve', lambda e, cc=cc, cw=cw: e.tensor_tensor(
                ASB[0:cw, cc * 512:(cc + 1) * 512], ASB[0:cw, cc * 512:(cc + 1) * 512],
                BSB[0:cw, cc * 512:(cc + 1) * 512], ALU.mult),
              reads=[(ASB, cc), (BSB, cc)], writes=[(ASB, cc)])
        for dc in range(NFC):
            psf = PS[2 + dc % 2]
            ps = psf[:, 0:512]
            mm_group(P, psf, 'so', [
                ((lambda e, start, stop, cc=cc, dc=dc, ps=ps: e.matmul(
                    ps, WO2[0:(LASTW if cc == NCC - 1 else 128), cc * D + dc * 128: cc * D + (dc + 1) * 128],
                    ASB[0:(LASTW if cc == NCC - 1 else 128), cc * 512:(cc + 1) * 512],
                    start=start, stop=stop)),
                 [(WO2, cc), (ASB, cc)])
                for cc in range(NCC)])
            P('act', lambda e, dc=dc, ps=ps: e.activation(
                SWEV[:, (dc % 2) * 512:(dc % 2) * 512 + 512], ps, AF.Identity,
                bias=BOUT[:, dc:dc + 1]),
              reads=[psf, BOUT], writes=[(SWEV, dc % 2)])
            def acc(e, t5=t5, dc=dc):
                dst = env["XB"][dc * 128:(dc + 1) * 128, t5 * 512:(t5 + 1) * 512]
                return e.dma_start(out=dst, in_=SWEV[:, (dc % 2) * 512:(dc % 2) * 512 + 512],
                                   accum_op=ALU.add)
            P('pool', acc, reads=[(SWEV, dc % 2), (env["XB"], ('init', t5))],
              writes=[(env["XB"], t5)], dma=True, stream=f"xb{t5}")


def build_swiglu_launch(T):
    nc = bass.Bass("TRN2", target_bir_lowering=False)
    cfg = dict(T=T)
    env = {}
    env["xin"] = nc.dram_tensor("xin", [D, T], F32, kind="ExternalInput").ap()
    env["hatt"] = nc.dram_tensor("hatt", [D, T], F32, kind="ExternalInput").ap()
    env["w_in_sw"] = nc.dram_tensor("w_in_sw", [D, 5460], F32, kind="ExternalInput").ap()
    env["w_out_sw"] = nc.dram_tensor("w_out_sw", [2730, D], F32, kind="ExternalInput").ap()
    env["ln2w"] = nc.dram_tensor("ln2w", [128, 8], F32, kind="ExternalInput").ap()
    env["ln2b"] = nc.dram_tensor("ln2b", [128, 8], F32, kind="ExternalInput").ap()
    env["binr"] = nc.dram_tensor("binr", [2, 128, NCC], F32, kind="ExternalInput").ap()
    env["boutr"] = nc.dram_tensor("boutr", [128, 8], F32, kind="ExternalInput").ap()
    env["xout"] = nc.dram_tensor("xout", [D, T], F32, kind="ExternalOutput").ap()
    env["XA"] = nc.dram_tensor("xa", [D, T], F32, kind="Internal").ap()
    env["XB"] = nc.dram_tensor("xb", [D, T], F32, kind="Internal").ap()
    consts = make_consts(nc)
    with ExitStack() as stack:
        sb = lambda name, shape, dt=F32: stack.enter_context(nc.sbuf_tensor(name, shape, dt))
        psb = lambda name, shape, dt=F32: stack.enter_context(nc.psum_tensor(name, shape, dt))
        env["XN2"] = sb("XN2", [128, NFC * T], BF16)
        env["WIN"] = sb("WIN", [128, NFC * 128], BF16)
        env["ASB"] = sb("ASB", [128, NCC * 512], BF16)
        env["BSB"] = sb("BSB", [128, NCC * 512], BF16)
        env["WO2"] = sb("WO2", [128, NCC * D], BF16)
        env["SWEV"] = sb("SWEV", [128, 1024], F32)
        env["BIN"] = sb("BIN", [128, 2 * NCC], F32)
        env["BOUT"] = sb("BOUT", [128, 8], F32)
        env["SCR"] = sb("SCR", [128, 3072], F32)
        env["RW"] = sb("RW", [1, 512], F32)
        env["OC32"] = sb("OC32", [128, 1], F32)
        env["OR32"] = sb("OR32", [1, 128], F32)
        env["LN2W"] = sb("LN2W", [128, 8], F32)
        env["LN2B"] = sb("LN2B", [128, 8], F32)
        env["PS"] = [psb(f"PS{i}", [128, 512]) for i in range(5)] + [None] + \
                    [psb("PS6", [1, 512]), psb("PS7", [1, 512])]
        print("C2b SBUF remaining:", nc.sbuf_bytes_remaining)
        pr = Prog(nc)
        P = pr.op
        for nm, h in [("OC32", consts["ones_col_f32"]), ("OR32", consts["ones_row_f32"])]:
            P('sp', lambda e, nm=nm, h=h: e.dma_start(out=env[nm][:, :], in_=h.ap()),
              writes=[env[nm]], dma=True, stream=f"c_{nm}")
        for nm, srcn in [("LN2W", "ln2w"), ("LN2B", "ln2b"), ("BOUT", "boutr")]:
            P('sp', lambda e, nm=nm, srcn=srcn: e.dma_start(out=env[nm][:, :], in_=env[srcn][:, :]),
              writes=[env[nm]], dma=True, stream=f"c_{nm}")
        P('sp', lambda e: e.dma_start(out=env["BIN"][:, 0:NCC], in_=env["binr"][0, :, :]),
          writes=[(env["BIN"], 0)], dma=True, stream="c_bin0")
        P('sp', lambda e: e.dma_start(out=env["BIN"][:, NCC:2 * NCC], in_=env["binr"][1, :, :]),
          writes=[(env["BIN"], 1)], dma=True, stream="c_bin1")
        pr.op('dve', lambda e: e.engine_nop(), reads=[(env["BIN"], 0), (env["BIN"], 1)],
              writes=[env["BIN"]])
        emit_ln_sum(pr, env, cfg, env["xin"], env["hatt"], None,
                    affine=(env["LN2W"], env["LN2B"]), out_sbuf=(env["XN2"], T),
                    sum_dram=env["XA"])
        pr.fence()
        for t5 in range(T // 512):
            P('sp', lambda e, t5=t5: e.dma_start(
                out=env["XB"][:, t5 * 512:(t5 + 1) * 512],
                in_=env["XA"][:, t5 * 512:(t5 + 1) * 512]),
              reads=[(env["XA"], ('sum', g)) for g in range(t5 * 4, t5 * 4 + 4)],
              writes=[(env["XB"], ('init', t5))], dma=True, stream=f"xbi{t5}")
        emit_swiglu(pr, env, cfg)
        pr.fence()
        emit_ln_sum(pr, env, cfg, env["xin"], env["XB"], env["xout"],
                    b_deps=[(env["XB"], t5) for t5 in range(T // 512)])
        pr.finish()
        with nc.Block() as block:
            pr.build(block)
    return nc


LVL_BS = [4, 8, 16, 32]
LVL_NN = [8, 4, 2, 1]
LVL_R = [4, 8, 8, 8]
LVL_OFF = [0, 32, 64, 80]
NNR = 88
DHD = 64


def hss_host_embeds(UL, VR, UR, VL):
    EUL = np.zeros((64, NNR), np.float32)
    EVR = np.zeros((64, NNR), np.float32)
    EVL = np.zeros((64, NNR), np.float32)
    EULT = np.zeros((NNR, 64), np.float32)
    EURT = np.zeros((NNR, 64), np.float32)
    for l in range(4):
        bs, nn, r, off = LVL_BS[l], LVL_NN[l], LVL_R[l], LVL_OFF[l]
        for n in range(nn):
            lo = 2 * bs * n
            hi = lo + bs
            cs = off + n * r
            EUL[lo:lo + bs, cs:cs + r] = UL[l]
            EVR[hi:hi + bs, cs:cs + r] = VR[l]
            EVL[lo:lo + bs, cs:cs + r] = VL[l]
            EULT[cs:cs + r, lo:lo + bs] = UL[l].T
            EURT[cs:cs + r, hi:hi + bs] = UR[l].T
    return {"EUL": EUL, "EVR": EVR, "EVL": EVL, "EULT": EULT, "EURT": EURT}


def hss_state_masks():
    mleaf = np.zeros((64, 65), np.float32)
    for n in range(16):
        mleaf[4 * n:4 * n + 4, 4 * n:4 * n + 4] = 1
    mleaf[:, 64] = 1
    mlvl = np.zeros((32, 176), np.float32)
    col = 0
    for l in range(4):
        nn, r = LVL_NN[l], LVL_R[l]
        for rep in range(2):
            for n in range(nn):
                mlvl[n * r:(n + 1) * r, col + n * r: col + (n + 1) * r] = 1
            col += nn * r
    return mleaf, mlvl


LP_OFF = []
_c = 0
for _l in range(4):
    LP_OFF.append((_c, _c + LVL_NN[_l] * LVL_R[_l]))
    _c += 2 * LVL_NN[_l] * LVL_R[_l]


def make_hss_consts(nc):
    c = {}
    el = np.zeros((4, 16 * 64), np.float32)
    for n in range(16):
        for j in range(4):
            el[j, n * 64 + 4 * n + j] = 1.0
    c["eleaf"] = nc.inline_tensor(bfc(el), "c_eleaf")
    mleaf, mlvl = hss_state_masks()
    c["mskleaf"] = nc.inline_tensor(mleaf, "c_mskleaf")
    c["msklvl"] = nc.inline_tensor(mlvl, "c_msklvl")
    return c


def emit_hss_proj(pr, env, cfg):
    T = cfg["T"]
    P = pr.op
    PS = env["PS"]
    SCR = env["SCR"]
    for name, wkey, elu in [("QF", "Wq", True), ("KF", "Wk", True), ("VF", "Wv", False)]:
        OUT = env[name]
        WSB = env["HWSB"]
        for fc in range(NFC):
            P('pool', lambda e, fc=fc, wkey=wkey: e.dma_start(
                out=WSB[:, fc * DHD:(fc + 1) * DHD],
                in_=env[wkey][fc * 128:(fc + 1) * 128, :]),
              writes=[(WSB, fc)], dma=True, stream=f"hw{fc}")
        for t5 in range(T // 512):
            psf = PS[t5 % 2]
            ps = psf[0:64, 0:512]
            mm_group(P, psf, 'hp', [
                ((lambda e, start, stop, fc=fc, t5=t5, ps=ps: e.matmul(
                    ps, WSB[:, fc * DHD:(fc + 1) * DHD],
                    env["XBF"][:, fc * T + t5 * 512: fc * T + (t5 + 1) * 512],
                    start=start, stop=stop)),
                 [(WSB, fc), (env["XBF"], fc)])
                for fc in range(NFC)])
            if elu:
                P('dve', lambda e, ps=ps: e.tensor_scalar_min(SCR[0:64, 0:512], ps, 0.0),
                  reads=[psf], writes=[(SCR, 'e1')])
                P('act', lambda e: e.activation(SCR[0:64, 512:1024], SCR[0:64, 0:512], AF.Exp),
                  reads=[(SCR, 'e1')], writes=[(SCR, 'e2')])
                P('dve', lambda e, ps=ps: e.tensor_scalar_add(SCR[0:64, 1024:1536], ps, 1.0),
                  reads=[psf], writes=[(SCR, 'e3')])
                P('dve', lambda e, t5=t5, OUT=OUT: e.tensor_tensor(
                    OUT[0:64, t5 * 512:(t5 + 1) * 512], SCR[0:64, 1024:1536],
                    SCR[0:64, 512:1024], ALU.max),
                  reads=[(SCR, 'e2'), (SCR, 'e3')], writes=[(OUT, t5)])
            else:
                P('dve', lambda e, t5=t5, ps=ps, OUT=OUT: e.tensor_copy(
                    OUT[0:64, t5 * 512:(t5 + 1) * 512], ps),
                  reads=[psf], writes=[(OUT, t5)])


WOFF_L = [0, 8, 12, 14]


def emit_hss_compact(pr, env, cfg):
    T = cfg["T"]
    P = pr.op
    PS = env["PS"]
    for cname, ename, src, resident, fbslot in [
            ("AQHC", "EVRB", "QF", True, 2), ("AQLC", "EVLB", "QF", True, 3),
            ("AKC", "EULB", "KF", False, 0), ("BVC", "EVRB", "VF", False, 1)]:
        for l in range(4):
            nnr = LVL_NN[l] * LVL_R[l]
            for t5 in range(T // 512):
                psf = PS[t5 % 2]
                ps = psf[0:nnr, 0:512]
                P('pe', lambda e, l=l, t5=t5, ps=ps, ename=ename, src=src, nnr=nnr: e.matmul(
                    ps, env[ename][:, LVL_OFF[l]:LVL_OFF[l] + nnr],
                    env[src][0:64, t5 * 512:(t5 + 1) * 512], start=True, stop=True),
                  reads=[env[ename], (env[src], t5)], writes=[psf])
                if resident:
                    CT = env[cname]
                    P('dve', lambda e, l=l, t5=t5, ps=ps, CT=CT, nnr=nnr: e.tensor_copy(
                        CT[0:nnr, l * T + t5 * 512: l * T + (t5 + 1) * 512], ps),
                      reads=[psf], writes=[(CT, (l, t5))])
                else:
                    CR = env["CROT"]
                    P('dve', lambda e, ps=ps, CR=CR, nnr=nnr: e.tensor_copy(
                        CR[0:nnr, 0:512], ps),
                      reads=[psf], writes=[CR])
                    P('sp', lambda e, l=l, t5=t5, CR=CR, nnr=nnr, fbslot=fbslot: e.dma_start(
                        out=env["FBD"][fbslot, LVL_OFF[l]:LVL_OFF[l] + nnr,
                                       t5 * 512:(t5 + 1) * 512],
                        in_=CR[0:nnr, 0:512]),
                      reads=[CR], writes=[(env["FBD"], (fbslot, l, t5))], dma=True,
                      stream="bounceA")
        if resident:
            CT = env[cname]
            for l in range(4):
                nnr = LVL_NN[l] * LVL_R[l]
                P('sp', lambda e, l=l, CT=CT, nnr=nnr, fbslot=fbslot: e.dma_start(
                    out=env["FBD"][fbslot, LVL_OFF[l]:LVL_OFF[l] + nnr, :],
                    in_=CT[0:nnr, l * T:(l + 1) * T]),
                  reads=[(CT, (l, t5)) for t5 in range(T // 512)],
                  writes=[(env["FBD"], (fbslot, l))], dma=True, stream="bounceB")
    for i, src in enumerate(["KF", "VF", "QF"]):
        P('sp', lambda e, i=i, src=src: e.dma_start(
            out=env["FBL"][i, :, :], in_=env[src][0:64, :]),
          reads=[(env[src], t5) for t5 in range(T // 512)],
          writes=[(env["FBL"], i)], dma=True, stream="bounceB")


def emit_hss_wide_group(pr, env, cfg, g2):
    T = cfg["T"]
    P = pr.op
    GW = 256
    sl = slice(g2 * GW, (g2 + 1) * GW)
    for wname, fbslot in [("WAK", 0), ("WBV", 1), ("WAQH", 2), ("WAQL", 3)]:
        WT = env[wname]
        deps = [(env["FBD"], (fbslot, l)) for l in range(4)] + \
               [(env["FBD"], (fbslot, l, t5)) for l in range(4) for t5 in range(T // 512)]
        def ld0(e, fbslot=fbslot, WT=WT, sl=sl):
            s = env["FBD"][fbslot, 0:32, sl].rearrange("(n r) t -> r n t", r=4)
            d = WT[0:4, 0:8 * GW].rearrange("p (n t) -> p n t", n=8)
            return e.dma_start(out=d, in_=s)
        P('sp', ld0, reads=deps, writes=[(WT, 0)], dma=True, stream=f"w_{wname}0")
        def ld1(e, fbslot=fbslot, WT=WT, sl=sl):
            s = env["FBD"][fbslot, 32:88, sl].rearrange("(n r) t -> r n t", r=8)
            d = WT[0:8, 8 * GW:15 * GW].rearrange("p (n t) -> p n t", n=7)
            return e.dma_start(out=d, in_=s)
        P('sp', ld1, reads=deps, writes=[(WT, 1)], dma=True, stream=f"w_{wname}1")
    for i, wname in [(0, "WKL"), (1, "WVL"), (2, "WQL")]:
        WT = env[wname]
        def ldl(e, i=i, WT=WT, sl=sl):
            s = env["FBL"][i, :, sl].rearrange("(n r) t -> r n t", r=4)
            d = WT[0:4, 0:16 * GW].rearrange("p (n t) -> p n t", n=16)
            return e.dma_start(out=d, in_=s)
        P('sp', ldl, reads=[(env["FBL"], i)], writes=[WT], dma=True, stream=f"w_{wname}")


def hss_pieces():
    ps = []
    for l in range(4):
        for n in range(LVL_NN[l]):
            ps.append(("top", l, n))
            ps.append(("bot", l, n))
    for n in range(16):
        ps.append(("lf", n))
    ps.append(("den",))
    return ps


def emit_hss_chunk(pr, env, cfg, ci, statesonly=False):
    T = cfg["T"]
    P = pr.op
    PS = env["PS"]
    GW = 256
    co = (ci % 2) * 128
    QF, KF, VF = env["QF"], env["KF"], env["VF"]
    t5 = (ci * 128) // 512
    qk = (QF, t5); kk = (KF, t5)

    def P3(eng, fn, reads=(), writes=()):
        P(eng, fn, reads=list(reads) + [(PS[3], 'cl')], writes=list(writes) + [(PS[3], 'cl')])

    def P4(eng, fn, reads=(), writes=()):
        P(eng, fn, reads=list(reads) + [(PS[4], 'cl')], writes=list(writes) + [(PS[4], 'cl')])

    for i, (src, ename, OUTT) in enumerate([("KF", "EULB", "AKTM"), ("VF", "EVRB", "BVTM")]):
        for l in range(4):
            nnr = LVL_NN[l] * LVL_R[l]
            P4('pe', lambda e, l=l, src=src, ename=ename, i=i, nnr=nnr: e.matmul(
                PS[4][:, i * 128 + LVL_OFF[l]: i * 128 + LVL_OFF[l] + nnr],
                env[src][0:64, ci * 128:(ci + 1) * 128],
                env[ename][:, LVL_OFF[l]:LVL_OFF[l] + nnr], start=True, stop=True),
               reads=[(env[src], t5), env[ename]])
        P4('dve', lambda e, i=i, OUTT=OUTT: e.tensor_copy(
            env[OUTT][:, 0:NNR], PS[4][:, i * 128:i * 128 + NNR]),
           writes=[env[OUTT]])
    for src, OUTT in [("KF", "KTM"), ("VF", "VTM")]:
        P('pe', lambda e, src=src: e.transpose(
            env["TPSB"][:, 0:64], env[src][0:64, ci * 128:(ci + 1) * 128],
            env["IDB"][0:64, 0:64]),
          reads=[(env[src], t5), env["IDB"]], writes=[env["TPSB"]])
        P('dve', lambda e, OUTT=OUTT: e.tensor_copy(env[OUTT][:, 0:64], env["TPSB"][:, 0:64]),
          reads=[env["TPSB"]], writes=[env[OUTT]])

    if not statesonly:
        pieces = hss_pieces()

        def piece_score_ops(pc):
            if pc[0] in ("top", "bot"):
                _, l, n = pc
                r = LVL_R[l]
                off = (WOFF_L[l] + n) * GW + co
                wk = (0 if l == 0 else 1)
                if pc[0] == "top":
                    lh, rh = env["WBV"], env["WAQH"]
                else:
                    lh, rh = env["WAK"], env["WAQL"]
                return lh[0:r, off:off + 128], rh[0:r, off:off + 128], [(lh, wk), (rh, wk)]
            if pc[0] == "lf":
                n = pc[1]
                off = n * GW + co
                return (env["WVL"][0:4, off:off + 128], env["WQL"][0:4, off:off + 128],
                        [env["WVL"], env["WQL"]])
            return (KF[0:64, ci * 128:(ci + 1) * 128], QF[0:64, ci * 128:(ci + 1) * 128],
                    [kk, qk])

        for r0 in range(0, len(pieces), 4):
            rnd = pieces[r0:r0 + 4]
            psf = PS[(r0 // 4) % 2]
            for k, pc in enumerate(rnd):
                lhs, rhs, lk = piece_score_ops(pc)
                P('pe', lambda e, lhs=lhs, rhs=rhs, k=k, psf=psf: e.matmul(
                    psf[:, k * 128:(k + 1) * 128], lhs, rhs, start=True, stop=True),
                  reads=lk, writes=[(psf, k)])
            nk = len(rnd)
            P('dve', lambda e, r0=r0, nk=nk, psf=psf: e.tensor_tensor(
                env["PMSC"][:, r0 * 128:(r0 + nk) * 128], psf[:, 0:nk * 128],
                env["TRIU4"][:, 0:nk * 128], ALU.mult),
              reads=[(psf, k) for k in range(nk)] + [env["TRIU4"]],
              writes=[(env["PMSC"], r0 // 4)])

        def piece_um_ops(pc):
            if pc[0] in ("top", "bot"):
                _, l, n = pc
                r = LVL_R[l]
                off = (WOFF_L[l] + n) * GW + co
                wk = (0 if l == 0 else 1)
                eoff = (WOFF_L[l] + n) * 64
                if pc[0] == "top":
                    return (env["WAK"][0:r, off:off + 128],
                            env["EULTW"][0:r, eoff:eoff + 64],
                            [(env["WAK"], wk), env["EULTW"]])
                return (env["WBV"][0:r, off:off + 128],
                        env["EURTW"][0:r, eoff:eoff + 64],
                        [(env["WBV"], wk), env["EURTW"]])
            n = pc[1]
            off = n * GW + co
            return (env["WKL"][0:4, off:off + 128], env["ELEAF"][0:4, n * 64:(n + 1) * 64],
                    [env["WKL"], env["ELEAF"]])

        for r0 in range(0, 46, 8):
            rnd = pieces[r0:min(r0 + 8, 46)]
            for k, pc in enumerate(rnd):
                lhs, rhs, lk = piece_um_ops(pc)
                P3('pe', lambda e, lhs=lhs, rhs=rhs, k=k: e.matmul(
                    PS[3][:, k * 64:(k + 1) * 64], lhs, rhs, start=True, stop=True),
                   reads=lk)
            nk = len(rnd)
            P3('dve', lambda e, r0=r0, nk=nk: e.tensor_copy(
                env["UMS"][:, r0 * 64:(r0 + nk) * 64], PS[3][:, 0:nk * 64]),
               writes=[(env["UMS"], r0 // 8)])

        gspecs = []
        for l in range(4):
            nnr = LVL_NN[l] * LVL_R[l]
            mo, mto = 65 + LP_OFF[l][0], 65 + LP_OFF[l][1]
            for side in range(2):
                gspecs.append((l, nnr, (mto if side == 0 else mo), side))
        for r0 in range(0, 8, 4):
            for k in range(4):
                l, nnr, soff, side = gspecs[r0 + k]
                CQC = env["AQHC"] if side == 0 else env["AQLC"]
                P4('pe', lambda e, l=l, nnr=nnr, soff=soff, CQC=CQC, k=k: e.matmul(
                    PS[4][0:nnr, k * 128:(k + 1) * 128],
                    env["MSTB"][0:nnr, soff:soff + nnr],
                    CQC[0:nnr, l * T + ci * 128: l * T + (ci + 1) * 128],
                    start=True, stop=True),
                   reads=[env["MSTB"], (CQC, (l, t5))])
            P4('dve', lambda e, r0=r0: e.tensor_copy(
                env["GEV"][0:32, r0 * 128:(r0 + 4) * 128], PS[4][0:32, 0:512]),
               writes=[(env["GEV"], r0 // 4)])

        P('pe', lambda e: e.matmul(
            PS[2][0:64, 0:128], env["MSTB"][0:64, 0:64],
            QF[0:64, ci * 128:(ci + 1) * 128], start=True, stop=False),
          reads=[env["MSTB"], qk], writes=[(PS[2], 'y')])
        P3('pe', lambda e: e.matmul(
            PS[3][0:1, 0:128], env["MSTB"][0:64, 64:65],
            QF[0:64, ci * 128:(ci + 1) * 128], start=True, stop=False),
           reads=[env["MSTB"], qk], writes=[(PS[3], 'den')])
        gidx = 0
        for l in range(4):
            nnr = LVL_NN[l] * LVL_R[l]
            for side in range(2):
                ET = env["EULTL"] if side == 0 else env["EURTL"]
                P('pe', lambda e, l=l, nnr=nnr, ET=ET, gidx=gidx: e.matmul(
                    PS[2][0:64, 0:128], ET[0:nnr, l * 64:(l + 1) * 64],
                    env["GEV"][0:nnr, gidx * 128:(gidx + 1) * 128],
                    start=False, stop=False),
                  reads=[ET, (env["GEV"], gidx // 4), (PS[2], 'y')], writes=[(PS[2], 'y')])
                gidx += 1
        for pi, pc in enumerate(pieces):
            if pc[0] == "den":
                P3('pe', lambda e, pi=pi: e.matmul(
                    PS[3][0:1, 0:128], env["ONESB"][:, 0:1],
                    env["PMSC"][:, pi * 128:(pi + 1) * 128], start=False, stop=True),
                   reads=[env["ONESB"], (env["PMSC"], pi // 4), (PS[3], 'den')],
                   writes=[(PS[3], 'den')])
            else:
                stp = pi == len(pieces) - 2
                P('pe', lambda e, pi=pi, stp=stp: e.matmul(
                    PS[2][0:64, 0:128], env["UMS"][:, pi * 64:(pi + 1) * 64],
                    env["PMSC"][:, pi * 128:(pi + 1) * 128], start=False, stop=stp),
                  reads=[(env["UMS"], pi // 8), (env["PMSC"], pi // 4), (PS[2], 'y')],
                  writes=[(PS[2], 'y')])
        P('dve', lambda e, ci=ci: e.tensor_copy(
            env["YSEG"][0:64, ci * 128:(ci + 1) * 128], PS[2][0:64, 0:128]),
          reads=[(PS[2], 'y')], writes=[(env["YSEG"], ci)])
        P3('dve', lambda e, ci=ci: e.tensor_copy(
            env["YSEG"][64:65, ci * 128:(ci + 1) * 128], PS[3][0:1, 0:128]),
           reads=[(PS[3], 'den')], writes=[(env["YSEG"], (ci, 'd'))])

    P3('pe', lambda e: e.matmul(
        PS[3][0:64, 0:64], env["VTM"][:, 0:64], env["KTM"][:, 0:64], start=True, stop=True),
       reads=[env["VTM"], env["KTM"]])
    P3('pe', lambda e: e.matmul(
        PS[3][0:64, 64:65], env["KTM"][:, 0:64], env["ONESB"][:, 0:1], start=True, stop=True),
       reads=[env["KTM"], env["ONESB"]])
    for l in range(4):
        nnr = LVL_NN[l] * LVL_R[l]
        mo, mto = 65 + LP_OFF[l][0], 65 + LP_OFF[l][1]
        P3('pe', lambda e, l=l, nnr=nnr, mo=mo: e.matmul(
            PS[3][0:nnr, mo:mo + nnr], env["AKTM"][:, LVL_OFF[l]:LVL_OFF[l] + nnr],
            env["BVTM"][:, LVL_OFF[l]:LVL_OFF[l] + nnr], start=True, stop=True),
           reads=[env["AKTM"], env["BVTM"]])
        P3('pe', lambda e, l=l, nnr=nnr, mto=mto: e.matmul(
            PS[3][0:nnr, mto:mto + nnr], env["BVTM"][:, LVL_OFF[l]:LVL_OFF[l] + nnr],
            env["AKTM"][:, LVL_OFF[l]:LVL_OFF[l] + nnr], start=True, stop=True),
           reads=[env["AKTM"], env["BVTM"]])
    P3('dve', lambda e: e.tensor_tensor(env["SCR"][0:64, 0:65], PS[3][0:64, 0:65],
                                        env["MSKS"][:, 0:65], ALU.mult),
       reads=[env["MSKS"]], writes=[(env["SCR"], 'st')])
    P3('dve', lambda e: e.tensor_tensor(env["SCR"][0:32, 65:241], PS[3][0:32, 65:241],
                                        env["MSKS"][0:32, 65:241], ALU.mult),
       reads=[env["MSKS"]], writes=[(env["SCR"], 'st2')])
    P('dve', lambda e: e.tensor_tensor(env["MST"][:, 0:65], env["MST"][:, 0:65],
                                       env["SCR"][0:64, 0:65], ALU.add),
      reads=[env["MST"], (env["SCR"], 'st')], writes=[env["MST"]])
    P('dve', lambda e: e.tensor_tensor(env["MST"][0:32, 65:241], env["MST"][0:32, 65:241],
                                       env["SCR"][0:32, 65:241], ALU.add),
      reads=[env["MST"], (env["SCR"], 'st2')], writes=[env["MST"]])
    P('dve', lambda e: e.tensor_copy(env["MSTB"][:, 0:241], env["MST"][:, 0:241]),
      reads=[env["MST"]], writes=[env["MSTB"]])


def emit_hss_final(pr, env, cfg):
    T = cfg["T"]
    P = pr.op
    PS = env["PS"]
    YSEG = env["YSEG"]
    P('dve', lambda e: e.tensor_scalar_max(env["SCR"][0:1, 0:T], YSEG[64:65, 0:T], 1e-6),
      reads=[(YSEG, (ci, 'd')) for ci in range(T // 128)], writes=[(env["SCR"], 'd')])
    P('dve', lambda e: e.reciprocal(env["SCR"][0:1, 0:T], env["SCR"][0:1, 0:T]),
      reads=[(env["SCR"], 'd')], writes=[(env["SCR"], 'r')])
    for t5 in range(T // 512):
        P('pe', lambda e, t5=t5: e.matmul(
            PS[0][0:64, 0:512], env["OR32"][0:1, 0:64],
            env["SCR"][0:1, t5 * 512:(t5 + 1) * 512], start=True, stop=True),
          reads=[(env["SCR"], 'r'), env["OR32"]], writes=[(PS[0], 'b')])
        P('dve', lambda e, t5=t5: e.tensor_tensor(
            env["YN"][0:64, t5 * 512:(t5 + 1) * 512], YSEG[0:64, t5 * 512:(t5 + 1) * 512],
            PS[0][0:64, 0:512], ALU.mult),
          reads=[(PS[0], 'b')] + [(YSEG, ci) for ci in range(t5 * 4, t5 * 4 + 4)],
          writes=[(env["YN"], t5)])
        for dc in range(NFC):
            P('pe', lambda e, t5=t5, dc=dc: e.matmul(
                PS[1][:, 0:512], env["WOSB2"][0:64, dc * 128:(dc + 1) * 128],
                env["YN"][0:64, t5 * 512:(t5 + 1) * 512], start=True, stop=True),
              reads=[env["WOSB2"], (env["YN"], t5)], writes=[(PS[1], 'w')])
            P('dve', lambda e, dc=dc: e.tensor_copy(
                env["HEV"][:, 0:512], PS[1][:, 0:512]),
              reads=[(PS[1], 'w')], writes=[(env["HEV"], 0)])
            P('sp', lambda e, t5=t5, dc=dc: e.dma_start(
                out=env["ATT"][dc * 128:(dc + 1) * 128, t5 * 512:(t5 + 1) * 512],
                in_=env["HEV"][:, 0:512]),
              reads=[(env["HEV"], 0)],
              writes=[(env["ATT"], (t5, dc))], dma=True, stream=f"att{dc}")


def hss_dram_and_inputs(nc, env, T, full):
    env["xin"] = nc.dram_tensor("xin", [D, T], F32, kind="ExternalInput").ap()
    for w in ["Wq", "Wk", "Wv"]:
        env[w] = nc.dram_tensor(w, [D, DHD], F32, kind="ExternalInput").ap()
    env["EULd"] = nc.dram_tensor("EULs", [64, NNR], F32, kind="ExternalInput").ap()
    env["EVRd"] = nc.dram_tensor("EVRs", [64, NNR], F32, kind="ExternalInput").ap()
    env["EVLd"] = nc.dram_tensor("EVLs", [64, NNR], F32, kind="ExternalInput").ap()
    if full:
        env["Wo"] = nc.dram_tensor("Wo", [DHD, D], F32, kind="ExternalInput").ap()
        env["EULTWd"] = nc.dram_tensor("EULTW", [8, 15 * 64], F32, kind="ExternalInput").ap()
        env["EURTWd"] = nc.dram_tensor("EURTW", [8, 15 * 64], F32, kind="ExternalInput").ap()
        env["EULTLd"] = nc.dram_tensor("EULTL", [32, 4 * 64], F32, kind="ExternalInput").ap()
        env["EURTLd"] = nc.dram_tensor("EURTL", [32, 4 * 64], F32, kind="ExternalInput").ap()
        env["sprev"] = nc.dram_tensor("sprev", [3, 64, 241], F32, kind="ExternalInput").ap()
        env["FBD"] = nc.dram_tensor("fbd", [4, NNR, T], BF16, kind="Internal").ap()
        env["FBL"] = nc.dram_tensor("fbl", [3, 64, T], BF16, kind="Internal").ap()
        env["ATT"] = nc.dram_tensor("att", [D, T], F32, kind="Internal").ap()


def alloc_hss_tiles(nc, stack, env, T, full):
    sb = lambda name, shape, dt=F32: stack.enter_context(nc.sbuf_tensor(name, shape, dt))
    psb = lambda name, shape, dt=F32: stack.enter_context(nc.psum_tensor(name, shape, dt))
    GW = 256
    env["XBF"] = sb("XBF", [128, NFC * T], BF16)
    env["QF"] = sb("QF", [64, T], BF16)
    env["KF"] = sb("KF", [64, T], BF16)
    env["VF"] = sb("VF", [64, T], BF16)
    env["HWSB"] = sb("HWSB", [128, NFC * DHD], BF16)
    env["EULB"] = sb("EULB", [64, NNR], BF16)
    env["EVRB"] = sb("EVRB", [64, NNR], BF16)
    env["EVLB"] = sb("EVLB", [64, NNR], BF16)
    env["AKTM"] = sb("AKTM", [128, NNR], BF16)
    env["BVTM"] = sb("BVTM", [128, NNR], BF16)
    env["KTM"] = sb("KTM", [128, 64], BF16)
    env["VTM"] = sb("VTM", [128, 64], BF16)
    env["MST"] = sb("MST", [64, 256], F32)
    env["MSTB"] = sb("MSTB", [64, 256], BF16)
    env["MSKS"] = sb("MSKS", [64, 256], BF16)
    env["ONESB"] = sb("ONESB", [128, 1], BF16)
    env["SCR"] = sb("SCR", [128, 3072], F32)
    env["RW"] = sb("RW", [1, 512], F32)
    env["OC32"] = sb("OC32", [128, 1], F32)
    env["OR32"] = sb("OR32", [1, 128], F32)
    env["IDB"] = sb("IDB", [128, 128], BF16)
    if full:
        env["AQHC"] = sb("AQHC", [32, 4 * T], BF16)
        env["AQLC"] = sb("AQLC", [32, 4 * T], BF16)
        env["CROT"] = sb("CROT", [32, 512], BF16)
        for wname in ["WAK", "WBV", "WAQH", "WAQL"]:
            env[wname] = sb(wname, [8, 15 * GW], BF16)
        for wname in ["WKL", "WVL", "WQL"]:
            env[wname] = sb(wname, [4, 16 * GW], BF16)
        env["PMSC"] = sb("PMSC", [128, 47 * 128], BF16)
        env["UMS"] = sb("UMS", [128, 46 * 64], BF16)
        env["GEV"] = sb("GEV", [32, 8 * 128], BF16)
        env["YSEG"] = sb("YSEG", [65, T], F32)
        env["YN"] = sb("YN", [64, T], BF16)
        env["HEV"] = sb("HEV", [128, 1024], F32)
        env["WOSB2"] = sb("WOSB2", [64, D], BF16)
        env["EULTW"] = sb("EULTW_sb", [8, 15 * 64], BF16)
        env["EURTW"] = sb("EURTW_sb", [8, 15 * 64], BF16)
        env["EULTL"] = sb("EULTL_sb", [32, 4 * 64], BF16)
        env["EURTL"] = sb("EURTL_sb", [32, 4 * 64], BF16)
        env["ELEAF"] = sb("ELEAF", [4, 16 * 64], BF16)
        env["TRIU4"] = sb("TRIU4", [128, 512], BF16)
    env["PS"] = [psb(f"PS{i}", [128, 512]) for i in range(5)] + [None] + \
                [psb("PS6", [1, 512]), psb("PS7", [1, 512])]
    env["TPSB"] = psb("TPSB", [128, 1024], BF16)


def emit_hss_setup(pr, env, cfg, consts, hconsts, full):
    P = pr.op
    T = cfg["T"]
    for nm, h in [("OC32", consts["ones_col_f32"]), ("OR32", consts["ones_row_f32"]),
                  ("IDB", consts["id_bf"]), ("ONESB", consts["ones_row_bf"])]:
        if nm == "ONESB":
            P('pool', lambda e, h=h: e.dma_start(
                out=env["ONESB"][:, 0:1], in_=h.ap().rearrange("a b -> b a")),
              writes=[env["ONESB"]], dma=True, stream="c_onesb")
        else:
            P('sp', lambda e, nm=nm, h=h: e.dma_start(out=env[nm][:, :], in_=h.ap()),
              writes=[env[nm]], dma=True, stream=f"c_{nm}")
    P('sp', lambda e: e.dma_start(out=env["MSKS"][:, 0:241], in_=hconsts["msks"].ap()),
      writes=[env["MSKS"]], dma=True, stream="c_msks")
    for nm, d in [("EULB", "EULd"), ("EVRB", "EVRd"), ("EVLB", "EVLd")]:
        P('pool', lambda e, nm=nm, d=d: e.dma_start(out=env[nm][:, :], in_=env[d][:, :]),
          writes=[env[nm]], dma=True, stream=f"c_{nm}")
    if full:
        for nm, d in [("EULTW", "EULTWd"), ("EURTW", "EURTWd"),
                      ("EULTL", "EULTLd"), ("EURTL", "EURTLd")]:
            P('pool', lambda e, nm=nm, d=d: e.dma_start(out=env[nm][:, :], in_=env[d][:, :]),
              writes=[env[nm]], dma=True, stream=f"c_{nm}")
        P('sp', lambda e: e.dma_start(out=env["ELEAF"][:, :], in_=hconsts["eleaf"].ap()),
          writes=[env["ELEAF"]], dma=True, stream="c_eleaf")
        P('sp', lambda e: e.dma_start(out=env["TRIU4"][:, :], in_=hconsts["triu4b"].ap()),
          writes=[env["TRIU4"]], dma=True, stream="c_triu4")
        P('pool', lambda e: e.dma_start(out=env["WOSB2"][:, :], in_=env["Wo"][:, :]),
          writes=[env["WOSB2"]], dma=True, stream="c_wo")
    for fc in range(NFC):
        P('pool', lambda e, fc=fc: e.dma_start(
            out=env["XBF"][:, fc * T:(fc + 1) * T], in_=env["xin"][fc * 128:(fc + 1) * 128, :]),
          writes=[(env["XBF"], fc)], dma=True, stream=f"xbf{fc}")
    if full:
        P('sp', lambda e: e.dma_start(out=env["SCR"][0:64, 0:241], in_=env["sprev"][0, :, :]),
          writes=[(env["SCR"], 'sp0')], dma=True, stream="sp0")
        P('sp', lambda e: e.dma_start(out=env["SCR"][0:64, 256:497], in_=env["sprev"][1, :, :]),
          writes=[(env["SCR"], 'sp1')], dma=True, stream="sp1")
        P('sp', lambda e: e.dma_start(out=env["SCR"][0:64, 512:753], in_=env["sprev"][2, :, :]),
          writes=[(env["SCR"], 'sp2')], dma=True, stream="sp2")
        P('dve', lambda e: e.tensor_tensor(env["MST"][:, 0:241], env["SCR"][0:64, 0:241],
                                           env["SCR"][0:64, 256:497], ALU.add),
          reads=[(env["SCR"], 'sp0'), (env["SCR"], 'sp1')], writes=[env["MST"]])
        P('dve', lambda e: e.tensor_tensor(env["MST"][:, 0:241], env["MST"][:, 0:241],
                                           env["SCR"][0:64, 512:753], ALU.add),
          reads=[env["MST"], (env["SCR"], 'sp2')], writes=[env["MST"]])
    else:
        P('dve', lambda e: e.memset(env["MST"][:, 0:256], 0.0), writes=[env["MST"]])
    P('dve', lambda e: e.tensor_copy(env["MSTB"][:, 0:241], env["MST"][:, 0:241]),
      reads=[env["MST"]], writes=[env["MSTB"]])
    P('dve', lambda e: e.memset(env["PS"][3][:, 0:512], 0.0), writes=[(env["PS"][3], 'cl')])
    P('dve', lambda e: e.memset(env["PS"][4][:, 0:512], 0.0), writes=[(env["PS"][4], 'cl')])


def make_hss_consts2(nc):
    c = {}
    el = np.zeros((4, 16 * 64), np.float32)
    for n in range(16):
        for j in range(4):
            el[j, n * 64 + 4 * n + j] = 1.0
    c["eleaf"] = nc.inline_tensor(bfc(el), "c_eleaf")
    mleaf, mlvl = hss_state_masks()
    msks = np.zeros((64, 241), np.float32)
    msks[:, 0:65] = mleaf
    msks[0:32, 65:241] = mlvl
    c["msks"] = nc.inline_tensor(bfc(msks), "c_msks")
    tri = np.triu(np.ones((128, 128), np.float32))
    c["triu4b"] = nc.inline_tensor(bfc(np.tile(tri, (1, 4))), "c_triu4b")
    return c


def build_hss_state_launch(T):
    nc = bass.Bass("TRN2", target_bir_lowering=False)
    cfg = dict(T=T)
    env = {}
    hss_dram_and_inputs(nc, env, T, full=False)
    env["stateout"] = nc.dram_tensor("stateout", [64, 241], F32, kind="ExternalOutput").ap()
    consts = make_consts(nc)
    hconsts = make_hss_consts2(nc)
    with ExitStack() as stack:
        alloc_hss_tiles(nc, stack, env, T, full=False)
        print("A1 SBUF remaining:", nc.sbuf_bytes_remaining)
        pr = Prog(nc)
        emit_hss_setup(pr, env, cfg, consts, hconsts, full=False)
        emit_hss_proj(pr, env, cfg)
        pr.fence()
        for ci in range(T // 128):
            emit_hss_chunk(pr, env, cfg, ci, statesonly=True)
        pr.op('sp', lambda e: e.dma_start(out=env["stateout"][:, :], in_=env["MST"][:, 0:241]),
              reads=[env["MST"]], dma=True, stream="out")
        pr.finish()
        with nc.Block() as block:
            pr.build(block)
    return nc


def build_hss_launch(T):
    nc = bass.Bass("TRN2", target_bir_lowering=False)
    cfg = dict(T=T)
    env = {}
    hss_dram_and_inputs(nc, env, T, full=True)
    env["xout"] = nc.dram_tensor("xout", [D, T], F32, kind="ExternalOutput").ap()
    consts = make_consts(nc)
    hconsts = make_hss_consts2(nc)
    with ExitStack() as stack:
        alloc_hss_tiles(nc, stack, env, T, full=True)
        print("A2 SBUF remaining:", nc.sbuf_bytes_remaining)
        pr = Prog(nc)
        emit_hss_setup(pr, env, cfg, consts, hconsts, full=True)
        emit_hss_proj(pr, env, cfg)
        pr.fence()
        emit_hss_compact(pr, env, cfg)
        pr.fence()
        for g2 in range(T // 256):
            emit_hss_wide_group(pr, env, cfg, g2)
            for ci in (g2 * 2, g2 * 2 + 1):
                emit_hss_chunk(pr, env, cfg, ci, statesonly=False)
        pr.fence()
        emit_hss_final(pr, env, cfg)
        pr.fence()
        emit_ln_sum(pr, env, cfg, env["xin"], env["ATT"], env["xout"],
                    b_deps=[(env["ATT"], (t5, dc)) for t5 in range(T // 512) for dc in range(NFC)])
        pr.finish()
        with nc.Block() as block:
            pr.build(block)
    return nc





_ALIAS_RULES = [
    (r"^c_.*$|^c2_.*$", lambda m: "cst"),
    (r"^xbf(\d+)$", lambda m: f"ld{m.group(1)}"),
    (r"^hw(\d+)$", lambda m: f"ld2{m.group(1)}"),
    (r"^w1_(\d+)$", lambda m: f"ld2{m.group(1)}"),
    (r"^w2_(\d+)$", lambda m: f"st{m.group(1)}"),
    (r"^pw(\d+)$", lambda m: f"ld{m.group(1)}"),
    (r"^wo(\d+)$", lambda m: f"ld2{m.group(1)}"),
    (r"^yacc(\d+)$", lambda m: f"acc{m.group(1)}"),
    (r"^att(\d+)$", lambda m: f"acc{m.group(1)}"),
    (r"^hatt(\d+)$", lambda m: f"acc{m.group(1)}"),
    (r"^w_WAK0$", lambda m: "st0"), (r"^w_WAK1$", lambda m: "st1"),
    (r"^w_WBV0$", lambda m: "st2"), (r"^w_WBV1$", lambda m: "st3"),
    (r"^w_WAQH0$", lambda m: "st4"), (r"^w_WAQH1$", lambda m: "st5"),
    (r"^w_WAQL0$", lambda m: "st6"), (r"^w_WAQL1$", lambda m: "st7"),
    (r"^w_WKL$", lambda m: "lnA"), (r"^w_WVL$", lambda m: "lnB"),
    (r"^w_WQL$", lambda m: "lnC"),
    (r"^lnx0$|^lnsum$", lambda m: "lnA"),
    (r"^lnx1$", lambda m: "lnB"),
    (r"^lny0$|^lnout0$", lambda m: "lnC"),
    (r"^lny1$|^lnout1$", lambda m: "lnD"),
    (r"^xg0$", lambda m: "misc2"), (r"^xg1$", lambda m: "misc3"),
    (r"^xbi(\d+)$", lambda m: f"acc{m.group(1)}"),
    (r"^xb(\d+)$", lambda m: f"acc{4 + int(m.group(1))}"),
    (r"^cc1i$|^cc2i$", lambda m: "ccin"),
    (r"^wsg$", lambda m: "misc2"), (r"^wsw$", lambda m: "misc3"),
    (r"^ldg2?$", lambda m: "misc1"),
    (r"^b1$", lambda m: "misc0"), (r"^b2$", lambda m: "misc1"),
    (r"^xg$", lambda m: "misc2"), (r"^wgt$", lambda m: "misc3"),
    (r"^wo2a$", lambda m: "misc0"), (r"^wo2b$", lambda m: "misc1"),
    (r"^win$", lambda m: "misc2"),
    (r"^bounceA$", lambda m: "bounceA"), (r"^bounceB$", lambda m: "bounceB"),
]

import re as _re

def _alias(s):
    for pat, fn in _ALIAS_RULES:
        m = _re.match(pat, s)
        if m:
            return fn(m)
    return s


def _scr_view(scr, off, w):
    return scr[0:128, off:off + w]


class Arena:
    def __init__(self, nc, stack, bf_cols, f32_cols):
        self.bft = stack.enter_context(nc.sbuf_tensor("ARENA_BF", [128, bf_cols], BF16))
        self.f3t = stack.enter_context(nc.sbuf_tensor("ARENA_F3", [128, f32_cols], F32))
        self.bf_cols = bf_cols
        self.f32_cols = f32_cols
        self.hwm_b = 0
        self.hwm_f = 0
        self.reset()

    def reset(self):
        self.bo = 0
        self.fo = 0

    def alloc(self, name, shape, dt=F32):
        p, cols = shape
        if dt == BF16:
            off = (self.bo + 15) // 16 * 16
            self.bo = off + cols
            self.hwm_b = max(self.hwm_b, self.bo)
            assert self.bo <= self.bf_cols, f"bf arena overflow at {name}: {self.bo}"
            return self.bft[0:p, off:off + cols]
        off = (self.fo + 15) // 16 * 16
        self.fo = off + cols
        self.hwm_f = max(self.hwm_f, self.fo)
        assert self.fo <= self.f32_cols, f"f32 arena overflow at {name}: {self.fo}"
        return self.f3t[0:p, off:off + cols]


def alloc_common(nc, stack, env):
    psb = lambda name, shape, dt=F32: stack.enter_context(nc.psum_tensor(name, shape, dt))
    env["PS"] = [psb(f"PS{i}", [128, 512]) for i in range(5)] + [None] + \
                [psb("PS6", [1, 512]), psb("PS7", [1, 512])]
    env["TPSB"] = psb("TPSB", [128, 1024], BF16)
    sb = lambda name, shape, dt=F32: stack.enter_context(nc.sbuf_tensor(name, shape, dt))
    env["OC32"] = sb("OC32", [128, 1], F32)
    env["OR32"] = sb("OR32", [1, 128], F32)
    env["ID32"] = sb("ID32", [128, 128], F32)
    env["IDB"] = sb("IDB", [128, 128], BF16)
    env["ONESB"] = sb("ONESB", [128, 1], BF16)
    env["RW"] = sb("RW", [1, 1024], F32)


def load_common_consts(pr, env, consts):
    P = pr.op
    for nm, h in [("OC32", "ones_col_f32"), ("OR32", "ones_row_f32"),
                  ("ID32", "id_f32"), ("IDB", "id_bf")]:
        P('sp', lambda e, nm=nm, h=h: e.dma_start(out=env[nm][:, :], in_=consts[h].ap()),
          writes=[env[nm]], dma=True, stream=f"c_{nm}")
    P('pool', lambda e: e.dma_start(
        out=env["ONESB"][:, 0:1], in_=consts["ones_row_bf"].ap().rearrange("a b -> b a")),
      writes=[env["ONESB"]], dma=True, stream="c_onesb")


def alloc_hss_arena(sb, env, T, full):
    env["XBF"] = sb("XBF", [128, NFC * T], BF16)
    env["QF"] = sb("QF", [64, T], BF16)
    env["KF"] = sb("KF", [64, T], BF16)
    env["VF"] = sb("VF", [64, T], BF16)
    env["HWSB"] = sb("HWSB", [128, NFC * DHD], BF16)
    env["EULB"] = sb("EULB", [64, NNR], BF16)
    env["EVRB"] = sb("EVRB", [64, NNR], BF16)
    env["EVLB"] = sb("EVLB", [64, NNR], BF16)
    env["AKTM"] = sb("AKTM", [128, NNR], BF16)
    env["BVTM"] = sb("BVTM", [128, NNR], BF16)
    env["KTM"] = sb("KTM", [128, 64], BF16)
    env["VTM"] = sb("VTM", [128, 64], BF16)
    env["MST"] = sb("MST", [64, 256], F32)
    env["MSTB"] = sb("MSTB", [64, 256], BF16)
    env["MSKS"] = sb("MSKS", [64, 256], BF16)
    env["SCR"] = sb("SCR", [128, 5120], F32)
    if full:
        GW = 256
        env["AQHC"] = sb("AQHC", [32, 4 * T], BF16)
        env["AQLC"] = sb("AQLC", [32, 4 * T], BF16)
        env["CROT"] = sb("CROT", [32, 512], BF16)
        for wname in ["WAK", "WBV", "WAQH", "WAQL"]:
            env[wname] = sb(wname, [8, 15 * GW], BF16)
        for wname in ["WKL", "WVL", "WQL"]:
            env[wname] = sb(wname, [4, 16 * GW], BF16)
        env["PMSC"] = sb("PMSC", [128, 47 * 128], BF16)
        env["UMS"] = sb("UMS", [128, 46 * 64], BF16)
        env["GEV"] = sb("GEV", [32, 8 * 128], BF16)
        env["YSEG"] = sb("YSEG", [65, T], F32)
        env["YN"] = sb("YN", [64, T], BF16)
        env["HEV"] = sb("HEV", [128, 512], F32)
        env["WOSB2"] = sb("WOSB2", [64, D], BF16)
        env["EULTW"] = sb("EULTW_sb", [8, 960], BF16)
        env["EURTW"] = sb("EURTW_sb", [8, 960], BF16)
        env["EULTL"] = sb("EULTL_sb", [32, 256], BF16)
        env["EURTL"] = sb("EURTL_sb", [32, 256], BF16)
        env["ELEAF"] = sb("ELEAF", [4, 16 * 64], BF16)
        env["TRIU4"] = sb("TRIU4", [128, 512], BF16)
        env["GATH"] = env["SCR"][0:64, 2048:2304]
        env["HWT"] = env["SCR"][0:64, 2304:2560]


def alloc_moe_arena(env, sb, T, cfg):
    TS = cfg["TS"]
    env["XBF"] = sb("XBF", [128, NFC * (T // 2)], BF16)
    env["W1SB"] = sb("W1SB", [128, 32768], BF16)
    env["W2SB"] = sb("W2SB", [128, 32768], BF16)
    env["HSB"] = sb("HSB", [128, NHC * TS], BF16)
    env["YEV"] = sb("YEV", [128, 4 * TS], F32)
    env["CB"] = sb("CBt", [128, TS], BF16)
    env["SCR"] = sb("SCR", [128, 5120], F32)
    env["WGT"] = sb("WGT", [128, NFC * 8], F32)
    env["GS"] = sb("GS", [128, 128], F32)
    env["CM"] = sb("CMtm", [128, 16], F32)
    env["CMFB"] = sb("CMFB", [8, T], BF16)
    env["OHR"] = sb("OHR", [8, 8 * 128], BF16)
    if cfg["has_bias"]:
        env["B1SB"] = sb("B1SB", [128, NHC], F32)
        env["B2SB"] = sb("B2SB", [8, D], BF16)


def alloc_ret_arena2(sb, env, T, tw, with_main):
    env["XN"] = sb("XN", [128, NFC * T], BF16)
    env["KT"] = sb("KT", [128, (tw // 128) * D], BF16)
    env["VT"] = sb("VT", [128, (tw // 128) * D], BF16)
    env["PWSB"] = sb("PWSB", [128, NFC * D], BF16)
    env["SAL"] = sb("SAL", [64, 1024], F32)
    env["SCR"] = sb("SCR", [128, 5120], F32)
    env["LN1W"] = sb("LN1W", [128, 8], F32)
    env["LN1B"] = sb("LN1B", [128, 8], F32)
    env["GCC"] = sb("GCC", [64, 1024], F32)
    env["DECKTM"] = sb("DECKTM", [128, 1024], BF16)
    if with_main:
        env["QDZE"] = sb("QDZE", [128, NFC * tw], BF16)
        env["QDZO"] = sb("QDZO", [128, NFC * tw], BF16)
        env["KD"] = sb("KD", [128, NFC * tw], BF16)
        env["GT"] = sb("GT", [128, (tw // 128) * D], BF16)
        env["WOSB"] = sb("WOSB", [128, NFC * D], BF16)
        env["SCHKB"] = sb("SCHKB", [128, (tw // CC) * 1024], BF16)
        env["PMS"] = sb("PMS", [128, NH * 128], BF16)
        env["OEV"] = env["SCR"].tensor.ap()[0:128, 1024:2048] if False else None
        env["OEV"] = _scr_view(env["SCR"], 0, 1024)
        env["SQT"] = _scr_view(env["SCR"], 1024, 1024)
        env["STT"] = sb("STT", [128, 64], F32)
        env["GTD"] = sb("GTD", [128, 1024], BF16)
        env["GFM"] = sb("GFM", [128, NFC * 512], BF16)
        env["HEV"] = _scr_view(env["SCR"], 2048, 1024)
        env["TRIU4"] = sb("TRIU4", [128, 512], BF16)
        env["DECQ"] = sb("DECQ", [128, 8 * 512], BF16)
        env["DECK"] = sb("DECK", [128, 8 * 512], BF16)
        env["GB2"] = env["SCR"][0:64, 0:1024]
        env["RWT"] = env["SCR"][0:64, 1024:2048]


def emit_wsum_state(pr, env, ccout, wdram, dst, gbuf, wbuf, width):
    P = pr.op
    P('dve', lambda e: e.memset(dst[:, 0:width], 0.0), writes=[dst])
    for r in range(8):
        P('sp', lambda e, r=r: e.dma_start(
            out=gbuf[:, 0:width], in_=ccout[r * 64:(r + 1) * 64, 0:width]),
          reads=[(ccout, 'd'), (gbuf, 'used')], writes=[gbuf], dma=True, stream="wsg")
        P('sp', lambda e, r=r: e.dma_start(out=wbuf[:, 0:width], in_=wdram[r, :, 0:width]),
          reads=[(wbuf, 'used')], writes=[wbuf], dma=True, stream="wsw")
        P('dve', lambda e: e.tensor_tensor(gbuf[:, 0:width], gbuf[:, 0:width],
                                           wbuf[:, 0:width], ALU.mult),
          reads=[gbuf, wbuf], writes=[gbuf, (gbuf, 'used'), (wbuf, 'used')])
        P('dve', lambda e: e.tensor_tensor(dst[:, 0:width], dst[:, 0:width],
                                           gbuf[:, 0:width], ALU.add),
          reads=[dst, gbuf], writes=[dst])


def build_fused(T):
    nc = bass.Bass("TRN2", target_bir_lowering=False)
    cfg = dict(T=T, TS=256)
    env = {}
    dram = lambda name, shape, dt=F32, kind="ExternalInput": \
        nc.dram_tensor(name, shape, dt, kind=kind).ap()
    env["x0"] = dram("x0", [D, T])
    env["out"] = dram("xfinal", [D, T], kind="ExternalOutput")
    for w in ["Wq", "Wk", "Wv"]:
        env[w] = dram(w, [D, DHD])
    env["Wo"] = dram("Wo", [DHD, D])
    env["EULd"] = dram("EULs", [64, NNR])
    env["EVRd"] = dram("EVRs", [64, NNR])
    env["EVLd"] = dram("EVLs", [64, NNR])
    env["EULTWd"] = dram("EULTW", [8, 960])
    env["EURTWd"] = dram("EURTW", [8, 960])
    env["EULTLd"] = dram("EULTL", [32, 256])
    env["EURTLd"] = dram("EURTL", [32, 256])
    env["hssw"] = dram("hssw", [8, 64, 256])
    env["retw"] = dram("retw", [8, 64, 1024])
    for L in (0, 1):
        env[f"m{L}_W1"] = dram(f"m{L}_W1", [NE, D, HID])
        env[f"m{L}_W2"] = dram(f"m{L}_W2", [NE, HID, D])
        env[f"m{L}_wgateT"] = dram(f"m{L}_wgateT", [D, NE])
    env["b1rd"] = dram("b1r", [NE, 128, NHC])
    env["b2d"] = dram("b2", [NE, D])
    for w in ["wq", "wk", "wv", "wg", "wo"]:
        env["r_" + w] = dram("r_" + w, [D, D])
    for nm in ["ln1wd", "ln1bd", "ln2w", "ln2b", "boutr"]:
        env[nm] = dram(nm, [128, 8])
    env["binr"] = dram("binr", [2, 128, NCC])
    env["w_in_sw"] = dram("w_in_sw", [D, 5460])
    env["w_out_sw"] = dram("w_out_sw", [2730, D])
    env["X1"] = dram("X1", [D, T], kind="Internal")
    env["X2"] = dram("X2", [D, T], kind="Internal")
    env["HATT"] = dram("hatt_s", [D, T], kind="Internal")
    env["YACC"] = dram("yacc_s", [D, T], kind="Internal")
    env["XA"] = dram("xa_s", [D, T], kind="Internal")
    env["XB"] = dram("xb_s", [D, T], kind="Internal")
    env["X3"] = dram("X3", [D, T], kind="Internal")
    env["ATT"] = dram("att_s", [D, T], kind="Internal")
    env["FBD"] = nc.dram_tensor("fbd_s", [4, NNR, T], BF16, kind="Internal").ap()
    env["FBL"] = nc.dram_tensor("fbl_s", [3, 64, T], BF16, kind="Internal").ap()
    env["cc1_in"] = dram("cc1_in", [64, 256], kind="Internal")
    env["cc1_out"] = dram("cc1_out", [512, 256], kind="Internal")
    env["cc2_in"] = dram("cc2_in", [64, 1024], kind="Internal")
    env["cc2_out"] = dram("cc2_out", [512, 1024], kind="Internal")

    consts = make_consts(nc)
    rconsts = make_ret_consts(nc)
    hconsts = make_hss_consts2(nc)

    with ExitStack() as stack:
        alloc_common(nc, stack, env)
        arena = Arena(nc, stack, bf_cols=86400, f32_cols=8000)
        pr = Prog(nc)
        pr.alias = _alias
        P = pr.op
        load_common_consts(pr, dict(env), consts)

        arena.reset()
        sb = arena.alloc
        alloc_hss_arena(sb, env, T, full=True)
        env["xin"] = env["x0"]
        emit_hss_setup(pr, dict(env), cfg, consts, hconsts, full=False)
        emit_hss_proj(pr, dict(env), cfg)
        pr.fence()
        for ci in range(T // 128):
            emit_hss_chunk(pr, dict(env), cfg, ci, statesonly=True)
        _mstA, _mstbA = env["MST"], env["MSTB"]
        P('sp', lambda e, t=_mstA: e.dma_start(out=env["cc1_in"][:, 0:241], in_=t[:, 0:241]),
          reads=[_mstA], writes=[(env["cc1_in"], 'd')], dma=True, stream="cc1i")
        pr.collective(lambda e: e.collective_compute(
            "AllGather", ALU.bypass, replica_groups=[list(range(8))],
            ins=[env["cc1_in"].opt()], outs=[env["cc1_out"].opt()]),
            reads=[(env["cc1_in"], 'd')], writes=[(env["cc1_out"], 'd')], tag="cc1")
        for nm, d in [("EULTW", "EULTWd"), ("EURTW", "EURTWd"),
                      ("EULTL", "EULTLd"), ("EURTL", "EURTLd")]:
            P('pool', lambda e, nm=nm, d=d: e.dma_start(out=env[nm][:, :], in_=env[d][:, :]),
              writes=[env[nm]], dma=True, stream=f"c_{nm}")
        _elf, _tri, _wos = env["ELEAF"], env["TRIU4"], env["WOSB2"]
        P('sp', lambda e, t=_elf: e.dma_start(out=t[:, :], in_=hconsts["eleaf"].ap()),
          writes=[_elf], dma=True, stream="c_eleaf")
        P('sp', lambda e, t=_tri: e.dma_start(out=t[:, :], in_=hconsts["triu4b"].ap()),
          writes=[_tri], dma=True, stream="c_triu4")
        P('pool', lambda e, t=_wos: e.dma_start(out=t[:, :], in_=env["Wo"][:, :]),
          writes=[_wos], dma=True, stream="c_wo")
        emit_hss_compact(pr, dict(env), cfg)
        pr.fence()
        emit_wsum_state(pr, env, env["cc1_out"], env["hssw"], env["MST"],
                        env["GATH"], env["HWT"], 256)
        P('dve', lambda e, a=_mstbA, b=_mstA: e.tensor_copy(a[:, 0:241], b[:, 0:241]),
          reads=[_mstA], writes=[_mstbA])
        P('dve', lambda e: e.memset(env["PS"][3][:, 0:512], 0.0), writes=[(env["PS"][3], 'cl')])
        P('dve', lambda e: e.memset(env["PS"][4][:, 0:512], 0.0), writes=[(env["PS"][4], 'cl')])
        for g2 in range(T // 256):
            emit_hss_wide_group(pr, dict(env), cfg, g2)
            for ci in (g2 * 2, g2 * 2 + 1):
                emit_hss_chunk(pr, dict(env), cfg, ci, statesonly=False)
        pr.fence()
        emit_hss_final(pr, dict(env), cfg)
        pr.fence()
        emit_ln_sum(pr, dict(env), cfg, env["x0"], env["ATT"], env["X1"],
                    b_deps=[(env["ATT"], (t5, dc)) for t5 in range(T // 512) for dc in range(NFC)])
        pr.fence()

        arena.reset()
        cfgB = dict(T=T, TS=256, gate_mode="l0", has_act=False, has_bias=False)
        env["xin"] = env["X1"]
        env["xout"] = env["X2"]
        env["W1"] = env["m0_W1"]; env["W2"] = env["m0_W2"]; env["wgateT"] = env["m0_wgateT"]
        alloc_moe_arena(env, sb, T, cfgB)
        _ohrB = env["OHR"]
        P('sp', lambda e, t=_ohrB: e.dma_start(out=t[:, :], in_=consts["ohr_bf"].ap()),
          writes=[_ohrB], dma=True, stream="c_ohr")
        emit_moe(pr, dict(env), cfgB)
        pr.fence()

        arena.reset()
        env["xin"] = env["X2"]
        env["ln1w"] = env["ln1wd"]; env["ln1b"] = env["ln1bd"]
        alloc_ret_arena2(sb, env, T, 512, with_main=False)
        load_ret_consts(pr, dict(env), consts, rconsts, with_main=False)
        emit_xn1(pr, dict(env), cfg, out_sbuf=(env["XN"], T))
        pr.fence()
        _salC1 = env["SAL"]
        P('dve', lambda e, t=_salC1: e.memset(t[:, :], 0.0), writes=[_salC1])
        for q in range(T // 512):
            emit_proj_tm(pr, dict(env), cfg, env["r_wk"], env["XN"], env["KT"], q * 512, 512,
                         dec_const=env["DECKTM"])
            emit_proj_tm(pr, dict(env), cfg, env["r_wv"], env["XN"], env["VT"], q * 512, 512)
            pr.fence()
            emit_ret_state_chunks(pr, dict(env), cfg, env["KT"], env["VT"], q * 4, 4, save=False)
            pr.fence()
        P('sp', lambda e, t=_salC1: e.dma_start(out=env["cc2_in"][:, :], in_=t[:, :]),
          reads=[_salC1], writes=[(env["cc2_in"], 'd')], dma=True, stream="cc2i")
        pr.collective(lambda e: e.collective_compute(
            "AllGather", ALU.bypass, replica_groups=[list(range(8))],
            ins=[env["cc2_in"].opt()], outs=[env["cc2_out"].opt()]),
            reads=[(env["cc2_in"], 'd')], writes=[(env["cc2_out"], 'd')], tag="cc2")
        pr.fence()

        arena.reset()
        env["xin"] = env["X2"]
        alloc_ret_arena2(sb, env, T, 512, with_main=True)
        load_ret_consts(pr, dict(env), consts, rconsts, with_main=True)
        for fc in range(NFC):
            P('pool', lambda e, fc=fc: e.dma_start(
                out=env["WOSB"][:, fc * D:(fc + 1) * D],
                in_=env["r_wo"][fc * 128:(fc + 1) * 128, :]),
              writes=[(env["WOSB"], fc)], dma=True, stream=f"wo{fc}")
        emit_xn1(pr, dict(env), cfg, out_sbuf=(env["XN"], T))
        pr.fence()
        emit_wsum_state(pr, env, env["cc2_out"], env["retw"], env["SAL"],
                        env["GB2"], env["RWT"], 1024)
        pr.fence()
        _qze, _qzo = env["QDZE"], env["QDZO"]
        P('dve', lambda e, t=_qze: e.memset(t[64:128, :], 0.0), writes=[(_qze, 'z')])
        P('dve', lambda e, t=_qzo: e.memset(t[0:64, :], 0.0), writes=[(_qzo, 'z')])
        for q in range(T // 512):
            emit_proj_fm(pr, dict(env), cfg, env["r_wq"], env["XN"], None, q * 512, 512,
                         dec_const=env["DECQ"], qdz=(env["QDZE"], env["QDZO"]))
            emit_proj_fm(pr, dict(env), cfg, env["r_wk"], env["XN"], env["KD"], q * 512, 512,
                         dec_const=env["DECK"])
            emit_proj_tm(pr, dict(env), cfg, env["r_wk"], env["XN"], env["KT"], q * 512, 512,
                         dec_const=env["DECKTM"])
            emit_proj_tm(pr, dict(env), cfg, env["r_wv"], env["XN"], env["VT"], q * 512, 512)
            emit_proj_tm(pr, dict(env), cfg, env["r_wg"], env["XN"], env["GT"], q * 512, 512,
                         act=AF.Silu)
            pr.fence()
            emit_ret_state_chunks(pr, dict(env), cfg, env["KT"], env["VT"], q * 4, 4, save=True)
            pr.fence()
            emit_ret_main_half(pr, dict(env), cfg, env["QDZE"], env["KD"], env["KT"], env["VT"],
                               env["GT"], q * 512, 512)
            pr.fence()

        arena.reset()
        env["XN2"] = sb("XN2", [128, NFC * T], BF16)
        env["WIN"] = sb("WIN", [128, NFC * 128], BF16)
        env["ASB"] = sb("ASB", [128, NCC * 512], BF16)
        env["BSB"] = sb("BSB", [128, NCC * 512], BF16)
        env["WO2"] = sb("WO2", [128, NCC * D], BF16)
        env["SWEV"] = sb("SWEV", [128, 1024], F32)
        env["BIN"] = sb("BIN", [128, 2 * NCC], F32)
        env["BOUT"] = sb("BOUT", [128, 8], F32)
        env["SCR"] = sb("SCR", [128, 5120], F32)
        env["LN2W"] = sb("LN2W", [128, 8], F32)
        env["LN2B"] = sb("LN2B", [128, 8], F32)
        for nm, srcn in [("LN2W", "ln2w"), ("LN2B", "ln2b"), ("BOUT", "boutr")]:
            P('sp', lambda e, nm=nm, srcn=srcn: e.dma_start(
                out=env[nm][:, :], in_=env[srcn][:, :]),
              writes=[env[nm]], dma=True, stream=f"c2_{nm}")
        P('sp', lambda e: e.dma_start(out=env["BIN"][:, 0:NCC], in_=env["binr"][0, :, :]),
          writes=[(env["BIN"], 0)], dma=True, stream="c_bin0")
        P('sp', lambda e: e.dma_start(out=env["BIN"][:, NCC:2 * NCC], in_=env["binr"][1, :, :]),
          writes=[(env["BIN"], 1)], dma=True, stream="c_bin1")
        pr.op('dve', lambda e: e.engine_nop(), reads=[(env["BIN"], 0), (env["BIN"], 1)],
              writes=[env["BIN"]])
        pr.fence()
        hdeps = [(env["HATT"], (t0, 0, dc)) for t0 in range(0, T, 512) for dc in range(NFC)]
        emit_ln_sum(pr, dict(env), cfg, env["X2"], env["HATT"], None,
                    affine=(env["LN2W"], env["LN2B"]), out_sbuf=(env["XN2"], T),
                    sum_dram=env["XA"], b_deps=hdeps)
        pr.fence()
        for t5 in range(T // 512):
            P('sp', lambda e, t5=t5: e.dma_start(
                out=env["XB"][:, t5 * 512:(t5 + 1) * 512],
                in_=env["XA"][:, t5 * 512:(t5 + 1) * 512]),
              reads=[(env["XA"], ('sum', g)) for g in range(t5 * 4, t5 * 4 + 4)],
              writes=[(env["XB"], ('init', t5))], dma=True, stream=f"xbi{t5}")
        env["xin"] = env["X2"]
        emit_swiglu(pr, dict(env), cfg)
        pr.fence()
        emit_ln_sum(pr, dict(env), cfg, env["X2"], env["XB"], env["X3"],
                    b_deps=[(env["XB"], t5) for t5 in range(T // 512)])
        pr.fence()

        arena.reset()
        cfgD = dict(T=T, TS=256, gate_mode="l1", has_act=True, has_bias=True)
        env["xin"] = env["X3"]
        env["xout"] = env["out"]
        env["W1"] = env["m1_W1"]; env["W2"] = env["m1_W2"]; env["wgateT"] = env["m1_wgateT"]
        env["b1r"] = env["b1rd"]; env["b2"] = env["b2d"]
        alloc_moe_arena(env, sb, T, cfgD)
        _ohrD = env["OHR"]
        P('sp', lambda e, t=_ohrD: e.dma_start(out=t[:, :], in_=consts["ohr_bf"].ap()),
          writes=[_ohrD], dma=True, stream="c_ohr")
        emit_moe(pr, dict(env), cfgD)
        pr.finish()
        print("FUSED arena hwm: bf", arena.hwm_b, "f32", arena.hwm_f,
              "sbuf remaining:", nc.sbuf_bytes_remaining)
        with nc.Block() as block:
            pr.build(block)
    return nc



from concourse.bass_utils import run_bass_kernel_spmd

TSEG = 2048
NCORES = 8
_prog_cache = {}


def _ct(a):
    return np.ascontiguousarray(np.asarray(a, dtype=np.float32))


def _host_prep(inp):
    T = TSEG
    UL = [np.asarray(inp[f"L0_attn_UL{l}"]) for l in range(4)]
    VR = [np.asarray(inp[f"L0_attn_VR{l}"]) for l in range(4)]
    UR = [np.asarray(inp[f"L0_attn_UR{l}"]) for l in range(4)]
    VL = [np.asarray(inp[f"L0_attn_VL{l}"]) for l in range(4)]
    emb = hss_host_embeds(UL, VR, UR, VL)
    EULT, EURT = emb["EULT"], emb["EURT"]
    EULTW = np.zeros((8, 960), np.float32)
    EURTW = np.zeros((8, 960), np.float32)
    EULTL = np.zeros((32, 256), np.float32)
    EURTL = np.zeros((32, 256), np.float32)
    for l in range(4):
        nn, r, off, woff = LVL_NN[l], LVL_R[l], LVL_OFF[l], WOFF_L[l]
        nnr = nn * r
        EULTL[0:nnr, l * 64:(l + 1) * 64] = EULT[off:off + nnr, :]
        EURTL[0:nnr, l * 64:(l + 1) * 64] = EURT[off:off + nnr, :]
        for n in range(nn):
            EULTW[0:r, (woff + n) * 64:(woff + n + 1) * 64] = EULT[off + n * r: off + (n + 1) * r, :]
            EURTW[0:r, (woff + n) * 64:(woff + n + 1) * 64] = EURT[off + n * r: off + (n + 1) * r, :]
    rshp = lambda v: _ct(np.asarray(v).reshape(8, 128).T)
    b_in = np.asarray(inp["L1_attn_b_in"])
    bin_a = np.zeros((128, NCC), np.float32)
    bin_b = np.zeros((128, NCC), np.float32)
    for cc in range(NCC):
        cw = LASTW if cc == NCC - 1 else 128
        bin_a[:cw, cc] = b_in[cc * 128: cc * 128 + cw]
        bin_b[:cw, cc] = b_in[2730 + cc * 128: 2730 + cc * 128 + cw]
    shared = {
        "Wq": _ct(inp["L0_attn_Wq"]), "Wk": _ct(inp["L0_attn_Wk"]),
        "Wv": _ct(inp["L0_attn_Wv"]), "Wo": _ct(inp["L0_attn_Wo"]),
        "EULs": emb["EUL"], "EVRs": emb["EVR"], "EVLs": emb["EVL"],
        "EULTW": EULTW, "EURTW": EURTW, "EULTL": EULTL, "EURTL": EURTL,
        "m0_W1": _ct(np.asarray(inp["L0_ffn_w_in"]).transpose(0, 2, 1)),
        "m0_W2": _ct(np.asarray(inp["L0_ffn_w_out"]).transpose(0, 2, 1)),
        "m0_wgateT": _ct(np.asarray(inp["L0_ffn_w_gate"]).T),
        "m1_W1": _ct(inp["L1_ffn_W1"]), "m1_W2": _ct(inp["L1_ffn_W2"]),
        "m1_wgateT": _ct(np.asarray(inp["L1_ffn_w_gate"]).T),
        "b1r": _ct(np.asarray(inp["L1_ffn_b1"]).reshape(8, 32, 128).transpose(0, 2, 1)),
        "b2": _ct(inp["L1_ffn_b2"]),
        "r_wq": _ct(inp["L1_attn_wq"]), "r_wk": _ct(inp["L1_attn_wk"]),
        "r_wv": _ct(inp["L1_attn_wv"]), "r_wg": _ct(inp["L1_attn_wg"]),
        "r_wo": _ct(inp["L1_attn_wo"]),
        "ln1wd": rshp(inp["L1_attn_ln1_w"]), "ln1bd": rshp(inp["L1_attn_ln1_b"]),
        "ln2w": rshp(inp["L1_attn_ln2_w"]), "ln2b": rshp(inp["L1_attn_ln2_b"]),
        "binr": np.stack([bin_a, bin_b]),
        "boutr": rshp(inp["L1_attn_b_out"]),
        "w_in_sw": _ct(inp["L1_attn_w_in"]), "w_out_sw": _ct(inp["L1_attn_w_out"]),
    }
    hsswc, retwc = [], []
    for c in range(NCORES):
        b, s = c // 4, c % 4
        hw = np.zeros((8, 64, 256), np.float32)
        rw = np.zeros((8, 64, 1024), np.float32)
        for j in range(s):
            hw[b * 4 + j] = 1.0
            for h in range(NH):
                rw[b * 4 + j, :, h * 64:(h + 1) * 64] = \
                    np.float32(GAMMAS[h]) ** (T * (s - 1 - j))
        hsswc.append(hw)
        retwc.append(rw)
    return shared, hsswc, retwc


def kernel(**inp):
    T = TSEG
    x = np.asarray(inp["L0_attn_x"])
    B, S, Dm = x.shape
    nseg = S // T
    shared, hsswc, retwc = _host_prep(inp)
    if "fused" not in _prog_cache:
        _prog_cache["fused"] = build_fused(T)
    nc = _prog_cache["fused"]
    in_maps = []
    for c in range(NCORES):
        im = dict(shared)
        im["x0"] = np.ascontiguousarray(x[c // nseg, (c % nseg) * T:(c % nseg + 1) * T, :].T)
        im["hssw"] = hsswc[c]
        im["retw"] = retwc[c]
        in_maps.append(im)
    res = run_bass_kernel_spmd(nc, in_maps, core_ids=list(range(NCORES)))
    out = np.empty((B, S, Dm), np.float32)
    for c in range(NCORES):
        out[c // nseg, (c % nseg) * T:(c % nseg + 1) * T, :] = res.results[c]["xfinal"].T
    return out


# version 10
# speedup vs baseline: 1.1498x; 1.1498x over previous
import numpy as np
import ml_dtypes
from collections import defaultdict
from contextlib import ExitStack

import concourse.bass as bass
import concourse.mybir as mybir

F32 = mybir.dt.float32
BF16 = mybir.dt.bfloat16
AF = mybir.ActivationFunctionType
ALU = mybir.AluOpType
AX = mybir.AxisListType

D = 1024
NFC = 8
HID = 4096
NHC = 32
NE = 8
LN_EPS = 1e-5

bfc = lambda a: np.ascontiguousarray(np.asarray(a, dtype=np.float32)).astype(ml_dtypes.bfloat16)


ENG_ATTR = {"pe": "tensor", "act": "scalar", "dve": "vector", "pool": "gpsimd", "sp": "sync"}


class Prog:
    def __init__(self, nc):
        self.nc = nc
        self.queues = {e: [] for e in ENG_ATTR}
        self.sem = {}
        for e in ENG_ATTR:
            self.sem[e] = nc.alloc_semaphore(name=f"sem_{e}")
        self.count = defaultdict(int)
        self.last_write = {}
        self.readers = defaultdict(list)
        self.floors = {e: defaultdict(int) for e in ENG_ATTR}
        self.alias = lambda s: s

    def dma_sem(self, stream):
        stream = self.alias(stream)
        name = f"dma_{stream}"
        if name not in self.sem:
            self.sem[name] = self.nc.alloc_semaphore(name=f"sem_{name}")
        return name

    @staticmethod
    def key(r):
        if isinstance(r, tuple):
            return (id(r[0]), r[1])
        return id(r)

    def _wait(self, eng, sem_name, val):
        if val <= 0:
            return
        if eng == 'pe' and sem_name == 'pe':
            return
        if self.floors[eng][sem_name] >= val:
            return
        self.floors[eng][sem_name] = val
        self.queues[eng].append(("wait", self.sem[sem_name], val))

    def op(self, eng, fn, reads=(), writes=(), dma=False, stream=None):
        for r in reads:
            k = self.key(r)
            if k in self.last_write:
                self._wait(eng, *self.last_write[k])
        for w in writes:
            k = self.key(w)
            if k in self.last_write:
                self._wait(eng, *self.last_write[k])
            for rd in self.readers[k]:
                self._wait(eng, *rd)
        if dma:
            if stream is None:
                stream = self.key(writes[0]) if writes else "out"
            sem_name = self.dma_sem(stream)
            amt = 16
        else:
            sem_name = eng
            amt = 1
        self.count[sem_name] += amt
        cnt = self.count[sem_name]
        self.queues[eng].append(("op", fn, self.sem[sem_name], amt))
        done = (sem_name, cnt)
        for w in writes:
            k = self.key(w)
            self.last_write[k] = done
            self.readers[k] = []
        for r in reads:
            self.readers[self.key(r)].append(done)
        return done

    def collective(self, fn, reads=(), writes=(), tag="cc"):
        for r in reads:
            k = self.key(r)
            if k in self.last_write:
                self._wait("pool", *self.last_write[k])
        for w in writes:
            k = self.key(w)
            if k in self.last_write:
                self._wait("pool", *self.last_write[k])
            for rd in self.readers[k]:
                self._wait("pool", *rd)
        name = f"cc_{tag}"
        if name not in self.sem:
            self.sem[name] = self.nc.alloc_semaphore(name=f"sem_{name}")
        self.count[name] += 1
        cnt = self.count[name]
        self.queues["pool"].append(("op", fn, self.sem[name], None))
        done = (name, cnt)
        for w in writes:
            k = self.key(w)
            self.last_write[k] = done
            self.readers[k] = []
        for r in reads:
            self.readers[self.key(r)].append(done)
        return done

    def fence(self, engs=("pe", "act", "dve", "pool", "sp")):
        snap = list(self.count.items())
        for e in engs:
            for name, cnt in snap:
                self._wait(e, name, cnt)

    def finish(self, eng="sp"):
        for name, cnt in sorted(self.count.items(), key=lambda kv: str(kv[0])):
            self._wait(eng, name, cnt)

    def build(self, block):
        for e, attr in ENG_ATTR.items():
            q = self.queues[e]
            if not q:
                continue
            def make(q=q):
                def emit(engine):
                    for item in q:
                        if item[0] == "wait":
                            engine.wait_ge(item[1], item[2])
                        else:
                            inst = item[1](engine)
                            if inst is not None:
                                if item[3] is None:
                                    inst.then_inc(item[2])
                                else:
                                    inst.then_inc(item[2], item[3])
                return emit
            getattr(block, attr)(make())


def mm_group(P, ps, tag, emits):
    n = len(emits)
    for i, (fn, rds) in enumerate(emits):
        start = i == 0
        stop = i == n - 1
        w = []
        if start:
            w.append(ps)
        if stop:
            w = [ps]
        else:
            w.append((ps, tag))
        r = list(rds) + ([(ps, tag)] if not start else [])
        P('pe', (lambda e, fn=fn, start=start, stop=stop: fn(e, start, stop)), reads=r, writes=w)



def make_consts(nc):
    c = {}
    c["id_f32"] = nc.inline_tensor(np.eye(128, dtype=np.float32), "c_id_f32")
    c["id_bf"] = nc.inline_tensor(bfc(np.eye(128)), "c_id_bf")
    c["ones_col_f32"] = nc.inline_tensor(np.ones((128, 1), np.float32), "c_onescol32")
    c["ones_row_f32"] = nc.inline_tensor(np.ones((1, 128), np.float32), "c_onesrow32")
    c["ones_row_bf"] = nc.inline_tensor(bfc(np.ones((1, 128))), "c_onesrow")
    ohr = np.zeros((8, 8 * 128), np.float32)
    for e in range(8):
        ohr[e, e * 128:(e + 1) * 128] = 1.0
    c["ohr_bf"] = nc.inline_tensor(bfc(ohr), "c_ohr")
    c["triu_bf"] = nc.inline_tensor(bfc(np.triu(np.ones((128, 128)))), "c_triu")
    return c


def fm(dram, sl):
    return dram[:, sl].rearrange("(fc p) t -> p fc t", p=128)


def v3(sbuf_ap, nfc=8):
    return sbuf_ap.rearrange("p (fc t) -> p fc t", fc=nfc)



def emit_gating(pr, env, cfg):
    T = cfg["T"]
    P = pr.op
    SCR = env["SCR"]
    WGT = env["WGT"]
    GS = env["GS"]
    CM = env["CM"]
    CMFB = env["CMFB"]
    ID32 = env["ID32"]
    PS4, PS5 = env["PS"][4], env["PS"][3]

    def ldwg(e):
        src = env["wgateT"][:, :].rearrange("(fc p) x -> p fc x", p=128)
        return e.dma_start(out=v3(WGT[:, :]), in_=src)
    P('sp', ldwg, writes=[WGT], dma=True, stream='wgt')

    def _gctx(tg):
        s = tg % 2
        return dict(
            s=s, XG=SCR[:, s * 1024:(s + 1) * 1024], kG=(SCR, ('G', s)),
            LPS=PS4[:, s * 8:(s + 1) * 8], CPS=PS5[0:8, s * 128:(s + 1) * 128],
            g=(lambda a, b, gb=s * 64: GS[:, gb + a: gb + b]),
            kg=(lambda t, s=s: (GS, (s, t))),
            cm=CM[:, s * 8:(s + 1) * 8], kcm=(CM, s),
            qa='sp')

    def _gfront(tg):
        c = _gctx(tg)
        s = c['s']; XG = c['XG']; kG = c['kG']; LPS = c['LPS']
        P(c['qa'], lambda e, tg=tg, XG=XG: e.dma_start(
            out=v3(XG[:, :]), in_=fm(env["xin"], slice(tg * 128, (tg + 1) * 128))),
          writes=[kG], dma=True, stream=f"xg{s}")
        mm_group(P, PS4, f'lg{s}', [
            ((lambda e, start, stop, fc=fc, XG=XG, LPS=LPS: e.matmul(
                LPS, XG[:, fc * 128:(fc + 1) * 128],
                WGT[:, fc * 8:(fc + 1) * 8], start=start, stop=stop)),
             [kG, WGT])
            for fc in range(NFC)])

    def _gback(tg):
        c = _gctx(tg)
        s = c['s']; g = c['g']; kg = c['kg']; cm = c['cm']; kcm = c['kcm']
        LPS = c['LPS']; CPS = c['CPS']
        P('dve', lambda e, g=g, LPS=LPS: e.tensor_copy(g(0, 8), LPS), reads=[PS4], writes=[kg('LG')])
        P('dve', lambda e, g=g: e.max(out=g(8, 16), in_=g(0, 8)), reads=[kg('LG')], writes=[kg('MX')])
        P('dve', lambda e, g=g: e.tensor_single_scalar(g(16, 24), g(0, 8), g(8, 9), ALU.is_ge),
          reads=[kg('LG'), kg('MX')], writes=[kg('E1')])
        P('dve', lambda e, g=g: e.tensor_single_scalar(g(24, 32), g(0, 8), g(9, 10), ALU.is_ge),
          reads=[kg('LG'), kg('MX')], writes=[kg('E2')])
        if cfg["gate_mode"] == "l0":
            P('dve', lambda e, g=g: e.tensor_tensor(g(32, 33), g(8, 9), g(9, 10), ALU.subtract),
              reads=[kg('MX')], writes=[kg('DD')])
            P('act', lambda e, g=g: e.activation(g(33, 34), g(32, 33), AF.Sigmoid),
              reads=[kg('DD')], writes=[kg('G1')])
            P('dve', lambda e, g=g: e.tensor_scalar(g(34, 35), g(33, 34), -1.0, 1.0, ALU.mult, ALU.add),
              reads=[kg('G1')], writes=[kg('G2')])
            P('dve', lambda e, g=g: e.tensor_tensor(g(24, 32), g(24, 32), g(16, 24), ALU.subtract),
              reads=[kg('E1'), kg('E2')], writes=[kg('E2')])
            P('dve', lambda e, g=g: e.tensor_scalar_mul(g(40, 48), g(16, 24), g(33, 34)),
              reads=[kg('E1'), kg('G1')], writes=[kg('T1')])
            P('dve', lambda e, g=g: e.tensor_scalar_mul(g(48, 56), g(24, 32), g(34, 35)),
              reads=[kg('E2'), kg('G2')], writes=[kg('T2')])
            P('dve', lambda e, g=g, cm=cm: e.tensor_tensor(cm, g(40, 48), g(48, 56), ALU.add),
              reads=[kg('T1'), kg('T2')], writes=[kcm])
        else:
            P('dve', lambda e, g=g: e.tensor_scalar_mul(g(32, 33), g(8, 9), -1.0),
              reads=[kg('MX')], writes=[kg('NM')])
            P('act', lambda e, g=g: e.activation(g(40, 48), g(0, 8), AF.Exp, bias=g(32, 33)),
              reads=[kg('LG'), kg('NM')], writes=[kg('EXP')])
            P('dve', lambda e, g=g: e.reduce_sum(g(33, 34), g(40, 48), AX.X),
              reads=[kg('EXP')], writes=[kg('SM')])
            P('dve', lambda e, g=g: e.reciprocal(g(34, 35), g(33, 34)),
              reads=[kg('SM')], writes=[kg('RS')])
            P('dve', lambda e, g=g: e.tensor_scalar_mul(g(40, 48), g(40, 48), g(34, 35)),
              reads=[kg('EXP'), kg('RS')], writes=[kg('EXP')])
            P('dve', lambda e, g=g: e.tensor_tensor(g(48, 56), g(16, 24), g(24, 32), ALU.max),
              reads=[kg('E1'), kg('E2')], writes=[kg('MK')])
            P('dve', lambda e, g=g, cm=cm: e.tensor_tensor(cm, g(40, 48), g(48, 56), ALU.mult),
              reads=[kg('EXP'), kg('MK')], writes=[kcm])
        P('pe', lambda e, cm=cm, CPS=CPS: e.transpose(CPS, cm, ID32[:, :]),
          reads=[kcm, env["ID32"]], writes=[(PS5, s)])
        P('dve', lambda e, tg=tg, CPS=CPS: e.tensor_copy(CMFB[0:8, tg * 128:(tg + 1) * 128], CPS),
          reads=[(PS5, s)], writes=[(CMFB, tg)])

    ng = T // 128
    for tg in range(ng):
        _gfront(tg)
        if tg >= 1:
            _gback(tg - 1)
    _gback(ng - 1)



def emit_moe(pr, env, cfg):
    T = cfg["T"]
    TS = cfg.get("TS", 256)
    nts = T // TS
    ntc = TS // 128
    P = pr.op
    XBF = env["XBF"]; W1SB = env["W1SB"]; W2SB = env["W2SB"]; HSB = env["HSB"]
    YEV = env["YEV"]; CB = env["CB"]; CMFB = env["CMFB"]
    PS = env["PS"]
    HPS = [PS[0][:, 0:TS], PS[1][:, 0:TS]]
    YPS = [PS[2][:, 0:TS], PS[3][:, 0:TS]]
    CBPS = PS[4][:, 0:TS]
    B1SB = env.get("B1SB"); B2SB = env.get("B2SB")
    YACC = env["YACC"]

    TH = T // 2
    nth = nts // 2

    def load_xbf(th):
        for fc in range(NFC):
            P('pool', lambda e, fc=fc, th=th: e.dma_start(
                out=XBF[:, fc * TH:(fc + 1) * TH],
                in_=env["xin"][fc * 128:(fc + 1) * 128, th * TH:(th + 1) * TH]),
              writes=[(XBF, fc)], dma=True, stream=f"xbf{fc}")
    if cfg["has_bias"]:
        P('pool', lambda e: e.dma_start(out=B2SB[:, :], in_=env["b2"][:, :]),
          writes=[B2SB], dma=True, stream="b2")

    def load_w1(ei):
        for hq in range(4):
            for fc in range(NFC):
                P('pool', lambda e, ei=ei, fc=fc, hq=hq: e.dma_start(
                    out=W1SB[:, fc * HID + hq * 1024: fc * HID + (hq + 1) * 1024],
                    in_=env["W1"][ei, fc * 128:(fc + 1) * 128, hq * 1024:(hq + 1) * 1024]),
                  writes=[(W1SB, (fc, hq))], dma=True, stream=f"w1_{fc}")

    def load_w2(ei, deps=()):
        for hg in range(8):
            def ld(e, ei=ei, hg=hg):
                src = env["W2"][ei, hg * 512:(hg + 1) * 512, :].rearrange(
                    "(h p) d -> p h d", p=128)
                dst = W2SB[:, hg * 4096:(hg + 1) * 4096].rearrange("p (h d) -> p h d", h=4)
                return e.dma_start(out=dst, in_=src)
            P('pool', ld, reads=list(deps), writes=[(W2SB, hg)], dma=True,
              stream=f"w2_{hg}")

    def load_b1(ei):
        P('pool', lambda e, ei=ei: e.dma_start(out=B1SB[:, 0:NHC], in_=env["b1r"][ei, :, :]),
          writes=[B1SB], dma=True, stream="b1")

    load_w1(0)
    load_w2(0)
    if cfg["has_bias"]:
        load_b1(0)
    load_xbf(0)
    emit_gating(pr, env, cfg)
    pr.fence()

    prev_th = 0
    for ei in range(NE):
      ths = (0, 1) if ei % 2 == 0 else (1, 0)
      for ti, th in enumerate(ths):
        if th != prev_th:
            load_xbf(th)
            prev_th = th
        for tsi, ts in enumerate(range(th * nth, (th + 1) * nth)):
            t0 = ts * TS
            cmfb_keys = [(CMFB, (t0 // 128) + i) for i in range(ntc)]
            P('pe', lambda e, ei=ei, t0=t0: e.matmul(
                CBPS, env["OHR"][:, ei * 128:(ei + 1) * 128],
                CMFB[0:8, t0:t0 + TS], start=True, stop=True),
              reads=[env["OHR"]] + cmfb_keys, writes=[PS[4]])
            P('dve', lambda e: e.tensor_copy(CB[:, 0:TS], CBPS),
              reads=[PS[4]], writes=[CB])
            for hc in range(NHC):
                hp = HPS[hc % 2]
                hps_full = PS[hc % 2]
                tl = t0 - th * TH
                mm_group(P, hps_full, 'h', [
                    ((lambda e, start, stop, hc=hc, fc=fc, hp=hp, tl=tl: e.matmul(
                        hp, W1SB[:, fc * HID + hc * 128: fc * HID + (hc + 1) * 128],
                        XBF[:, fc * TH + tl: fc * TH + tl + TS], start=start, stop=stop)),
                     [(W1SB, (fc, hc // 8)), (XBF, fc)])
                    for fc in range(NFC)])
                if cfg["has_act"]:
                    P('act', lambda e, hc=hc, hp=hp: e.activation(
                        HSB[:, hc * TS:(hc + 1) * TS], hp, AF.Gelu, bias=B1SB[:, hc:hc + 1]),
                      reads=[PS[hc % 2], B1SB], writes=[(HSB, hc)])
                else:
                    P('act', lambda e, hc=hc, hp=hp: e.activation(
                        HSB[:, hc * TS:(hc + 1) * TS], hp, AF.Copy),
                      reads=[PS[hc % 2]], writes=[(HSB, hc)])
            if ei > 0 and ti == 0 and tsi == 0:
                load_w2(ei, deps=[(HSB, NHC - 1)])
            if ei + 1 < NE and ti == 1 and tsi == nth - 1:
                load_w1(ei + 1)
                if cfg["has_bias"]:
                    load_b1(ei + 1)
            for dc in range(NFC):
                yp = YPS[dc % 2]
                yps_full = PS[2 + dc % 2]
                emits = []
                if cfg["has_bias"]:
                    emits.append((
                        (lambda e, start, stop, dc=dc, yp=yp, t0=t0: e.matmul(
                            yp, B2SB[0:8, dc * 128:(dc + 1) * 128],
                            CMFB[0:8, t0:t0 + TS], start=start, stop=stop)),
                        [B2SB] + cmfb_keys))
                for hc in range(NHC):
                    emits.append((
                        (lambda e, start, stop, dc=dc, hc=hc, yp=yp: e.matmul(
                            yp, W2SB[:, hc * D + dc * 128: hc * D + (dc + 1) * 128],
                            HSB[:, hc * TS:(hc + 1) * TS], start=start, stop=stop)),
                        [(W2SB, hc // 4), (HSB, hc)]))
                mm_group(P, yps_full, 'y', emits)
                P('dve', lambda e, dc=dc, yp=yp: e.tensor_tensor(
                    YEV[:, (dc % 4) * TS:(dc % 4 + 1) * TS], yp, CB[:, 0:TS], ALU.mult),
                  reads=[PS[2 + dc % 2], CB], writes=[(YEV, dc % 4)])
                if dc % 4 == 3:
                    half = dc // 4
                    def acc(e, ei=ei, ts=ts, half=half):
                        dst = YACC[half * 512:(half + 1) * 512, ts * TS:(ts + 1) * TS] \
                            .rearrange("(dc p) t -> p dc t", p=128)
                        if ei == 0:
                            return e.dma_start(out=dst, in_=v3(YEV[:, :], nfc=4))
                        return e.dma_start(out=dst, in_=v3(YEV[:, :], nfc=4), accum_op=ALU.add)
                    P('pool', acc,
                      reads=[(YEV, d) for d in range(4)] +
                            ([(YACC, ts)] if (ei or half) else []),
                      writes=[(YACC, ts)], dma=True, stream=f"yacc{ts}")
    pr.fence()
    emit_ln_sum(pr, env, cfg, env["xin"], YACC, env["xout"],
                b_deps=[(YACC, ts) for ts in range(nts)])



def emit_ln_sum(pr, env, cfg, a_dram, b_dram, out_dram, b_deps=(), a_deps=(),
                affine=None, out_sbuf=None, sum_dram=None, out_deps_tag=None):
    T = cfg["T"]
    P = pr.op
    SCR = env["SCR"]
    SQ = SCR[:, 4096:5120]
    RW = env["RW"]
    OC32 = env["OC32"]; OR32 = env["OR32"]
    PS = env["PS"]
    W = 128
    assert not (out_dram is not None and out_sbuf is not None)

    def _ctx(tg):
        s = tg % 2
        return dict(
            s=s, sl=slice(tg * W, (tg + 1) * W),
            X=SCR[:, s * 1024:(s + 1) * 1024],
            Bv=SCR[:, 2048 + s * 1024: 3072 + s * 1024],
            kS=(SCR, ('S', s)), kO=(SCR, ('O', s)), kB=(SCR, ('B', s)), kQ=(SCR, 'Q'),
            RPS=PS[6][0:1, s * 128:(s + 1) * 128],
            R2PS=PS[7][0:1, s * 128:(s + 1) * 128],
            BPSF=PS[s], B2PSF=PS[2 + s],
            qa='sp',
            rw=(lambda i, ro=s * 512: RW[0:1, ro + i * 128: ro + i * 128 + W]),
            kr=(lambda i, s=s: (RW, (s, i))))

    def _front(tg):
        c = _ctx(tg)
        s = c['s']; sl = c['sl']; X = c['X']; Bv = c['Bv']
        kS = c['kS']; kO = c['kO']; kB = c['kB']; kQ = c['kQ']
        P(c['qa'], lambda e, sl=sl, X=X: e.dma_start(out=v3(X[:, :]), in_=fm(a_dram, sl)),
          reads=list(a_deps), writes=[kS], dma=True, stream=f"lnx{s}")
        if b_dram is not None:
            P('sp', lambda e, sl=sl, Bv=Bv: e.dma_start(out=v3(Bv[:, :]), in_=fm(b_dram, sl)),
              reads=list(b_deps), writes=[kB, kO], dma=True, stream=f"lny{s}")
            P('dve', lambda e, X=X, Bv=Bv: e.tensor_tensor(X[:, :], X[:, :], Bv[:, :], ALU.add),
              reads=[kS, kB], writes=[kS])
        P('act', lambda e, X=X: e.activation(SQ[:, :], X[:, :], AF.Square),
          reads=[kS], writes=[kQ])
        if sum_dram is not None:
            P('sp', lambda e, sl=sl, X=X: e.dma_start(out=fm(sum_dram, sl), in_=v3(X[:, :])),
              reads=[kS], writes=[(sum_dram, ('sum', tg))], dma=True, stream="lnsum")
        mm_group(P, PS[6], f'r{s}', [
            ((lambda e, start, stop, fc=fc, X=X, RPS=c['RPS']: e.matmul(
                RPS, OC32[:, :], X[:, fc * W:(fc + 1) * W], start=start, stop=stop)),
             [kS, env["OC32"]]) for fc in range(NFC)])
        mm_group(P, PS[7], f'r2{s}', [
            ((lambda e, start, stop, fc=fc, R2PS=c['R2PS']: e.matmul(
                R2PS, OC32[:, :], SQ[:, fc * W:(fc + 1) * W], start=start, stop=stop)),
             [kQ, env["OC32"]]) for fc in range(NFC)])
        rw = c['rw']; kr = c['kr']
        P('dve', lambda e, rw=rw, RPS=c['RPS']: e.tensor_scalar_mul(rw(0), RPS, 1.0 / D),
          reads=[PS[6]], writes=[kr(0)])
        P('dve', lambda e, rw=rw, R2PS=c['R2PS']: e.tensor_scalar_mul(rw(1), R2PS, 1.0 / D),
          reads=[PS[7]], writes=[kr(1)])
        P('dve', lambda e, rw=rw: e.tensor_tensor(rw(2), rw(0), rw(0), ALU.mult),
          reads=[kr(0)], writes=[kr(2)])
        P('dve', lambda e, rw=rw: e.tensor_tensor(rw(1), rw(1), rw(2), ALU.subtract),
          reads=[kr(1), kr(2)], writes=[kr(1)])
        P('dve', lambda e, rw=rw: e.tensor_scalar_add(rw(1), rw(1), LN_EPS),
          reads=[kr(1)], writes=[kr(1)])
        P('act', lambda e, rw=rw: e.activation(rw(2), rw(1), AF.Sqrt),
          reads=[kr(1)], writes=[kr(2)])
        P('dve', lambda e, rw=rw: e.reciprocal(rw(3), rw(2)), reads=[kr(2)], writes=[kr(3)])

    def _back(tg):
        c = _ctx(tg)
        s = c['s']; sl = c['sl']; X = c['X']; Bv = c['Bv']
        kS = c['kS']; kO = c['kO']; kB = c['kB']
        rw = c['rw']; kr = c['kr']
        BPSF = c['BPSF']; B2PSF = c['B2PSF']
        for q in range(4):
            P('pe', lambda e, q=q, rw=rw, BPSF=BPSF: e.matmul(
                BPSF[:, q * 128:(q + 1) * 128], OR32[0:1, :], rw(0), start=True, stop=True),
              reads=[kr(0), env["OR32"]], writes=[(BPSF, 'b')])
            P('pe', lambda e, q=q, rw=rw, B2PSF=B2PSF: e.matmul(
                B2PSF[:, q * 128:(q + 1) * 128], OR32[0:1, :], rw(3), start=True, stop=True),
              reads=[kr(3), env["OR32"]], writes=[(B2PSF, 'b2')])
        for h in range(2):
            P('dve', lambda e, h=h, X=X, Bv=Bv, BPSF=BPSF: e.tensor_tensor(
                Bv[:, h * 512:(h + 1) * 512], X[:, h * 512:(h + 1) * 512],
                BPSF[:, :], ALU.subtract),
              reads=[kS, (BPSF, 'b')], writes=[kB, kO])
        for h in range(2):
            P('dve', lambda e, h=h, Bv=Bv, B2PSF=B2PSF: e.tensor_tensor(
                Bv[:, h * 512:(h + 1) * 512], Bv[:, h * 512:(h + 1) * 512],
                B2PSF[:, :], ALU.mult),
              reads=[kO, (B2PSF, 'b2')], writes=[kO])
        if out_sbuf is not None:
            XS, XT = out_sbuf
            WT, BT = affine if affine is not None else (None, None)
            for fc in range(NFC):
                if affine is not None:
                    P('dve', lambda e, fc=fc, tg=tg, Bv=Bv: e.tensor_scalar(
                        XS[:, fc * XT + tg * W: fc * XT + tg * W + W],
                        Bv[:, fc * W:(fc + 1) * W],
                        WT[:, fc:fc + 1], BT[:, fc:fc + 1], ALU.mult, ALU.add),
                      reads=[kO, WT, BT], writes=[(XS, (fc, tg * W // 128))])
                else:
                    P('dve', lambda e, fc=fc, tg=tg, Bv=Bv: e.tensor_copy(
                        XS[:, fc * XT + tg * W: fc * XT + tg * W + W],
                        Bv[:, fc * W:(fc + 1) * W]),
                      reads=[kO], writes=[(XS, (fc, tg * W // 128))])
        if out_dram is not None:
            P(c['qa'], lambda e, sl=sl, Bv=Bv: e.dma_start(out=fm(out_dram, sl), in_=v3(Bv[:, :])),
              reads=[kO], writes=[(out_dram, ('ln', tg))], dma=True, stream=f"lnout{s}")

    n = T // W
    for tg in range(n):
        _front(tg)
        if tg >= 1:
            _back(tg - 1)
    _back(n - 1)



def build_moe_launch(T, layer):
    nc = bass.Bass("TRN2", target_bir_lowering=False)
    cfg = dict(T=T, TS=256, gate_mode=("l0" if layer == 0 else "l1"),
               has_act=(layer == 1), has_bias=(layer == 1))
    env = {}
    env["xin"] = nc.dram_tensor("xin", [D, T], F32, kind="ExternalInput").ap()
    env["xout"] = nc.dram_tensor("xout", [D, T], F32, kind="ExternalOutput").ap()
    env["W1"] = nc.dram_tensor("W1", [NE, D, HID], F32, kind="ExternalInput").ap()
    env["W2"] = nc.dram_tensor("W2", [NE, HID, D], F32, kind="ExternalInput").ap()
    env["wgateT"] = nc.dram_tensor("wgateT", [D, NE], F32, kind="ExternalInput").ap()
    if layer == 1:
        env["b1r"] = nc.dram_tensor("b1r", [NE, 128, NHC], F32, kind="ExternalInput").ap()
        env["b2"] = nc.dram_tensor("b2", [NE, D], F32, kind="ExternalInput").ap()
    env["YACC"] = nc.dram_tensor("yacc", [D, T], F32, kind="Internal").ap()
    consts = make_consts(nc)
    TS = cfg["TS"]
    with ExitStack() as stack:
        sb = lambda name, shape, dt=F32: stack.enter_context(nc.sbuf_tensor(name, shape, dt))
        psb = lambda name, shape, dt=F32: stack.enter_context(nc.psum_tensor(name, shape, dt))
        env["XBF"] = sb("XBF", [128, NFC * (T // 2)], BF16)
        env["W1SB"] = sb("W1SB", [128, 32768], BF16)
        env["W2SB"] = sb("W2SB", [128, 32768], BF16)
        env["HSB"] = sb("HSB", [128, NHC * TS], BF16)
        env["YEV"] = sb("YEV", [128, 4 * TS], F32)
        env["CB"] = sb("CBt", [128, TS], BF16)
        env["SCR"] = sb("SCR", [128, 3072], F32)
        env["WGT"] = sb("WGT", [128, NFC * 8], F32)
        env["GS"] = sb("GS", [128, 64], F32)
        env["CM"] = sb("CMtm", [128, 8], F32)
        env["CMFB"] = sb("CMFB", [8, T], BF16)
        env["ID32"] = sb("ID32", [128, 128], F32)
        env["OHR"] = sb("OHR", [8, 8 * 128], BF16)
        env["RW"] = sb("RW", [1, 512], F32)
        env["OC32"] = sb("OC32", [128, 1], F32)
        env["OR32"] = sb("OR32", [1, 128], F32)
        if layer == 1:
            env["B1SB"] = sb("B1SB", [128, NHC], F32)
            env["B2SB"] = sb("B2SB", [8, D], BF16)
        env["PS"] = [psb(f"PS{i}", [128, 512]) for i in range(6)] + \
                    [psb("PS6", [1, 512]), psb("PS7", [1, 512])]
        print("SBUF bytes remaining:", nc.sbuf_bytes_remaining,
              "PSUM banks remaining:", nc.psum_banks_remaining)
        pr = Prog(nc)
        for nm, cn in [("ID32", "id_f32"), ("OHR", "ohr_bf"), ("OC32", "ones_col_f32"),
                       ("OR32", "ones_row_f32")]:
            pr.op('sp', lambda e, nm=nm, cn=cn: e.dma_start(out=env[nm][:, :], in_=consts[cn].ap()),
                  writes=[env[nm]], dma=True)
        emit_moe(pr, env, cfg)
        pr.finish()
        with nc.Block() as block:
            pr.build(block)
    return nc



NH = 16
DH = 64
CC = 128
HIDS = 2730
GAMMAS = [1.0 - 2.0 ** (-5.0 - h) for h in range(NH)]


def make_ret_consts(nc):
    c = {}
    tri = np.triu(np.ones((128, 128), np.float32))
    c["triu4"] = nc.inline_tensor(bfc(np.tile(tri, (1, 4))), "c_triu4")
    decq = np.zeros((128, 8 * 512), np.float32)
    deck = np.zeros((128, 8 * 512), np.float32)
    tr = np.arange(512) % 128
    for dc in range(8):
        for j in range(128):
            h = 2 * dc + j // 64
            decq[j, dc * 512:(dc + 1) * 512] = GAMMAS[h] ** tr * (DH ** -0.5)
            deck[j, dc * 512:(dc + 1) * 512] = GAMMAS[h] ** (-tr)
    c["decq"] = nc.inline_tensor(bfc(decq), "c_decq")
    c["deck"] = nc.inline_tensor(bfc(deck), "c_deck")
    decktm = np.zeros((128, 1024), np.float32)
    for h in range(NH):
        decktm[:, h * 64:(h + 1) * 64] = (np.asarray(GAMMAS[h]) ** (-np.arange(128.0)))[:, None]
    c["decktm"] = nc.inline_tensor(bfc(decktm), "c_decktm")
    gcc = np.zeros((64, 1024), np.float32)
    for h in range(NH):
        gcc[:, h * 64:(h + 1) * 64] = GAMMAS[h] ** CC
    c["gcc"] = nc.inline_tensor(gcc, "c_gcc")
    return c


def emit_proj_fm(pr, env, cfg, wdram, XS, OUT, t0, tw, dec_const=None, qdz=None):
    T = cfg["T"]
    P = pr.op
    WSB = env["PWSB"]
    for fc in range(NFC):
        P('pool', lambda e, fc=fc: e.dma_start(
            out=WSB[:, fc * D:(fc + 1) * D], in_=wdram[fc * 128:(fc + 1) * 128, :]),
          writes=[(WSB, fc)], dma=True, stream=f"pw{fc}")
    for dc in range(NFC):
        for t5 in range(tw // 512):
            psf = env["PS"][t5 % 2]
            ps = psf[:, 0:512]
            xk = [(XS, (fc, (t0 + t5 * 512) // 128 + i)) for fc in range(NFC) for i in range(4)]
            mm_group(P, psf, 'p', [
                ((lambda e, start, stop, fc=fc, dc=dc, t5=t5, ps=ps: e.matmul(
                    ps, WSB[:, fc * D + dc * 128: fc * D + (dc + 1) * 128],
                    XS[:, fc * T + t0 + t5 * 512: fc * T + t0 + (t5 + 1) * 512],
                    start=start, stop=stop)),
                 [(WSB, fc)] + [(XS, (fc, (t0 + t5 * 512) // 128 + i)) for i in range(4)])
                for fc in range(NFC)])
            if qdz is not None:
                QDZE, QDZO = qdz
                P('dve', lambda e, dc=dc, t5=t5: e.tensor_tensor(
                    QDZE[0:64, dc * tw + t5 * 512: dc * tw + (t5 + 1) * 512],
                    psf[0:64, 0:512], dec_const[0:64, dc * 512:(dc + 1) * 512], ALU.mult),
                  reads=[psf, dec_const], writes=[(QDZE, (dc, t5))])
                P('dve', lambda e, dc=dc, t5=t5: e.tensor_tensor(
                    QDZO[64:128, dc * tw + t5 * 512: dc * tw + (t5 + 1) * 512],
                    psf[64:128, 0:512], dec_const[64:128, dc * 512:(dc + 1) * 512], ALU.mult),
                  reads=[psf, dec_const], writes=[(QDZO, (dc, t5))])
            elif dec_const is not None:
                P('dve', lambda e, dc=dc, t5=t5, ps=ps: e.tensor_tensor(
                    OUT[:, dc * tw + t5 * 512: dc * tw + (t5 + 1) * 512], ps,
                    dec_const[:, dc * 512:(dc + 1) * 512], ALU.mult),
                  reads=[psf, dec_const], writes=[(OUT, (dc, t5))])
            else:
                P('dve', lambda e, dc=dc, t5=t5, ps=ps: e.tensor_copy(
                    OUT[:, dc * tw + t5 * 512: dc * tw + (t5 + 1) * 512], ps),
                  reads=[psf], writes=[(OUT, (dc, t5))])


def emit_proj_tm(pr, env, cfg, wdram, XS, OUT, t0, tw, dec_const=None, act=None):
    T = cfg["T"]
    P = pr.op
    WSB = env["PWSB"]
    for fc in range(NFC):
        P('pool', lambda e, fc=fc: e.dma_start(
            out=WSB[:, fc * D:(fc + 1) * D], in_=wdram[fc * 128:(fc + 1) * 128, :]),
          writes=[(WSB, fc)], dma=True, stream=f"pw{fc}")
    for tc in range(tw // 128):
        gtc = (t0 // 128) + tc
        for half in range(2):
            psf = env["PS"][half % 2]
            ps = psf[:, 0:512]
            mm_group(P, psf, 'p', [
                ((lambda e, start, stop, fc=fc, tc=tc, half=half, ps=ps, gtc=gtc: e.matmul(
                    ps, XS[:, fc * T + gtc * 128: fc * T + (gtc + 1) * 128],
                    WSB[:, fc * D + half * 512: fc * D + (half + 1) * 512],
                    start=start, stop=stop)),
                 [(WSB, fc), (XS, (fc, gtc))])
                for fc in range(NFC)])
            if act is not None:
                P('act', lambda e, tc=tc, half=half, ps=ps: e.activation(
                    OUT[:, tc * D + half * 512: tc * D + (half + 1) * 512], ps, act),
                  reads=[psf], writes=[(OUT, (tc, half))])
            elif dec_const is not None:
                P('dve', lambda e, tc=tc, half=half, ps=ps: e.tensor_tensor(
                    OUT[:, tc * D + half * 512: tc * D + (half + 1) * 512], ps,
                    dec_const[:, half * 512:(half + 1) * 512], ALU.mult),
                  reads=[psf, dec_const], writes=[(OUT, (tc, half))])
            else:
                P('dve', lambda e, tc=tc, half=half, ps=ps: e.tensor_copy(
                    OUT[:, tc * D + half * 512: tc * D + (half + 1) * 512], ps),
                  reads=[psf], writes=[(OUT, (tc, half))])


def emit_ret_state_chunks(pr, env, cfg, KT, VT, c0, nch, save):
    P = pr.op
    SAL = env["SAL"]
    GCC = env["GCC"]
    for li in range(nch):
        ci = c0 + li
        if save:
            P('dve', lambda e, li=li: e.tensor_copy(
                env["SCHKB"][0:64, li * 1024:(li + 1) * 1024], SAL[:, :]),
              reads=[SAL], writes=[(env["SCHKB"], li)])
            P('dve', lambda e, li=li: e.tensor_copy(
                env["SCHKB"][64:128, li * 1024:(li + 1) * 1024], SAL[:, :]),
              reads=[SAL], writes=[(env["SCHKB"], (li, 'hi'))])
        for g in range(2):
            psf = env["PS"][2 + g]
            for hh in range(8):
                h = g * 8 + hh
                P('pe', lambda e, li=li, h=h, hh=hh, psf=psf: e.matmul(
                    psf[0:64, hh * 64:(hh + 1) * 64],
                    KT[:, li * D + h * 64: li * D + (h + 1) * 64],
                    VT[:, li * D + h * 64: li * D + (h + 1) * 64],
                    start=True, stop=True),
                  reads=[(KT, (li, h // 8)), (VT, (li, h // 8))], writes=[(psf, hh)])
            P('dve', lambda e, g=g, psf=psf: e.tensor_tensor(
                SAL[:, g * 512:(g + 1) * 512], SAL[:, g * 512:(g + 1) * 512],
                psf[0:64, 0:512], ALU.add),
              reads=[SAL] + [(psf, i) for i in range(8)], writes=[SAL])
        P('dve', lambda e: e.tensor_tensor(SAL[:, :], SAL[:, :], GCC[:, :], ALU.mult),
          reads=[SAL, env["GCC"]], writes=[SAL])


def emit_ret_main_half(pr, env, cfg, QD, KD, KT, VT, GT, t0, tw):
    T = cfg["T"]
    P = pr.op
    TRIU4 = env["TRIU4"]
    PMS = env["PMS"]
    OEV = env["OEV"]
    STT = env["STT"]
    GTD = env["GTD"]
    GFM = env["GFM"]
    HEV = env["HEV"]
    WOSB = env["WOSB"]
    SCHKB = env["SCHKB"]
    TPSB = env["TPSB"]
    PS = env["PS"]
    nch = tw // CC
    for lci in range(nch):
        ci = (t0 // CC) + lci
        if cfg.get("mtrunc", 99) < 1:
            continue
        for hg in range(cfg.get("mhg", 4)):
            psf = PS[hg % 2]
            for hh in range(4):
                h = hg * 4 + hh
                dc = h // 2
                QZ = env["QDZE"] if h % 2 == 0 else env["QDZO"]
                P('pe', lambda e, h=h, hh=hh, dc=dc, lci=lci, psf=psf, QZ=QZ: e.matmul(
                    psf[:, hh * 128:(hh + 1) * 128],
                    KD[:, dc * tw + lci * 128: dc * tw + (lci + 1) * 128],
                    QZ[:, dc * tw + lci * 128: dc * tw + (lci + 1) * 128],
                    start=True, stop=True),
                  reads=[(KD, (dc, (lci * 128) // 512)), (QZ, (dc, (lci * 128) // 512)),
                         (QZ, 'z')],
                  writes=[(psf, hh)])
            if cfg.get("mmask", True):
                P('dve', lambda e, hg=hg, psf=psf: e.tensor_tensor(
                    PMS[:, hg * 512:(hg + 1) * 512], psf[:, 0:512], TRIU4[:, :], ALU.mult),
                  reads=[(psf, 0), (psf, 1), (psf, 2), (psf, 3), env["TRIU4"]],
                  writes=[(PMS, hg)])
            else:
                P('dve', lambda e, hg=hg, psf=psf: e.tensor_copy(
                    PMS[:, hg * 512:(hg + 1) * 512], psf[:, 0:512]),
                  reads=[(psf, 0), (psf, 1), (psf, 2), (psf, 3)],
                  writes=[(PMS, hg)])
        if cfg.get("mtrunc", 99) < 2:
            continue
        for g in range(2):
            psf = PS[2 + g]
            for hh in range(8):
                h = g * 8 + hh
                P('pe', lambda e, h=h, hh=hh, lci=lci, psf=psf: e.matmul(
                    psf[:, hh * 64:(hh + 1) * 64],
                    PMS[:, h * 128:(h + 1) * 128],
                    VT[:, lci * D + h * 64: lci * D + (h + 1) * 64],
                    start=True, stop=False),
                  reads=[(PMS, h // 4), (VT, (lci, h // 8))], writes=[(psf, ('o', hh))])
                dc = h // 2
                QZ = env["QDZE"] if h % 2 == 0 else env["QDZO"]
                P('pe', lambda e, h=h, hh=hh, dc=dc, lci=lci, psf=psf, QZ=QZ: e.matmul(
                    psf[:, hh * 64:(hh + 1) * 64],
                    QZ[:, dc * tw + lci * 128: dc * tw + (lci + 1) * 128],
                    SCHKB[:, lci * 1024 + h * 64: lci * 1024 + (h + 1) * 64],
                    start=False, stop=True),
                  reads=[(QZ, (dc, (lci * 128) // 512)), (QZ, 'z'), (SCHKB, lci),
                         (SCHKB, (lci, 'hi')), (psf, ('o', hh))],
                  writes=[(psf, ('of', hh))])
        for g in range(2):
            psf = PS[2 + g]
            P('dve', lambda e, g=g, psf=psf: e.tensor_copy(
                OEV[:, g * 512:(g + 1) * 512], psf[:, 0:512]),
              reads=[(psf, ('of', hh)) for hh in range(8)], writes=[(OEV, g)])
        if cfg.get("mtrunc", 99) < 3:
            continue
        P('dve', lambda e: e.tensor_reduce(
            STT[:, 0:16], OEV[:, :].rearrange("p (h x) -> p h x", h=16), AX.X, ALU.add),
          reads=[(OEV, 0), (OEV, 1)], writes=[(STT, 'sum')])
        P('act', lambda e: e.activation(env["SQT"][:, :], OEV[:, :], AF.Square),
          reads=[(OEV, 0), (OEV, 1)], writes=[env["SQT"]])
        P('dve', lambda e: e.tensor_reduce(
            STT[:, 16:32], env["SQT"][:, :].rearrange("p (h x) -> p h x", h=16), AX.X, ALU.add),
          reads=[env["SQT"]], writes=[(STT, 'sq'), (STT, 'ex2'), (STT, 'var')])
        P('dve', lambda e: e.tensor_scalar_mul(STT[:, 0:16], STT[:, 0:16], 1.0 / DH),
          reads=[(STT, 'sum')], writes=[(STT, 'mean')])
        P('dve', lambda e: e.tensor_scalar_mul(STT[:, 16:32], STT[:, 16:32], 1.0 / DH),
          reads=[(STT, 'sq')], writes=[(STT, 'ex2')])
        P('dve', lambda e: e.tensor_tensor(STT[:, 32:48], STT[:, 0:16], STT[:, 0:16], ALU.mult),
          reads=[(STT, 'mean')], writes=[(STT, 'm2'), (STT, 'sd')])
        P('dve', lambda e: e.tensor_tensor(STT[:, 16:32], STT[:, 16:32], STT[:, 32:48], ALU.subtract),
          reads=[(STT, 'ex2'), (STT, 'm2')], writes=[(STT, 'var')])
        P('dve', lambda e: e.tensor_scalar_add(STT[:, 16:32], STT[:, 16:32], 1e-6),
          reads=[(STT, 'var')], writes=[(STT, 'var')])
        P('act', lambda e: e.activation(STT[:, 32:48], STT[:, 16:32], AF.Sqrt),
          reads=[(STT, 'var')], writes=[(STT, 'sd')])
        P('dve', lambda e: e.reciprocal(STT[:, 48:64], STT[:, 32:48]),
          reads=[(STT, 'sd')], writes=[(STT, 'rstd')])
        for h in range(NH):
            P('dve', lambda e, h=h: e.tensor_scalar(
                OEV[:, h * 64:(h + 1) * 64], OEV[:, h * 64:(h + 1) * 64],
                STT[:, h:h + 1], STT[:, 48 + h:49 + h], ALU.subtract, ALU.mult),
              reads=[(OEV, h // 8), (STT, 'mean'), (STT, 'rstd')], writes=[(OEV, h // 8)])
        if cfg.get("mtrunc", 99) < 4:
            continue
        P('dve', lambda e, lci=lci: e.tensor_tensor(
            GTD[:, :], OEV[:, :], GT[:, lci * D:(lci + 1) * D], ALU.mult),
          reads=[(OEV, 0), (OEV, 1), (GT, (lci, 0)), (GT, (lci, 1))], writes=[GTD])
        if cfg.get("mtrunc", 99) < 5:
            continue
        for fc in range(NFC):
            P('pe', lambda e, fc=fc: e.transpose(
                TPSB[:, fc * 128:(fc + 1) * 128], GTD[:, fc * 128:(fc + 1) * 128],
                env["IDB"][:, :]),
              reads=[GTD, env["IDB"]], writes=[(TPSB, fc)])
        P('dve', lambda e, lci=lci: e.tensor_copy(
            GFM[:, :].rearrange("p (fc t) -> p fc t", fc=8)[:, :, (lci % 4) * 128:(lci % 4 + 1) * 128],
            TPSB[:, :].rearrange("p (fc t) -> p fc t", fc=8)),
          reads=[(TPSB, fc) for fc in range(NFC)], writes=[(GFM, lci % 4)])
        if lci % 4 == 3:
            g5 = lci // 4
            psf = PS[4]
            for dc in range(NFC):
                mm_group(P, psf, 'wo', [
                    ((lambda e, start, stop, fc=fc, dc=dc: e.matmul(
                        psf[:, 0:512], WOSB[:, fc * D + dc * 128: fc * D + (dc + 1) * 128],
                        GFM[:, fc * 512:(fc + 1) * 512], start=start, stop=stop)),
                     [(WOSB, fc)] + [(GFM, i) for i in range(4)])
                    for fc in range(NFC)])
                P('dve', lambda e, dc=dc, psf=psf: e.tensor_copy(
                    HEV[:, (dc % 2) * 512:(dc % 2) * 512 + 512], psf[:, 0:512]),
                  reads=[psf], writes=[(HEV, dc % 2)])
                def sto(e, t0=t0, g5=g5, dc=dc):
                    sl = slice(t0 + g5 * 512, t0 + (g5 + 1) * 512)
                    return e.dma_start(
                        out=env["HATT"][dc * 128:(dc + 1) * 128, sl],
                        in_=HEV[:, (dc % 2) * 512:(dc % 2) * 512 + 512])
                P('sp', sto, reads=[(HEV, dc % 2)],
                  writes=[(env["HATT"], (t0, g5, dc))], dma=True, stream=f"hatt{dc}")


def emit_soff_init(pr, env, nprev=3):
    P = pr.op
    SAL = env["SAL"]
    SPV = env["SPV"]
    WOF = env["WOF"]
    P('dve', lambda e: e.memset(SAL[:, :], 0.0), writes=[env["SAL"]])
    for j in range(nprev):
        P('sp', lambda e, j=j: e.dma_start(out=SPV[:, :], in_=env["sprev"][j, :, :]),
          writes=[SPV], dma=True, stream="spv")
        P('sp', lambda e, j=j: e.dma_start(out=WOF[:, :], in_=env["woff"][j, :, :]),
          writes=[WOF], dma=True, stream="wof")
        P('dve', lambda e: e.tensor_tensor(SPV[:, :], SPV[:, :], WOF[:, :], ALU.mult),
          reads=[SPV, WOF], writes=[SPV])
        P('dve', lambda e: e.tensor_tensor(SAL[:, :], SAL[:, :], SPV[:, :], ALU.add),
          reads=[SAL, SPV], writes=[SAL])


def alloc_ret_tiles(nc, stack, env, T, tw, with_main):
    sb = lambda name, shape, dt=F32: stack.enter_context(nc.sbuf_tensor(name, shape, dt))
    psb = lambda name, shape, dt=F32: stack.enter_context(nc.psum_tensor(name, shape, dt))
    env["XN"] = sb("XN", [128, NFC * T], BF16)
    env["KT"] = sb("KT", [128, (tw // 128) * D], BF16)
    env["VT"] = sb("VT", [128, (tw // 128) * D], BF16)
    env["PWSB"] = sb("PWSB", [128, NFC * D], BF16)
    env["SAL"] = sb("SAL", [64, 1024], F32)
    env["SPV"] = sb("SPV", [64, 1024], F32)
    env["WOF"] = sb("WOF", [64, 1024], F32)
    env["SCR"] = sb("SCR", [128, 3072], F32)
    env["RW"] = sb("RW", [1, 512], F32)
    env["OC32"] = sb("OC32", [128, 1], F32)
    env["OR32"] = sb("OR32", [1, 128], F32)
    env["LN1W"] = sb("LN1W", [128, 8], F32)
    env["LN1B"] = sb("LN1B", [128, 8], F32)
    env["GCC"] = sb("GCC", [64, 1024], F32)
    if with_main:
        env["QDZE"] = sb("QDZE", [128, NFC * tw], BF16)
        env["QDZO"] = sb("QDZO", [128, NFC * tw], BF16)
        env["KD"] = sb("KD", [128, NFC * tw], BF16)
        env["GT"] = sb("GT", [128, (tw // 128) * D], BF16)
        env["WOSB"] = sb("WOSB", [128, NFC * D], BF16)
        env["SCHKB"] = sb("SCHKB", [128, (tw // CC) * 1024], BF16)
        env["PMS"] = sb("PMS", [128, NH * 128], BF16)
        env["OEV"] = sb("OEV", [128, 1024], F32)
        env["SQT"] = sb("SQT", [128, 1024], F32)
        env["STT"] = sb("STT", [128, 64], F32)
        env["GTD"] = sb("GTD", [128, 1024], BF16)
        env["GFM"] = sb("GFM", [128, NFC * 512], BF16)
        env["HEV"] = sb("HEV", [128, 1024], F32)
        env["TRIU4"] = sb("TRIU4", [128, 512], BF16)
        env["DECQ"] = sb("DECQ", [128, 8 * 512], BF16)
        env["DECK"] = sb("DECK", [128, 8 * 512], BF16)
        env["IDB"] = sb("IDB", [128, 128], BF16)
    env["DECKTM"] = sb("DECKTM", [128, 1024], BF16)
    env["PS"] = [psb(f"PS{i}", [128, 512]) for i in range(5)] + [None] + \
                [psb("PS6", [1, 512]), psb("PS7", [1, 512])]
    if with_main:
        env["TPSB"] = psb("TPSB", [128, 1024], BF16)


def load_ret_consts(pr, env, consts, rconsts, with_main):
    P = pr.op
    loads = [("OC32", consts["ones_col_f32"]), ("OR32", consts["ones_row_f32"]),
             ("GCC", rconsts["gcc"]), ("DECKTM", rconsts["decktm"])]
    if with_main:
        loads += [("TRIU4", rconsts["triu4"]), ("DECQ", rconsts["decq"]),
                  ("DECK", rconsts["deck"]), ("IDB", consts["id_bf"])]
    for nm, h in loads:
        P('sp', lambda e, nm=nm, h=h: e.dma_start(out=env[nm][:, :], in_=h.ap()),
          writes=[env[nm]], dma=True, stream=f"c_{nm}")
    for nm, src in [("LN1W", "ln1w"), ("LN1B", "ln1b")]:
        P('sp', lambda e, nm=nm, src=src: e.dma_start(out=env[nm][:, :], in_=env[src][:, :]),
          writes=[env[nm]], dma=True, stream=f"c_{nm}")


def build_retstate_launch(T):
    nc = bass.Bass("TRN2", target_bir_lowering=False)
    cfg = dict(T=T)
    env = {}
    env["xin"] = nc.dram_tensor("xin", [D, T], F32, kind="ExternalInput").ap()
    env["wk"] = nc.dram_tensor("wk", [D, D], F32, kind="ExternalInput").ap()
    env["wv"] = nc.dram_tensor("wv", [D, D], F32, kind="ExternalInput").ap()
    env["ln1w"] = nc.dram_tensor("ln1w", [128, 8], F32, kind="ExternalInput").ap()
    env["ln1b"] = nc.dram_tensor("ln1b", [128, 8], F32, kind="ExternalInput").ap()
    env["segstate"] = nc.dram_tensor("segstate", [64, 1024], F32, kind="ExternalOutput").ap()
    consts = make_consts(nc)
    rconsts = make_ret_consts(nc)
    tw = 512
    with ExitStack() as stack:
        alloc_ret_tiles(nc, stack, env, T, tw, with_main=False)
        print("C1 SBUF remaining:", nc.sbuf_bytes_remaining)
        pr = Prog(nc)
        load_ret_consts(pr, env, consts, rconsts, with_main=False)
        emit_xn1(pr, env, cfg, out_sbuf=(env["XN"], T))
        pr.fence()
        P = pr.op
        P('dve', lambda e: e.memset(env["SAL"][:, :], 0.0), writes=[env["SAL"]])
        for q in range(T // tw):
            emit_proj_tm(pr, env, cfg, env["wk"], env["XN"], env["KT"], q * tw, tw,
                         dec_const=env["DECKTM"])
            emit_proj_tm(pr, env, cfg, env["wv"], env["XN"], env["VT"], q * tw, tw)
            pr.fence()
            emit_ret_state_chunks(pr, env, cfg, env["KT"], env["VT"], q * (tw // CC),
                                  tw // CC, save=False)
            pr.fence()
        P('sp', lambda e: e.dma_start(out=env["segstate"][:, :], in_=env["SAL"][:, :]),
          reads=[env["SAL"]], dma=True, stream="out")
        pr.finish()
        with nc.Block() as block:
            pr.build(block)
    return nc


def build_l1attn_launch(T, trunc=99, mtrunc=99, mhg=4, mmask=True):
    nc = bass.Bass("TRN2", target_bir_lowering=False)
    cfg = dict(T=T)
    env = {}
    env["xin"] = nc.dram_tensor("xin", [D, T], F32, kind="ExternalInput").ap()
    for w in ["wq", "wk", "wv", "wg", "wo"]:
        env[w] = nc.dram_tensor(w, [D, D], F32, kind="ExternalInput").ap()
    env["ln1w"] = nc.dram_tensor("ln1w", [128, 8], F32, kind="ExternalInput").ap()
    env["ln1b"] = nc.dram_tensor("ln1b", [128, 8], F32, kind="ExternalInput").ap()
    env["sprev"] = nc.dram_tensor("sprev", [3, 64, 1024], F32, kind="ExternalInput").ap()
    env["woff"] = nc.dram_tensor("woff", [3, 64, 1024], F32, kind="ExternalInput").ap()
    env["HATT"] = nc.dram_tensor("hatt", [D, T], F32, kind="ExternalOutput").ap()
    consts = make_consts(nc)
    rconsts = make_ret_consts(nc)
    tw = 512
    with ExitStack() as stack:
        alloc_ret_tiles(nc, stack, env, T, tw, with_main=True)
        print("C2a SBUF remaining:", nc.sbuf_bytes_remaining)
        pr = Prog(nc)
        P = pr.op
        load_ret_consts(pr, env, consts, rconsts, with_main=True)
        for fc in range(NFC):
            P('pool', lambda e, fc=fc: e.dma_start(
                out=env["WOSB"][:, fc * D:(fc + 1) * D], in_=env["wo"][fc * 128:(fc + 1) * 128, :]),
              writes=[(env["WOSB"], fc)], dma=True, stream=f"wo{fc}")
        emit_xn1(pr, env, cfg, out_sbuf=(env["XN"], T))
        emit_soff_init(pr, env)
        pr.fence()
        cfg["mtrunc"] = mtrunc; cfg["mhg"] = mhg; cfg["mmask"] = mmask
        P('dve', lambda e: e.memset(env["QDZE"][64:128, :], 0.0),
          writes=[(env["QDZE"], 'z')])
        P('dve', lambda e: e.memset(env["QDZO"][0:64, :], 0.0),
          writes=[(env["QDZO"], 'z')])
        for q in range(T // tw) if trunc >= 4 else range(1 if trunc >= 2 else 0):
            emit_proj_fm(pr, env, cfg, env["wq"], env["XN"], None, q * tw, tw,
                         dec_const=env["DECQ"], qdz=(env["QDZE"], env["QDZO"]))
            emit_proj_fm(pr, env, cfg, env["wk"], env["XN"], env["KD"], q * tw, tw,
                         dec_const=env["DECK"])
            emit_proj_tm(pr, env, cfg, env["wk"], env["XN"], env["KT"], q * tw, tw,
                         dec_const=env["DECKTM"])
            emit_proj_tm(pr, env, cfg, env["wv"], env["XN"], env["VT"], q * tw, tw)
            emit_proj_tm(pr, env, cfg, env["wg"], env["XN"], env["GT"], q * tw, tw,
                         act=AF.Silu)
            pr.fence()
            if trunc >= 3:
                emit_ret_state_chunks(pr, env, cfg, env["KT"], env["VT"], q * (tw // CC),
                                      tw // CC, save=True)
                pr.fence()
            if trunc >= 4:
                emit_ret_main_half(pr, env, cfg, env["QDZE"], env["KD"], env["KT"], env["VT"],
                                   env["GT"], q * tw, tw)
                pr.fence()
        pr.finish()
        with nc.Block() as block:
            pr.build(block)
    return nc


def emit_xn1(pr, env, cfg, out_sbuf):
    emit_ln_sum(pr, env, cfg, env["xin"], None, None,
                affine=(env["LN1W"], env["LN1B"]), out_sbuf=out_sbuf)



NCC = 22
LASTW = 2730 - 21 * 128


def emit_swiglu(pr, env, cfg):
    T = cfg["T"]
    P = pr.op
    XN2 = env["XN2"]
    WIN = env["WIN"]
    ASB = env["ASB"]; BSB = env["BSB"]
    WO2 = env["WO2"]
    SWEV = env["SWEV"]
    BIN = env["BIN"]
    BOUT = env["BOUT"]
    PS = env["PS"]
    def ldwo2a(e):
        srcw = env["w_out_sw"][0:21 * 128, :].rearrange("(cc p) d -> p cc d", p=128)
        dstw = WO2[:, 0:21 * D].rearrange("p (cc d) -> p cc d", cc=21)
        return e.dma_start(out=dstw, in_=srcw)
    P('pool', ldwo2a, writes=[(WO2, cc) for cc in range(21)], dma=True, stream="wo2a")
    P('pool', lambda e: e.dma_start(
        out=WO2[0:LASTW, 21 * D:22 * D], in_=env["w_out_sw"][21 * 128:2730, :]),
      writes=[(WO2, 21)], dma=True, stream="wo2b")
    for t5 in range(T // 512):
        t0 = t5 * 512
        for part in range(2):
            TGT = ASB if part == 0 else BSB
            for cc in range(NCC):
                cw = LASTW if cc == NCC - 1 else 128
                c0 = part * 2730 + cc * 128
                def ldw(e, c0=c0, cw=cw):
                    src = env["w_in_sw"][:, c0:c0 + cw].rearrange("(fc p) c -> p fc c", p=128)
                    dst = WIN[:, 0:NFC * cw].rearrange("p (fc c) -> p fc c", fc=NFC)
                    return e.dma_start(out=dst, in_=src)
                P('pool', ldw, writes=[WIN], dma=True, stream="win")
                psf = PS[cc % 2]
                ps = psf[0:cw, 0:512]
                mm_group(P, psf, 'sg', [
                    ((lambda e, start, stop, fc=fc, cw=cw, t0=t0, ps=ps: e.matmul(
                        ps, WIN[:, fc * cw:(fc + 1) * cw],
                        XN2[:, fc * T + t0: fc * T + t0 + 512], start=start, stop=stop)),
                     [WIN] + [(XN2, (fc, t0 // 128 + i)) for i in range(4)])
                    for fc in range(NFC)])
                af = AF.Identity if part == 0 else AF.Silu
                P('act', lambda e, cc=cc, cw=cw, part=part, af=af, ps=ps, TGT=TGT: e.activation(
                    TGT[0:cw, cc * 512:(cc + 1) * 512], ps, af,
                    bias=BIN[0:cw, part * NCC + cc: part * NCC + cc + 1]),
                  reads=[psf, BIN], writes=[(TGT, cc)])
        for cc in range(NCC):
            cw = LASTW if cc == NCC - 1 else 128
            P('dve', lambda e, cc=cc, cw=cw: e.tensor_tensor(
                ASB[0:cw, cc * 512:(cc + 1) * 512], ASB[0:cw, cc * 512:(cc + 1) * 512],
                BSB[0:cw, cc * 512:(cc + 1) * 512], ALU.mult),
              reads=[(ASB, cc), (BSB, cc)], writes=[(ASB, cc)])
        for dc in range(NFC):
            psf = PS[2 + dc % 2]
            ps = psf[:, 0:512]
            mm_group(P, psf, 'so', [
                ((lambda e, start, stop, cc=cc, dc=dc, ps=ps: e.matmul(
                    ps, WO2[0:(LASTW if cc == NCC - 1 else 128), cc * D + dc * 128: cc * D + (dc + 1) * 128],
                    ASB[0:(LASTW if cc == NCC - 1 else 128), cc * 512:(cc + 1) * 512],
                    start=start, stop=stop)),
                 [(WO2, cc), (ASB, cc)])
                for cc in range(NCC)])
            P('act', lambda e, dc=dc, ps=ps: e.activation(
                SWEV[:, (dc % 2) * 512:(dc % 2) * 512 + 512], ps, AF.Identity,
                bias=BOUT[:, dc:dc + 1]),
              reads=[psf, BOUT], writes=[(SWEV, dc % 2)])
            def acc(e, t5=t5, dc=dc):
                dst = env["XB"][dc * 128:(dc + 1) * 128, t5 * 512:(t5 + 1) * 512]
                return e.dma_start(out=dst, in_=SWEV[:, (dc % 2) * 512:(dc % 2) * 512 + 512],
                                   accum_op=ALU.add)
            P('pool', acc, reads=[(SWEV, dc % 2), (env["XB"], ('init', t5))],
              writes=[(env["XB"], t5)], dma=True, stream=f"xb{t5}")


def build_swiglu_launch(T):
    nc = bass.Bass("TRN2", target_bir_lowering=False)
    cfg = dict(T=T)
    env = {}
    env["xin"] = nc.dram_tensor("xin", [D, T], F32, kind="ExternalInput").ap()
    env["hatt"] = nc.dram_tensor("hatt", [D, T], F32, kind="ExternalInput").ap()
    env["w_in_sw"] = nc.dram_tensor("w_in_sw", [D, 5460], F32, kind="ExternalInput").ap()
    env["w_out_sw"] = nc.dram_tensor("w_out_sw", [2730, D], F32, kind="ExternalInput").ap()
    env["ln2w"] = nc.dram_tensor("ln2w", [128, 8], F32, kind="ExternalInput").ap()
    env["ln2b"] = nc.dram_tensor("ln2b", [128, 8], F32, kind="ExternalInput").ap()
    env["binr"] = nc.dram_tensor("binr", [2, 128, NCC], F32, kind="ExternalInput").ap()
    env["boutr"] = nc.dram_tensor("boutr", [128, 8], F32, kind="ExternalInput").ap()
    env["xout"] = nc.dram_tensor("xout", [D, T], F32, kind="ExternalOutput").ap()
    env["XA"] = nc.dram_tensor("xa", [D, T], F32, kind="Internal").ap()
    env["XB"] = nc.dram_tensor("xb", [D, T], F32, kind="Internal").ap()
    consts = make_consts(nc)
    with ExitStack() as stack:
        sb = lambda name, shape, dt=F32: stack.enter_context(nc.sbuf_tensor(name, shape, dt))
        psb = lambda name, shape, dt=F32: stack.enter_context(nc.psum_tensor(name, shape, dt))
        env["XN2"] = sb("XN2", [128, NFC * T], BF16)
        env["WIN"] = sb("WIN", [128, NFC * 128], BF16)
        env["ASB"] = sb("ASB", [128, NCC * 512], BF16)
        env["BSB"] = sb("BSB", [128, NCC * 512], BF16)
        env["WO2"] = sb("WO2", [128, NCC * D], BF16)
        env["SWEV"] = sb("SWEV", [128, 1024], F32)
        env["BIN"] = sb("BIN", [128, 2 * NCC], F32)
        env["BOUT"] = sb("BOUT", [128, 8], F32)
        env["SCR"] = sb("SCR", [128, 3072], F32)
        env["RW"] = sb("RW", [1, 512], F32)
        env["OC32"] = sb("OC32", [128, 1], F32)
        env["OR32"] = sb("OR32", [1, 128], F32)
        env["LN2W"] = sb("LN2W", [128, 8], F32)
        env["LN2B"] = sb("LN2B", [128, 8], F32)
        env["PS"] = [psb(f"PS{i}", [128, 512]) for i in range(5)] + [None] + \
                    [psb("PS6", [1, 512]), psb("PS7", [1, 512])]
        print("C2b SBUF remaining:", nc.sbuf_bytes_remaining)
        pr = Prog(nc)
        P = pr.op
        for nm, h in [("OC32", consts["ones_col_f32"]), ("OR32", consts["ones_row_f32"])]:
            P('sp', lambda e, nm=nm, h=h: e.dma_start(out=env[nm][:, :], in_=h.ap()),
              writes=[env[nm]], dma=True, stream=f"c_{nm}")
        for nm, srcn in [("LN2W", "ln2w"), ("LN2B", "ln2b"), ("BOUT", "boutr")]:
            P('sp', lambda e, nm=nm, srcn=srcn: e.dma_start(out=env[nm][:, :], in_=env[srcn][:, :]),
              writes=[env[nm]], dma=True, stream=f"c_{nm}")
        P('sp', lambda e: e.dma_start(out=env["BIN"][:, 0:NCC], in_=env["binr"][0, :, :]),
          writes=[(env["BIN"], 0)], dma=True, stream="c_bin0")
        P('sp', lambda e: e.dma_start(out=env["BIN"][:, NCC:2 * NCC], in_=env["binr"][1, :, :]),
          writes=[(env["BIN"], 1)], dma=True, stream="c_bin1")
        pr.op('dve', lambda e: e.engine_nop(), reads=[(env["BIN"], 0), (env["BIN"], 1)],
              writes=[env["BIN"]])
        emit_ln_sum(pr, env, cfg, env["xin"], env["hatt"], None,
                    affine=(env["LN2W"], env["LN2B"]), out_sbuf=(env["XN2"], T),
                    sum_dram=env["XA"])
        pr.fence()
        for t5 in range(T // 512):
            P('sp', lambda e, t5=t5: e.dma_start(
                out=env["XB"][:, t5 * 512:(t5 + 1) * 512],
                in_=env["XA"][:, t5 * 512:(t5 + 1) * 512]),
              reads=[(env["XA"], ('sum', g)) for g in range(t5 * 4, t5 * 4 + 4)],
              writes=[(env["XB"], ('init', t5))], dma=True, stream=f"xbi{t5}")
        emit_swiglu(pr, env, cfg)
        pr.fence()
        emit_ln_sum(pr, env, cfg, env["xin"], env["XB"], env["xout"],
                    b_deps=[(env["XB"], t5) for t5 in range(T // 512)])
        pr.finish()
        with nc.Block() as block:
            pr.build(block)
    return nc


LVL_BS = [4, 8, 16, 32]
LVL_NN = [8, 4, 2, 1]
LVL_R = [4, 8, 8, 8]
LVL_OFF = [0, 32, 64, 80]
NNR = 88
DHD = 64


def hss_host_embeds(UL, VR, UR, VL):
    EUL = np.zeros((64, NNR), np.float32)
    EVR = np.zeros((64, NNR), np.float32)
    EVL = np.zeros((64, NNR), np.float32)
    EULT = np.zeros((NNR, 64), np.float32)
    EURT = np.zeros((NNR, 64), np.float32)
    for l in range(4):
        bs, nn, r, off = LVL_BS[l], LVL_NN[l], LVL_R[l], LVL_OFF[l]
        for n in range(nn):
            lo = 2 * bs * n
            hi = lo + bs
            cs = off + n * r
            EUL[lo:lo + bs, cs:cs + r] = UL[l]
            EVR[hi:hi + bs, cs:cs + r] = VR[l]
            EVL[lo:lo + bs, cs:cs + r] = VL[l]
            EULT[cs:cs + r, lo:lo + bs] = UL[l].T
            EURT[cs:cs + r, hi:hi + bs] = UR[l].T
    return {"EUL": EUL, "EVR": EVR, "EVL": EVL, "EULT": EULT, "EURT": EURT}


def hss_state_masks():
    mleaf = np.zeros((64, 65), np.float32)
    for n in range(16):
        mleaf[4 * n:4 * n + 4, 4 * n:4 * n + 4] = 1
    mleaf[:, 64] = 1
    mlvl = np.zeros((32, 176), np.float32)
    col = 0
    for l in range(4):
        nn, r = LVL_NN[l], LVL_R[l]
        for rep in range(2):
            for n in range(nn):
                mlvl[n * r:(n + 1) * r, col + n * r: col + (n + 1) * r] = 1
            col += nn * r
    return mleaf, mlvl


LP_OFF = []
_c = 0
for _l in range(4):
    LP_OFF.append((_c, _c + LVL_NN[_l] * LVL_R[_l]))
    _c += 2 * LVL_NN[_l] * LVL_R[_l]


def make_hss_consts(nc):
    c = {}
    el = np.zeros((4, 16 * 64), np.float32)
    for n in range(16):
        for j in range(4):
            el[j, n * 64 + 4 * n + j] = 1.0
    c["eleaf"] = nc.inline_tensor(bfc(el), "c_eleaf")
    mleaf, mlvl = hss_state_masks()
    c["mskleaf"] = nc.inline_tensor(mleaf, "c_mskleaf")
    c["msklvl"] = nc.inline_tensor(mlvl, "c_msklvl")
    return c


def emit_hss_proj(pr, env, cfg):
    T = cfg["T"]
    P = pr.op
    PS = env["PS"]
    SCR = env["SCR"]
    for name, wkey, elu in [("QF", "Wq", True), ("KF", "Wk", True), ("VF", "Wv", False)]:
        OUT = env[name]
        WSB = env["HWSB"]
        for fc in range(NFC):
            P('pool', lambda e, fc=fc, wkey=wkey: e.dma_start(
                out=WSB[:, fc * DHD:(fc + 1) * DHD],
                in_=env[wkey][fc * 128:(fc + 1) * 128, :]),
              writes=[(WSB, fc)], dma=True, stream=f"hw{fc}")
        for t5 in range(T // 512):
            psf = PS[t5 % 2]
            ps = psf[0:64, 0:512]
            mm_group(P, psf, 'hp', [
                ((lambda e, start, stop, fc=fc, t5=t5, ps=ps: e.matmul(
                    ps, WSB[:, fc * DHD:(fc + 1) * DHD],
                    env["XBF"][:, fc * T + t5 * 512: fc * T + (t5 + 1) * 512],
                    start=start, stop=stop)),
                 [(WSB, fc), (env["XBF"], fc)])
                for fc in range(NFC)])
            if elu:
                P('dve', lambda e, ps=ps: e.tensor_scalar_min(SCR[0:64, 0:512], ps, 0.0),
                  reads=[psf], writes=[(SCR, 'e1')])
                P('act', lambda e: e.activation(SCR[0:64, 512:1024], SCR[0:64, 0:512], AF.Exp),
                  reads=[(SCR, 'e1')], writes=[(SCR, 'e2')])
                P('dve', lambda e, ps=ps: e.tensor_scalar_add(SCR[0:64, 1024:1536], ps, 1.0),
                  reads=[psf], writes=[(SCR, 'e3')])
                P('dve', lambda e, t5=t5, OUT=OUT: e.tensor_tensor(
                    OUT[0:64, t5 * 512:(t5 + 1) * 512], SCR[0:64, 1024:1536],
                    SCR[0:64, 512:1024], ALU.max),
                  reads=[(SCR, 'e2'), (SCR, 'e3')], writes=[(OUT, t5)])
            else:
                P('dve', lambda e, t5=t5, ps=ps, OUT=OUT: e.tensor_copy(
                    OUT[0:64, t5 * 512:(t5 + 1) * 512], ps),
                  reads=[psf], writes=[(OUT, t5)])


WOFF_L = [0, 8, 12, 14]


def emit_hss_compact(pr, env, cfg):
    T = cfg["T"]
    P = pr.op
    PS = env["PS"]
    for cname, ename, src, resident, fbslot in [
            ("AQHC", "EVRB", "QF", True, 2), ("AQLC", "EVLB", "QF", True, 3),
            ("AKC", "EULB", "KF", False, 0), ("BVC", "EVRB", "VF", False, 1)]:
        for l in range(4):
            nnr = LVL_NN[l] * LVL_R[l]
            for t5 in range(T // 512):
                psf = PS[t5 % 2]
                ps = psf[0:nnr, 0:512]
                P('pe', lambda e, l=l, t5=t5, ps=ps, ename=ename, src=src, nnr=nnr: e.matmul(
                    ps, env[ename][:, LVL_OFF[l]:LVL_OFF[l] + nnr],
                    env[src][0:64, t5 * 512:(t5 + 1) * 512], start=True, stop=True),
                  reads=[env[ename], (env[src], t5)], writes=[psf])
                if resident:
                    CT = env[cname]
                    P('dve', lambda e, l=l, t5=t5, ps=ps, CT=CT, nnr=nnr: e.tensor_copy(
                        CT[0:nnr, l * T + t5 * 512: l * T + (t5 + 1) * 512], ps),
                      reads=[psf], writes=[(CT, (l, t5))])
                else:
                    CR = env["CROT"]
                    P('dve', lambda e, ps=ps, CR=CR, nnr=nnr: e.tensor_copy(
                        CR[0:nnr, 0:512], ps),
                      reads=[psf], writes=[CR])
                    P('sp', lambda e, l=l, t5=t5, CR=CR, nnr=nnr, fbslot=fbslot: e.dma_start(
                        out=env["FBD"][fbslot, LVL_OFF[l]:LVL_OFF[l] + nnr,
                                       t5 * 512:(t5 + 1) * 512],
                        in_=CR[0:nnr, 0:512]),
                      reads=[CR], writes=[(env["FBD"], (fbslot, l, t5))], dma=True,
                      stream="bounceA")
        if resident:
            CT = env[cname]
            for l in range(4):
                nnr = LVL_NN[l] * LVL_R[l]
                P('sp', lambda e, l=l, CT=CT, nnr=nnr, fbslot=fbslot: e.dma_start(
                    out=env["FBD"][fbslot, LVL_OFF[l]:LVL_OFF[l] + nnr, :],
                    in_=CT[0:nnr, l * T:(l + 1) * T]),
                  reads=[(CT, (l, t5)) for t5 in range(T // 512)],
                  writes=[(env["FBD"], (fbslot, l))], dma=True, stream="bounceB")
    for i, src in enumerate(["KF", "VF", "QF"]):
        P('sp', lambda e, i=i, src=src: e.dma_start(
            out=env["FBL"][i, :, :], in_=env[src][0:64, :]),
          reads=[(env[src], t5) for t5 in range(T // 512)],
          writes=[(env["FBL"], i)], dma=True, stream="bounceB")


def emit_hss_wide_group(pr, env, cfg, g2):
    T = cfg["T"]
    P = pr.op
    GW = 256
    sl = slice(g2 * GW, (g2 + 1) * GW)
    for wname, fbslot in [("WAK", 0), ("WBV", 1), ("WAQH", 2), ("WAQL", 3)]:
        WT = env[wname]
        deps = [(env["FBD"], (fbslot, l)) for l in range(4)] + \
               [(env["FBD"], (fbslot, l, t5)) for l in range(4) for t5 in range(T // 512)]
        def ld0(e, fbslot=fbslot, WT=WT, sl=sl):
            s = env["FBD"][fbslot, 0:32, sl].rearrange("(n r) t -> r n t", r=4)
            d = WT[0:4, 0:8 * GW].rearrange("p (n t) -> p n t", n=8)
            return e.dma_start(out=d, in_=s)
        P('sp', ld0, reads=deps, writes=[(WT, 0)], dma=True, stream=f"w_{wname}0")
        def ld1(e, fbslot=fbslot, WT=WT, sl=sl):
            s = env["FBD"][fbslot, 32:88, sl].rearrange("(n r) t -> r n t", r=8)
            d = WT[0:8, 8 * GW:15 * GW].rearrange("p (n t) -> p n t", n=7)
            return e.dma_start(out=d, in_=s)
        P('sp', ld1, reads=deps, writes=[(WT, 1)], dma=True, stream=f"w_{wname}1")
    for i, wname in [(0, "WKL"), (1, "WVL"), (2, "WQL")]:
        WT = env[wname]
        def ldl(e, i=i, WT=WT, sl=sl):
            s = env["FBL"][i, :, sl].rearrange("(n r) t -> r n t", r=4)
            d = WT[0:4, 0:16 * GW].rearrange("p (n t) -> p n t", n=16)
            return e.dma_start(out=d, in_=s)
        P('sp', ldl, reads=[(env["FBL"], i)], writes=[WT], dma=True, stream=f"w_{wname}")


def hss_pieces():
    ps = []
    for l in range(4):
        for n in range(LVL_NN[l]):
            ps.append(("top", l, n))
            ps.append(("bot", l, n))
    for n in range(16):
        ps.append(("lf", n))
    ps.append(("den",))
    return ps


def emit_hss_chunk(pr, env, cfg, ci, statesonly=False):
    T = cfg["T"]
    P = pr.op
    PS = env["PS"]
    GW = 256
    co = (ci % 2) * 128
    QF, KF, VF = env["QF"], env["KF"], env["VF"]
    t5 = (ci * 128) // 512
    qk = (QF, t5); kk = (KF, t5)

    def P3(eng, fn, reads=(), writes=()):
        P(eng, fn, reads=list(reads) + [(PS[3], 'cl')], writes=list(writes) + [(PS[3], 'cl')])

    def P4(eng, fn, reads=(), writes=()):
        P(eng, fn, reads=list(reads) + [(PS[4], 'cl')], writes=list(writes) + [(PS[4], 'cl')])

    for i, (src, ename, OUTT) in enumerate([("KF", "EULB", "AKTM"), ("VF", "EVRB", "BVTM")]):
        for l in range(4):
            nnr = LVL_NN[l] * LVL_R[l]
            P4('pe', lambda e, l=l, src=src, ename=ename, i=i, nnr=nnr: e.matmul(
                PS[4][:, i * 128 + LVL_OFF[l]: i * 128 + LVL_OFF[l] + nnr],
                env[src][0:64, ci * 128:(ci + 1) * 128],
                env[ename][:, LVL_OFF[l]:LVL_OFF[l] + nnr], start=True, stop=True),
               reads=[(env[src], t5), env[ename]])
        P4('dve', lambda e, i=i, OUTT=OUTT: e.tensor_copy(
            env[OUTT][:, 0:NNR], PS[4][:, i * 128:i * 128 + NNR]),
           writes=[env[OUTT]])
    for src, OUTT in [("KF", "KTM"), ("VF", "VTM")]:
        P('pe', lambda e, src=src: e.transpose(
            env["TPSB"][:, 0:64], env[src][0:64, ci * 128:(ci + 1) * 128],
            env["IDB"][0:64, 0:64]),
          reads=[(env[src], t5), env["IDB"]], writes=[env["TPSB"]])
        P('dve', lambda e, OUTT=OUTT: e.tensor_copy(env[OUTT][:, 0:64], env["TPSB"][:, 0:64]),
          reads=[env["TPSB"]], writes=[env[OUTT]])

    if not statesonly:
        pieces = hss_pieces()

        def piece_score_ops(pc):
            if pc[0] in ("top", "bot"):
                _, l, n = pc
                r = LVL_R[l]
                off = (WOFF_L[l] + n) * GW + co
                wk = (0 if l == 0 else 1)
                if pc[0] == "top":
                    lh, rh = env["WBV"], env["WAQH"]
                else:
                    lh, rh = env["WAK"], env["WAQL"]
                return lh[0:r, off:off + 128], rh[0:r, off:off + 128], [(lh, wk), (rh, wk)]
            if pc[0] == "lf":
                n = pc[1]
                off = n * GW + co
                return (env["WVL"][0:4, off:off + 128], env["WQL"][0:4, off:off + 128],
                        [env["WVL"], env["WQL"]])
            return (KF[0:64, ci * 128:(ci + 1) * 128], QF[0:64, ci * 128:(ci + 1) * 128],
                    [kk, qk])

        for r0 in range(0, len(pieces), 4):
            rnd = pieces[r0:r0 + 4]
            psf = PS[(r0 // 4) % 2]
            for k, pc in enumerate(rnd):
                lhs, rhs, lk = piece_score_ops(pc)
                P('pe', lambda e, lhs=lhs, rhs=rhs, k=k, psf=psf: e.matmul(
                    psf[:, k * 128:(k + 1) * 128], lhs, rhs, start=True, stop=True),
                  reads=lk, writes=[(psf, k)])
            nk = len(rnd)
            P('dve', lambda e, r0=r0, nk=nk, psf=psf: e.tensor_tensor(
                env["PMSC"][:, r0 * 128:(r0 + nk) * 128], psf[:, 0:nk * 128],
                env["TRIU4"][:, 0:nk * 128], ALU.mult),
              reads=[(psf, k) for k in range(nk)] + [env["TRIU4"]],
              writes=[(env["PMSC"], r0 // 4)])

        def piece_um_ops(pc):
            if pc[0] in ("top", "bot"):
                _, l, n = pc
                r = LVL_R[l]
                off = (WOFF_L[l] + n) * GW + co
                wk = (0 if l == 0 else 1)
                eoff = (WOFF_L[l] + n) * 64
                if pc[0] == "top":
                    return (env["WAK"][0:r, off:off + 128],
                            env["EULTW"][0:r, eoff:eoff + 64],
                            [(env["WAK"], wk), env["EULTW"]])
                return (env["WBV"][0:r, off:off + 128],
                        env["EURTW"][0:r, eoff:eoff + 64],
                        [(env["WBV"], wk), env["EURTW"]])
            n = pc[1]
            off = n * GW + co
            return (env["WKL"][0:4, off:off + 128], env["ELEAF"][0:4, n * 64:(n + 1) * 64],
                    [env["WKL"], env["ELEAF"]])

        for r0 in range(0, 46, 8):
            rnd = pieces[r0:min(r0 + 8, 46)]
            for k, pc in enumerate(rnd):
                lhs, rhs, lk = piece_um_ops(pc)
                P3('pe', lambda e, lhs=lhs, rhs=rhs, k=k: e.matmul(
                    PS[3][:, k * 64:(k + 1) * 64], lhs, rhs, start=True, stop=True),
                   reads=lk)
            nk = len(rnd)
            P3('dve', lambda e, r0=r0, nk=nk: e.tensor_copy(
                env["UMS"][:, r0 * 64:(r0 + nk) * 64], PS[3][:, 0:nk * 64]),
               writes=[(env["UMS"], r0 // 8)])

        gspecs = []
        for l in range(4):
            nnr = LVL_NN[l] * LVL_R[l]
            mo, mto = 65 + LP_OFF[l][0], 65 + LP_OFF[l][1]
            for side in range(2):
                gspecs.append((l, nnr, (mto if side == 0 else mo), side))
        for r0 in range(0, 8, 4):
            for k in range(4):
                l, nnr, soff, side = gspecs[r0 + k]
                CQC = env["AQHC"] if side == 0 else env["AQLC"]
                P4('pe', lambda e, l=l, nnr=nnr, soff=soff, CQC=CQC, k=k: e.matmul(
                    PS[4][0:nnr, k * 128:(k + 1) * 128],
                    env["MSTB"][0:nnr, soff:soff + nnr],
                    CQC[0:nnr, l * T + ci * 128: l * T + (ci + 1) * 128],
                    start=True, stop=True),
                   reads=[env["MSTB"], (CQC, (l, t5))])
            P4('dve', lambda e, r0=r0: e.tensor_copy(
                env["GEV"][0:32, r0 * 128:(r0 + 4) * 128], PS[4][0:32, 0:512]),
               writes=[(env["GEV"], r0 // 4)])

        P('pe', lambda e: e.matmul(
            PS[2][0:64, 0:128], env["MSTB"][0:64, 0:64],
            QF[0:64, ci * 128:(ci + 1) * 128], start=True, stop=False),
          reads=[env["MSTB"], qk], writes=[(PS[2], 'y')])
        P3('pe', lambda e: e.matmul(
            PS[3][0:1, 0:128], env["MSTB"][0:64, 64:65],
            QF[0:64, ci * 128:(ci + 1) * 128], start=True, stop=False),
           reads=[env["MSTB"], qk], writes=[(PS[3], 'den')])
        gidx = 0
        for l in range(4):
            nnr = LVL_NN[l] * LVL_R[l]
            for side in range(2):
                ET = env["EULTL"] if side == 0 else env["EURTL"]
                P('pe', lambda e, l=l, nnr=nnr, ET=ET, gidx=gidx: e.matmul(
                    PS[2][0:64, 0:128], ET[0:nnr, l * 64:(l + 1) * 64],
                    env["GEV"][0:nnr, gidx * 128:(gidx + 1) * 128],
                    start=False, stop=False),
                  reads=[ET, (env["GEV"], gidx // 4), (PS[2], 'y')], writes=[(PS[2], 'y')])
                gidx += 1
        for pi, pc in enumerate(pieces):
            if pc[0] == "den":
                P3('pe', lambda e, pi=pi: e.matmul(
                    PS[3][0:1, 0:128], env["ONESB"][:, 0:1],
                    env["PMSC"][:, pi * 128:(pi + 1) * 128], start=False, stop=True),
                   reads=[env["ONESB"], (env["PMSC"], pi // 4), (PS[3], 'den')],
                   writes=[(PS[3], 'den')])
            else:
                stp = pi == len(pieces) - 2
                P('pe', lambda e, pi=pi, stp=stp: e.matmul(
                    PS[2][0:64, 0:128], env["UMS"][:, pi * 64:(pi + 1) * 64],
                    env["PMSC"][:, pi * 128:(pi + 1) * 128], start=False, stop=stp),
                  reads=[(env["UMS"], pi // 8), (env["PMSC"], pi // 4), (PS[2], 'y')],
                  writes=[(PS[2], 'y')])
        P('dve', lambda e, ci=ci: e.tensor_copy(
            env["YSEG"][0:64, ci * 128:(ci + 1) * 128], PS[2][0:64, 0:128]),
          reads=[(PS[2], 'y')], writes=[(env["YSEG"], ci)])
        P3('dve', lambda e, ci=ci: e.tensor_copy(
            env["YSEG"][64:65, ci * 128:(ci + 1) * 128], PS[3][0:1, 0:128]),
           reads=[(PS[3], 'den')], writes=[(env["YSEG"], (ci, 'd'))])

    P3('pe', lambda e: e.matmul(
        PS[3][0:64, 0:64], env["VTM"][:, 0:64], env["KTM"][:, 0:64], start=True, stop=True),
       reads=[env["VTM"], env["KTM"]])
    P3('pe', lambda e: e.matmul(
        PS[3][0:64, 64:65], env["KTM"][:, 0:64], env["ONESB"][:, 0:1], start=True, stop=True),
       reads=[env["KTM"], env["ONESB"]])
    for l in range(4):
        nnr = LVL_NN[l] * LVL_R[l]
        mo, mto = 65 + LP_OFF[l][0], 65 + LP_OFF[l][1]
        P3('pe', lambda e, l=l, nnr=nnr, mo=mo: e.matmul(
            PS[3][0:nnr, mo:mo + nnr], env["AKTM"][:, LVL_OFF[l]:LVL_OFF[l] + nnr],
            env["BVTM"][:, LVL_OFF[l]:LVL_OFF[l] + nnr], start=True, stop=True),
           reads=[env["AKTM"], env["BVTM"]])
        P3('pe', lambda e, l=l, nnr=nnr, mto=mto: e.matmul(
            PS[3][0:nnr, mto:mto + nnr], env["BVTM"][:, LVL_OFF[l]:LVL_OFF[l] + nnr],
            env["AKTM"][:, LVL_OFF[l]:LVL_OFF[l] + nnr], start=True, stop=True),
           reads=[env["AKTM"], env["BVTM"]])
    P3('dve', lambda e: e.tensor_tensor(env["SCR"][0:64, 0:65], PS[3][0:64, 0:65],
                                        env["MSKS"][:, 0:65], ALU.mult),
       reads=[env["MSKS"]], writes=[(env["SCR"], 'st')])
    P3('dve', lambda e: e.tensor_tensor(env["SCR"][0:32, 65:241], PS[3][0:32, 65:241],
                                        env["MSKS"][0:32, 65:241], ALU.mult),
       reads=[env["MSKS"]], writes=[(env["SCR"], 'st2')])
    P('dve', lambda e: e.tensor_tensor(env["MST"][:, 0:65], env["MST"][:, 0:65],
                                       env["SCR"][0:64, 0:65], ALU.add),
      reads=[env["MST"], (env["SCR"], 'st')], writes=[env["MST"]])
    P('dve', lambda e: e.tensor_tensor(env["MST"][0:32, 65:241], env["MST"][0:32, 65:241],
                                       env["SCR"][0:32, 65:241], ALU.add),
      reads=[env["MST"], (env["SCR"], 'st2')], writes=[env["MST"]])
    P('dve', lambda e: e.tensor_copy(env["MSTB"][:, 0:241], env["MST"][:, 0:241]),
      reads=[env["MST"]], writes=[env["MSTB"]])


def emit_hss_final(pr, env, cfg):
    T = cfg["T"]
    P = pr.op
    PS = env["PS"]
    YSEG = env["YSEG"]
    P('dve', lambda e: e.tensor_scalar_max(env["SCR"][0:1, 0:T], YSEG[64:65, 0:T], 1e-6),
      reads=[(YSEG, (ci, 'd')) for ci in range(T // 128)], writes=[(env["SCR"], 'd')])
    P('dve', lambda e: e.reciprocal(env["SCR"][0:1, 0:T], env["SCR"][0:1, 0:T]),
      reads=[(env["SCR"], 'd')], writes=[(env["SCR"], 'r')])
    for t5 in range(T // 512):
        P('pe', lambda e, t5=t5: e.matmul(
            PS[0][0:64, 0:512], env["OR32"][0:1, 0:64],
            env["SCR"][0:1, t5 * 512:(t5 + 1) * 512], start=True, stop=True),
          reads=[(env["SCR"], 'r'), env["OR32"]], writes=[(PS[0], 'b')])
        P('dve', lambda e, t5=t5: e.tensor_tensor(
            env["YN"][0:64, t5 * 512:(t5 + 1) * 512], YSEG[0:64, t5 * 512:(t5 + 1) * 512],
            PS[0][0:64, 0:512], ALU.mult),
          reads=[(PS[0], 'b')] + [(YSEG, ci) for ci in range(t5 * 4, t5 * 4 + 4)],
          writes=[(env["YN"], t5)])
        for dc in range(NFC):
            P('pe', lambda e, t5=t5, dc=dc: e.matmul(
                PS[1][:, 0:512], env["WOSB2"][0:64, dc * 128:(dc + 1) * 128],
                env["YN"][0:64, t5 * 512:(t5 + 1) * 512], start=True, stop=True),
              reads=[env["WOSB2"], (env["YN"], t5)], writes=[(PS[1], 'w')])
            P('dve', lambda e, dc=dc: e.tensor_copy(
                env["HEV"][:, 0:512], PS[1][:, 0:512]),
              reads=[(PS[1], 'w')], writes=[(env["HEV"], 0)])
            P('sp', lambda e, t5=t5, dc=dc: e.dma_start(
                out=env["ATT"][dc * 128:(dc + 1) * 128, t5 * 512:(t5 + 1) * 512],
                in_=env["HEV"][:, 0:512]),
              reads=[(env["HEV"], 0)],
              writes=[(env["ATT"], (t5, dc))], dma=True, stream=f"att{dc}")


def hss_dram_and_inputs(nc, env, T, full):
    env["xin"] = nc.dram_tensor("xin", [D, T], F32, kind="ExternalInput").ap()
    for w in ["Wq", "Wk", "Wv"]:
        env[w] = nc.dram_tensor(w, [D, DHD], F32, kind="ExternalInput").ap()
    env["EULd"] = nc.dram_tensor("EULs", [64, NNR], F32, kind="ExternalInput").ap()
    env["EVRd"] = nc.dram_tensor("EVRs", [64, NNR], F32, kind="ExternalInput").ap()
    env["EVLd"] = nc.dram_tensor("EVLs", [64, NNR], F32, kind="ExternalInput").ap()
    if full:
        env["Wo"] = nc.dram_tensor("Wo", [DHD, D], F32, kind="ExternalInput").ap()
        env["EULTWd"] = nc.dram_tensor("EULTW", [8, 15 * 64], F32, kind="ExternalInput").ap()
        env["EURTWd"] = nc.dram_tensor("EURTW", [8, 15 * 64], F32, kind="ExternalInput").ap()
        env["EULTLd"] = nc.dram_tensor("EULTL", [32, 4 * 64], F32, kind="ExternalInput").ap()
        env["EURTLd"] = nc.dram_tensor("EURTL", [32, 4 * 64], F32, kind="ExternalInput").ap()
        env["sprev"] = nc.dram_tensor("sprev", [3, 64, 241], F32, kind="ExternalInput").ap()
        env["FBD"] = nc.dram_tensor("fbd", [4, NNR, T], BF16, kind="Internal").ap()
        env["FBL"] = nc.dram_tensor("fbl", [3, 64, T], BF16, kind="Internal").ap()
        env["ATT"] = nc.dram_tensor("att", [D, T], F32, kind="Internal").ap()


def alloc_hss_tiles(nc, stack, env, T, full):
    sb = lambda name, shape, dt=F32: stack.enter_context(nc.sbuf_tensor(name, shape, dt))
    psb = lambda name, shape, dt=F32: stack.enter_context(nc.psum_tensor(name, shape, dt))
    GW = 256
    env["XBF"] = sb("XBF", [128, NFC * T], BF16)
    env["QF"] = sb("QF", [64, T], BF16)
    env["KF"] = sb("KF", [64, T], BF16)
    env["VF"] = sb("VF", [64, T], BF16)
    env["HWSB"] = sb("HWSB", [128, NFC * DHD], BF16)
    env["EULB"] = sb("EULB", [64, NNR], BF16)
    env["EVRB"] = sb("EVRB", [64, NNR], BF16)
    env["EVLB"] = sb("EVLB", [64, NNR], BF16)
    env["AKTM"] = sb("AKTM", [128, NNR], BF16)
    env["BVTM"] = sb("BVTM", [128, NNR], BF16)
    env["KTM"] = sb("KTM", [128, 64], BF16)
    env["VTM"] = sb("VTM", [128, 64], BF16)
    env["MST"] = sb("MST", [64, 256], F32)
    env["MSTB"] = sb("MSTB", [64, 256], BF16)
    env["MSKS"] = sb("MSKS", [64, 256], BF16)
    env["ONESB"] = sb("ONESB", [128, 1], BF16)
    env["SCR"] = sb("SCR", [128, 3072], F32)
    env["RW"] = sb("RW", [1, 512], F32)
    env["OC32"] = sb("OC32", [128, 1], F32)
    env["OR32"] = sb("OR32", [1, 128], F32)
    env["IDB"] = sb("IDB", [128, 128], BF16)
    if full:
        env["AQHC"] = sb("AQHC", [32, 4 * T], BF16)
        env["AQLC"] = sb("AQLC", [32, 4 * T], BF16)
        env["CROT"] = sb("CROT", [32, 512], BF16)
        for wname in ["WAK", "WBV", "WAQH", "WAQL"]:
            env[wname] = sb(wname, [8, 15 * GW], BF16)
        for wname in ["WKL", "WVL", "WQL"]:
            env[wname] = sb(wname, [4, 16 * GW], BF16)
        env["PMSC"] = sb("PMSC", [128, 47 * 128], BF16)
        env["UMS"] = sb("UMS", [128, 46 * 64], BF16)
        env["GEV"] = sb("GEV", [32, 8 * 128], BF16)
        env["YSEG"] = sb("YSEG", [65, T], F32)
        env["YN"] = sb("YN", [64, T], BF16)
        env["HEV"] = sb("HEV", [128, 1024], F32)
        env["WOSB2"] = sb("WOSB2", [64, D], BF16)
        env["EULTW"] = sb("EULTW_sb", [8, 15 * 64], BF16)
        env["EURTW"] = sb("EURTW_sb", [8, 15 * 64], BF16)
        env["EULTL"] = sb("EULTL_sb", [32, 4 * 64], BF16)
        env["EURTL"] = sb("EURTL_sb", [32, 4 * 64], BF16)
        env["ELEAF"] = sb("ELEAF", [4, 16 * 64], BF16)
        env["TRIU4"] = sb("TRIU4", [128, 512], BF16)
    env["PS"] = [psb(f"PS{i}", [128, 512]) for i in range(5)] + [None] + \
                [psb("PS6", [1, 512]), psb("PS7", [1, 512])]
    env["TPSB"] = psb("TPSB", [128, 1024], BF16)


def emit_hss_setup(pr, env, cfg, consts, hconsts, full):
    P = pr.op
    T = cfg["T"]
    for nm, h in [("OC32", consts["ones_col_f32"]), ("OR32", consts["ones_row_f32"]),
                  ("IDB", consts["id_bf"]), ("ONESB", consts["ones_row_bf"])]:
        if nm == "ONESB":
            P('pool', lambda e, h=h: e.dma_start(
                out=env["ONESB"][:, 0:1], in_=h.ap().rearrange("a b -> b a")),
              writes=[env["ONESB"]], dma=True, stream="c_onesb")
        else:
            P('sp', lambda e, nm=nm, h=h: e.dma_start(out=env[nm][:, :], in_=h.ap()),
              writes=[env[nm]], dma=True, stream=f"c_{nm}")
    P('sp', lambda e: e.dma_start(out=env["MSKS"][:, 0:241], in_=hconsts["msks"].ap()),
      writes=[env["MSKS"]], dma=True, stream="c_msks")
    for nm, d in [("EULB", "EULd"), ("EVRB", "EVRd"), ("EVLB", "EVLd")]:
        P('pool', lambda e, nm=nm, d=d: e.dma_start(out=env[nm][:, :], in_=env[d][:, :]),
          writes=[env[nm]], dma=True, stream=f"c_{nm}")
    if full:
        for nm, d in [("EULTW", "EULTWd"), ("EURTW", "EURTWd"),
                      ("EULTL", "EULTLd"), ("EURTL", "EURTLd")]:
            P('pool', lambda e, nm=nm, d=d: e.dma_start(out=env[nm][:, :], in_=env[d][:, :]),
              writes=[env[nm]], dma=True, stream=f"c_{nm}")
        P('sp', lambda e: e.dma_start(out=env["ELEAF"][:, :], in_=hconsts["eleaf"].ap()),
          writes=[env["ELEAF"]], dma=True, stream="c_eleaf")
        P('sp', lambda e: e.dma_start(out=env["TRIU4"][:, :], in_=hconsts["triu4b"].ap()),
          writes=[env["TRIU4"]], dma=True, stream="c_triu4")
        P('pool', lambda e: e.dma_start(out=env["WOSB2"][:, :], in_=env["Wo"][:, :]),
          writes=[env["WOSB2"]], dma=True, stream="c_wo")
    for fc in range(NFC):
        P('pool', lambda e, fc=fc: e.dma_start(
            out=env["XBF"][:, fc * T:(fc + 1) * T], in_=env["xin"][fc * 128:(fc + 1) * 128, :]),
          writes=[(env["XBF"], fc)], dma=True, stream=f"xbf{fc}")
    if full:
        P('sp', lambda e: e.dma_start(out=env["SCR"][0:64, 0:241], in_=env["sprev"][0, :, :]),
          writes=[(env["SCR"], 'sp0')], dma=True, stream="sp0")
        P('sp', lambda e: e.dma_start(out=env["SCR"][0:64, 256:497], in_=env["sprev"][1, :, :]),
          writes=[(env["SCR"], 'sp1')], dma=True, stream="sp1")
        P('sp', lambda e: e.dma_start(out=env["SCR"][0:64, 512:753], in_=env["sprev"][2, :, :]),
          writes=[(env["SCR"], 'sp2')], dma=True, stream="sp2")
        P('dve', lambda e: e.tensor_tensor(env["MST"][:, 0:241], env["SCR"][0:64, 0:241],
                                           env["SCR"][0:64, 256:497], ALU.add),
          reads=[(env["SCR"], 'sp0'), (env["SCR"], 'sp1')], writes=[env["MST"]])
        P('dve', lambda e: e.tensor_tensor(env["MST"][:, 0:241], env["MST"][:, 0:241],
                                           env["SCR"][0:64, 512:753], ALU.add),
          reads=[env["MST"], (env["SCR"], 'sp2')], writes=[env["MST"]])
    else:
        P('dve', lambda e: e.memset(env["MST"][:, 0:256], 0.0), writes=[env["MST"]])
    P('dve', lambda e: e.tensor_copy(env["MSTB"][:, 0:241], env["MST"][:, 0:241]),
      reads=[env["MST"]], writes=[env["MSTB"]])
    P('dve', lambda e: e.memset(env["PS"][3][:, 0:512], 0.0), writes=[(env["PS"][3], 'cl')])
    P('dve', lambda e: e.memset(env["PS"][4][:, 0:512], 0.0), writes=[(env["PS"][4], 'cl')])


def make_hss_consts2(nc):
    c = {}
    el = np.zeros((4, 16 * 64), np.float32)
    for n in range(16):
        for j in range(4):
            el[j, n * 64 + 4 * n + j] = 1.0
    c["eleaf"] = nc.inline_tensor(bfc(el), "c_eleaf")
    mleaf, mlvl = hss_state_masks()
    msks = np.zeros((64, 241), np.float32)
    msks[:, 0:65] = mleaf
    msks[0:32, 65:241] = mlvl
    c["msks"] = nc.inline_tensor(bfc(msks), "c_msks")
    tri = np.triu(np.ones((128, 128), np.float32))
    c["triu4b"] = nc.inline_tensor(bfc(np.tile(tri, (1, 4))), "c_triu4b")
    return c


def build_hss_state_launch(T):
    nc = bass.Bass("TRN2", target_bir_lowering=False)
    cfg = dict(T=T)
    env = {}
    hss_dram_and_inputs(nc, env, T, full=False)
    env["stateout"] = nc.dram_tensor("stateout", [64, 241], F32, kind="ExternalOutput").ap()
    consts = make_consts(nc)
    hconsts = make_hss_consts2(nc)
    with ExitStack() as stack:
        alloc_hss_tiles(nc, stack, env, T, full=False)
        print("A1 SBUF remaining:", nc.sbuf_bytes_remaining)
        pr = Prog(nc)
        emit_hss_setup(pr, env, cfg, consts, hconsts, full=False)
        emit_hss_proj(pr, env, cfg)
        pr.fence()
        for ci in range(T // 128):
            emit_hss_chunk(pr, env, cfg, ci, statesonly=True)
        pr.op('sp', lambda e: e.dma_start(out=env["stateout"][:, :], in_=env["MST"][:, 0:241]),
              reads=[env["MST"]], dma=True, stream="out")
        pr.finish()
        with nc.Block() as block:
            pr.build(block)
    return nc


def build_hss_launch(T):
    nc = bass.Bass("TRN2", target_bir_lowering=False)
    cfg = dict(T=T)
    env = {}
    hss_dram_and_inputs(nc, env, T, full=True)
    env["xout"] = nc.dram_tensor("xout", [D, T], F32, kind="ExternalOutput").ap()
    consts = make_consts(nc)
    hconsts = make_hss_consts2(nc)
    with ExitStack() as stack:
        alloc_hss_tiles(nc, stack, env, T, full=True)
        print("A2 SBUF remaining:", nc.sbuf_bytes_remaining)
        pr = Prog(nc)
        emit_hss_setup(pr, env, cfg, consts, hconsts, full=True)
        emit_hss_proj(pr, env, cfg)
        pr.fence()
        emit_hss_compact(pr, env, cfg)
        pr.fence()
        for g2 in range(T // 256):
            emit_hss_wide_group(pr, env, cfg, g2)
            for ci in (g2 * 2, g2 * 2 + 1):
                emit_hss_chunk(pr, env, cfg, ci, statesonly=False)
        pr.fence()
        emit_hss_final(pr, env, cfg)
        pr.fence()
        emit_ln_sum(pr, env, cfg, env["xin"], env["ATT"], env["xout"],
                    b_deps=[(env["ATT"], (t5, dc)) for t5 in range(T // 512) for dc in range(NFC)])
        pr.finish()
        with nc.Block() as block:
            pr.build(block)
    return nc





_ALIAS_RULES = [
    (r"^c_.*$|^c2_.*$", lambda m: "cst"),
    (r"^xbf(\d+)$", lambda m: f"ld{m.group(1)}"),
    (r"^hw(\d+)$", lambda m: f"ld2{m.group(1)}"),
    (r"^w1_(\d+)$", lambda m: f"ld2{m.group(1)}"),
    (r"^w2_(\d+)$", lambda m: f"st{m.group(1)}"),
    (r"^pw(\d+)$", lambda m: f"ld{m.group(1)}"),
    (r"^wo(\d+)$", lambda m: f"ld2{m.group(1)}"),
    (r"^yacc(\d+)$", lambda m: f"acc{m.group(1)}"),
    (r"^att(\d+)$", lambda m: f"acc{m.group(1)}"),
    (r"^hatt(\d+)$", lambda m: f"acc{m.group(1)}"),
    (r"^w_WAK0$", lambda m: "st0"), (r"^w_WAK1$", lambda m: "st1"),
    (r"^w_WBV0$", lambda m: "st2"), (r"^w_WBV1$", lambda m: "st3"),
    (r"^w_WAQH0$", lambda m: "st4"), (r"^w_WAQH1$", lambda m: "st5"),
    (r"^w_WAQL0$", lambda m: "st6"), (r"^w_WAQL1$", lambda m: "st7"),
    (r"^w_WKL$", lambda m: "lnA"), (r"^w_WVL$", lambda m: "lnB"),
    (r"^w_WQL$", lambda m: "lnC"),
    (r"^lnx0$|^lnsum$", lambda m: "lnA"),
    (r"^lnx1$", lambda m: "lnB"),
    (r"^lny0$|^lnout0$", lambda m: "lnC"),
    (r"^lny1$|^lnout1$", lambda m: "lnD"),
    (r"^xg0$", lambda m: "misc2"), (r"^xg1$", lambda m: "misc3"),
    (r"^xbi(\d+)$", lambda m: f"acc{m.group(1)}"),
    (r"^xb(\d+)$", lambda m: f"acc{4 + int(m.group(1))}"),
    (r"^cc1i$|^cc2i$", lambda m: "ccin"),
    (r"^wsg$", lambda m: "misc2"), (r"^wsw$", lambda m: "misc3"),
    (r"^ldg2?$", lambda m: "misc1"),
    (r"^b1$", lambda m: "misc0"), (r"^b2$", lambda m: "misc1"),
    (r"^xg$", lambda m: "misc2"), (r"^wgt$", lambda m: "misc3"),
    (r"^wo2a$", lambda m: "misc0"), (r"^wo2b$", lambda m: "misc1"),
    (r"^win$", lambda m: "misc2"),
    (r"^bounceA$", lambda m: "bounceA"), (r"^bounceB$", lambda m: "bounceB"),
]

import re as _re

def _alias(s):
    for pat, fn in _ALIAS_RULES:
        m = _re.match(pat, s)
        if m:
            return fn(m)
    return s


def _scr_view(scr, off, w):
    return scr[0:128, off:off + w]


class Arena:
    def __init__(self, nc, stack, bf_cols, f32_cols):
        self.bft = stack.enter_context(nc.sbuf_tensor("ARENA_BF", [128, bf_cols], BF16))
        self.f3t = stack.enter_context(nc.sbuf_tensor("ARENA_F3", [128, f32_cols], F32))
        self.bf_cols = bf_cols
        self.f32_cols = f32_cols
        self.hwm_b = 0
        self.hwm_f = 0
        self.reset()

    def reset(self):
        self.bo = 0
        self.fo = 0

    def alloc(self, name, shape, dt=F32):
        p, cols = shape
        if dt == BF16:
            off = (self.bo + 15) // 16 * 16
            self.bo = off + cols
            self.hwm_b = max(self.hwm_b, self.bo)
            assert self.bo <= self.bf_cols, f"bf arena overflow at {name}: {self.bo}"
            return self.bft[0:p, off:off + cols]
        off = (self.fo + 15) // 16 * 16
        self.fo = off + cols
        self.hwm_f = max(self.hwm_f, self.fo)
        assert self.fo <= self.f32_cols, f"f32 arena overflow at {name}: {self.fo}"
        return self.f3t[0:p, off:off + cols]


def alloc_common(nc, stack, env):
    psb = lambda name, shape, dt=F32: stack.enter_context(nc.psum_tensor(name, shape, dt))
    env["PS"] = [psb(f"PS{i}", [128, 512]) for i in range(5)] + [None] + \
                [psb("PS6", [1, 512]), psb("PS7", [1, 512])]
    env["TPSB"] = psb("TPSB", [128, 1024], BF16)
    sb = lambda name, shape, dt=F32: stack.enter_context(nc.sbuf_tensor(name, shape, dt))
    env["OC32"] = sb("OC32", [128, 1], F32)
    env["OR32"] = sb("OR32", [1, 128], F32)
    env["ID32"] = sb("ID32", [128, 128], F32)
    env["IDB"] = sb("IDB", [128, 128], BF16)
    env["ONESB"] = sb("ONESB", [128, 1], BF16)
    env["RW"] = sb("RW", [1, 1024], F32)


def load_common_consts(pr, env, consts):
    P = pr.op
    for nm, h in [("OC32", "ones_col_f32"), ("OR32", "ones_row_f32"),
                  ("ID32", "id_f32"), ("IDB", "id_bf")]:
        P('sp', lambda e, nm=nm, h=h: e.dma_start(out=env[nm][:, :], in_=consts[h].ap()),
          writes=[env[nm]], dma=True, stream=f"c_{nm}")
    P('pool', lambda e: e.dma_start(
        out=env["ONESB"][:, 0:1], in_=consts["ones_row_bf"].ap().rearrange("a b -> b a")),
      writes=[env["ONESB"]], dma=True, stream="c_onesb")


def alloc_hss_arena(sb, env, T, full):
    env["XBF"] = sb("XBF", [128, NFC * T], BF16)
    env["QF"] = sb("QF", [64, T], BF16)
    env["KF"] = sb("KF", [64, T], BF16)
    env["VF"] = sb("VF", [64, T], BF16)
    env["HWSB"] = sb("HWSB", [128, NFC * DHD], BF16)
    env["EULB"] = sb("EULB", [64, NNR], BF16)
    env["EVRB"] = sb("EVRB", [64, NNR], BF16)
    env["EVLB"] = sb("EVLB", [64, NNR], BF16)
    env["AKTM"] = sb("AKTM", [128, NNR], BF16)
    env["BVTM"] = sb("BVTM", [128, NNR], BF16)
    env["KTM"] = sb("KTM", [128, 64], BF16)
    env["VTM"] = sb("VTM", [128, 64], BF16)
    env["MST"] = sb("MST", [64, 256], F32)
    env["MSTB"] = sb("MSTB", [64, 256], BF16)
    env["MSKS"] = sb("MSKS", [64, 256], BF16)
    env["SCR"] = sb("SCR", [128, 5120], F32)
    if full:
        GW = 256
        env["AQHC"] = sb("AQHC", [32, 4 * T], BF16)
        env["AQLC"] = sb("AQLC", [32, 4 * T], BF16)
        env["CROT"] = sb("CROT", [32, 512], BF16)
        for wname in ["WAK", "WBV", "WAQH", "WAQL"]:
            env[wname] = sb(wname, [8, 15 * GW], BF16)
        for wname in ["WKL", "WVL", "WQL"]:
            env[wname] = sb(wname, [4, 16 * GW], BF16)
        env["PMSC"] = sb("PMSC", [128, 47 * 128], BF16)
        env["UMS"] = sb("UMS", [128, 46 * 64], BF16)
        env["GEV"] = sb("GEV", [32, 8 * 128], BF16)
        env["YSEG"] = sb("YSEG", [65, T], F32)
        env["YN"] = sb("YN", [64, T], BF16)
        env["HEV"] = sb("HEV", [128, 512], F32)
        env["WOSB2"] = sb("WOSB2", [64, D], BF16)
        env["EULTW"] = sb("EULTW_sb", [8, 960], BF16)
        env["EURTW"] = sb("EURTW_sb", [8, 960], BF16)
        env["EULTL"] = sb("EULTL_sb", [32, 256], BF16)
        env["EURTL"] = sb("EURTL_sb", [32, 256], BF16)
        env["ELEAF"] = sb("ELEAF", [4, 16 * 64], BF16)
        env["TRIU4"] = sb("TRIU4", [128, 512], BF16)
        env["GATH"] = env["SCR"][0:64, 2048:2304]
        env["HWT"] = env["SCR"][0:64, 2304:2560]


def alloc_moe_arena(env, sb, T, cfg):
    TS = cfg["TS"]
    env["XBF"] = sb("XBF", [128, NFC * (T // 2)], BF16)
    env["W1SB"] = sb("W1SB", [128, 32768], BF16)
    env["W2SB"] = sb("W2SB", [128, 32768], BF16)
    env["HSB"] = sb("HSB", [128, NHC * TS], BF16)
    env["YEV"] = sb("YEV", [128, 4 * TS], F32)
    env["CB"] = sb("CBt", [128, TS], BF16)
    env["SCR"] = sb("SCR", [128, 5120], F32)
    env["WGT"] = sb("WGT", [128, NFC * 8], F32)
    env["GS"] = sb("GS", [128, 128], F32)
    env["CM"] = sb("CMtm", [128, 16], F32)
    env["CMFB"] = sb("CMFB", [8, T], BF16)
    env["OHR"] = sb("OHR", [8, 8 * 128], BF16)
    if cfg["has_bias"]:
        env["B1SB"] = sb("B1SB", [128, NHC], F32)
        env["B2SB"] = sb("B2SB", [8, D], BF16)


def alloc_ret_arena2(sb, env, T, tw, with_main):
    env["XN"] = sb("XN", [128, NFC * T], BF16)
    env["KT"] = sb("KT", [128, (tw // 128) * D], BF16)
    env["VT"] = sb("VT", [128, (tw // 128) * D], BF16)
    env["PWSB"] = sb("PWSB", [128, NFC * D], BF16)
    env["SAL"] = sb("SAL", [64, 1024], F32)
    env["SCR"] = sb("SCR", [128, 5120], F32)
    env["LN1W"] = sb("LN1W", [128, 8], F32)
    env["LN1B"] = sb("LN1B", [128, 8], F32)
    env["GCC"] = sb("GCC", [64, 1024], F32)
    env["DECKTM"] = sb("DECKTM", [128, 1024], BF16)
    if with_main:
        env["QDZE"] = sb("QDZE", [128, NFC * tw], BF16)
        env["QDZO"] = sb("QDZO", [128, NFC * tw], BF16)
        env["KD"] = sb("KD", [128, NFC * tw], BF16)
        env["GT"] = sb("GT", [128, (tw // 128) * D], BF16)
        env["WOSB"] = sb("WOSB", [128, NFC * D], BF16)
        env["SCHKB"] = sb("SCHKB", [128, (tw // CC) * 1024], BF16)
        env["PMS"] = sb("PMS", [128, NH * 128], BF16)
        env["OEV"] = env["SCR"].tensor.ap()[0:128, 1024:2048] if False else None
        env["OEV"] = _scr_view(env["SCR"], 0, 1024)
        env["SQT"] = _scr_view(env["SCR"], 1024, 1024)
        env["STT"] = sb("STT", [128, 64], F32)
        env["GTD"] = sb("GTD", [128, 1024], BF16)
        env["GFM"] = sb("GFM", [128, NFC * 512], BF16)
        env["HEV"] = _scr_view(env["SCR"], 2048, 1024)
        env["TRIU4"] = sb("TRIU4", [128, 512], BF16)
        env["DECQ"] = sb("DECQ", [128, 8 * 512], BF16)
        env["DECK"] = sb("DECK", [128, 8 * 512], BF16)
        env["GB2"] = env["SCR"][0:64, 0:1024]
        env["RWT"] = env["SCR"][0:64, 1024:2048]


def emit_wsum_state(pr, env, ccout, wdram, dst, gbuf, wbuf, width):
    P = pr.op
    P('dve', lambda e: e.memset(dst[:, 0:width], 0.0), writes=[dst])
    for r in range(8):
        P('sp', lambda e, r=r: e.dma_start(
            out=gbuf[:, 0:width], in_=ccout[r * 64:(r + 1) * 64, 0:width]),
          reads=[(ccout, 'd'), (gbuf, 'used')], writes=[gbuf], dma=True, stream="wsg")
        P('sp', lambda e, r=r: e.dma_start(out=wbuf[:, 0:width], in_=wdram[r, :, 0:width]),
          reads=[(wbuf, 'used')], writes=[wbuf], dma=True, stream="wsw")
        P('dve', lambda e: e.tensor_tensor(gbuf[:, 0:width], gbuf[:, 0:width],
                                           wbuf[:, 0:width], ALU.mult),
          reads=[gbuf, wbuf], writes=[gbuf, (gbuf, 'used'), (wbuf, 'used')])
        P('dve', lambda e: e.tensor_tensor(dst[:, 0:width], dst[:, 0:width],
                                           gbuf[:, 0:width], ALU.add),
          reads=[dst, gbuf], writes=[dst])


def build_fused(T):
    nc = bass.Bass("TRN2", target_bir_lowering=False)
    cfg = dict(T=T, TS=256)
    env = {}
    dram = lambda name, shape, dt=F32, kind="ExternalInput": \
        nc.dram_tensor(name, shape, dt, kind=kind).ap()
    env["x0"] = dram("x0", [D, T])
    env["out"] = dram("xfinal", [D, T], kind="ExternalOutput")
    for w in ["Wq", "Wk", "Wv"]:
        env[w] = dram(w, [D, DHD])
    env["Wo"] = dram("Wo", [DHD, D])
    env["EULd"] = dram("EULs", [64, NNR])
    env["EVRd"] = dram("EVRs", [64, NNR])
    env["EVLd"] = dram("EVLs", [64, NNR])
    env["EULTWd"] = dram("EULTW", [8, 960])
    env["EURTWd"] = dram("EURTW", [8, 960])
    env["EULTLd"] = dram("EULTL", [32, 256])
    env["EURTLd"] = dram("EURTL", [32, 256])
    env["hssw"] = dram("hssw", [8, 64, 256])
    env["retw"] = dram("retw", [8, 64, 1024])
    for L in (0, 1):
        env[f"m{L}_W1"] = dram(f"m{L}_W1", [NE, D, HID])
        env[f"m{L}_W2"] = dram(f"m{L}_W2", [NE, HID, D])
        env[f"m{L}_wgateT"] = dram(f"m{L}_wgateT", [D, NE])
    env["b1rd"] = dram("b1r", [NE, 128, NHC])
    env["b2d"] = dram("b2", [NE, D])
    for w in ["wq", "wk", "wv", "wg", "wo"]:
        env["r_" + w] = dram("r_" + w, [D, D])
    for nm in ["ln1wd", "ln1bd", "ln2w", "ln2b", "boutr"]:
        env[nm] = dram(nm, [128, 8])
    env["binr"] = dram("binr", [2, 128, NCC])
    env["w_in_sw"] = dram("w_in_sw", [D, 5460])
    env["w_out_sw"] = dram("w_out_sw", [2730, D])
    env["X1"] = dram("X1", [D, T], kind="Internal")
    env["X2"] = dram("X2", [D, T], kind="Internal")
    env["HATT"] = dram("hatt_s", [D, T], kind="Internal")
    env["YACC"] = dram("yacc_s", [D, T], kind="Internal")
    env["XA"] = dram("xa_s", [D, T], kind="Internal")
    env["XB"] = dram("xb_s", [D, T], kind="Internal")
    env["X3"] = dram("X3", [D, T], kind="Internal")
    env["ATT"] = dram("att_s", [D, T], kind="Internal")
    env["FBD"] = nc.dram_tensor("fbd_s", [4, NNR, T], BF16, kind="Internal").ap()
    env["FBL"] = nc.dram_tensor("fbl_s", [3, 64, T], BF16, kind="Internal").ap()
    env["cc1_in"] = dram("cc1_in", [64, 256], kind="Internal")
    env["cc1_out"] = dram("cc1_out", [512, 256], kind="Internal")
    env["cc2_in"] = dram("cc2_in", [64, 1024], kind="Internal")
    env["cc2_out"] = dram("cc2_out", [512, 1024], kind="Internal")

    consts = make_consts(nc)
    rconsts = make_ret_consts(nc)
    hconsts = make_hss_consts2(nc)

    with ExitStack() as stack:
        alloc_common(nc, stack, env)
        arena = Arena(nc, stack, bf_cols=86400, f32_cols=8000)
        pr = Prog(nc)
        pr.alias = _alias
        P = pr.op
        load_common_consts(pr, dict(env), consts)

        arena.reset()
        sb = arena.alloc
        alloc_hss_arena(sb, env, T, full=True)
        env["xin"] = env["x0"]
        emit_hss_setup(pr, dict(env), cfg, consts, hconsts, full=False)
        emit_hss_proj(pr, dict(env), cfg)
        pr.fence()
        for ci in range(T // 128):
            emit_hss_chunk(pr, dict(env), cfg, ci, statesonly=True)
        _mstA, _mstbA = env["MST"], env["MSTB"]
        P('sp', lambda e, t=_mstA: e.dma_start(out=env["cc1_in"][:, 0:241], in_=t[:, 0:241]),
          reads=[_mstA], writes=[(env["cc1_in"], 'd')], dma=True, stream="cc1i")
        pr.collective(lambda e: e.collective_compute(
            "AllGather", ALU.bypass, replica_groups=[list(range(8))],
            ins=[env["cc1_in"].opt()], outs=[env["cc1_out"].opt()]),
            reads=[(env["cc1_in"], 'd')], writes=[(env["cc1_out"], 'd')], tag="cc1")
        for nm, d in [("EULTW", "EULTWd"), ("EURTW", "EURTWd"),
                      ("EULTL", "EULTLd"), ("EURTL", "EURTLd")]:
            P('pool', lambda e, nm=nm, d=d: e.dma_start(out=env[nm][:, :], in_=env[d][:, :]),
              writes=[env[nm]], dma=True, stream=f"c_{nm}")
        _elf, _tri, _wos = env["ELEAF"], env["TRIU4"], env["WOSB2"]
        P('sp', lambda e, t=_elf: e.dma_start(out=t[:, :], in_=hconsts["eleaf"].ap()),
          writes=[_elf], dma=True, stream="c_eleaf")
        P('sp', lambda e, t=_tri: e.dma_start(out=t[:, :], in_=hconsts["triu4b"].ap()),
          writes=[_tri], dma=True, stream="c_triu4")
        P('pool', lambda e, t=_wos: e.dma_start(out=t[:, :], in_=env["Wo"][:, :]),
          writes=[_wos], dma=True, stream="c_wo")
        emit_hss_compact(pr, dict(env), cfg)
        pr.fence()
        emit_wsum_state(pr, env, env["cc1_out"], env["hssw"], env["MST"],
                        env["GATH"], env["HWT"], 256)
        P('dve', lambda e, a=_mstbA, b=_mstA: e.tensor_copy(a[:, 0:241], b[:, 0:241]),
          reads=[_mstA], writes=[_mstbA])
        P('dve', lambda e: e.memset(env["PS"][3][:, 0:512], 0.0), writes=[(env["PS"][3], 'cl')])
        P('dve', lambda e: e.memset(env["PS"][4][:, 0:512], 0.0), writes=[(env["PS"][4], 'cl')])
        for g2 in range(T // 256):
            emit_hss_wide_group(pr, dict(env), cfg, g2)
            for ci in (g2 * 2, g2 * 2 + 1):
                emit_hss_chunk(pr, dict(env), cfg, ci, statesonly=False)
        pr.fence()
        emit_hss_final(pr, dict(env), cfg)
        pr.fence()
        emit_ln_sum(pr, dict(env), cfg, env["x0"], env["ATT"], env["X1"],
                    b_deps=[(env["ATT"], (t5, dc)) for t5 in range(T // 512) for dc in range(NFC)])
        pr.fence()

        arena.reset()
        cfgB = dict(T=T, TS=256, gate_mode="l0", has_act=False, has_bias=False)
        env["xin"] = env["X1"]
        env["xout"] = env["X2"]
        env["W1"] = env["m0_W1"]; env["W2"] = env["m0_W2"]; env["wgateT"] = env["m0_wgateT"]
        alloc_moe_arena(env, sb, T, cfgB)
        _ohrB = env["OHR"]
        P('sp', lambda e, t=_ohrB: e.dma_start(out=t[:, :], in_=consts["ohr_bf"].ap()),
          writes=[_ohrB], dma=True, stream="c_ohr")
        emit_moe(pr, dict(env), cfgB)
        pr.fence()

        arena.reset()
        env["xin"] = env["X2"]
        env["ln1w"] = env["ln1wd"]; env["ln1b"] = env["ln1bd"]
        alloc_ret_arena2(sb, env, T, 512, with_main=False)
        load_ret_consts(pr, dict(env), consts, rconsts, with_main=False)
        emit_xn1(pr, dict(env), cfg, out_sbuf=(env["XN"], T))
        pr.fence()
        _salC1 = env["SAL"]
        P('dve', lambda e, t=_salC1: e.memset(t[:, :], 0.0), writes=[_salC1])
        for q in range(T // 512):
            emit_proj_tm(pr, dict(env), cfg, env["r_wk"], env["XN"], env["KT"], q * 512, 512,
                         dec_const=env["DECKTM"])
            emit_proj_tm(pr, dict(env), cfg, env["r_wv"], env["XN"], env["VT"], q * 512, 512)
            pr.fence()
            emit_ret_state_chunks(pr, dict(env), cfg, env["KT"], env["VT"], q * 4, 4, save=False)
            pr.fence()
        P('sp', lambda e, t=_salC1: e.dma_start(out=env["cc2_in"][:, :], in_=t[:, :]),
          reads=[_salC1], writes=[(env["cc2_in"], 'd')], dma=True, stream="cc2i")
        pr.fence()
        pr.collective(lambda e: e.collective_compute(
            "AllGather", ALU.bypass, replica_groups=[list(range(8))],
            ins=[env["cc2_in"].opt()], outs=[env["cc2_out"].opt()]),
            reads=[(env["cc2_in"], 'd')], writes=[(env["cc2_out"], 'd')], tag="cc2")

        arena.reset()
        env["xin"] = env["X2"]
        alloc_ret_arena2(sb, env, T, 512, with_main=True)
        load_ret_consts(pr, dict(env), consts, rconsts, with_main=True)
        for fc in range(NFC):
            P('pool', lambda e, fc=fc: e.dma_start(
                out=env["WOSB"][:, fc * D:(fc + 1) * D],
                in_=env["r_wo"][fc * 128:(fc + 1) * 128, :]),
              writes=[(env["WOSB"], fc)], dma=True, stream=f"wo{fc}")
        emit_xn1(pr, dict(env), cfg, out_sbuf=(env["XN"], T))
        pr.fence()
        emit_wsum_state(pr, env, env["cc2_out"], env["retw"], env["SAL"],
                        env["GB2"], env["RWT"], 1024)
        pr.fence()
        _qze, _qzo = env["QDZE"], env["QDZO"]
        P('dve', lambda e, t=_qze: e.memset(t[64:128, :], 0.0), writes=[(_qze, 'z')])
        P('dve', lambda e, t=_qzo: e.memset(t[0:64, :], 0.0), writes=[(_qzo, 'z')])
        for q in range(T // 512):
            emit_proj_fm(pr, dict(env), cfg, env["r_wq"], env["XN"], None, q * 512, 512,
                         dec_const=env["DECQ"], qdz=(env["QDZE"], env["QDZO"]))
            emit_proj_fm(pr, dict(env), cfg, env["r_wk"], env["XN"], env["KD"], q * 512, 512,
                         dec_const=env["DECK"])
            emit_proj_tm(pr, dict(env), cfg, env["r_wk"], env["XN"], env["KT"], q * 512, 512,
                         dec_const=env["DECKTM"])
            emit_proj_tm(pr, dict(env), cfg, env["r_wv"], env["XN"], env["VT"], q * 512, 512)
            emit_proj_tm(pr, dict(env), cfg, env["r_wg"], env["XN"], env["GT"], q * 512, 512,
                         act=AF.Silu)
            pr.fence()
            emit_ret_state_chunks(pr, dict(env), cfg, env["KT"], env["VT"], q * 4, 4, save=True)
            pr.fence()
            emit_ret_main_half(pr, dict(env), cfg, env["QDZE"], env["KD"], env["KT"], env["VT"],
                               env["GT"], q * 512, 512)
            pr.fence()

        arena.reset()
        env["XN2"] = sb("XN2", [128, NFC * T], BF16)
        env["WIN"] = sb("WIN", [128, NFC * 128], BF16)
        env["ASB"] = sb("ASB", [128, NCC * 512], BF16)
        env["BSB"] = sb("BSB", [128, NCC * 512], BF16)
        env["WO2"] = sb("WO2", [128, NCC * D], BF16)
        env["SWEV"] = sb("SWEV", [128, 1024], F32)
        env["BIN"] = sb("BIN", [128, 2 * NCC], F32)
        env["BOUT"] = sb("BOUT", [128, 8], F32)
        env["SCR"] = sb("SCR", [128, 5120], F32)
        env["LN2W"] = sb("LN2W", [128, 8], F32)
        env["LN2B"] = sb("LN2B", [128, 8], F32)
        for nm, srcn in [("LN2W", "ln2w"), ("LN2B", "ln2b"), ("BOUT", "boutr")]:
            P('sp', lambda e, nm=nm, srcn=srcn: e.dma_start(
                out=env[nm][:, :], in_=env[srcn][:, :]),
              writes=[env[nm]], dma=True, stream=f"c2_{nm}")
        P('sp', lambda e: e.dma_start(out=env["BIN"][:, 0:NCC], in_=env["binr"][0, :, :]),
          writes=[(env["BIN"], 0)], dma=True, stream="c_bin0")
        P('sp', lambda e: e.dma_start(out=env["BIN"][:, NCC:2 * NCC], in_=env["binr"][1, :, :]),
          writes=[(env["BIN"], 1)], dma=True, stream="c_bin1")
        pr.op('dve', lambda e: e.engine_nop(), reads=[(env["BIN"], 0), (env["BIN"], 1)],
              writes=[env["BIN"]])
        pr.fence()
        hdeps = [(env["HATT"], (t0, 0, dc)) for t0 in range(0, T, 512) for dc in range(NFC)]
        emit_ln_sum(pr, dict(env), cfg, env["X2"], env["HATT"], None,
                    affine=(env["LN2W"], env["LN2B"]), out_sbuf=(env["XN2"], T),
                    sum_dram=env["XA"], b_deps=hdeps)
        pr.fence()
        for t5 in range(T // 512):
            P('sp', lambda e, t5=t5: e.dma_start(
                out=env["XB"][:, t5 * 512:(t5 + 1) * 512],
                in_=env["XA"][:, t5 * 512:(t5 + 1) * 512]),
              reads=[(env["XA"], ('sum', g)) for g in range(t5 * 4, t5 * 4 + 4)],
              writes=[(env["XB"], ('init', t5))], dma=True, stream=f"xbi{t5}")
        env["xin"] = env["X2"]
        emit_swiglu(pr, dict(env), cfg)
        pr.fence()
        emit_ln_sum(pr, dict(env), cfg, env["X2"], env["XB"], env["X3"],
                    b_deps=[(env["XB"], t5) for t5 in range(T // 512)])
        pr.fence()

        arena.reset()
        cfgD = dict(T=T, TS=256, gate_mode="l1", has_act=True, has_bias=True)
        env["xin"] = env["X3"]
        env["xout"] = env["out"]
        env["W1"] = env["m1_W1"]; env["W2"] = env["m1_W2"]; env["wgateT"] = env["m1_wgateT"]
        env["b1r"] = env["b1rd"]; env["b2"] = env["b2d"]
        alloc_moe_arena(env, sb, T, cfgD)
        _ohrD = env["OHR"]
        P('sp', lambda e, t=_ohrD: e.dma_start(out=t[:, :], in_=consts["ohr_bf"].ap()),
          writes=[_ohrD], dma=True, stream="c_ohr")
        emit_moe(pr, dict(env), cfgD)
        pr.finish()
        print("FUSED arena hwm: bf", arena.hwm_b, "f32", arena.hwm_f,
              "sbuf remaining:", nc.sbuf_bytes_remaining)
        with nc.Block() as block:
            pr.build(block)
    return nc



from concourse.bass_utils import run_bass_kernel_spmd

TSEG = 2048
NCORES = 8
_prog_cache = {}


def _ct(a):
    return np.ascontiguousarray(np.asarray(a, dtype=np.float32))


def _host_prep(inp):
    T = TSEG
    UL = [np.asarray(inp[f"L0_attn_UL{l}"]) for l in range(4)]
    VR = [np.asarray(inp[f"L0_attn_VR{l}"]) for l in range(4)]
    UR = [np.asarray(inp[f"L0_attn_UR{l}"]) for l in range(4)]
    VL = [np.asarray(inp[f"L0_attn_VL{l}"]) for l in range(4)]
    emb = hss_host_embeds(UL, VR, UR, VL)
    EULT, EURT = emb["EULT"], emb["EURT"]
    EULTW = np.zeros((8, 960), np.float32)
    EURTW = np.zeros((8, 960), np.float32)
    EULTL = np.zeros((32, 256), np.float32)
    EURTL = np.zeros((32, 256), np.float32)
    for l in range(4):
        nn, r, off, woff = LVL_NN[l], LVL_R[l], LVL_OFF[l], WOFF_L[l]
        nnr = nn * r
        EULTL[0:nnr, l * 64:(l + 1) * 64] = EULT[off:off + nnr, :]
        EURTL[0:nnr, l * 64:(l + 1) * 64] = EURT[off:off + nnr, :]
        for n in range(nn):
            EULTW[0:r, (woff + n) * 64:(woff + n + 1) * 64] = EULT[off + n * r: off + (n + 1) * r, :]
            EURTW[0:r, (woff + n) * 64:(woff + n + 1) * 64] = EURT[off + n * r: off + (n + 1) * r, :]
    rshp = lambda v: _ct(np.asarray(v).reshape(8, 128).T)
    b_in = np.asarray(inp["L1_attn_b_in"])
    bin_a = np.zeros((128, NCC), np.float32)
    bin_b = np.zeros((128, NCC), np.float32)
    for cc in range(NCC):
        cw = LASTW if cc == NCC - 1 else 128
        bin_a[:cw, cc] = b_in[cc * 128: cc * 128 + cw]
        bin_b[:cw, cc] = b_in[2730 + cc * 128: 2730 + cc * 128 + cw]
    shared = {
        "Wq": _ct(inp["L0_attn_Wq"]), "Wk": _ct(inp["L0_attn_Wk"]),
        "Wv": _ct(inp["L0_attn_Wv"]), "Wo": _ct(inp["L0_attn_Wo"]),
        "EULs": emb["EUL"], "EVRs": emb["EVR"], "EVLs": emb["EVL"],
        "EULTW": EULTW, "EURTW": EURTW, "EULTL": EULTL, "EURTL": EURTL,
        "m0_W1": _ct(np.asarray(inp["L0_ffn_w_in"]).transpose(0, 2, 1)),
        "m0_W2": _ct(np.asarray(inp["L0_ffn_w_out"]).transpose(0, 2, 1)),
        "m0_wgateT": _ct(np.asarray(inp["L0_ffn_w_gate"]).T),
        "m1_W1": _ct(inp["L1_ffn_W1"]), "m1_W2": _ct(inp["L1_ffn_W2"]),
        "m1_wgateT": _ct(np.asarray(inp["L1_ffn_w_gate"]).T),
        "b1r": _ct(np.asarray(inp["L1_ffn_b1"]).reshape(8, 32, 128).transpose(0, 2, 1)),
        "b2": _ct(inp["L1_ffn_b2"]),
        "r_wq": _ct(inp["L1_attn_wq"]), "r_wk": _ct(inp["L1_attn_wk"]),
        "r_wv": _ct(inp["L1_attn_wv"]), "r_wg": _ct(inp["L1_attn_wg"]),
        "r_wo": _ct(inp["L1_attn_wo"]),
        "ln1wd": rshp(inp["L1_attn_ln1_w"]), "ln1bd": rshp(inp["L1_attn_ln1_b"]),
        "ln2w": rshp(inp["L1_attn_ln2_w"]), "ln2b": rshp(inp["L1_attn_ln2_b"]),
        "binr": np.stack([bin_a, bin_b]),
        "boutr": rshp(inp["L1_attn_b_out"]),
        "w_in_sw": _ct(inp["L1_attn_w_in"]), "w_out_sw": _ct(inp["L1_attn_w_out"]),
    }
    hsswc, retwc = [], []
    for c in range(NCORES):
        b, s = c // 4, c % 4
        hw = np.zeros((8, 64, 256), np.float32)
        rw = np.zeros((8, 64, 1024), np.float32)
        for j in range(s):
            hw[b * 4 + j] = 1.0
            for h in range(NH):
                rw[b * 4 + j, :, h * 64:(h + 1) * 64] = \
                    np.float32(GAMMAS[h]) ** (T * (s - 1 - j))
        hsswc.append(hw)
        retwc.append(rw)
    return shared, hsswc, retwc


def kernel(**inp):
    T = TSEG
    x = np.asarray(inp["L0_attn_x"])
    B, S, Dm = x.shape
    nseg = S // T
    shared, hsswc, retwc = _host_prep(inp)
    if "fused" not in _prog_cache:
        _prog_cache["fused"] = build_fused(T)
    nc = _prog_cache["fused"]
    in_maps = []
    for c in range(NCORES):
        im = dict(shared)
        im["x0"] = np.ascontiguousarray(x[c // nseg, (c % nseg) * T:(c % nseg + 1) * T, :].T)
        im["hssw"] = hsswc[c]
        im["retw"] = retwc[c]
        in_maps.append(im)
    res = run_bass_kernel_spmd(nc, in_maps, core_ids=list(range(NCORES)))
    out = np.empty((B, S, Dm), np.float32)
    for c in range(NCORES):
        out[c // nseg, (c % nseg) * T:(c % nseg + 1) * T, :] = res.results[c]["xfinal"].T
    return out


# version 12 (final)
# speedup vs baseline: 1.2527x; 1.0896x over previous
import numpy as np
import ml_dtypes
from collections import defaultdict
from contextlib import ExitStack

import concourse.bass as bass
import concourse.mybir as mybir

F32 = mybir.dt.float32
BF16 = mybir.dt.bfloat16
AF = mybir.ActivationFunctionType
ALU = mybir.AluOpType
AX = mybir.AxisListType

D = 1024
NFC = 8
HID = 4096
NHC = 32
NE = 8
LN_EPS = 1e-5

bfc = lambda a: np.ascontiguousarray(np.asarray(a, dtype=np.float32)).astype(ml_dtypes.bfloat16)


ENG_ATTR = {"pe": "tensor", "act": "scalar", "dve": "vector", "pool": "gpsimd", "sp": "sync"}


class Prog:
    def __init__(self, nc):
        self.nc = nc
        self.queues = {e: [] for e in ENG_ATTR}
        self.sem = {}
        for e in ENG_ATTR:
            self.sem[e] = nc.alloc_semaphore(name=f"sem_{e}")
        self.count = defaultdict(int)
        self.last_write = {}
        self.readers = defaultdict(list)
        self.floors = {e: defaultdict(int) for e in ENG_ATTR}
        self.alias = lambda s: s

    def dma_sem(self, stream):
        stream = self.alias(stream)
        name = f"dma_{stream}"
        if name not in self.sem:
            self.sem[name] = self.nc.alloc_semaphore(name=f"sem_{name}")
        return name

    @staticmethod
    def key(r):
        if isinstance(r, tuple):
            return (id(r[0]), r[1])
        return id(r)

    def _wait(self, eng, sem_name, val):
        if val <= 0:
            return
        if eng == 'pe' and sem_name == 'pe':
            return
        if self.floors[eng][sem_name] >= val:
            return
        self.floors[eng][sem_name] = val
        self.queues[eng].append(("wait", self.sem[sem_name], val))

    def op(self, eng, fn, reads=(), writes=(), dma=False, stream=None):
        for r in reads:
            k = self.key(r)
            if k in self.last_write:
                self._wait(eng, *self.last_write[k])
        for w in writes:
            k = self.key(w)
            if k in self.last_write:
                self._wait(eng, *self.last_write[k])
            for rd in self.readers[k]:
                self._wait(eng, *rd)
        if dma:
            if stream is None:
                stream = self.key(writes[0]) if writes else "out"
            sem_name = self.dma_sem(stream)
            amt = 16
        else:
            sem_name = eng
            amt = 1
        self.count[sem_name] += amt
        cnt = self.count[sem_name]
        self.queues[eng].append(("op", fn, self.sem[sem_name], amt))
        done = (sem_name, cnt)
        for w in writes:
            k = self.key(w)
            self.last_write[k] = done
            self.readers[k] = []
        for r in reads:
            self.readers[self.key(r)].append(done)
        return done

    def collective(self, fn, reads=(), writes=(), tag="cc"):
        for r in reads:
            k = self.key(r)
            if k in self.last_write:
                self._wait("pool", *self.last_write[k])
        for w in writes:
            k = self.key(w)
            if k in self.last_write:
                self._wait("pool", *self.last_write[k])
            for rd in self.readers[k]:
                self._wait("pool", *rd)
        name = f"cc_{tag}"
        if name not in self.sem:
            self.sem[name] = self.nc.alloc_semaphore(name=f"sem_{name}")
        self.count[name] += 1
        cnt = self.count[name]
        self.queues["pool"].append(("op", fn, self.sem[name], None))
        done = (name, cnt)
        for w in writes:
            k = self.key(w)
            self.last_write[k] = done
            self.readers[k] = []
        for r in reads:
            self.readers[self.key(r)].append(done)
        return done

    def fence(self, engs=("pe", "act", "dve", "pool", "sp")):
        snap = list(self.count.items())
        for e in engs:
            for name, cnt in snap:
                self._wait(e, name, cnt)

    def finish(self, eng="sp"):
        for name, cnt in sorted(self.count.items(), key=lambda kv: str(kv[0])):
            self._wait(eng, name, cnt)

    def build(self, block):
        for e, attr in ENG_ATTR.items():
            q = self.queues[e]
            if not q:
                continue
            def make(q=q):
                def emit(engine):
                    for item in q:
                        if item[0] == "wait":
                            engine.wait_ge(item[1], item[2])
                        else:
                            inst = item[1](engine)
                            if inst is not None:
                                if item[3] is None:
                                    inst.then_inc(item[2])
                                else:
                                    inst.then_inc(item[2], item[3])
                return emit
            getattr(block, attr)(make())


def mm_group(P, ps, tag, emits):
    n = len(emits)
    for i, (fn, rds) in enumerate(emits):
        start = i == 0
        stop = i == n - 1
        w = []
        if start:
            w.append(ps)
        if stop:
            w = [ps]
        else:
            w.append((ps, tag))
        r = list(rds) + ([(ps, tag)] if not start else [])
        P('pe', (lambda e, fn=fn, start=start, stop=stop: fn(e, start, stop)), reads=r, writes=w)



def make_consts(nc):
    c = {}
    c["id_f32"] = nc.inline_tensor(np.eye(128, dtype=np.float32), "c_id_f32")
    c["id_bf"] = nc.inline_tensor(bfc(np.eye(128)), "c_id_bf")
    c["ones_col_f32"] = nc.inline_tensor(np.ones((128, 1), np.float32), "c_onescol32")
    c["ones_row_f32"] = nc.inline_tensor(np.ones((1, 128), np.float32), "c_onesrow32")
    c["ones_row_bf"] = nc.inline_tensor(bfc(np.ones((1, 128))), "c_onesrow")
    ohr = np.zeros((8, 8 * 128), np.float32)
    for e in range(8):
        ohr[e, e * 128:(e + 1) * 128] = 1.0
    c["ohr_bf"] = nc.inline_tensor(bfc(ohr), "c_ohr")
    c["triu_bf"] = nc.inline_tensor(bfc(np.triu(np.ones((128, 128)))), "c_triu")
    return c


def fm(dram, sl):
    return dram[:, sl].rearrange("(fc p) t -> p fc t", p=128)


def v3(sbuf_ap, nfc=8):
    return sbuf_ap.rearrange("p (fc t) -> p fc t", fc=nfc)



def emit_gating(pr, env, cfg):
    T = cfg["T"]
    P = pr.op
    SCR = env["SCR"]
    WGT = env["WGT"]
    GS = env["GS"]
    CM = env["CM"]
    CMFB = env["CMFB"]
    ID32 = env["ID32"]
    PS4, PS5 = env["PS"][4], env["PS"][3]

    def ldwg(e):
        src = env["wgateT"][:, :].rearrange("(fc p) x -> p fc x", p=128)
        return e.dma_start(out=v3(WGT[:, :]), in_=src)
    P('sp', ldwg, writes=[WGT], dma=True, stream='wgt')

    def _gctx(tg):
        s = tg % 2
        return dict(
            s=s, XG=SCR[:, s * 1024:(s + 1) * 1024], kG=(SCR, ('G', s)),
            LPS=PS4[:, s * 8:(s + 1) * 8], CPS=PS5[0:8, s * 128:(s + 1) * 128],
            g=(lambda a, b, gb=s * 64: GS[:, gb + a: gb + b]),
            kg=(lambda t, s=s: (GS, (s, t))),
            cm=CM[:, s * 8:(s + 1) * 8], kcm=(CM, s),
            qa='sp')

    def _gfront(tg):
        c = _gctx(tg)
        s = c['s']; XG = c['XG']; kG = c['kG']; LPS = c['LPS']
        P(c['qa'], lambda e, tg=tg, XG=XG: e.dma_start(
            out=v3(XG[:, :]), in_=fm(env["xin"], slice(tg * 128, (tg + 1) * 128))),
          writes=[kG], dma=True, stream=f"xg{s}")
        mm_group(P, PS4, f'lg{s}', [
            ((lambda e, start, stop, fc=fc, XG=XG, LPS=LPS: e.matmul(
                LPS, XG[:, fc * 128:(fc + 1) * 128],
                WGT[:, fc * 8:(fc + 1) * 8], start=start, stop=stop)),
             [kG, WGT])
            for fc in range(NFC)])

    def _gback(tg):
        c = _gctx(tg)
        s = c['s']; g = c['g']; kg = c['kg']; cm = c['cm']; kcm = c['kcm']
        LPS = c['LPS']; CPS = c['CPS']
        P('dve', lambda e, g=g, LPS=LPS: e.tensor_copy(g(0, 8), LPS), reads=[PS4], writes=[kg('LG')])
        P('dve', lambda e, g=g: e.max(out=g(8, 16), in_=g(0, 8)), reads=[kg('LG')], writes=[kg('MX')])
        P('dve', lambda e, g=g: e.tensor_single_scalar(g(16, 24), g(0, 8), g(8, 9), ALU.is_ge),
          reads=[kg('LG'), kg('MX')], writes=[kg('E1')])
        P('dve', lambda e, g=g: e.tensor_single_scalar(g(24, 32), g(0, 8), g(9, 10), ALU.is_ge),
          reads=[kg('LG'), kg('MX')], writes=[kg('E2')])
        if cfg["gate_mode"] == "l0":
            P('dve', lambda e, g=g: e.tensor_tensor(g(32, 33), g(8, 9), g(9, 10), ALU.subtract),
              reads=[kg('MX')], writes=[kg('DD')])
            P('act', lambda e, g=g: e.activation(g(33, 34), g(32, 33), AF.Sigmoid),
              reads=[kg('DD')], writes=[kg('G1')])
            P('dve', lambda e, g=g: e.tensor_scalar(g(34, 35), g(33, 34), -1.0, 1.0, ALU.mult, ALU.add),
              reads=[kg('G1')], writes=[kg('G2')])
            P('dve', lambda e, g=g: e.tensor_tensor(g(24, 32), g(24, 32), g(16, 24), ALU.subtract),
              reads=[kg('E1'), kg('E2')], writes=[kg('E2')])
            P('dve', lambda e, g=g: e.tensor_scalar_mul(g(40, 48), g(16, 24), g(33, 34)),
              reads=[kg('E1'), kg('G1')], writes=[kg('T1')])
            P('dve', lambda e, g=g: e.tensor_scalar_mul(g(48, 56), g(24, 32), g(34, 35)),
              reads=[kg('E2'), kg('G2')], writes=[kg('T2')])
            P('dve', lambda e, g=g, cm=cm: e.tensor_tensor(cm, g(40, 48), g(48, 56), ALU.add),
              reads=[kg('T1'), kg('T2')], writes=[kcm])
        else:
            P('dve', lambda e, g=g: e.tensor_scalar_mul(g(32, 33), g(8, 9), -1.0),
              reads=[kg('MX')], writes=[kg('NM')])
            P('act', lambda e, g=g: e.activation(g(40, 48), g(0, 8), AF.Exp, bias=g(32, 33)),
              reads=[kg('LG'), kg('NM')], writes=[kg('EXP')])
            P('dve', lambda e, g=g: e.reduce_sum(g(33, 34), g(40, 48), AX.X),
              reads=[kg('EXP')], writes=[kg('SM')])
            P('dve', lambda e, g=g: e.reciprocal(g(34, 35), g(33, 34)),
              reads=[kg('SM')], writes=[kg('RS')])
            P('dve', lambda e, g=g: e.tensor_scalar_mul(g(40, 48), g(40, 48), g(34, 35)),
              reads=[kg('EXP'), kg('RS')], writes=[kg('EXP')])
            P('dve', lambda e, g=g: e.tensor_tensor(g(48, 56), g(16, 24), g(24, 32), ALU.max),
              reads=[kg('E1'), kg('E2')], writes=[kg('MK')])
            P('dve', lambda e, g=g, cm=cm: e.tensor_tensor(cm, g(40, 48), g(48, 56), ALU.mult),
              reads=[kg('EXP'), kg('MK')], writes=[kcm])
        P('pe', lambda e, cm=cm, CPS=CPS: e.transpose(CPS, cm, ID32[:, :]),
          reads=[kcm, env["ID32"]], writes=[(PS5, s)])
        P('dve', lambda e, tg=tg, CPS=CPS: e.tensor_copy(CMFB[0:8, tg * 128:(tg + 1) * 128], CPS),
          reads=[(PS5, s)], writes=[(CMFB, tg)])

    ng = T // 128
    for tg in range(ng):
        _gfront(tg)
        if tg >= 1:
            _gback(tg - 1)
    _gback(ng - 1)



def emit_moe(pr, env, cfg):
    T = cfg["T"]
    TS = cfg.get("TS", 256)
    nts = T // TS
    ntc = TS // 128
    P = pr.op
    XBF = env["XBF"]; W1SB = env["W1SB"]; W2SB = env["W2SB"]; HSB = env["HSB"]
    YEV = env["YEV"]; CB = env["CB"]; CMFB = env["CMFB"]
    PS = env["PS"]
    HPSI = [0, 1, 4]
    HPS = [PS[i][:, 0:TS] for i in HPSI]
    YPS = [PS[2][:, 0:TS], PS[3][:, 0:TS]]
    CBPS = PS[4][:, 0:TS]
    B1SB = env.get("B1SB"); B2SB = env.get("B2SB")
    YACC = env["YACC"]

    TH = T // 2
    nth = nts // 2

    def load_xbf(th):
        for fc in range(NFC):
            P('pool', lambda e, fc=fc, th=th: e.dma_start(
                out=XBF[:, fc * TH:(fc + 1) * TH],
                in_=env["xin"][fc * 128:(fc + 1) * 128, th * TH:(th + 1) * TH]),
              writes=[(XBF, fc)], dma=True, stream=f"xbf{fc}")
    if cfg["has_bias"]:
        P('pool', lambda e: e.dma_start(out=B2SB[:, :], in_=env["b2"][:, :]),
          writes=[B2SB], dma=True, stream="b2")

    def load_w1(ei):
        for hq in range(4):
            for fc in range(NFC):
                P('pool', lambda e, ei=ei, fc=fc, hq=hq: e.dma_start(
                    out=W1SB[:, fc * HID + hq * 1024: fc * HID + (hq + 1) * 1024],
                    in_=env["W1"][ei, fc * 128:(fc + 1) * 128, hq * 1024:(hq + 1) * 1024]),
                  writes=[(W1SB, (fc, hq))], dma=True, stream=f"w1_{fc}")

    def load_w2(ei, deps=()):
        for hg in range(8):
            def ld(e, ei=ei, hg=hg):
                src = env["W2"][ei, hg * 512:(hg + 1) * 512, :].rearrange(
                    "(h p) d -> p h d", p=128)
                dst = W2SB[:, hg * 4096:(hg + 1) * 4096].rearrange("p (h d) -> p h d", h=4)
                return e.dma_start(out=dst, in_=src)
            P('pool', ld, reads=list(deps), writes=[(W2SB, hg)], dma=True,
              stream=f"w2_{hg}")

    def load_b1(ei):
        P('pool', lambda e, ei=ei: e.dma_start(out=B1SB[:, 0:NHC], in_=env["b1r"][ei, :, :]),
          writes=[B1SB], dma=True, stream="b1")

    load_w1(0)
    load_w2(0)
    if cfg["has_bias"]:
        load_b1(0)
    load_xbf(0)
    emit_gating(pr, env, cfg)
    pr.fence()

    prev_th = 0
    for ei in range(NE):
      ths = (0, 1) if ei % 2 == 0 else (1, 0)
      for ti, th in enumerate(ths):
        if th != prev_th:
            load_xbf(th)
            prev_th = th
        for tsi, ts in enumerate(range(th * nth, (th + 1) * nth)):
            t0 = ts * TS
            cmfb_keys = [(CMFB, (t0 // 128) + i) for i in range(ntc)]
            P('pe', lambda e, ei=ei, t0=t0: e.matmul(
                CBPS, env["OHR"][:, ei * 128:(ei + 1) * 128],
                CMFB[0:8, t0:t0 + TS], start=True, stop=True),
              reads=[env["OHR"]] + cmfb_keys, writes=[PS[4]])
            P('dve', lambda e: e.tensor_copy(CB[:, 0:TS], CBPS),
              reads=[PS[4]], writes=[CB])
            for hc in range(NHC):
                hp = HPS[hc % 3]
                hps_full = PS[HPSI[hc % 3]]
                tl = t0 - th * TH
                mm_group(P, hps_full, 'h', [
                    ((lambda e, start, stop, hc=hc, fc=fc, hp=hp, tl=tl: e.matmul(
                        hp, W1SB[:, fc * HID + hc * 128: fc * HID + (hc + 1) * 128],
                        XBF[:, fc * TH + tl: fc * TH + tl + TS], start=start, stop=stop)),
                     [(W1SB, (fc, hc // 8)), (XBF, fc)])
                    for fc in range(NFC)])
                if cfg["has_act"]:
                    P('act', lambda e, hc=hc, hp=hp: e.activation(
                        HSB[:, hc * TS:(hc + 1) * TS], hp, AF.Gelu, bias=B1SB[:, hc:hc + 1]),
                      reads=[hps_full, B1SB], writes=[(HSB, hc)])
                else:
                    P('act', lambda e, hc=hc, hp=hp: e.activation(
                        HSB[:, hc * TS:(hc + 1) * TS], hp, AF.Copy),
                      reads=[hps_full], writes=[(HSB, hc)])
            if ei > 0 and ti == 0 and tsi == 0:
                load_w2(ei, deps=[(HSB, NHC - 1)])
            if ei + 1 < NE and ti == 1 and tsi == nth - 1:
                load_w1(ei + 1)
                if cfg["has_bias"]:
                    load_b1(ei + 1)
            for dc in range(NFC):
                yp = YPS[dc % 2]
                yps_full = PS[2 + dc % 2]
                emits = []
                if cfg["has_bias"]:
                    emits.append((
                        (lambda e, start, stop, dc=dc, yp=yp, t0=t0: e.matmul(
                            yp, B2SB[0:8, dc * 128:(dc + 1) * 128],
                            CMFB[0:8, t0:t0 + TS], start=start, stop=stop)),
                        [B2SB] + cmfb_keys))
                for hc in range(NHC):
                    emits.append((
                        (lambda e, start, stop, dc=dc, hc=hc, yp=yp: e.matmul(
                            yp, W2SB[:, hc * D + dc * 128: hc * D + (dc + 1) * 128],
                            HSB[:, hc * TS:(hc + 1) * TS], start=start, stop=stop)),
                        [(W2SB, hc // 4), (HSB, hc)]))
                mm_group(P, yps_full, 'y', emits)
                P('dve', lambda e, dc=dc, yp=yp: e.tensor_tensor(
                    YEV[:, (dc % 4) * TS:(dc % 4 + 1) * TS], yp, CB[:, 0:TS], ALU.mult),
                  reads=[PS[2 + dc % 2], CB], writes=[(YEV, dc % 4)])
                if dc % 4 == 3:
                    half = dc // 4
                    def acc(e, ei=ei, ts=ts, half=half):
                        dst = YACC[half * 512:(half + 1) * 512, ts * TS:(ts + 1) * TS] \
                            .rearrange("(dc p) t -> p dc t", p=128)
                        if ei == 0:
                            return e.dma_start(out=dst, in_=v3(YEV[:, :], nfc=4))
                        return e.dma_start(out=dst, in_=v3(YEV[:, :], nfc=4), accum_op=ALU.add)
                    P('pool', acc,
                      reads=[(YEV, d) for d in range(4)] +
                            ([(YACC, ts)] if (ei or half) else []),
                      writes=[(YACC, ts)], dma=True, stream=f"yacc{ts}")
    pr.fence()
    emit_ln_sum(pr, env, cfg, env["xin"], YACC, env["xout"],
                b_deps=[(YACC, ts) for ts in range(nts)])



def emit_ln_sum(pr, env, cfg, a_dram, b_dram, out_dram, b_deps=(), a_deps=(),
                affine=None, out_sbuf=None, sum_dram=None, out_deps_tag=None):
    T = cfg["T"]
    P = pr.op
    SCR = env["SCR"]
    SQ = SCR[:, 4096:5120]
    RW = env["RW"]
    OC32 = env["OC32"]; OR32 = env["OR32"]
    PS = env["PS"]
    W = 128
    assert not (out_dram is not None and out_sbuf is not None)

    def _ctx(tg):
        s = tg % 2
        return dict(
            s=s, sl=slice(tg * W, (tg + 1) * W),
            X=SCR[:, s * 1024:(s + 1) * 1024],
            Bv=SCR[:, 2048 + s * 1024: 3072 + s * 1024],
            kS=(SCR, ('S', s)), kO=(SCR, ('O', s)), kB=(SCR, ('B', s)), kQ=(SCR, 'Q'),
            RPS=PS[6][0:1, s * 128:(s + 1) * 128],
            R2PS=PS[7][0:1, s * 128:(s + 1) * 128],
            BPSF=PS[s], B2PSF=PS[2 + s],
            qa='sp',
            rw=(lambda i, ro=s * 512: RW[0:1, ro + i * 128: ro + i * 128 + W]),
            kr=(lambda i, s=s: (RW, (s, i))))

    def _front(tg):
        c = _ctx(tg)
        s = c['s']; sl = c['sl']; X = c['X']; Bv = c['Bv']
        kS = c['kS']; kO = c['kO']; kB = c['kB']; kQ = c['kQ']
        P(c['qa'], lambda e, sl=sl, X=X: e.dma_start(out=v3(X[:, :]), in_=fm(a_dram, sl)),
          reads=list(a_deps), writes=[kS], dma=True, stream=f"lnx{s}")
        if b_dram is not None:
            P('sp', lambda e, sl=sl, Bv=Bv: e.dma_start(out=v3(Bv[:, :]), in_=fm(b_dram, sl)),
              reads=list(b_deps), writes=[kB, kO], dma=True, stream=f"lny{s}")
            P('dve', lambda e, X=X, Bv=Bv: e.tensor_tensor(X[:, :], X[:, :], Bv[:, :], ALU.add),
              reads=[kS, kB], writes=[kS])
        P('act', lambda e, X=X: e.activation(SQ[:, :], X[:, :], AF.Square),
          reads=[kS], writes=[kQ])
        if sum_dram is not None:
            P('sp', lambda e, sl=sl, X=X: e.dma_start(out=fm(sum_dram, sl), in_=v3(X[:, :])),
              reads=[kS], writes=[(sum_dram, ('sum', tg))], dma=True, stream="lnsum")
        mm_group(P, PS[6], f'r{s}', [
            ((lambda e, start, stop, fc=fc, X=X, RPS=c['RPS']: e.matmul(
                RPS, OC32[:, :], X[:, fc * W:(fc + 1) * W], start=start, stop=stop)),
             [kS, env["OC32"]]) for fc in range(NFC)])
        mm_group(P, PS[7], f'r2{s}', [
            ((lambda e, start, stop, fc=fc, R2PS=c['R2PS']: e.matmul(
                R2PS, OC32[:, :], SQ[:, fc * W:(fc + 1) * W], start=start, stop=stop)),
             [kQ, env["OC32"]]) for fc in range(NFC)])
        rw = c['rw']; kr = c['kr']
        P('dve', lambda e, rw=rw, RPS=c['RPS']: e.tensor_scalar_mul(rw(0), RPS, 1.0 / D),
          reads=[PS[6]], writes=[kr(0)])
        P('dve', lambda e, rw=rw, R2PS=c['R2PS']: e.tensor_scalar_mul(rw(1), R2PS, 1.0 / D),
          reads=[PS[7]], writes=[kr(1)])
        P('dve', lambda e, rw=rw: e.tensor_tensor(rw(2), rw(0), rw(0), ALU.mult),
          reads=[kr(0)], writes=[kr(2)])
        P('dve', lambda e, rw=rw: e.tensor_tensor(rw(1), rw(1), rw(2), ALU.subtract),
          reads=[kr(1), kr(2)], writes=[kr(1)])
        P('dve', lambda e, rw=rw: e.tensor_scalar_add(rw(1), rw(1), LN_EPS),
          reads=[kr(1)], writes=[kr(1)])
        P('act', lambda e, rw=rw: e.activation(rw(2), rw(1), AF.Sqrt),
          reads=[kr(1)], writes=[kr(2)])
        P('dve', lambda e, rw=rw: e.reciprocal(rw(3), rw(2)), reads=[kr(2)], writes=[kr(3)])

    def _back(tg):
        c = _ctx(tg)
        s = c['s']; sl = c['sl']; X = c['X']; Bv = c['Bv']
        kS = c['kS']; kO = c['kO']; kB = c['kB']
        rw = c['rw']; kr = c['kr']
        BPSF = c['BPSF']; B2PSF = c['B2PSF']
        for q in range(4):
            P('pe', lambda e, q=q, rw=rw, BPSF=BPSF: e.matmul(
                BPSF[:, q * 128:(q + 1) * 128], OR32[0:1, :], rw(0), start=True, stop=True),
              reads=[kr(0), env["OR32"]], writes=[(BPSF, 'b')])
            P('pe', lambda e, q=q, rw=rw, B2PSF=B2PSF: e.matmul(
                B2PSF[:, q * 128:(q + 1) * 128], OR32[0:1, :], rw(3), start=True, stop=True),
              reads=[kr(3), env["OR32"]], writes=[(B2PSF, 'b2')])
        for h in range(2):
            P('dve', lambda e, h=h, X=X, Bv=Bv, BPSF=BPSF: e.tensor_tensor(
                Bv[:, h * 512:(h + 1) * 512], X[:, h * 512:(h + 1) * 512],
                BPSF[:, :], ALU.subtract),
              reads=[kS, (BPSF, 'b')], writes=[kB, kO])
        for h in range(2):
            P('dve', lambda e, h=h, Bv=Bv, B2PSF=B2PSF: e.tensor_tensor(
                Bv[:, h * 512:(h + 1) * 512], Bv[:, h * 512:(h + 1) * 512],
                B2PSF[:, :], ALU.mult),
              reads=[kO, (B2PSF, 'b2')], writes=[kO])
        if out_sbuf is not None:
            XS, XT = out_sbuf
            WT, BT = affine if affine is not None else (None, None)
            for fc in range(NFC):
                if affine is not None:
                    P('dve', lambda e, fc=fc, tg=tg, Bv=Bv: e.tensor_scalar(
                        XS[:, fc * XT + tg * W: fc * XT + tg * W + W],
                        Bv[:, fc * W:(fc + 1) * W],
                        WT[:, fc:fc + 1], BT[:, fc:fc + 1], ALU.mult, ALU.add),
                      reads=[kO, WT, BT], writes=[(XS, (fc, tg * W // 128))])
                else:
                    P('dve', lambda e, fc=fc, tg=tg, Bv=Bv: e.tensor_copy(
                        XS[:, fc * XT + tg * W: fc * XT + tg * W + W],
                        Bv[:, fc * W:(fc + 1) * W]),
                      reads=[kO], writes=[(XS, (fc, tg * W // 128))])
        if out_dram is not None:
            P(c['qa'], lambda e, sl=sl, Bv=Bv: e.dma_start(out=fm(out_dram, sl), in_=v3(Bv[:, :])),
              reads=[kO], writes=[(out_dram, ('ln', tg))], dma=True, stream=f"lnout{s}")

    n = T // W
    for tg in range(n):
        _front(tg)
        if tg >= 1:
            _back(tg - 1)
    _back(n - 1)



def build_moe_launch(T, layer):
    nc = bass.Bass("TRN2", target_bir_lowering=False)
    cfg = dict(T=T, TS=256, gate_mode=("l0" if layer == 0 else "l1"),
               has_act=(layer == 1), has_bias=(layer == 1))
    env = {}
    env["xin"] = nc.dram_tensor("xin", [D, T], F32, kind="ExternalInput").ap()
    env["xout"] = nc.dram_tensor("xout", [D, T], F32, kind="ExternalOutput").ap()
    env["W1"] = nc.dram_tensor("W1", [NE, D, HID], F32, kind="ExternalInput").ap()
    env["W2"] = nc.dram_tensor("W2", [NE, HID, D], F32, kind="ExternalInput").ap()
    env["wgateT"] = nc.dram_tensor("wgateT", [D, NE], F32, kind="ExternalInput").ap()
    if layer == 1:
        env["b1r"] = nc.dram_tensor("b1r", [NE, 128, NHC], F32, kind="ExternalInput").ap()
        env["b2"] = nc.dram_tensor("b2", [NE, D], F32, kind="ExternalInput").ap()
    env["YACC"] = nc.dram_tensor("yacc", [D, T], F32, kind="Internal").ap()
    consts = make_consts(nc)
    TS = cfg["TS"]
    with ExitStack() as stack:
        sb = lambda name, shape, dt=F32: stack.enter_context(nc.sbuf_tensor(name, shape, dt))
        psb = lambda name, shape, dt=F32: stack.enter_context(nc.psum_tensor(name, shape, dt))
        env["XBF"] = sb("XBF", [128, NFC * (T // 2)], BF16)
        env["W1SB"] = sb("W1SB", [128, 32768], BF16)
        env["W2SB"] = sb("W2SB", [128, 32768], BF16)
        env["HSB"] = sb("HSB", [128, NHC * TS], BF16)
        env["YEV"] = sb("YEV", [128, 4 * TS], F32)
        env["CB"] = sb("CBt", [128, TS], BF16)
        env["SCR"] = sb("SCR", [128, 3072], F32)
        env["WGT"] = sb("WGT", [128, NFC * 8], F32)
        env["GS"] = sb("GS", [128, 64], F32)
        env["CM"] = sb("CMtm", [128, 8], F32)
        env["CMFB"] = sb("CMFB", [8, T], BF16)
        env["ID32"] = sb("ID32", [128, 128], F32)
        env["OHR"] = sb("OHR", [8, 8 * 128], BF16)
        env["RW"] = sb("RW", [1, 512], F32)
        env["OC32"] = sb("OC32", [128, 1], F32)
        env["OR32"] = sb("OR32", [1, 128], F32)
        if layer == 1:
            env["B1SB"] = sb("B1SB", [128, NHC], F32)
            env["B2SB"] = sb("B2SB", [8, D], BF16)
        env["PS"] = [psb(f"PS{i}", [128, 512]) for i in range(6)] + \
                    [psb("PS6", [1, 512]), psb("PS7", [1, 512])]
        print("SBUF bytes remaining:", nc.sbuf_bytes_remaining,
              "PSUM banks remaining:", nc.psum_banks_remaining)
        pr = Prog(nc)
        for nm, cn in [("ID32", "id_f32"), ("OHR", "ohr_bf"), ("OC32", "ones_col_f32"),
                       ("OR32", "ones_row_f32")]:
            pr.op('sp', lambda e, nm=nm, cn=cn: e.dma_start(out=env[nm][:, :], in_=consts[cn].ap()),
                  writes=[env[nm]], dma=True)
        emit_moe(pr, env, cfg)
        pr.finish()
        with nc.Block() as block:
            pr.build(block)
    return nc



NH = 16
DH = 64
CC = 128
HIDS = 2730
GAMMAS = [1.0 - 2.0 ** (-5.0 - h) for h in range(NH)]


def make_ret_consts(nc):
    c = {}
    tri = np.triu(np.ones((128, 128), np.float32))
    c["triu4"] = nc.inline_tensor(bfc(np.tile(tri, (1, 4))), "c_triu4")
    decq = np.zeros((128, 8 * 512), np.float32)
    deck = np.zeros((128, 8 * 512), np.float32)
    tr = np.arange(512) % 128
    for dc in range(8):
        for j in range(128):
            h = 2 * dc + j // 64
            decq[j, dc * 512:(dc + 1) * 512] = GAMMAS[h] ** tr * (DH ** -0.5)
            deck[j, dc * 512:(dc + 1) * 512] = GAMMAS[h] ** (-tr)
    c["decq"] = nc.inline_tensor(bfc(decq), "c_decq")
    c["deck"] = nc.inline_tensor(bfc(deck), "c_deck")
    decktm = np.zeros((128, 1024), np.float32)
    for h in range(NH):
        decktm[:, h * 64:(h + 1) * 64] = (np.asarray(GAMMAS[h]) ** (-np.arange(128.0)))[:, None]
    c["decktm"] = nc.inline_tensor(bfc(decktm), "c_decktm")
    gcc = np.zeros((64, 1024), np.float32)
    for h in range(NH):
        gcc[:, h * 64:(h + 1) * 64] = GAMMAS[h] ** CC
    c["gcc"] = nc.inline_tensor(gcc, "c_gcc")
    return c


def emit_proj_fm(pr, env, cfg, wdram, XS, OUT, t0, tw, dec_const=None, qdz=None):
    T = cfg["T"]
    P = pr.op
    WSB = env["PWSB"]
    pi = pr.pw_idx = getattr(pr, 'pw_idx', 1) ^ 1
    wb = pi * NFC * D
    for fc in range(NFC):
        P('pool', lambda e, fc=fc, wb=wb: e.dma_start(
            out=WSB[:, wb + fc * D: wb + (fc + 1) * D], in_=wdram[fc * 128:(fc + 1) * 128, :]),
          writes=[(WSB, (pi, fc))], dma=True, stream=f"pw{fc}")
    for dc in range(NFC):
        for t5 in range(tw // 512):
            psf = env["PS"][(dc + t5) % 2]
            ps = psf[:, 0:512]
            mm_group(P, psf, 'p', [
                ((lambda e, start, stop, fc=fc, dc=dc, t5=t5, ps=ps, wb=wb: e.matmul(
                    ps, WSB[:, wb + fc * D + dc * 128: wb + fc * D + (dc + 1) * 128],
                    XS[:, fc * T + t0 + t5 * 512: fc * T + t0 + (t5 + 1) * 512],
                    start=start, stop=stop)),
                 [(WSB, (pi, fc))] + [(XS, (fc, (t0 + t5 * 512) // 128 + i)) for i in range(4)])
                for fc in range(NFC)])
            if qdz is not None:
                QDZE, QDZO = qdz
                P('dve', lambda e, dc=dc, t5=t5, psf=psf: e.tensor_tensor(
                    QDZE[0:64, dc * tw + t5 * 512: dc * tw + (t5 + 1) * 512],
                    psf[0:64, 0:512], dec_const[0:64, dc * 512:(dc + 1) * 512], ALU.mult),
                  reads=[psf, dec_const], writes=[(QDZE, (dc, t5))])
                P('dve', lambda e, dc=dc, t5=t5, psf=psf: e.tensor_tensor(
                    QDZO[64:128, dc * tw + t5 * 512: dc * tw + (t5 + 1) * 512],
                    psf[64:128, 0:512], dec_const[64:128, dc * 512:(dc + 1) * 512], ALU.mult),
                  reads=[psf, dec_const], writes=[(QDZO, (dc, t5))])
            elif dec_const is not None:
                P('dve', lambda e, dc=dc, t5=t5, ps=ps: e.tensor_tensor(
                    OUT[:, dc * tw + t5 * 512: dc * tw + (t5 + 1) * 512], ps,
                    dec_const[:, dc * 512:(dc + 1) * 512], ALU.mult),
                  reads=[psf, dec_const], writes=[(OUT, (dc, t5))])
            else:
                P('dve', lambda e, dc=dc, t5=t5, ps=ps: e.tensor_copy(
                    OUT[:, dc * tw + t5 * 512: dc * tw + (t5 + 1) * 512], ps),
                  reads=[psf], writes=[(OUT, (dc, t5))])


def emit_proj_tm(pr, env, cfg, wdram, XS, OUT, t0, tw, dec_const=None, act=None):
    T = cfg["T"]
    P = pr.op
    WSB = env["PWSB"]
    pi = pr.pw_idx = getattr(pr, 'pw_idx', 1) ^ 1
    wb = pi * NFC * D
    for fc in range(NFC):
        P('pool', lambda e, fc=fc, wb=wb: e.dma_start(
            out=WSB[:, wb + fc * D: wb + (fc + 1) * D], in_=wdram[fc * 128:(fc + 1) * 128, :]),
          writes=[(WSB, (pi, fc))], dma=True, stream=f"pw{fc}")
    for tc in range(tw // 128):
        gtc = (t0 // 128) + tc
        for half in range(2):
            psf = env["PS"][half % 2]
            ps = psf[:, 0:512]
            mm_group(P, psf, 'p', [
                ((lambda e, start, stop, fc=fc, tc=tc, half=half, ps=ps, gtc=gtc, wb=wb: e.matmul(
                    ps, XS[:, fc * T + gtc * 128: fc * T + (gtc + 1) * 128],
                    WSB[:, wb + fc * D + half * 512: wb + fc * D + (half + 1) * 512],
                    start=start, stop=stop)),
                 [(WSB, (pi, fc)), (XS, (fc, gtc))])
                for fc in range(NFC)])
            if act is not None:
                P('act', lambda e, tc=tc, half=half, ps=ps: e.activation(
                    OUT[:, tc * D + half * 512: tc * D + (half + 1) * 512], ps, act),
                  reads=[psf], writes=[(OUT, (tc, half))])
            elif dec_const is not None:
                P('dve', lambda e, tc=tc, half=half, ps=ps: e.tensor_tensor(
                    OUT[:, tc * D + half * 512: tc * D + (half + 1) * 512], ps,
                    dec_const[:, half * 512:(half + 1) * 512], ALU.mult),
                  reads=[psf, dec_const], writes=[(OUT, (tc, half))])
            else:
                P('dve', lambda e, tc=tc, half=half, ps=ps: e.tensor_copy(
                    OUT[:, tc * D + half * 512: tc * D + (half + 1) * 512], ps),
                  reads=[psf], writes=[(OUT, (tc, half))])


def emit_ret_state_chunks(pr, env, cfg, KT, VT, c0, nch, save):
    P = pr.op
    SAL = env["SAL"]
    GCC = env["GCC"]
    for li in range(nch):
        ci = c0 + li
        if save:
            P('dve', lambda e, li=li: e.tensor_copy(
                env["SCHKB"][0:64, li * 1024:(li + 1) * 1024], SAL[:, :]),
              reads=[SAL], writes=[(env["SCHKB"], li)])
            P('dve', lambda e, li=li: e.tensor_copy(
                env["SCHKB"][64:128, li * 1024:(li + 1) * 1024], SAL[:, :]),
              reads=[SAL], writes=[(env["SCHKB"], (li, 'hi'))])
        for g in range(2):
            psf = env["PS"][2 + g]
            for hh in range(8):
                h = g * 8 + hh
                P('pe', lambda e, li=li, h=h, hh=hh, psf=psf: e.matmul(
                    psf[0:64, hh * 64:(hh + 1) * 64],
                    KT[:, li * D + h * 64: li * D + (h + 1) * 64],
                    VT[:, li * D + h * 64: li * D + (h + 1) * 64],
                    start=True, stop=True),
                  reads=[(KT, (li, h // 8)), (VT, (li, h // 8))],
                  writes=[(psf, hh), (psf, ('of', hh))])
            P('dve', lambda e, g=g, psf=psf: e.tensor_tensor(
                SAL[:, g * 512:(g + 1) * 512], SAL[:, g * 512:(g + 1) * 512],
                psf[0:64, 0:512], ALU.add),
              reads=[SAL] + [(psf, i) for i in range(8)], writes=[SAL])
        P('dve', lambda e: e.tensor_tensor(SAL[:, :], SAL[:, :], GCC[:, :], ALU.mult),
          reads=[SAL, env["GCC"]], writes=[SAL])


def emit_ret_main_half(pr, env, cfg, QD, KD, KT, VT, GT, t0, tw):
    T = cfg["T"]
    P = pr.op
    TRIU4 = env["TRIU4"]
    PMS = env["PMS"]
    OEV = env["OEV"]
    STT = env["STT"]
    GTD = env["GTD"]
    GFM = env["GFM"]
    HEV = env["HEV"]
    WOSB = env["WOSB"]
    SCHKB = env["SCHKB"]
    TPSB = env["TPSB"]
    PS = env["PS"]
    nch = tw // CC
    for lci in range(nch):
        ci = (t0 // CC) + lci
        if cfg.get("mtrunc", 99) < 1:
            continue
        for hg in range(cfg.get("mhg", 4)):
            psf = PS[hg % 2]
            for hh in range(4):
                h = hg * 4 + hh
                dc = h // 2
                QZ = env["QDZE"] if h % 2 == 0 else env["QDZO"]
                P('pe', lambda e, h=h, hh=hh, dc=dc, lci=lci, psf=psf, QZ=QZ: e.matmul(
                    psf[:, hh * 128:(hh + 1) * 128],
                    KD[:, dc * tw + lci * 128: dc * tw + (lci + 1) * 128],
                    QZ[:, dc * tw + lci * 128: dc * tw + (lci + 1) * 128],
                    start=True, stop=True),
                  reads=[(KD, (dc, (lci * 128) // 512)), (QZ, (dc, (lci * 128) // 512)),
                         (QZ, 'z')],
                  writes=[(psf, hh)] + ([psf] if hh == 0 else []))
            if cfg.get("mmask", True):
                P('dve', lambda e, hg=hg, psf=psf: e.tensor_tensor(
                    PMS[:, hg * 512:(hg + 1) * 512], psf[:, 0:512], TRIU4[:, :], ALU.mult),
                  reads=[(psf, 0), (psf, 1), (psf, 2), (psf, 3), psf, env["TRIU4"]],
                  writes=[(PMS, hg)])
            else:
                P('dve', lambda e, hg=hg, psf=psf: e.tensor_copy(
                    PMS[:, hg * 512:(hg + 1) * 512], psf[:, 0:512]),
                  reads=[(psf, 0), (psf, 1), (psf, 2), (psf, 3), psf],
                  writes=[(PMS, hg)])
        if cfg.get("mtrunc", 99) < 2:
            continue
        for g in range(2):
            psf = PS[2 + g]
            for hh in range(8):
                h = g * 8 + hh
                P('pe', lambda e, h=h, hh=hh, lci=lci, psf=psf: e.matmul(
                    psf[:, hh * 64:(hh + 1) * 64],
                    PMS[:, h * 128:(h + 1) * 128],
                    VT[:, lci * D + h * 64: lci * D + (h + 1) * 64],
                    start=True, stop=False),
                  reads=[(PMS, h // 4), (VT, (lci, h // 8))],
                  writes=[(psf, ('o', hh)), (psf, hh)])
                dc = h // 2
                QZ = env["QDZE"] if h % 2 == 0 else env["QDZO"]
                P('pe', lambda e, h=h, hh=hh, dc=dc, lci=lci, psf=psf, QZ=QZ: e.matmul(
                    psf[:, hh * 64:(hh + 1) * 64],
                    QZ[:, dc * tw + lci * 128: dc * tw + (lci + 1) * 128],
                    SCHKB[:, lci * 1024 + h * 64: lci * 1024 + (h + 1) * 64],
                    start=False, stop=True),
                  reads=[(QZ, (dc, (lci * 128) // 512)), (QZ, 'z'), (SCHKB, lci),
                         (SCHKB, (lci, 'hi')), (psf, ('o', hh))],
                  writes=[(psf, ('of', hh))])
        for g in range(2):
            psf = PS[2 + g]
            P('dve', lambda e, g=g, psf=psf: e.tensor_copy(
                OEV[:, g * 512:(g + 1) * 512], psf[:, 0:512]),
              reads=[(psf, ('of', hh)) for hh in range(8)], writes=[(OEV, g)])
        if cfg.get("mtrunc", 99) < 3:
            continue
        P('dve', lambda e: e.tensor_reduce(
            STT[:, 0:16], OEV[:, :].rearrange("p (h x) -> p h x", h=16), AX.X, ALU.add),
          reads=[(OEV, 0), (OEV, 1)], writes=[(STT, 'sum')])
        P('act', lambda e: e.activation(env["SQT"][:, :], OEV[:, :], AF.Square),
          reads=[(OEV, 0), (OEV, 1)], writes=[env["SQT"]])
        P('dve', lambda e: e.tensor_reduce(
            STT[:, 16:32], env["SQT"][:, :].rearrange("p (h x) -> p h x", h=16), AX.X, ALU.add),
          reads=[env["SQT"]], writes=[(STT, 'sq'), (STT, 'ex2'), (STT, 'var')])
        P('dve', lambda e: e.tensor_scalar_mul(STT[:, 0:16], STT[:, 0:16], 1.0 / DH),
          reads=[(STT, 'sum')], writes=[(STT, 'mean')])
        P('dve', lambda e: e.tensor_scalar_mul(STT[:, 16:32], STT[:, 16:32], 1.0 / DH),
          reads=[(STT, 'sq')], writes=[(STT, 'ex2')])
        P('dve', lambda e: e.tensor_tensor(STT[:, 32:48], STT[:, 0:16], STT[:, 0:16], ALU.mult),
          reads=[(STT, 'mean')], writes=[(STT, 'm2'), (STT, 'sd')])
        P('dve', lambda e: e.tensor_tensor(STT[:, 16:32], STT[:, 16:32], STT[:, 32:48], ALU.subtract),
          reads=[(STT, 'ex2'), (STT, 'm2')], writes=[(STT, 'var')])
        P('dve', lambda e: e.tensor_scalar_add(STT[:, 16:32], STT[:, 16:32], 1e-6),
          reads=[(STT, 'var')], writes=[(STT, 'var')])
        P('act', lambda e: e.activation(STT[:, 32:48], STT[:, 16:32], AF.Sqrt),
          reads=[(STT, 'var')], writes=[(STT, 'sd')])
        P('dve', lambda e: e.reciprocal(STT[:, 48:64], STT[:, 32:48]),
          reads=[(STT, 'sd')], writes=[(STT, 'rstd')])
        for h in range(NH):
            P('dve', lambda e, h=h: e.tensor_scalar(
                OEV[:, h * 64:(h + 1) * 64], OEV[:, h * 64:(h + 1) * 64],
                STT[:, h:h + 1], STT[:, 48 + h:49 + h], ALU.subtract, ALU.mult),
              reads=[(OEV, h // 8), (STT, 'mean'), (STT, 'rstd')], writes=[(OEV, h // 8)])
        if cfg.get("mtrunc", 99) < 4:
            continue
        P('dve', lambda e, lci=lci: e.tensor_tensor(
            GTD[:, :], OEV[:, :], GT[:, lci * D:(lci + 1) * D], ALU.mult),
          reads=[(OEV, 0), (OEV, 1), (GT, (lci, 0)), (GT, (lci, 1))], writes=[GTD])
        if cfg.get("mtrunc", 99) < 5:
            continue
        for fc in range(NFC):
            P('pe', lambda e, fc=fc: e.transpose(
                TPSB[:, fc * 128:(fc + 1) * 128], GTD[:, fc * 128:(fc + 1) * 128],
                env["IDB"][:, :]),
              reads=[GTD, env["IDB"]], writes=[(TPSB, fc)])
        P('dve', lambda e, lci=lci: e.tensor_copy(
            GFM[:, :].rearrange("p (fc t) -> p fc t", fc=8)[:, :, (lci % 4) * 128:(lci % 4 + 1) * 128],
            TPSB[:, :].rearrange("p (fc t) -> p fc t", fc=8)),
          reads=[(TPSB, fc) for fc in range(NFC)], writes=[(GFM, lci % 4)])
        if lci % 4 == 3:
            g5 = lci // 4
            psf = PS[4]
            for dc in range(NFC):
                mm_group(P, psf, 'wo', [
                    ((lambda e, start, stop, fc=fc, dc=dc: e.matmul(
                        psf[:, 0:512], WOSB[:, fc * D + dc * 128: fc * D + (dc + 1) * 128],
                        GFM[:, fc * 512:(fc + 1) * 512], start=start, stop=stop)),
                     [(WOSB, fc)] + [(GFM, i) for i in range(4)])
                    for fc in range(NFC)])
                P('dve', lambda e, dc=dc, psf=psf: e.tensor_copy(
                    HEV[:, (dc % 2) * 512:(dc % 2) * 512 + 512], psf[:, 0:512]),
                  reads=[psf], writes=[(HEV, dc % 2)])
                def sto(e, t0=t0, g5=g5, dc=dc):
                    sl = slice(t0 + g5 * 512, t0 + (g5 + 1) * 512)
                    return e.dma_start(
                        out=env["HATT"][dc * 128:(dc + 1) * 128, sl],
                        in_=HEV[:, (dc % 2) * 512:(dc % 2) * 512 + 512])
                P('sp', sto, reads=[(HEV, dc % 2)],
                  writes=[(env["HATT"], (t0, g5, dc))], dma=True, stream=f"hatt{dc}")


def emit_soff_init(pr, env, nprev=3):
    P = pr.op
    SAL = env["SAL"]
    SPV = env["SPV"]
    WOF = env["WOF"]
    P('dve', lambda e: e.memset(SAL[:, :], 0.0), writes=[env["SAL"]])
    for j in range(nprev):
        P('sp', lambda e, j=j: e.dma_start(out=SPV[:, :], in_=env["sprev"][j, :, :]),
          writes=[SPV], dma=True, stream="spv")
        P('sp', lambda e, j=j: e.dma_start(out=WOF[:, :], in_=env["woff"][j, :, :]),
          writes=[WOF], dma=True, stream="wof")
        P('dve', lambda e: e.tensor_tensor(SPV[:, :], SPV[:, :], WOF[:, :], ALU.mult),
          reads=[SPV, WOF], writes=[SPV])
        P('dve', lambda e: e.tensor_tensor(SAL[:, :], SAL[:, :], SPV[:, :], ALU.add),
          reads=[SAL, SPV], writes=[SAL])


def alloc_ret_tiles(nc, stack, env, T, tw, with_main):
    sb = lambda name, shape, dt=F32: stack.enter_context(nc.sbuf_tensor(name, shape, dt))
    psb = lambda name, shape, dt=F32: stack.enter_context(nc.psum_tensor(name, shape, dt))
    env["XN"] = sb("XN", [128, NFC * T], BF16)
    env["KT"] = sb("KT", [128, (tw // 128) * D], BF16)
    env["VT"] = sb("VT", [128, (tw // 128) * D], BF16)
    env["PWSB"] = sb("PWSB", [128, 2 * NFC * D], BF16)
    env["SAL"] = sb("SAL", [64, 1024], F32)
    env["SPV"] = sb("SPV", [64, 1024], F32)
    env["WOF"] = sb("WOF", [64, 1024], F32)
    env["SCR"] = sb("SCR", [128, 3072], F32)
    env["RW"] = sb("RW", [1, 512], F32)
    env["OC32"] = sb("OC32", [128, 1], F32)
    env["OR32"] = sb("OR32", [1, 128], F32)
    env["LN1W"] = sb("LN1W", [128, 8], F32)
    env["LN1B"] = sb("LN1B", [128, 8], F32)
    env["GCC"] = sb("GCC", [64, 1024], F32)
    if with_main:
        env["QDZE"] = sb("QDZE", [128, NFC * tw], BF16)
        env["QDZO"] = sb("QDZO", [128, NFC * tw], BF16)
        env["KD"] = sb("KD", [128, NFC * tw], BF16)
        env["GT"] = sb("GT", [128, (tw // 128) * D], BF16)
        env["WOSB"] = sb("WOSB", [128, NFC * D], BF16)
        env["SCHKB"] = sb("SCHKB", [128, (tw // CC) * 1024], BF16)
        env["PMS"] = sb("PMS", [128, NH * 128], BF16)
        env["OEV"] = sb("OEV", [128, 1024], F32)
        env["SQT"] = sb("SQT", [128, 1024], F32)
        env["STT"] = sb("STT", [128, 64], F32)
        env["GTD"] = sb("GTD", [128, 1024], BF16)
        env["GFM"] = sb("GFM", [128, NFC * 512], BF16)
        env["HEV"] = sb("HEV", [128, 1024], F32)
        env["TRIU4"] = sb("TRIU4", [128, 512], BF16)
        env["DECQ"] = sb("DECQ", [128, 8 * 512], BF16)
        env["DECK"] = sb("DECK", [128, 8 * 512], BF16)
        env["IDB"] = sb("IDB", [128, 128], BF16)
    env["DECKTM"] = sb("DECKTM", [128, 1024], BF16)
    env["PS"] = [psb(f"PS{i}", [128, 512]) for i in range(5)] + [None] + \
                [psb("PS6", [1, 512]), psb("PS7", [1, 512])]
    if with_main:
        env["TPSB"] = psb("TPSB", [128, 1024], BF16)


def load_ret_consts(pr, env, consts, rconsts, with_main):
    P = pr.op
    loads = [("OC32", consts["ones_col_f32"]), ("OR32", consts["ones_row_f32"]),
             ("GCC", rconsts["gcc"]), ("DECKTM", rconsts["decktm"])]
    if with_main:
        loads += [("TRIU4", rconsts["triu4"]), ("DECQ", rconsts["decq"]),
                  ("DECK", rconsts["deck"]), ("IDB", consts["id_bf"])]
    for nm, h in loads:
        P('sp', lambda e, nm=nm, h=h: e.dma_start(out=env[nm][:, :], in_=h.ap()),
          writes=[env[nm]], dma=True, stream=f"c_{nm}")
    for nm, src in [("LN1W", "ln1w"), ("LN1B", "ln1b")]:
        P('sp', lambda e, nm=nm, src=src: e.dma_start(out=env[nm][:, :], in_=env[src][:, :]),
          writes=[env[nm]], dma=True, stream=f"c_{nm}")


def build_retstate_launch(T):
    nc = bass.Bass("TRN2", target_bir_lowering=False)
    cfg = dict(T=T)
    env = {}
    env["xin"] = nc.dram_tensor("xin", [D, T], F32, kind="ExternalInput").ap()
    env["wk"] = nc.dram_tensor("wk", [D, D], F32, kind="ExternalInput").ap()
    env["wv"] = nc.dram_tensor("wv", [D, D], F32, kind="ExternalInput").ap()
    env["ln1w"] = nc.dram_tensor("ln1w", [128, 8], F32, kind="ExternalInput").ap()
    env["ln1b"] = nc.dram_tensor("ln1b", [128, 8], F32, kind="ExternalInput").ap()
    env["segstate"] = nc.dram_tensor("segstate", [64, 1024], F32, kind="ExternalOutput").ap()
    consts = make_consts(nc)
    rconsts = make_ret_consts(nc)
    tw = 512
    with ExitStack() as stack:
        alloc_ret_tiles(nc, stack, env, T, tw, with_main=False)
        print("C1 SBUF remaining:", nc.sbuf_bytes_remaining)
        pr = Prog(nc)
        load_ret_consts(pr, env, consts, rconsts, with_main=False)
        emit_xn1(pr, env, cfg, out_sbuf=(env["XN"], T))
        pr.fence()
        P = pr.op
        P('dve', lambda e: e.memset(env["SAL"][:, :], 0.0), writes=[env["SAL"]])
        for q in range(T // tw):
            emit_proj_tm(pr, env, cfg, env["wk"], env["XN"], env["KT"], q * tw, tw,
                         dec_const=env["DECKTM"])
            emit_proj_tm(pr, env, cfg, env["wv"], env["XN"], env["VT"], q * tw, tw)
            pr.fence()
            emit_ret_state_chunks(pr, env, cfg, env["KT"], env["VT"], q * (tw // CC),
                                  tw // CC, save=False)
            pr.fence()
        P('sp', lambda e: e.dma_start(out=env["segstate"][:, :], in_=env["SAL"][:, :]),
          reads=[env["SAL"]], dma=True, stream="out")
        pr.finish()
        with nc.Block() as block:
            pr.build(block)
    return nc


def build_l1attn_launch(T, trunc=99, mtrunc=99, mhg=4, mmask=True):
    nc = bass.Bass("TRN2", target_bir_lowering=False)
    cfg = dict(T=T)
    env = {}
    env["xin"] = nc.dram_tensor("xin", [D, T], F32, kind="ExternalInput").ap()
    for w in ["wq", "wk", "wv", "wg", "wo"]:
        env[w] = nc.dram_tensor(w, [D, D], F32, kind="ExternalInput").ap()
    env["ln1w"] = nc.dram_tensor("ln1w", [128, 8], F32, kind="ExternalInput").ap()
    env["ln1b"] = nc.dram_tensor("ln1b", [128, 8], F32, kind="ExternalInput").ap()
    env["sprev"] = nc.dram_tensor("sprev", [3, 64, 1024], F32, kind="ExternalInput").ap()
    env["woff"] = nc.dram_tensor("woff", [3, 64, 1024], F32, kind="ExternalInput").ap()
    env["HATT"] = nc.dram_tensor("hatt", [D, T], F32, kind="ExternalOutput").ap()
    consts = make_consts(nc)
    rconsts = make_ret_consts(nc)
    tw = 512
    with ExitStack() as stack:
        alloc_ret_tiles(nc, stack, env, T, tw, with_main=True)
        print("C2a SBUF remaining:", nc.sbuf_bytes_remaining)
        pr = Prog(nc)
        P = pr.op
        load_ret_consts(pr, env, consts, rconsts, with_main=True)
        for fc in range(NFC):
            P('pool', lambda e, fc=fc: e.dma_start(
                out=env["WOSB"][:, fc * D:(fc + 1) * D], in_=env["wo"][fc * 128:(fc + 1) * 128, :]),
              writes=[(env["WOSB"], fc)], dma=True, stream=f"wo{fc}")
        emit_xn1(pr, env, cfg, out_sbuf=(env["XN"], T))
        emit_soff_init(pr, env)
        pr.fence()
        cfg["mtrunc"] = mtrunc; cfg["mhg"] = mhg; cfg["mmask"] = mmask
        P('dve', lambda e: e.memset(env["QDZE"][64:128, :], 0.0),
          writes=[(env["QDZE"], 'z')])
        P('dve', lambda e: e.memset(env["QDZO"][0:64, :], 0.0),
          writes=[(env["QDZO"], 'z')])
        for q in range(T // tw) if trunc >= 4 else range(1 if trunc >= 2 else 0):
            emit_proj_fm(pr, env, cfg, env["wq"], env["XN"], None, q * tw, tw,
                         dec_const=env["DECQ"], qdz=(env["QDZE"], env["QDZO"]))
            emit_proj_fm(pr, env, cfg, env["wk"], env["XN"], env["KD"], q * tw, tw,
                         dec_const=env["DECK"])
            emit_proj_tm(pr, env, cfg, env["wk"], env["XN"], env["KT"], q * tw, tw,
                         dec_const=env["DECKTM"])
            emit_proj_tm(pr, env, cfg, env["wv"], env["XN"], env["VT"], q * tw, tw)
            emit_proj_tm(pr, env, cfg, env["wg"], env["XN"], env["GT"], q * tw, tw,
                         act=AF.Silu)
            pr.fence()
            if trunc >= 3:
                emit_ret_state_chunks(pr, env, cfg, env["KT"], env["VT"], q * (tw // CC),
                                      tw // CC, save=True)
                pr.fence()
            if trunc >= 4:
                emit_ret_main_half(pr, env, cfg, env["QDZE"], env["KD"], env["KT"], env["VT"],
                                   env["GT"], q * tw, tw)
                pr.fence()
        pr.finish()
        with nc.Block() as block:
            pr.build(block)
    return nc


def emit_xn1(pr, env, cfg, out_sbuf):
    emit_ln_sum(pr, env, cfg, env["xin"], None, None,
                affine=(env["LN1W"], env["LN1B"]), out_sbuf=out_sbuf)



NCC = 22
LASTW = 2730 - 21 * 128


def emit_swiglu(pr, env, cfg):
    T = cfg["T"]
    P = pr.op
    XN2 = env["XN2"]
    WIN = env["WIN"]
    ASB = env["ASB"]; BSB = env["BSB"]
    WO2 = env["WO2"]
    SWEV = env["SWEV"]
    BIN = env["BIN"]
    BOUT = env["BOUT"]
    PS = env["PS"]
    def ldwo2a(e):
        srcw = env["w_out_sw"][0:21 * 128, :].rearrange("(cc p) d -> p cc d", p=128)
        dstw = WO2[:, 0:21 * D].rearrange("p (cc d) -> p cc d", cc=21)
        return e.dma_start(out=dstw, in_=srcw)
    P('pool', ldwo2a, writes=[(WO2, cc) for cc in range(21)], dma=True, stream="wo2a")
    P('pool', lambda e: e.dma_start(
        out=WO2[0:LASTW, 21 * D:22 * D], in_=env["w_out_sw"][21 * 128:2730, :]),
      writes=[(WO2, 21)], dma=True, stream="wo2b")
    kpan = 0
    for t5 in range(T // 512):
        t0 = t5 * 512
        for part in range(2):
            TGT = ASB if part == 0 else BSB
            for cc in range(NCC):
                cw = LASTW if cc == NCC - 1 else 128
                c0 = part * 2730 + cc * 128
                wi = kpan % 2
                kpan += 1
                wbase = wi * NFC * 128
                for h in range(2):
                    def ldw(e, c0=c0, cw=cw, wbase=wbase, h=h):
                        src = env["w_in_sw"][h * 512:(h + 1) * 512, c0:c0 + cw] \
                            .rearrange("(fc p) c -> p fc c", p=128)
                        dst = WIN[:, wbase + h * 4 * cw: wbase + (h + 1) * 4 * cw] \
                            .rearrange("p (fc c) -> p fc c", fc=4)
                        return e.dma_start(out=dst, in_=src)
                    P('pool', ldw, writes=[(WIN, (wi, h))], dma=True, stream=f"win{wi}{h}")
                psf = PS[cc % 2]
                ps = psf[0:cw, 0:512]
                mm_group(P, psf, 'sg', [
                    ((lambda e, start, stop, fc=fc, cw=cw, t0=t0, ps=ps, wbase=wbase: e.matmul(
                        ps, WIN[:, wbase + fc * cw: wbase + (fc + 1) * cw],
                        XN2[:, fc * T + t0: fc * T + t0 + 512], start=start, stop=stop)),
                     [(WIN, (wi, fc // 4))] + [(XN2, (fc, t0 // 128 + i)) for i in range(4)])
                    for fc in range(NFC)])
                af = AF.Identity if part == 0 else AF.Silu
                P('act', lambda e, cc=cc, cw=cw, part=part, af=af, ps=ps, TGT=TGT: e.activation(
                    TGT[0:cw, cc * 512:(cc + 1) * 512], ps, af,
                    bias=BIN[0:cw, part * NCC + cc: part * NCC + cc + 1]),
                  reads=[psf, BIN], writes=[(TGT, cc)])
        for cc in range(NCC):
            cw = LASTW if cc == NCC - 1 else 128
            P('dve', lambda e, cc=cc, cw=cw: e.tensor_tensor(
                ASB[0:cw, cc * 512:(cc + 1) * 512], ASB[0:cw, cc * 512:(cc + 1) * 512],
                BSB[0:cw, cc * 512:(cc + 1) * 512], ALU.mult),
              reads=[(ASB, cc), (BSB, cc)], writes=[(ASB, cc)])
        for dc in range(NFC):
            psf = PS[2 + dc % 2]
            ps = psf[:, 0:512]
            mm_group(P, psf, 'so', [
                ((lambda e, start, stop, cc=cc, dc=dc, ps=ps: e.matmul(
                    ps, WO2[0:(LASTW if cc == NCC - 1 else 128), cc * D + dc * 128: cc * D + (dc + 1) * 128],
                    ASB[0:(LASTW if cc == NCC - 1 else 128), cc * 512:(cc + 1) * 512],
                    start=start, stop=stop)),
                 [(WO2, cc), (ASB, cc)])
                for cc in range(NCC)])
            P('act', lambda e, dc=dc, ps=ps: e.activation(
                SWEV[:, (dc % 2) * 512:(dc % 2) * 512 + 512], ps, AF.Identity,
                bias=BOUT[:, dc:dc + 1]),
              reads=[psf, BOUT], writes=[(SWEV, dc % 2)])
            def acc(e, t5=t5, dc=dc):
                dst = env["XB"][dc * 128:(dc + 1) * 128, t5 * 512:(t5 + 1) * 512]
                return e.dma_start(out=dst, in_=SWEV[:, (dc % 2) * 512:(dc % 2) * 512 + 512],
                                   accum_op=ALU.add)
            P('pool', acc, reads=[(SWEV, dc % 2), (env["XB"], ('init', t5))],
              writes=[(env["XB"], t5)], dma=True, stream=f"xb{t5}")


def build_swiglu_launch(T):
    nc = bass.Bass("TRN2", target_bir_lowering=False)
    cfg = dict(T=T)
    env = {}
    env["xin"] = nc.dram_tensor("xin", [D, T], F32, kind="ExternalInput").ap()
    env["hatt"] = nc.dram_tensor("hatt", [D, T], F32, kind="ExternalInput").ap()
    env["w_in_sw"] = nc.dram_tensor("w_in_sw", [D, 5460], F32, kind="ExternalInput").ap()
    env["w_out_sw"] = nc.dram_tensor("w_out_sw", [2730, D], F32, kind="ExternalInput").ap()
    env["ln2w"] = nc.dram_tensor("ln2w", [128, 8], F32, kind="ExternalInput").ap()
    env["ln2b"] = nc.dram_tensor("ln2b", [128, 8], F32, kind="ExternalInput").ap()
    env["binr"] = nc.dram_tensor("binr", [2, 128, NCC], F32, kind="ExternalInput").ap()
    env["boutr"] = nc.dram_tensor("boutr", [128, 8], F32, kind="ExternalInput").ap()
    env["xout"] = nc.dram_tensor("xout", [D, T], F32, kind="ExternalOutput").ap()
    env["XA"] = nc.dram_tensor("xa", [D, T], F32, kind="Internal").ap()
    env["XB"] = nc.dram_tensor("xb", [D, T], F32, kind="Internal").ap()
    consts = make_consts(nc)
    with ExitStack() as stack:
        sb = lambda name, shape, dt=F32: stack.enter_context(nc.sbuf_tensor(name, shape, dt))
        psb = lambda name, shape, dt=F32: stack.enter_context(nc.psum_tensor(name, shape, dt))
        env["XN2"] = sb("XN2", [128, NFC * T], BF16)
        env["WIN"] = sb("WIN", [128, 2 * NFC * 128], BF16)
        env["ASB"] = sb("ASB", [128, NCC * 512], BF16)
        env["BSB"] = sb("BSB", [128, NCC * 512], BF16)
        env["WO2"] = sb("WO2", [128, NCC * D], BF16)
        env["SWEV"] = sb("SWEV", [128, 1024], F32)
        env["BIN"] = sb("BIN", [128, 2 * NCC], F32)
        env["BOUT"] = sb("BOUT", [128, 8], F32)
        env["SCR"] = sb("SCR", [128, 3072], F32)
        env["RW"] = sb("RW", [1, 512], F32)
        env["OC32"] = sb("OC32", [128, 1], F32)
        env["OR32"] = sb("OR32", [1, 128], F32)
        env["LN2W"] = sb("LN2W", [128, 8], F32)
        env["LN2B"] = sb("LN2B", [128, 8], F32)
        env["PS"] = [psb(f"PS{i}", [128, 512]) for i in range(5)] + [None] + \
                    [psb("PS6", [1, 512]), psb("PS7", [1, 512])]
        print("C2b SBUF remaining:", nc.sbuf_bytes_remaining)
        pr = Prog(nc)
        P = pr.op
        for nm, h in [("OC32", consts["ones_col_f32"]), ("OR32", consts["ones_row_f32"])]:
            P('sp', lambda e, nm=nm, h=h: e.dma_start(out=env[nm][:, :], in_=h.ap()),
              writes=[env[nm]], dma=True, stream=f"c_{nm}")
        for nm, srcn in [("LN2W", "ln2w"), ("LN2B", "ln2b"), ("BOUT", "boutr")]:
            P('sp', lambda e, nm=nm, srcn=srcn: e.dma_start(out=env[nm][:, :], in_=env[srcn][:, :]),
              writes=[env[nm]], dma=True, stream=f"c_{nm}")
        P('sp', lambda e: e.dma_start(out=env["BIN"][:, 0:NCC], in_=env["binr"][0, :, :]),
          writes=[(env["BIN"], 0)], dma=True, stream="c_bin0")
        P('sp', lambda e: e.dma_start(out=env["BIN"][:, NCC:2 * NCC], in_=env["binr"][1, :, :]),
          writes=[(env["BIN"], 1)], dma=True, stream="c_bin1")
        pr.op('dve', lambda e: e.engine_nop(), reads=[(env["BIN"], 0), (env["BIN"], 1)],
              writes=[env["BIN"]])
        emit_ln_sum(pr, env, cfg, env["xin"], env["hatt"], None,
                    affine=(env["LN2W"], env["LN2B"]), out_sbuf=(env["XN2"], T),
                    sum_dram=env["XA"])
        pr.fence()
        for t5 in range(T // 512):
            P('sp', lambda e, t5=t5: e.dma_start(
                out=env["XB"][:, t5 * 512:(t5 + 1) * 512],
                in_=env["XA"][:, t5 * 512:(t5 + 1) * 512]),
              reads=[(env["XA"], ('sum', g)) for g in range(t5 * 4, t5 * 4 + 4)],
              writes=[(env["XB"], ('init', t5))], dma=True, stream=f"xbi{t5}")
        emit_swiglu(pr, env, cfg)
        pr.fence()
        emit_ln_sum(pr, env, cfg, env["xin"], env["XB"], env["xout"],
                    b_deps=[(env["XB"], t5) for t5 in range(T // 512)])
        pr.finish()
        with nc.Block() as block:
            pr.build(block)
    return nc


LVL_BS = [4, 8, 16, 32]
LVL_NN = [8, 4, 2, 1]
LVL_R = [4, 8, 8, 8]
LVL_OFF = [0, 32, 64, 80]
NNR = 88
DHD = 64


def hss_host_embeds(UL, VR, UR, VL):
    EUL = np.zeros((64, NNR), np.float32)
    EVR = np.zeros((64, NNR), np.float32)
    EVL = np.zeros((64, NNR), np.float32)
    EULT = np.zeros((NNR, 64), np.float32)
    EURT = np.zeros((NNR, 64), np.float32)
    for l in range(4):
        bs, nn, r, off = LVL_BS[l], LVL_NN[l], LVL_R[l], LVL_OFF[l]
        for n in range(nn):
            lo = 2 * bs * n
            hi = lo + bs
            cs = off + n * r
            EUL[lo:lo + bs, cs:cs + r] = UL[l]
            EVR[hi:hi + bs, cs:cs + r] = VR[l]
            EVL[lo:lo + bs, cs:cs + r] = VL[l]
            EULT[cs:cs + r, lo:lo + bs] = UL[l].T
            EURT[cs:cs + r, hi:hi + bs] = UR[l].T
    return {"EUL": EUL, "EVR": EVR, "EVL": EVL, "EULT": EULT, "EURT": EURT}


def hss_state_masks():
    mleaf = np.zeros((64, 65), np.float32)
    for n in range(16):
        mleaf[4 * n:4 * n + 4, 4 * n:4 * n + 4] = 1
    mleaf[:, 64] = 1
    mlvl = np.zeros((32, 176), np.float32)
    col = 0
    for l in range(4):
        nn, r = LVL_NN[l], LVL_R[l]
        for rep in range(2):
            for n in range(nn):
                mlvl[n * r:(n + 1) * r, col + n * r: col + (n + 1) * r] = 1
            col += nn * r
    return mleaf, mlvl


LP_OFF = []
_c = 0
for _l in range(4):
    LP_OFF.append((_c, _c + LVL_NN[_l] * LVL_R[_l]))
    _c += 2 * LVL_NN[_l] * LVL_R[_l]


def make_hss_consts(nc):
    c = {}
    el = np.zeros((4, 16 * 64), np.float32)
    for n in range(16):
        for j in range(4):
            el[j, n * 64 + 4 * n + j] = 1.0
    c["eleaf"] = nc.inline_tensor(bfc(el), "c_eleaf")
    mleaf, mlvl = hss_state_masks()
    c["mskleaf"] = nc.inline_tensor(mleaf, "c_mskleaf")
    c["msklvl"] = nc.inline_tensor(mlvl, "c_msklvl")
    return c


def emit_hss_proj(pr, env, cfg):
    T = cfg["T"]
    P = pr.op
    PS = env["PS"]
    SCR = env["SCR"]
    for name, wkey, elu in [("QF", "Wq", True), ("KF", "Wk", True), ("VF", "Wv", False)]:
        OUT = env[name]
        WSB = env["HWSB"]
        for fc in range(NFC):
            P('pool', lambda e, fc=fc, wkey=wkey: e.dma_start(
                out=WSB[:, fc * DHD:(fc + 1) * DHD],
                in_=env[wkey][fc * 128:(fc + 1) * 128, :]),
              writes=[(WSB, fc)], dma=True, stream=f"hw{fc}")
        for t5 in range(T // 512):
            psf = PS[t5 % 2]
            ps = psf[0:64, 0:512]
            mm_group(P, psf, 'hp', [
                ((lambda e, start, stop, fc=fc, t5=t5, ps=ps: e.matmul(
                    ps, WSB[:, fc * DHD:(fc + 1) * DHD],
                    env["XBF"][:, fc * T + t5 * 512: fc * T + (t5 + 1) * 512],
                    start=start, stop=stop)),
                 [(WSB, fc), (env["XBF"], fc)])
                for fc in range(NFC)])
            if elu:
                P('dve', lambda e, ps=ps: e.tensor_scalar_min(SCR[0:64, 0:512], ps, 0.0),
                  reads=[psf], writes=[(SCR, 'e1')])
                P('act', lambda e: e.activation(SCR[0:64, 512:1024], SCR[0:64, 0:512], AF.Exp),
                  reads=[(SCR, 'e1')], writes=[(SCR, 'e2')])
                P('dve', lambda e, ps=ps: e.tensor_scalar_add(SCR[0:64, 1024:1536], ps, 1.0),
                  reads=[psf], writes=[(SCR, 'e3')])
                P('dve', lambda e, t5=t5, OUT=OUT: e.tensor_tensor(
                    OUT[0:64, t5 * 512:(t5 + 1) * 512], SCR[0:64, 1024:1536],
                    SCR[0:64, 512:1024], ALU.max),
                  reads=[(SCR, 'e2'), (SCR, 'e3')], writes=[(OUT, t5)])
            else:
                P('dve', lambda e, t5=t5, ps=ps, OUT=OUT: e.tensor_copy(
                    OUT[0:64, t5 * 512:(t5 + 1) * 512], ps),
                  reads=[psf], writes=[(OUT, t5)])


WOFF_L = [0, 8, 12, 14]


def emit_hss_compact(pr, env, cfg):
    T = cfg["T"]
    P = pr.op
    PS = env["PS"]
    for cname, ename, src, resident, fbslot in [
            ("AQHC", "EVRB", "QF", True, 2), ("AQLC", "EVLB", "QF", True, 3),
            ("AKC", "EULB", "KF", False, 0), ("BVC", "EVRB", "VF", False, 1)]:
        for l in range(4):
            nnr = LVL_NN[l] * LVL_R[l]
            for t5 in range(T // 512):
                psf = PS[t5 % 2]
                ps = psf[0:nnr, 0:512]
                P('pe', lambda e, l=l, t5=t5, ps=ps, ename=ename, src=src, nnr=nnr: e.matmul(
                    ps, env[ename][:, LVL_OFF[l]:LVL_OFF[l] + nnr],
                    env[src][0:64, t5 * 512:(t5 + 1) * 512], start=True, stop=True),
                  reads=[env[ename], (env[src], t5)], writes=[psf])
                if resident:
                    CT = env[cname]
                    P('dve', lambda e, l=l, t5=t5, ps=ps, CT=CT, nnr=nnr: e.tensor_copy(
                        CT[0:nnr, l * T + t5 * 512: l * T + (t5 + 1) * 512], ps),
                      reads=[psf], writes=[(CT, (l, t5))])
                else:
                    CR = env["CROT"]
                    P('dve', lambda e, ps=ps, CR=CR, nnr=nnr: e.tensor_copy(
                        CR[0:nnr, 0:512], ps),
                      reads=[psf], writes=[CR])
                    P('sp', lambda e, l=l, t5=t5, CR=CR, nnr=nnr, fbslot=fbslot: e.dma_start(
                        out=env["FBD"][fbslot, LVL_OFF[l]:LVL_OFF[l] + nnr,
                                       t5 * 512:(t5 + 1) * 512],
                        in_=CR[0:nnr, 0:512]),
                      reads=[CR], writes=[(env["FBD"], (fbslot, l, t5))], dma=True,
                      stream="bounceA")
        if resident:
            CT = env[cname]
            for l in range(4):
                nnr = LVL_NN[l] * LVL_R[l]
                P('sp', lambda e, l=l, CT=CT, nnr=nnr, fbslot=fbslot: e.dma_start(
                    out=env["FBD"][fbslot, LVL_OFF[l]:LVL_OFF[l] + nnr, :],
                    in_=CT[0:nnr, l * T:(l + 1) * T]),
                  reads=[(CT, (l, t5)) for t5 in range(T // 512)],
                  writes=[(env["FBD"], (fbslot, l))], dma=True, stream="bounceB")
    for i, src in enumerate(["KF", "VF", "QF"]):
        P('sp', lambda e, i=i, src=src: e.dma_start(
            out=env["FBL"][i, :, :], in_=env[src][0:64, :]),
          reads=[(env[src], t5) for t5 in range(T // 512)],
          writes=[(env["FBL"], i)], dma=True, stream="bounceB")


def emit_hss_wide_group(pr, env, cfg, g2):
    T = cfg["T"]
    P = pr.op
    GW = 256
    sl = slice(g2 * GW, (g2 + 1) * GW)
    for wname, fbslot in [("WAK", 0), ("WBV", 1), ("WAQH", 2), ("WAQL", 3)]:
        WT = env[wname]
        deps = [(env["FBD"], (fbslot, l)) for l in range(4)] + \
               [(env["FBD"], (fbslot, l, t5)) for l in range(4) for t5 in range(T // 512)]
        def ld0(e, fbslot=fbslot, WT=WT, sl=sl):
            s = env["FBD"][fbslot, 0:32, sl].rearrange("(n r) t -> r n t", r=4)
            d = WT[0:4, 0:8 * GW].rearrange("p (n t) -> p n t", n=8)
            return e.dma_start(out=d, in_=s)
        P('sp', ld0, reads=deps, writes=[(WT, 0)], dma=True, stream=f"w_{wname}0")
        def ld1(e, fbslot=fbslot, WT=WT, sl=sl):
            s = env["FBD"][fbslot, 32:88, sl].rearrange("(n r) t -> r n t", r=8)
            d = WT[0:8, 8 * GW:15 * GW].rearrange("p (n t) -> p n t", n=7)
            return e.dma_start(out=d, in_=s)
        P('sp', ld1, reads=deps, writes=[(WT, 1)], dma=True, stream=f"w_{wname}1")
    for i, wname in [(0, "WKL"), (1, "WVL"), (2, "WQL")]:
        WT = env[wname]
        def ldl(e, i=i, WT=WT, sl=sl):
            s = env["FBL"][i, :, sl].rearrange("(n r) t -> r n t", r=4)
            d = WT[0:4, 0:16 * GW].rearrange("p (n t) -> p n t", n=16)
            return e.dma_start(out=d, in_=s)
        P('sp', ldl, reads=[(env["FBL"], i)], writes=[WT], dma=True, stream=f"w_{wname}")


def hss_pieces():
    ps = []
    for l in range(4):
        for n in range(LVL_NN[l]):
            ps.append(("top", l, n))
            ps.append(("bot", l, n))
    for n in range(16):
        ps.append(("lf", n))
    ps.append(("den",))
    return ps


def emit_hss_chunk(pr, env, cfg, ci, statesonly=False):
    T = cfg["T"]
    P = pr.op
    PS = env["PS"]
    GW = 256
    co = (ci % 2) * 128
    QF, KF, VF = env["QF"], env["KF"], env["VF"]
    t5 = (ci * 128) // 512
    qk = (QF, t5); kk = (KF, t5)

    def P3(eng, fn, reads=(), writes=()):
        P(eng, fn, reads=list(reads) + [(PS[3], 'cl')], writes=list(writes) + [(PS[3], 'cl')])

    def P4(eng, fn, reads=(), writes=()):
        P(eng, fn, reads=list(reads) + [(PS[4], 'cl')], writes=list(writes) + [(PS[4], 'cl')])

    for i, (src, ename, OUTT) in enumerate([("KF", "EULB", "AKTM"), ("VF", "EVRB", "BVTM")]):
        for l in range(4):
            nnr = LVL_NN[l] * LVL_R[l]
            P4('pe', lambda e, l=l, src=src, ename=ename, i=i, nnr=nnr: e.matmul(
                PS[4][:, i * 128 + LVL_OFF[l]: i * 128 + LVL_OFF[l] + nnr],
                env[src][0:64, ci * 128:(ci + 1) * 128],
                env[ename][:, LVL_OFF[l]:LVL_OFF[l] + nnr], start=True, stop=True),
               reads=[(env[src], t5), env[ename]])
        P4('dve', lambda e, i=i, OUTT=OUTT: e.tensor_copy(
            env[OUTT][:, 0:NNR], PS[4][:, i * 128:i * 128 + NNR]),
           writes=[env[OUTT]])
    for src, OUTT in [("KF", "KTM"), ("VF", "VTM")]:
        P('pe', lambda e, src=src: e.transpose(
            env["TPSB"][:, 0:64], env[src][0:64, ci * 128:(ci + 1) * 128],
            env["IDB"][0:64, 0:64]),
          reads=[(env[src], t5), env["IDB"]], writes=[env["TPSB"]])
        P('dve', lambda e, OUTT=OUTT: e.tensor_copy(env[OUTT][:, 0:64], env["TPSB"][:, 0:64]),
          reads=[env["TPSB"]], writes=[env[OUTT]])

    if not statesonly:
        pieces = hss_pieces()

        def piece_score_ops(pc):
            if pc[0] in ("top", "bot"):
                _, l, n = pc
                r = LVL_R[l]
                off = (WOFF_L[l] + n) * GW + co
                wk = (0 if l == 0 else 1)
                if pc[0] == "top":
                    lh, rh = env["WBV"], env["WAQH"]
                else:
                    lh, rh = env["WAK"], env["WAQL"]
                return lh[0:r, off:off + 128], rh[0:r, off:off + 128], [(lh, wk), (rh, wk)]
            if pc[0] == "lf":
                n = pc[1]
                off = n * GW + co
                return (env["WVL"][0:4, off:off + 128], env["WQL"][0:4, off:off + 128],
                        [env["WVL"], env["WQL"]])
            return (KF[0:64, ci * 128:(ci + 1) * 128], QF[0:64, ci * 128:(ci + 1) * 128],
                    [kk, qk])

        for r0 in range(0, len(pieces), 4):
            rnd = pieces[r0:r0 + 4]
            psf = PS[(r0 // 4) % 2]
            for k, pc in enumerate(rnd):
                lhs, rhs, lk = piece_score_ops(pc)
                P('pe', lambda e, lhs=lhs, rhs=rhs, k=k, psf=psf: e.matmul(
                    psf[:, k * 128:(k + 1) * 128], lhs, rhs, start=True, stop=True),
                  reads=lk, writes=[(psf, k)])
            nk = len(rnd)
            P('dve', lambda e, r0=r0, nk=nk, psf=psf: e.tensor_tensor(
                env["PMSC"][:, r0 * 128:(r0 + nk) * 128], psf[:, 0:nk * 128],
                env["TRIU4"][:, 0:nk * 128], ALU.mult),
              reads=[(psf, k) for k in range(nk)] + [env["TRIU4"]],
              writes=[(env["PMSC"], r0 // 4)])

        def piece_um_ops(pc):
            if pc[0] in ("top", "bot"):
                _, l, n = pc
                r = LVL_R[l]
                off = (WOFF_L[l] + n) * GW + co
                wk = (0 if l == 0 else 1)
                eoff = (WOFF_L[l] + n) * 64
                if pc[0] == "top":
                    return (env["WAK"][0:r, off:off + 128],
                            env["EULTW"][0:r, eoff:eoff + 64],
                            [(env["WAK"], wk), env["EULTW"]])
                return (env["WBV"][0:r, off:off + 128],
                        env["EURTW"][0:r, eoff:eoff + 64],
                        [(env["WBV"], wk), env["EURTW"]])
            n = pc[1]
            off = n * GW + co
            return (env["WKL"][0:4, off:off + 128], env["ELEAF"][0:4, n * 64:(n + 1) * 64],
                    [env["WKL"], env["ELEAF"]])

        for r0 in range(0, 46, 8):
            rnd = pieces[r0:min(r0 + 8, 46)]
            for k, pc in enumerate(rnd):
                lhs, rhs, lk = piece_um_ops(pc)
                P3('pe', lambda e, lhs=lhs, rhs=rhs, k=k: e.matmul(
                    PS[3][:, k * 64:(k + 1) * 64], lhs, rhs, start=True, stop=True),
                   reads=lk)
            nk = len(rnd)
            P3('dve', lambda e, r0=r0, nk=nk: e.tensor_copy(
                env["UMS"][:, r0 * 64:(r0 + nk) * 64], PS[3][:, 0:nk * 64]),
               writes=[(env["UMS"], r0 // 8)])

        gspecs = []
        for l in range(4):
            nnr = LVL_NN[l] * LVL_R[l]
            mo, mto = 65 + LP_OFF[l][0], 65 + LP_OFF[l][1]
            for side in range(2):
                gspecs.append((l, nnr, (mto if side == 0 else mo), side))
        for r0 in range(0, 8, 4):
            for k in range(4):
                l, nnr, soff, side = gspecs[r0 + k]
                CQC = env["AQHC"] if side == 0 else env["AQLC"]
                P4('pe', lambda e, l=l, nnr=nnr, soff=soff, CQC=CQC, k=k: e.matmul(
                    PS[4][0:nnr, k * 128:(k + 1) * 128],
                    env["MSTB"][0:nnr, soff:soff + nnr],
                    CQC[0:nnr, l * T + ci * 128: l * T + (ci + 1) * 128],
                    start=True, stop=True),
                   reads=[env["MSTB"], (CQC, (l, t5))])
            P4('dve', lambda e, r0=r0: e.tensor_copy(
                env["GEV"][0:32, r0 * 128:(r0 + 4) * 128], PS[4][0:32, 0:512]),
               writes=[(env["GEV"], r0 // 4)])

        P('pe', lambda e: e.matmul(
            PS[2][0:64, 0:128], env["MSTB"][0:64, 0:64],
            QF[0:64, ci * 128:(ci + 1) * 128], start=True, stop=False),
          reads=[env["MSTB"], qk], writes=[(PS[2], 'y')])
        P3('pe', lambda e: e.matmul(
            PS[3][0:1, 0:128], env["MSTB"][0:64, 64:65],
            QF[0:64, ci * 128:(ci + 1) * 128], start=True, stop=False),
           reads=[env["MSTB"], qk], writes=[(PS[3], 'den')])
        gidx = 0
        for l in range(4):
            nnr = LVL_NN[l] * LVL_R[l]
            for side in range(2):
                ET = env["EULTL"] if side == 0 else env["EURTL"]
                P('pe', lambda e, l=l, nnr=nnr, ET=ET, gidx=gidx: e.matmul(
                    PS[2][0:64, 0:128], ET[0:nnr, l * 64:(l + 1) * 64],
                    env["GEV"][0:nnr, gidx * 128:(gidx + 1) * 128],
                    start=False, stop=False),
                  reads=[ET, (env["GEV"], gidx // 4), (PS[2], 'y')], writes=[(PS[2], 'y')])
                gidx += 1
        for pi, pc in enumerate(pieces):
            if pc[0] == "den":
                P3('pe', lambda e, pi=pi: e.matmul(
                    PS[3][0:1, 0:128], env["ONESB"][:, 0:1],
                    env["PMSC"][:, pi * 128:(pi + 1) * 128], start=False, stop=True),
                   reads=[env["ONESB"], (env["PMSC"], pi // 4), (PS[3], 'den')],
                   writes=[(PS[3], 'den')])
            else:
                stp = pi == len(pieces) - 2
                P('pe', lambda e, pi=pi, stp=stp: e.matmul(
                    PS[2][0:64, 0:128], env["UMS"][:, pi * 64:(pi + 1) * 64],
                    env["PMSC"][:, pi * 128:(pi + 1) * 128], start=False, stop=stp),
                  reads=[(env["UMS"], pi // 8), (env["PMSC"], pi // 4), (PS[2], 'y')],
                  writes=[(PS[2], 'y')])
        P('dve', lambda e, ci=ci: e.tensor_copy(
            env["YSEG"][0:64, ci * 128:(ci + 1) * 128], PS[2][0:64, 0:128]),
          reads=[(PS[2], 'y')], writes=[(env["YSEG"], ci)])
        P3('dve', lambda e, ci=ci: e.tensor_copy(
            env["YSEG"][64:65, ci * 128:(ci + 1) * 128], PS[3][0:1, 0:128]),
           reads=[(PS[3], 'den')], writes=[(env["YSEG"], (ci, 'd'))])

    P3('pe', lambda e: e.matmul(
        PS[3][0:64, 0:64], env["VTM"][:, 0:64], env["KTM"][:, 0:64], start=True, stop=True),
       reads=[env["VTM"], env["KTM"]])
    P3('pe', lambda e: e.matmul(
        PS[3][0:64, 64:65], env["KTM"][:, 0:64], env["ONESB"][:, 0:1], start=True, stop=True),
       reads=[env["KTM"], env["ONESB"]])
    for l in range(4):
        nnr = LVL_NN[l] * LVL_R[l]
        mo, mto = 65 + LP_OFF[l][0], 65 + LP_OFF[l][1]
        P3('pe', lambda e, l=l, nnr=nnr, mo=mo: e.matmul(
            PS[3][0:nnr, mo:mo + nnr], env["AKTM"][:, LVL_OFF[l]:LVL_OFF[l] + nnr],
            env["BVTM"][:, LVL_OFF[l]:LVL_OFF[l] + nnr], start=True, stop=True),
           reads=[env["AKTM"], env["BVTM"]])
        P3('pe', lambda e, l=l, nnr=nnr, mto=mto: e.matmul(
            PS[3][0:nnr, mto:mto + nnr], env["BVTM"][:, LVL_OFF[l]:LVL_OFF[l] + nnr],
            env["AKTM"][:, LVL_OFF[l]:LVL_OFF[l] + nnr], start=True, stop=True),
           reads=[env["AKTM"], env["BVTM"]])
    P3('dve', lambda e: e.tensor_tensor(env["SCR"][0:64, 0:65], PS[3][0:64, 0:65],
                                        env["MSKS"][:, 0:65], ALU.mult),
       reads=[env["MSKS"]], writes=[(env["SCR"], 'st')])
    P3('dve', lambda e: e.tensor_tensor(env["SCR"][0:32, 65:241], PS[3][0:32, 65:241],
                                        env["MSKS"][0:32, 65:241], ALU.mult),
       reads=[env["MSKS"]], writes=[(env["SCR"], 'st2')])
    P('dve', lambda e: e.tensor_tensor(env["MST"][:, 0:65], env["MST"][:, 0:65],
                                       env["SCR"][0:64, 0:65], ALU.add),
      reads=[env["MST"], (env["SCR"], 'st')], writes=[env["MST"]])
    P('dve', lambda e: e.tensor_tensor(env["MST"][0:32, 65:241], env["MST"][0:32, 65:241],
                                       env["SCR"][0:32, 65:241], ALU.add),
      reads=[env["MST"], (env["SCR"], 'st2')], writes=[env["MST"]])
    P('dve', lambda e: e.tensor_copy(env["MSTB"][:, 0:241], env["MST"][:, 0:241]),
      reads=[env["MST"]], writes=[env["MSTB"]])


def emit_hss_final(pr, env, cfg):
    T = cfg["T"]
    P = pr.op
    PS = env["PS"]
    YSEG = env["YSEG"]
    P('dve', lambda e: e.tensor_scalar_max(env["SCR"][0:1, 0:T], YSEG[64:65, 0:T], 1e-6),
      reads=[(YSEG, (ci, 'd')) for ci in range(T // 128)], writes=[(env["SCR"], 'd')])
    P('dve', lambda e: e.reciprocal(env["SCR"][0:1, 0:T], env["SCR"][0:1, 0:T]),
      reads=[(env["SCR"], 'd')], writes=[(env["SCR"], 'r')])
    for t5 in range(T // 512):
        P('pe', lambda e, t5=t5: e.matmul(
            PS[0][0:64, 0:512], env["OR32"][0:1, 0:64],
            env["SCR"][0:1, t5 * 512:(t5 + 1) * 512], start=True, stop=True),
          reads=[(env["SCR"], 'r'), env["OR32"]], writes=[(PS[0], 'b')])
        P('dve', lambda e, t5=t5: e.tensor_tensor(
            env["YN"][0:64, t5 * 512:(t5 + 1) * 512], YSEG[0:64, t5 * 512:(t5 + 1) * 512],
            PS[0][0:64, 0:512], ALU.mult),
          reads=[(PS[0], 'b')] + [(YSEG, ci) for ci in range(t5 * 4, t5 * 4 + 4)],
          writes=[(env["YN"], t5)])
        for dc in range(NFC):
            P('pe', lambda e, t5=t5, dc=dc: e.matmul(
                PS[1][:, 0:512], env["WOSB2"][0:64, dc * 128:(dc + 1) * 128],
                env["YN"][0:64, t5 * 512:(t5 + 1) * 512], start=True, stop=True),
              reads=[env["WOSB2"], (env["YN"], t5)], writes=[(PS[1], 'w')])
            P('dve', lambda e, dc=dc: e.tensor_copy(
                env["HEV"][:, 0:512], PS[1][:, 0:512]),
              reads=[(PS[1], 'w')], writes=[(env["HEV"], 0)])
            P('sp', lambda e, t5=t5, dc=dc: e.dma_start(
                out=env["ATT"][dc * 128:(dc + 1) * 128, t5 * 512:(t5 + 1) * 512],
                in_=env["HEV"][:, 0:512]),
              reads=[(env["HEV"], 0)],
              writes=[(env["ATT"], (t5, dc))], dma=True, stream=f"att{dc}")


def hss_dram_and_inputs(nc, env, T, full):
    env["xin"] = nc.dram_tensor("xin", [D, T], F32, kind="ExternalInput").ap()
    for w in ["Wq", "Wk", "Wv"]:
        env[w] = nc.dram_tensor(w, [D, DHD], F32, kind="ExternalInput").ap()
    env["EULd"] = nc.dram_tensor("EULs", [64, NNR], F32, kind="ExternalInput").ap()
    env["EVRd"] = nc.dram_tensor("EVRs", [64, NNR], F32, kind="ExternalInput").ap()
    env["EVLd"] = nc.dram_tensor("EVLs", [64, NNR], F32, kind="ExternalInput").ap()
    if full:
        env["Wo"] = nc.dram_tensor("Wo", [DHD, D], F32, kind="ExternalInput").ap()
        env["EULTWd"] = nc.dram_tensor("EULTW", [8, 15 * 64], F32, kind="ExternalInput").ap()
        env["EURTWd"] = nc.dram_tensor("EURTW", [8, 15 * 64], F32, kind="ExternalInput").ap()
        env["EULTLd"] = nc.dram_tensor("EULTL", [32, 4 * 64], F32, kind="ExternalInput").ap()
        env["EURTLd"] = nc.dram_tensor("EURTL", [32, 4 * 64], F32, kind="ExternalInput").ap()
        env["sprev"] = nc.dram_tensor("sprev", [3, 64, 241], F32, kind="ExternalInput").ap()
        env["FBD"] = nc.dram_tensor("fbd", [4, NNR, T], BF16, kind="Internal").ap()
        env["FBL"] = nc.dram_tensor("fbl", [3, 64, T], BF16, kind="Internal").ap()
        env["ATT"] = nc.dram_tensor("att", [D, T], F32, kind="Internal").ap()


def alloc_hss_tiles(nc, stack, env, T, full):
    sb = lambda name, shape, dt=F32: stack.enter_context(nc.sbuf_tensor(name, shape, dt))
    psb = lambda name, shape, dt=F32: stack.enter_context(nc.psum_tensor(name, shape, dt))
    GW = 256
    env["XBF"] = sb("XBF", [128, NFC * T], BF16)
    env["QF"] = sb("QF", [64, T], BF16)
    env["KF"] = sb("KF", [64, T], BF16)
    env["VF"] = sb("VF", [64, T], BF16)
    env["HWSB"] = sb("HWSB", [128, NFC * DHD], BF16)
    env["EULB"] = sb("EULB", [64, NNR], BF16)
    env["EVRB"] = sb("EVRB", [64, NNR], BF16)
    env["EVLB"] = sb("EVLB", [64, NNR], BF16)
    env["AKTM"] = sb("AKTM", [128, NNR], BF16)
    env["BVTM"] = sb("BVTM", [128, NNR], BF16)
    env["KTM"] = sb("KTM", [128, 64], BF16)
    env["VTM"] = sb("VTM", [128, 64], BF16)
    env["MST"] = sb("MST", [64, 256], F32)
    env["MSTB"] = sb("MSTB", [64, 256], BF16)
    env["MSKS"] = sb("MSKS", [64, 256], BF16)
    env["ONESB"] = sb("ONESB", [128, 1], BF16)
    env["SCR"] = sb("SCR", [128, 3072], F32)
    env["RW"] = sb("RW", [1, 512], F32)
    env["OC32"] = sb("OC32", [128, 1], F32)
    env["OR32"] = sb("OR32", [1, 128], F32)
    env["IDB"] = sb("IDB", [128, 128], BF16)
    if full:
        env["AQHC"] = sb("AQHC", [32, 4 * T], BF16)
        env["AQLC"] = sb("AQLC", [32, 4 * T], BF16)
        env["CROT"] = sb("CROT", [32, 512], BF16)
        for wname in ["WAK", "WBV", "WAQH", "WAQL"]:
            env[wname] = sb(wname, [8, 15 * GW], BF16)
        for wname in ["WKL", "WVL", "WQL"]:
            env[wname] = sb(wname, [4, 16 * GW], BF16)
        env["PMSC"] = sb("PMSC", [128, 47 * 128], BF16)
        env["UMS"] = sb("UMS", [128, 46 * 64], BF16)
        env["GEV"] = sb("GEV", [32, 8 * 128], BF16)
        env["YSEG"] = sb("YSEG", [65, T], F32)
        env["YN"] = sb("YN", [64, T], BF16)
        env["HEV"] = sb("HEV", [128, 1024], F32)
        env["WOSB2"] = sb("WOSB2", [64, D], BF16)
        env["EULTW"] = sb("EULTW_sb", [8, 15 * 64], BF16)
        env["EURTW"] = sb("EURTW_sb", [8, 15 * 64], BF16)
        env["EULTL"] = sb("EULTL_sb", [32, 4 * 64], BF16)
        env["EURTL"] = sb("EURTL_sb", [32, 4 * 64], BF16)
        env["ELEAF"] = sb("ELEAF", [4, 16 * 64], BF16)
        env["TRIU4"] = sb("TRIU4", [128, 512], BF16)
    env["PS"] = [psb(f"PS{i}", [128, 512]) for i in range(5)] + [None] + \
                [psb("PS6", [1, 512]), psb("PS7", [1, 512])]
    env["TPSB"] = psb("TPSB", [128, 1024], BF16)


def emit_hss_setup(pr, env, cfg, consts, hconsts, full):
    P = pr.op
    T = cfg["T"]
    for nm, h in [("OC32", consts["ones_col_f32"]), ("OR32", consts["ones_row_f32"]),
                  ("IDB", consts["id_bf"]), ("ONESB", consts["ones_row_bf"])]:
        if nm == "ONESB":
            P('pool', lambda e, h=h: e.dma_start(
                out=env["ONESB"][:, 0:1], in_=h.ap().rearrange("a b -> b a")),
              writes=[env["ONESB"]], dma=True, stream="c_onesb")
        else:
            P('sp', lambda e, nm=nm, h=h: e.dma_start(out=env[nm][:, :], in_=h.ap()),
              writes=[env[nm]], dma=True, stream=f"c_{nm}")
    P('sp', lambda e: e.dma_start(out=env["MSKS"][:, 0:241], in_=hconsts["msks"].ap()),
      writes=[env["MSKS"]], dma=True, stream="c_msks")
    for nm, d in [("EULB", "EULd"), ("EVRB", "EVRd"), ("EVLB", "EVLd")]:
        P('pool', lambda e, nm=nm, d=d: e.dma_start(out=env[nm][:, :], in_=env[d][:, :]),
          writes=[env[nm]], dma=True, stream=f"c_{nm}")
    if full:
        for nm, d in [("EULTW", "EULTWd"), ("EURTW", "EURTWd"),
                      ("EULTL", "EULTLd"), ("EURTL", "EURTLd")]:
            P('pool', lambda e, nm=nm, d=d: e.dma_start(out=env[nm][:, :], in_=env[d][:, :]),
              writes=[env[nm]], dma=True, stream=f"c_{nm}")
        P('sp', lambda e: e.dma_start(out=env["ELEAF"][:, :], in_=hconsts["eleaf"].ap()),
          writes=[env["ELEAF"]], dma=True, stream="c_eleaf")
        P('sp', lambda e: e.dma_start(out=env["TRIU4"][:, :], in_=hconsts["triu4b"].ap()),
          writes=[env["TRIU4"]], dma=True, stream="c_triu4")
        P('pool', lambda e: e.dma_start(out=env["WOSB2"][:, :], in_=env["Wo"][:, :]),
          writes=[env["WOSB2"]], dma=True, stream="c_wo")
    for fc in range(NFC):
        P('pool', lambda e, fc=fc: e.dma_start(
            out=env["XBF"][:, fc * T:(fc + 1) * T], in_=env["xin"][fc * 128:(fc + 1) * 128, :]),
          writes=[(env["XBF"], fc)], dma=True, stream=f"xbf{fc}")
    if full:
        P('sp', lambda e: e.dma_start(out=env["SCR"][0:64, 0:241], in_=env["sprev"][0, :, :]),
          writes=[(env["SCR"], 'sp0')], dma=True, stream="sp0")
        P('sp', lambda e: e.dma_start(out=env["SCR"][0:64, 256:497], in_=env["sprev"][1, :, :]),
          writes=[(env["SCR"], 'sp1')], dma=True, stream="sp1")
        P('sp', lambda e: e.dma_start(out=env["SCR"][0:64, 512:753], in_=env["sprev"][2, :, :]),
          writes=[(env["SCR"], 'sp2')], dma=True, stream="sp2")
        P('dve', lambda e: e.tensor_tensor(env["MST"][:, 0:241], env["SCR"][0:64, 0:241],
                                           env["SCR"][0:64, 256:497], ALU.add),
          reads=[(env["SCR"], 'sp0'), (env["SCR"], 'sp1')], writes=[env["MST"]])
        P('dve', lambda e: e.tensor_tensor(env["MST"][:, 0:241], env["MST"][:, 0:241],
                                           env["SCR"][0:64, 512:753], ALU.add),
          reads=[env["MST"], (env["SCR"], 'sp2')], writes=[env["MST"]])
    else:
        P('dve', lambda e: e.memset(env["MST"][:, 0:256], 0.0), writes=[env["MST"]])
    P('dve', lambda e: e.tensor_copy(env["MSTB"][:, 0:241], env["MST"][:, 0:241]),
      reads=[env["MST"]], writes=[env["MSTB"]])
    P('dve', lambda e: e.memset(env["PS"][3][:, 0:512], 0.0), writes=[(env["PS"][3], 'cl')])
    P('dve', lambda e: e.memset(env["PS"][4][:, 0:512], 0.0), writes=[(env["PS"][4], 'cl')])


def make_hss_consts2(nc):
    c = {}
    el = np.zeros((4, 16 * 64), np.float32)
    for n in range(16):
        for j in range(4):
            el[j, n * 64 + 4 * n + j] = 1.0
    c["eleaf"] = nc.inline_tensor(bfc(el), "c_eleaf")
    mleaf, mlvl = hss_state_masks()
    msks = np.zeros((64, 241), np.float32)
    msks[:, 0:65] = mleaf
    msks[0:32, 65:241] = mlvl
    c["msks"] = nc.inline_tensor(bfc(msks), "c_msks")
    tri = np.triu(np.ones((128, 128), np.float32))
    c["triu4b"] = nc.inline_tensor(bfc(np.tile(tri, (1, 4))), "c_triu4b")
    return c


def build_hss_state_launch(T):
    nc = bass.Bass("TRN2", target_bir_lowering=False)
    cfg = dict(T=T)
    env = {}
    hss_dram_and_inputs(nc, env, T, full=False)
    env["stateout"] = nc.dram_tensor("stateout", [64, 241], F32, kind="ExternalOutput").ap()
    consts = make_consts(nc)
    hconsts = make_hss_consts2(nc)
    with ExitStack() as stack:
        alloc_hss_tiles(nc, stack, env, T, full=False)
        print("A1 SBUF remaining:", nc.sbuf_bytes_remaining)
        pr = Prog(nc)
        emit_hss_setup(pr, env, cfg, consts, hconsts, full=False)
        emit_hss_proj(pr, env, cfg)
        pr.fence()
        for ci in range(T // 128):
            emit_hss_chunk(pr, env, cfg, ci, statesonly=True)
        pr.op('sp', lambda e: e.dma_start(out=env["stateout"][:, :], in_=env["MST"][:, 0:241]),
              reads=[env["MST"]], dma=True, stream="out")
        pr.finish()
        with nc.Block() as block:
            pr.build(block)
    return nc


def build_hss_launch(T):
    nc = bass.Bass("TRN2", target_bir_lowering=False)
    cfg = dict(T=T)
    env = {}
    hss_dram_and_inputs(nc, env, T, full=True)
    env["xout"] = nc.dram_tensor("xout", [D, T], F32, kind="ExternalOutput").ap()
    consts = make_consts(nc)
    hconsts = make_hss_consts2(nc)
    with ExitStack() as stack:
        alloc_hss_tiles(nc, stack, env, T, full=True)
        print("A2 SBUF remaining:", nc.sbuf_bytes_remaining)
        pr = Prog(nc)
        emit_hss_setup(pr, env, cfg, consts, hconsts, full=True)
        emit_hss_proj(pr, env, cfg)
        pr.fence()
        emit_hss_compact(pr, env, cfg)
        pr.fence()
        for g2 in range(T // 256):
            emit_hss_wide_group(pr, env, cfg, g2)
            for ci in (g2 * 2, g2 * 2 + 1):
                emit_hss_chunk(pr, env, cfg, ci, statesonly=False)
        pr.fence()
        emit_hss_final(pr, env, cfg)
        pr.fence()
        emit_ln_sum(pr, env, cfg, env["xin"], env["ATT"], env["xout"],
                    b_deps=[(env["ATT"], (t5, dc)) for t5 in range(T // 512) for dc in range(NFC)])
        pr.finish()
        with nc.Block() as block:
            pr.build(block)
    return nc





_ALIAS_RULES = [
    (r"^c_.*$|^c2_.*$", lambda m: "cst"),
    (r"^xbf(\d+)$", lambda m: f"ld{m.group(1)}"),
    (r"^hw(\d+)$", lambda m: f"ld2{m.group(1)}"),
    (r"^w1_(\d+)$", lambda m: f"ld2{m.group(1)}"),
    (r"^w2_(\d+)$", lambda m: f"st{m.group(1)}"),
    (r"^pw(\d+)$", lambda m: f"ld{m.group(1)}"),
    (r"^wo(\d+)$", lambda m: f"ld2{m.group(1)}"),
    (r"^yacc(\d+)$", lambda m: f"acc{m.group(1)}"),
    (r"^att(\d+)$", lambda m: f"acc{m.group(1)}"),
    (r"^hatt(\d+)$", lambda m: f"acc{m.group(1)}"),
    (r"^w_WAK0$", lambda m: "st0"), (r"^w_WAK1$", lambda m: "st1"),
    (r"^w_WBV0$", lambda m: "st2"), (r"^w_WBV1$", lambda m: "st3"),
    (r"^w_WAQH0$", lambda m: "st4"), (r"^w_WAQH1$", lambda m: "st5"),
    (r"^w_WAQL0$", lambda m: "st6"), (r"^w_WAQL1$", lambda m: "st7"),
    (r"^w_WKL$", lambda m: "lnA"), (r"^w_WVL$", lambda m: "lnB"),
    (r"^w_WQL$", lambda m: "lnC"),
    (r"^lnx0$|^lnsum$", lambda m: "lnA"),
    (r"^lnx1$", lambda m: "lnB"),
    (r"^lny0$|^lnout0$", lambda m: "lnC"),
    (r"^lny1$|^lnout1$", lambda m: "lnD"),
    (r"^xg0$", lambda m: "misc2"), (r"^xg1$", lambda m: "misc3"),
    (r"^xbi(\d+)$", lambda m: f"acc{m.group(1)}"),
    (r"^xb(\d+)$", lambda m: f"acc{4 + int(m.group(1))}"),
    (r"^cc1i$|^cc2i$", lambda m: "ccin"),
    (r"^wsg$", lambda m: "misc2"), (r"^wsw$", lambda m: "misc3"),
    (r"^ldg2?$", lambda m: "misc1"),
    (r"^b1$", lambda m: "misc0"), (r"^b2$", lambda m: "misc1"),
    (r"^xg$", lambda m: "misc2"), (r"^wgt$", lambda m: "misc3"),
    (r"^wo2a$", lambda m: "misc0"), (r"^wo2b$", lambda m: "misc1"),
    (r"^win$", lambda m: "misc2"),
    (r"^bounceA$", lambda m: "bounceA"), (r"^bounceB$", lambda m: "bounceB"),
]

import re as _re

def _alias(s):
    for pat, fn in _ALIAS_RULES:
        m = _re.match(pat, s)
        if m:
            return fn(m)
    return s


def _scr_view(scr, off, w):
    return scr[0:128, off:off + w]


class Arena:
    def __init__(self, nc, stack, bf_cols, f32_cols):
        self.bft = stack.enter_context(nc.sbuf_tensor("ARENA_BF", [128, bf_cols], BF16))
        self.f3t = stack.enter_context(nc.sbuf_tensor("ARENA_F3", [128, f32_cols], F32))
        self.bf_cols = bf_cols
        self.f32_cols = f32_cols
        self.hwm_b = 0
        self.hwm_f = 0
        self.reset()

    def reset(self):
        self.bo = 0
        self.fo = 0

    def alloc(self, name, shape, dt=F32):
        p, cols = shape
        if dt == BF16:
            off = (self.bo + 15) // 16 * 16
            self.bo = off + cols
            self.hwm_b = max(self.hwm_b, self.bo)
            assert self.bo <= self.bf_cols, f"bf arena overflow at {name}: {self.bo}"
            return self.bft[0:p, off:off + cols]
        off = (self.fo + 15) // 16 * 16
        self.fo = off + cols
        self.hwm_f = max(self.hwm_f, self.fo)
        assert self.fo <= self.f32_cols, f"f32 arena overflow at {name}: {self.fo}"
        return self.f3t[0:p, off:off + cols]


def alloc_common(nc, stack, env):
    psb = lambda name, shape, dt=F32: stack.enter_context(nc.psum_tensor(name, shape, dt))
    env["PS"] = [psb(f"PS{i}", [128, 512]) for i in range(5)] + [None] + \
                [psb("PS6", [1, 512]), psb("PS7", [1, 512])]
    env["TPSB"] = psb("TPSB", [128, 1024], BF16)
    sb = lambda name, shape, dt=F32: stack.enter_context(nc.sbuf_tensor(name, shape, dt))
    env["OC32"] = sb("OC32", [128, 1], F32)
    env["OR32"] = sb("OR32", [1, 128], F32)
    env["ID32"] = sb("ID32", [128, 128], F32)
    env["IDB"] = sb("IDB", [128, 128], BF16)
    env["ONESB"] = sb("ONESB", [128, 1], BF16)
    env["RW"] = sb("RW", [1, 1024], F32)


def load_common_consts(pr, env, consts):
    P = pr.op
    for nm, h in [("OC32", "ones_col_f32"), ("OR32", "ones_row_f32"),
                  ("ID32", "id_f32"), ("IDB", "id_bf")]:
        P('sp', lambda e, nm=nm, h=h: e.dma_start(out=env[nm][:, :], in_=consts[h].ap()),
          writes=[env[nm]], dma=True, stream=f"c_{nm}")
    P('pool', lambda e: e.dma_start(
        out=env["ONESB"][:, 0:1], in_=consts["ones_row_bf"].ap().rearrange("a b -> b a")),
      writes=[env["ONESB"]], dma=True, stream="c_onesb")


def alloc_hss_arena(sb, env, T, full):
    env["XBF"] = sb("XBF", [128, NFC * T], BF16)
    env["QF"] = sb("QF", [64, T], BF16)
    env["KF"] = sb("KF", [64, T], BF16)
    env["VF"] = sb("VF", [64, T], BF16)
    env["HWSB"] = sb("HWSB", [128, NFC * DHD], BF16)
    env["EULB"] = sb("EULB", [64, NNR], BF16)
    env["EVRB"] = sb("EVRB", [64, NNR], BF16)
    env["EVLB"] = sb("EVLB", [64, NNR], BF16)
    env["AKTM"] = sb("AKTM", [128, NNR], BF16)
    env["BVTM"] = sb("BVTM", [128, NNR], BF16)
    env["KTM"] = sb("KTM", [128, 64], BF16)
    env["VTM"] = sb("VTM", [128, 64], BF16)
    env["MST"] = sb("MST", [64, 256], F32)
    env["MSTB"] = sb("MSTB", [64, 256], BF16)
    env["MSKS"] = sb("MSKS", [64, 256], BF16)
    env["SCR"] = sb("SCR", [128, 5120], F32)
    if full:
        GW = 256
        env["AQHC"] = sb("AQHC", [32, 4 * T], BF16)
        env["AQLC"] = sb("AQLC", [32, 4 * T], BF16)
        env["CROT"] = sb("CROT", [32, 512], BF16)
        for wname in ["WAK", "WBV", "WAQH", "WAQL"]:
            env[wname] = sb(wname, [8, 15 * GW], BF16)
        for wname in ["WKL", "WVL", "WQL"]:
            env[wname] = sb(wname, [4, 16 * GW], BF16)
        env["PMSC"] = sb("PMSC", [128, 47 * 128], BF16)
        env["UMS"] = sb("UMS", [128, 46 * 64], BF16)
        env["GEV"] = sb("GEV", [32, 8 * 128], BF16)
        env["YSEG"] = sb("YSEG", [65, T], F32)
        env["YN"] = sb("YN", [64, T], BF16)
        env["HEV"] = sb("HEV", [128, 512], F32)
        env["WOSB2"] = sb("WOSB2", [64, D], BF16)
        env["EULTW"] = sb("EULTW_sb", [8, 960], BF16)
        env["EURTW"] = sb("EURTW_sb", [8, 960], BF16)
        env["EULTL"] = sb("EULTL_sb", [32, 256], BF16)
        env["EURTL"] = sb("EURTL_sb", [32, 256], BF16)
        env["ELEAF"] = sb("ELEAF", [4, 16 * 64], BF16)
        env["TRIU4"] = sb("TRIU4", [128, 512], BF16)
        env["GATH"] = env["SCR"][0:64, 2048:2304]
        env["HWT"] = env["SCR"][0:64, 2304:2560]


def alloc_moe_arena(env, sb, T, cfg):
    TS = cfg["TS"]
    env["XBF"] = sb("XBF", [128, NFC * (T // 2)], BF16)
    env["W1SB"] = sb("W1SB", [128, 32768], BF16)
    env["W2SB"] = sb("W2SB", [128, 32768], BF16)
    env["HSB"] = sb("HSB", [128, NHC * TS], BF16)
    env["YEV"] = sb("YEV", [128, 4 * TS], F32)
    env["CB"] = sb("CBt", [128, TS], BF16)
    env["SCR"] = sb("SCR", [128, 5120], F32)
    env["WGT"] = sb("WGT", [128, NFC * 8], F32)
    env["GS"] = sb("GS", [128, 128], F32)
    env["CM"] = sb("CMtm", [128, 16], F32)
    env["CMFB"] = sb("CMFB", [8, T], BF16)
    env["OHR"] = sb("OHR", [8, 8 * 128], BF16)
    if cfg["has_bias"]:
        env["B1SB"] = sb("B1SB", [128, NHC], F32)
        env["B2SB"] = sb("B2SB", [8, D], BF16)


def alloc_ret_arena2(sb, env, T, tw, with_main):
    env["XN"] = sb("XN", [128, NFC * T], BF16)
    env["KT"] = sb("KT", [128, (tw // 128) * D], BF16)
    env["VT"] = sb("VT", [128, (tw // 128) * D], BF16)
    env["PWSB"] = sb("PWSB", [128, 2 * NFC * D], BF16)
    env["SAL"] = sb("SAL", [64, 1024], F32)
    env["SCR"] = sb("SCR", [128, 5120], F32)
    env["LN1W"] = sb("LN1W", [128, 8], F32)
    env["LN1B"] = sb("LN1B", [128, 8], F32)
    env["GCC"] = sb("GCC", [64, 1024], F32)
    env["DECKTM"] = sb("DECKTM", [128, 1024], BF16)
    if with_main:
        env["QDZE"] = sb("QDZE", [128, NFC * tw], BF16)
        env["QDZO"] = sb("QDZO", [128, NFC * tw], BF16)
        env["KD"] = sb("KD", [128, NFC * tw], BF16)
        env["GT"] = sb("GT", [128, (tw // 128) * D], BF16)
        env["WOSB"] = sb("WOSB", [128, NFC * D], BF16)
        env["SCHKB"] = sb("SCHKB", [128, (tw // CC) * 1024], BF16)
        env["PMS"] = sb("PMS", [128, NH * 128], BF16)
        env["OEV"] = env["SCR"].tensor.ap()[0:128, 1024:2048] if False else None
        env["OEV"] = _scr_view(env["SCR"], 0, 1024)
        env["SQT"] = _scr_view(env["SCR"], 1024, 1024)
        env["STT"] = sb("STT", [128, 64], F32)
        env["GTD"] = sb("GTD", [128, 1024], BF16)
        env["GFM"] = sb("GFM", [128, NFC * 512], BF16)
        env["HEV"] = _scr_view(env["SCR"], 2048, 1024)
        env["TRIU4"] = sb("TRIU4", [128, 512], BF16)
        env["DECQ"] = sb("DECQ", [128, 8 * 512], BF16)
        env["DECK"] = sb("DECK", [128, 8 * 512], BF16)
        env["GB2"] = env["SCR"][0:64, 0:1024]
        env["RWT"] = env["SCR"][0:64, 1024:2048]


def emit_wsum_state(pr, env, ccout, wdram, dst, gbuf, wbuf, width):
    P = pr.op
    P('dve', lambda e: e.memset(dst[:, 0:width], 0.0), writes=[dst])
    for r in range(8):
        P('sp', lambda e, r=r: e.dma_start(
            out=gbuf[:, 0:width], in_=ccout[r * 64:(r + 1) * 64, 0:width]),
          reads=[(ccout, 'd'), (gbuf, 'used')], writes=[gbuf], dma=True, stream="wsg")
        P('sp', lambda e, r=r: e.dma_start(out=wbuf[:, 0:width], in_=wdram[r, :, 0:width]),
          reads=[(wbuf, 'used')], writes=[wbuf], dma=True, stream="wsw")
        P('dve', lambda e: e.tensor_tensor(gbuf[:, 0:width], gbuf[:, 0:width],
                                           wbuf[:, 0:width], ALU.mult),
          reads=[gbuf, wbuf], writes=[gbuf, (gbuf, 'used'), (wbuf, 'used')])
        P('dve', lambda e: e.tensor_tensor(dst[:, 0:width], dst[:, 0:width],
                                           gbuf[:, 0:width], ALU.add),
          reads=[dst, gbuf], writes=[dst])


def build_fused(T):
    nc = bass.Bass("TRN2", target_bir_lowering=False)
    cfg = dict(T=T, TS=256)
    env = {}
    dram = lambda name, shape, dt=F32, kind="ExternalInput": \
        nc.dram_tensor(name, shape, dt, kind=kind).ap()
    env["x0"] = dram("x0", [D, T])
    env["out"] = dram("xfinal", [D, T], kind="ExternalOutput")
    for w in ["Wq", "Wk", "Wv"]:
        env[w] = dram(w, [D, DHD])
    env["Wo"] = dram("Wo", [DHD, D])
    env["EULd"] = dram("EULs", [64, NNR])
    env["EVRd"] = dram("EVRs", [64, NNR])
    env["EVLd"] = dram("EVLs", [64, NNR])
    env["EULTWd"] = dram("EULTW", [8, 960])
    env["EURTWd"] = dram("EURTW", [8, 960])
    env["EULTLd"] = dram("EULTL", [32, 256])
    env["EURTLd"] = dram("EURTL", [32, 256])
    env["hssw"] = dram("hssw", [8, 64, 256])
    env["retw"] = dram("retw", [8, 64, 1024])
    for L in (0, 1):
        env[f"m{L}_W1"] = dram(f"m{L}_W1", [NE, D, HID])
        env[f"m{L}_W2"] = dram(f"m{L}_W2", [NE, HID, D])
        env[f"m{L}_wgateT"] = dram(f"m{L}_wgateT", [D, NE])
    env["b1rd"] = dram("b1r", [NE, 128, NHC])
    env["b2d"] = dram("b2", [NE, D])
    for w in ["wq", "wk", "wv", "wg", "wo"]:
        env["r_" + w] = dram("r_" + w, [D, D])
    for nm in ["ln1wd", "ln1bd", "ln2w", "ln2b", "boutr"]:
        env[nm] = dram(nm, [128, 8])
    env["binr"] = dram("binr", [2, 128, NCC])
    env["w_in_sw"] = dram("w_in_sw", [D, 5460])
    env["w_out_sw"] = dram("w_out_sw", [2730, D])
    env["X1"] = dram("X1", [D, T], kind="Internal")
    env["X2"] = dram("X2", [D, T], kind="Internal")
    env["HATT"] = dram("hatt_s", [D, T], kind="Internal")
    env["YACC"] = dram("yacc_s", [D, T], kind="Internal")
    env["XA"] = dram("xa_s", [D, T], kind="Internal")
    env["XB"] = dram("xb_s", [D, T], kind="Internal")
    env["X3"] = dram("X3", [D, T], kind="Internal")
    env["ATT"] = dram("att_s", [D, T], kind="Internal")
    env["FBD"] = nc.dram_tensor("fbd_s", [4, NNR, T], BF16, kind="Internal").ap()
    env["FBL"] = nc.dram_tensor("fbl_s", [3, 64, T], BF16, kind="Internal").ap()
    env["cc1_in"] = dram("cc1_in", [64, 256], kind="Internal")
    env["cc1_out"] = dram("cc1_out", [512, 256], kind="Internal")
    env["cc2_in"] = dram("cc2_in", [64, 1024], kind="Internal")
    env["cc2_out"] = dram("cc2_out", [512, 1024], kind="Internal")

    consts = make_consts(nc)
    rconsts = make_ret_consts(nc)
    hconsts = make_hss_consts2(nc)

    with ExitStack() as stack:
        alloc_common(nc, stack, env)
        arena = Arena(nc, stack, bf_cols=87360, f32_cols=8000)
        pr = Prog(nc)
        pr.alias = _alias
        P = pr.op
        load_common_consts(pr, dict(env), consts)

        arena.reset()
        sb = arena.alloc
        alloc_hss_arena(sb, env, T, full=True)
        env["xin"] = env["x0"]
        emit_hss_setup(pr, dict(env), cfg, consts, hconsts, full=False)
        emit_hss_proj(pr, dict(env), cfg)
        pr.fence()
        for ci in range(T // 128):
            emit_hss_chunk(pr, dict(env), cfg, ci, statesonly=True)
        _mstA, _mstbA = env["MST"], env["MSTB"]
        P('sp', lambda e, t=_mstA: e.dma_start(out=env["cc1_in"][:, 0:241], in_=t[:, 0:241]),
          reads=[_mstA], writes=[(env["cc1_in"], 'd')], dma=True, stream="cc1i")
        pr.collective(lambda e: e.collective_compute(
            "AllGather", ALU.bypass, replica_groups=[list(range(8))],
            ins=[env["cc1_in"].opt()], outs=[env["cc1_out"].opt()]),
            reads=[(env["cc1_in"], 'd')], writes=[(env["cc1_out"], 'd')], tag="cc1")
        for nm, d in [("EULTW", "EULTWd"), ("EURTW", "EURTWd"),
                      ("EULTL", "EULTLd"), ("EURTL", "EURTLd")]:
            P('pool', lambda e, nm=nm, d=d: e.dma_start(out=env[nm][:, :], in_=env[d][:, :]),
              writes=[env[nm]], dma=True, stream=f"c_{nm}")
        _elf, _tri, _wos = env["ELEAF"], env["TRIU4"], env["WOSB2"]
        P('sp', lambda e, t=_elf: e.dma_start(out=t[:, :], in_=hconsts["eleaf"].ap()),
          writes=[_elf], dma=True, stream="c_eleaf")
        P('sp', lambda e, t=_tri: e.dma_start(out=t[:, :], in_=hconsts["triu4b"].ap()),
          writes=[_tri], dma=True, stream="c_triu4")
        P('pool', lambda e, t=_wos: e.dma_start(out=t[:, :], in_=env["Wo"][:, :]),
          writes=[_wos], dma=True, stream="c_wo")
        emit_hss_compact(pr, dict(env), cfg)
        pr.fence()
        emit_wsum_state(pr, env, env["cc1_out"], env["hssw"], env["MST"],
                        env["GATH"], env["HWT"], 256)
        P('dve', lambda e, a=_mstbA, b=_mstA: e.tensor_copy(a[:, 0:241], b[:, 0:241]),
          reads=[_mstA], writes=[_mstbA])
        P('dve', lambda e: e.memset(env["PS"][3][:, 0:512], 0.0), writes=[(env["PS"][3], 'cl')])
        P('dve', lambda e: e.memset(env["PS"][4][:, 0:512], 0.0), writes=[(env["PS"][4], 'cl')])
        for g2 in range(T // 256):
            emit_hss_wide_group(pr, dict(env), cfg, g2)
            for ci in (g2 * 2, g2 * 2 + 1):
                emit_hss_chunk(pr, dict(env), cfg, ci, statesonly=False)
        pr.fence()
        emit_hss_final(pr, dict(env), cfg)
        pr.fence()
        emit_ln_sum(pr, dict(env), cfg, env["x0"], env["ATT"], env["X1"],
                    b_deps=[(env["ATT"], (t5, dc)) for t5 in range(T // 512) for dc in range(NFC)])
        pr.fence()

        arena.reset()
        cfgB = dict(T=T, TS=256, gate_mode="l0", has_act=False, has_bias=False)
        env["xin"] = env["X1"]
        env["xout"] = env["X2"]
        env["W1"] = env["m0_W1"]; env["W2"] = env["m0_W2"]; env["wgateT"] = env["m0_wgateT"]
        alloc_moe_arena(env, sb, T, cfgB)
        _ohrB = env["OHR"]
        P('sp', lambda e, t=_ohrB: e.dma_start(out=t[:, :], in_=consts["ohr_bf"].ap()),
          writes=[_ohrB], dma=True, stream="c_ohr")
        emit_moe(pr, dict(env), cfgB)
        pr.fence()

        arena.reset()
        env["xin"] = env["X2"]
        env["ln1w"] = env["ln1wd"]; env["ln1b"] = env["ln1bd"]
        alloc_ret_arena2(sb, env, T, 512, with_main=False)
        load_ret_consts(pr, dict(env), consts, rconsts, with_main=False)
        emit_xn1(pr, dict(env), cfg, out_sbuf=(env["XN"], T))
        pr.fence()
        _salC1 = env["SAL"]
        P('dve', lambda e, t=_salC1: e.memset(t[:, :], 0.0), writes=[_salC1])
        for q in range(T // 512):
            emit_proj_tm(pr, dict(env), cfg, env["r_wk"], env["XN"], env["KT"], q * 512, 512,
                         dec_const=env["DECKTM"])
            emit_proj_tm(pr, dict(env), cfg, env["r_wv"], env["XN"], env["VT"], q * 512, 512)
            emit_ret_state_chunks(pr, dict(env), cfg, env["KT"], env["VT"], q * 4, 4, save=False)
        P('sp', lambda e, t=_salC1: e.dma_start(out=env["cc2_in"][:, :], in_=t[:, :]),
          reads=[_salC1], writes=[(env["cc2_in"], 'd')], dma=True, stream="cc2i")
        pr.fence()
        pr.collective(lambda e: e.collective_compute(
            "AllGather", ALU.bypass, replica_groups=[list(range(8))],
            ins=[env["cc2_in"].opt()], outs=[env["cc2_out"].opt()]),
            reads=[(env["cc2_in"], 'd')], writes=[(env["cc2_out"], 'd')], tag="cc2")

        arena.reset()
        env["xin"] = env["X2"]
        alloc_ret_arena2(sb, env, T, 512, with_main=True)
        load_ret_consts(pr, dict(env), consts, rconsts, with_main=True)
        for fc in range(NFC):
            P('pool', lambda e, fc=fc: e.dma_start(
                out=env["WOSB"][:, fc * D:(fc + 1) * D],
                in_=env["r_wo"][fc * 128:(fc + 1) * 128, :]),
              writes=[(env["WOSB"], fc)], dma=True, stream=f"wo{fc}")
        emit_xn1(pr, dict(env), cfg, out_sbuf=(env["XN"], T))
        pr.fence()
        emit_wsum_state(pr, env, env["cc2_out"], env["retw"], env["SAL"],
                        env["GB2"], env["RWT"], 1024)
        pr.fence()
        _qze, _qzo = env["QDZE"], env["QDZO"]
        P('dve', lambda e, t=_qze: e.memset(t[64:128, :], 0.0), writes=[(_qze, 'z')])
        P('dve', lambda e, t=_qzo: e.memset(t[0:64, :], 0.0), writes=[(_qzo, 'z')])
        for q in range(T // 512):
            emit_proj_fm(pr, dict(env), cfg, env["r_wq"], env["XN"], None, q * 512, 512,
                         dec_const=env["DECQ"], qdz=(env["QDZE"], env["QDZO"]))
            emit_proj_fm(pr, dict(env), cfg, env["r_wk"], env["XN"], env["KD"], q * 512, 512,
                         dec_const=env["DECK"])
            emit_proj_tm(pr, dict(env), cfg, env["r_wk"], env["XN"], env["KT"], q * 512, 512,
                         dec_const=env["DECKTM"])
            emit_proj_tm(pr, dict(env), cfg, env["r_wv"], env["XN"], env["VT"], q * 512, 512)
            emit_proj_tm(pr, dict(env), cfg, env["r_wg"], env["XN"], env["GT"], q * 512, 512,
                         act=AF.Silu)
            emit_ret_state_chunks(pr, dict(env), cfg, env["KT"], env["VT"], q * 4, 4, save=True)
            emit_ret_main_half(pr, dict(env), cfg, env["QDZE"], env["KD"], env["KT"], env["VT"],
                               env["GT"], q * 512, 512)
        pr.fence()

        arena.reset()
        env["XN2"] = sb("XN2", [128, NFC * T], BF16)
        env["WIN"] = sb("WIN", [128, 2 * NFC * 128], BF16)
        env["ASB"] = sb("ASB", [128, NCC * 512], BF16)
        env["BSB"] = sb("BSB", [128, NCC * 512], BF16)
        env["WO2"] = sb("WO2", [128, NCC * D], BF16)
        env["SWEV"] = sb("SWEV", [128, 1024], F32)
        env["BIN"] = sb("BIN", [128, 2 * NCC], F32)
        env["BOUT"] = sb("BOUT", [128, 8], F32)
        env["SCR"] = sb("SCR", [128, 5120], F32)
        env["LN2W"] = sb("LN2W", [128, 8], F32)
        env["LN2B"] = sb("LN2B", [128, 8], F32)
        for nm, srcn in [("LN2W", "ln2w"), ("LN2B", "ln2b"), ("BOUT", "boutr")]:
            P('sp', lambda e, nm=nm, srcn=srcn: e.dma_start(
                out=env[nm][:, :], in_=env[srcn][:, :]),
              writes=[env[nm]], dma=True, stream=f"c2_{nm}")
        P('sp', lambda e: e.dma_start(out=env["BIN"][:, 0:NCC], in_=env["binr"][0, :, :]),
          writes=[(env["BIN"], 0)], dma=True, stream="c_bin0")
        P('sp', lambda e: e.dma_start(out=env["BIN"][:, NCC:2 * NCC], in_=env["binr"][1, :, :]),
          writes=[(env["BIN"], 1)], dma=True, stream="c_bin1")
        pr.op('dve', lambda e: e.engine_nop(), reads=[(env["BIN"], 0), (env["BIN"], 1)],
              writes=[env["BIN"]])
        pr.fence()
        hdeps = [(env["HATT"], (t0, 0, dc)) for t0 in range(0, T, 512) for dc in range(NFC)]
        emit_ln_sum(pr, dict(env), cfg, env["X2"], env["HATT"], None,
                    affine=(env["LN2W"], env["LN2B"]), out_sbuf=(env["XN2"], T),
                    sum_dram=env["XA"], b_deps=hdeps)
        pr.fence()
        for t5 in range(T // 512):
            P('sp', lambda e, t5=t5: e.dma_start(
                out=env["XB"][:, t5 * 512:(t5 + 1) * 512],
                in_=env["XA"][:, t5 * 512:(t5 + 1) * 512]),
              reads=[(env["XA"], ('sum', g)) for g in range(t5 * 4, t5 * 4 + 4)],
              writes=[(env["XB"], ('init', t5))], dma=True, stream=f"xbi{t5}")
        env["xin"] = env["X2"]
        emit_swiglu(pr, dict(env), cfg)
        pr.fence()
        emit_ln_sum(pr, dict(env), cfg, env["X2"], env["XB"], env["X3"],
                    b_deps=[(env["XB"], t5) for t5 in range(T // 512)])
        pr.fence()

        arena.reset()
        cfgD = dict(T=T, TS=256, gate_mode="l1", has_act=True, has_bias=True)
        env["xin"] = env["X3"]
        env["xout"] = env["out"]
        env["W1"] = env["m1_W1"]; env["W2"] = env["m1_W2"]; env["wgateT"] = env["m1_wgateT"]
        env["b1r"] = env["b1rd"]; env["b2"] = env["b2d"]
        alloc_moe_arena(env, sb, T, cfgD)
        _ohrD = env["OHR"]
        P('sp', lambda e, t=_ohrD: e.dma_start(out=t[:, :], in_=consts["ohr_bf"].ap()),
          writes=[_ohrD], dma=True, stream="c_ohr")
        emit_moe(pr, dict(env), cfgD)
        pr.finish()
        print("FUSED arena hwm: bf", arena.hwm_b, "f32", arena.hwm_f,
              "sbuf remaining:", nc.sbuf_bytes_remaining)
        with nc.Block() as block:
            pr.build(block)
    return nc



from concourse.bass_utils import run_bass_kernel_spmd

TSEG = 2048
NCORES = 8
_prog_cache = {}


def _ct(a):
    return np.ascontiguousarray(np.asarray(a, dtype=np.float32))


def _host_prep(inp):
    T = TSEG
    UL = [np.asarray(inp[f"L0_attn_UL{l}"]) for l in range(4)]
    VR = [np.asarray(inp[f"L0_attn_VR{l}"]) for l in range(4)]
    UR = [np.asarray(inp[f"L0_attn_UR{l}"]) for l in range(4)]
    VL = [np.asarray(inp[f"L0_attn_VL{l}"]) for l in range(4)]
    emb = hss_host_embeds(UL, VR, UR, VL)
    EULT, EURT = emb["EULT"], emb["EURT"]
    EULTW = np.zeros((8, 960), np.float32)
    EURTW = np.zeros((8, 960), np.float32)
    EULTL = np.zeros((32, 256), np.float32)
    EURTL = np.zeros((32, 256), np.float32)
    for l in range(4):
        nn, r, off, woff = LVL_NN[l], LVL_R[l], LVL_OFF[l], WOFF_L[l]
        nnr = nn * r
        EULTL[0:nnr, l * 64:(l + 1) * 64] = EULT[off:off + nnr, :]
        EURTL[0:nnr, l * 64:(l + 1) * 64] = EURT[off:off + nnr, :]
        for n in range(nn):
            EULTW[0:r, (woff + n) * 64:(woff + n + 1) * 64] = EULT[off + n * r: off + (n + 1) * r, :]
            EURTW[0:r, (woff + n) * 64:(woff + n + 1) * 64] = EURT[off + n * r: off + (n + 1) * r, :]
    rshp = lambda v: _ct(np.asarray(v).reshape(8, 128).T)
    b_in = np.asarray(inp["L1_attn_b_in"])
    bin_a = np.zeros((128, NCC), np.float32)
    bin_b = np.zeros((128, NCC), np.float32)
    for cc in range(NCC):
        cw = LASTW if cc == NCC - 1 else 128
        bin_a[:cw, cc] = b_in[cc * 128: cc * 128 + cw]
        bin_b[:cw, cc] = b_in[2730 + cc * 128: 2730 + cc * 128 + cw]
    shared = {
        "Wq": _ct(inp["L0_attn_Wq"]), "Wk": _ct(inp["L0_attn_Wk"]),
        "Wv": _ct(inp["L0_attn_Wv"]), "Wo": _ct(inp["L0_attn_Wo"]),
        "EULs": emb["EUL"], "EVRs": emb["EVR"], "EVLs": emb["EVL"],
        "EULTW": EULTW, "EURTW": EURTW, "EULTL": EULTL, "EURTL": EURTL,
        "m0_W1": _ct(np.asarray(inp["L0_ffn_w_in"]).transpose(0, 2, 1)),
        "m0_W2": _ct(np.asarray(inp["L0_ffn_w_out"]).transpose(0, 2, 1)),
        "m0_wgateT": _ct(np.asarray(inp["L0_ffn_w_gate"]).T),
        "m1_W1": _ct(inp["L1_ffn_W1"]), "m1_W2": _ct(inp["L1_ffn_W2"]),
        "m1_wgateT": _ct(np.asarray(inp["L1_ffn_w_gate"]).T),
        "b1r": _ct(np.asarray(inp["L1_ffn_b1"]).reshape(8, 32, 128).transpose(0, 2, 1)),
        "b2": _ct(inp["L1_ffn_b2"]),
        "r_wq": _ct(inp["L1_attn_wq"]), "r_wk": _ct(inp["L1_attn_wk"]),
        "r_wv": _ct(inp["L1_attn_wv"]), "r_wg": _ct(inp["L1_attn_wg"]),
        "r_wo": _ct(inp["L1_attn_wo"]),
        "ln1wd": rshp(inp["L1_attn_ln1_w"]), "ln1bd": rshp(inp["L1_attn_ln1_b"]),
        "ln2w": rshp(inp["L1_attn_ln2_w"]), "ln2b": rshp(inp["L1_attn_ln2_b"]),
        "binr": np.stack([bin_a, bin_b]),
        "boutr": rshp(inp["L1_attn_b_out"]),
        "w_in_sw": _ct(inp["L1_attn_w_in"]), "w_out_sw": _ct(inp["L1_attn_w_out"]),
    }
    hsswc, retwc = [], []
    for c in range(NCORES):
        b, s = c // 4, c % 4
        hw = np.zeros((8, 64, 256), np.float32)
        rw = np.zeros((8, 64, 1024), np.float32)
        for j in range(s):
            hw[b * 4 + j] = 1.0
            for h in range(NH):
                rw[b * 4 + j, :, h * 64:(h + 1) * 64] = \
                    np.float32(GAMMAS[h]) ** (T * (s - 1 - j))
        hsswc.append(hw)
        retwc.append(rw)
    return shared, hsswc, retwc


def kernel(**inp):
    T = TSEG
    x = np.asarray(inp["L0_attn_x"])
    B, S, Dm = x.shape
    nseg = S // T
    shared, hsswc, retwc = _host_prep(inp)
    if "fused" not in _prog_cache:
        _prog_cache["fused"] = build_fused(T)
    nc = _prog_cache["fused"]
    in_maps = []
    for c in range(NCORES):
        im = dict(shared)
        im["x0"] = np.ascontiguousarray(x[c // nseg, (c % nseg) * T:(c % nseg + 1) * T, :].T)
        im["hssw"] = hsswc[c]
        im["retw"] = retwc[c]
        in_maps.append(im)
    res = run_bass_kernel_spmd(nc, in_maps, core_ids=list(range(NCORES)))
    out = np.empty((B, S, Dm), np.float32)
    for c in range(NCORES):
        out[c // nseg, (c % nseg) * T:(c % nseg + 1) * T, :] = res.results[c]["xfinal"].T
    return out
